# version 01
# baseline (speedup 1.0000x reference)
import numpy as np
import concourse.bass as bass
import concourse.mybir as mybir
from concourse.bass_utils import run_bass_kernel_spmd

DT = mybir.dt
F32, BF16, I32 = DT.float32, DT.bfloat16, DT.int32
AF = mybir.ActivationFunctionType
ALU = mybir.AluOpType
AX = mybir.AxisListType

D = 1024
T = 2048
DBG_STOP = None
DBG_DUMP = False
DBG_SEL_OUT = False
DBG_SCR_OUT = ()
DBG_L1 = None
DBG_L1_NPAIR = 4
DBG_L1_NCHUNK = 64
DBG_L1_STEPS = ("g", "L", "newton", "out", "state")
NRC = D // 128
QHI = 127.49999
QLO = -128.49999


def pack_w(w):
    w = np.asarray(w, np.float32)
    K, M = w.shape
    Kp = (K + 127) // 128 * 128
    Mp = (M + 127) // 128 * 128
    wp = np.zeros((Kp, Mp), np.float32)
    wp[:K, :M] = w
    return np.ascontiguousarray(wp.reshape(Kp, Mp // 128, 128).transpose(1, 0, 2))


class Smalls:
    def __init__(self):
        self.cols = []
        self.off = {}
        self.n = 0

    def add(self, name, arr):
        a = np.asarray(arr, np.float32)
        assert a.shape[0] == 128
        self.off[name] = (self.n, a.shape[1])
        self.cols.append(a)
        self.n += a.shape[1]

    def addb(self, name, vec, nm=None):
        v = np.asarray(vec, np.float32).reshape(-1)
        nm_ = (len(v) + 127) // 128 if nm is None else nm
        bp = np.zeros((nm_ * 128,), np.float32)
        bp[:len(v)] = v
        self.add(name, np.ascontiguousarray(bp.reshape(nm_, 128).T))

    def array(self):
        return np.ascontiguousarray(np.concatenate(self.cols, axis=1))


def rope_tables():
    inv = 1.0 / 10000.0 ** (np.arange(0, 64, 2, dtype=np.float64) / 64.0)
    t = np.arange(T, dtype=np.float64)
    fr = np.outer(inv, t)
    cos = np.cos(fr); sin = np.sin(fr)
    cc = np.zeros((128, T), np.float64)
    ss = np.zeros((128, T), np.float64)
    for g in range(2):
        r = g * 64
        cc[r:r+32] = cos; cc[r+32:r+64] = cos
        ss[r:r+32] = -sin; ss[r+32:r+64] = sin
    return cc.astype(np.float32), ss.astype(np.float32)


def host_prep(ii):
    import ml_dtypes
    H = {}
    SM = Smalls()
    H["c_ident"] = np.eye(128, dtype=np.float32)
    H["c_identb"] = np.eye(128, dtype=np.float32).astype(ml_dtypes.bfloat16)
    H["c_ones"] = np.ones((128, 128), np.float32)
    ob = np.zeros((128, 2), np.float32)
    ob[0:64, 0] = 1.0; ob[64:128, 1] = 1.0
    SM.add("ones64bd", ob)
    pswap = np.zeros((128, 128), np.float32)
    for g in range(2):
        r = g * 64
        pswap[r:r+32, r+32:r+64] = np.eye(32)
        pswap[r+32:r+64, r:r+32] = np.eye(32)
    H["c_pswapb"] = pswap.astype(ml_dtypes.bfloat16)
    cc, ss = rope_tables()
    H["c_ropecc"] = cc
    H["c_ropess"] = ss
    m2 = np.ones((128, 128), np.float32)
    m2[:64, -64:] = 0.0
    m2[-64:, :64] = 0.0
    H["c_l2mask"] = m2
    C1 = 32
    st = np.tril(np.ones((C1, C1), np.float32), -1)
    blk = np.zeros((64, 64), np.float32)
    blk[:C1, :C1] = st; blk[C1:, C1:] = st
    H["c_l1stril"] = blk
    tu = np.triu(np.ones((C1, C1), np.float32), 0)
    blk2 = np.zeros((64, 64), np.float32)
    blk2[:C1, :C1] = tu; blk2[C1:, C1:] = tu
    H["c_l1triu"] = blk2
    mA = np.zeros((128, 1), np.float32); mA[0:64] = 1.0
    mB = np.zeros((128, 1), np.float32); mB[64:128] = 1.0
    SM.add("l1_mA", mA)
    SM.add("l1_mB", mB)
    bd2 = np.zeros((2, 128), np.float32)
    bd2[0, 0:64] = 1.0; bd2[1, 64:128] = 1.0
    H["c_bd2"] = bd2
    bd2a = np.zeros((2, 128), np.float32); bd2a[0, :] = 1.0
    bd2b = np.zeros((2, 128), np.float32); bd2b[1, :] = 1.0
    H["c_bd2A"] = bd2a
    H["c_bd2B"] = bd2b
    for nm_, v_ in (("k_eps5", 1e-5), ("k_eps6", 1e-6), ("k_eps3", 1e-3), ("k_half", 0.5)):
        SM.add(nm_, np.full((128, 1), v_, np.float32))

    H["l0_wqkv"] = pack_w(ii["L0_attn_Wqkv"].T)
    SM.addb("l0_bqkv", ii["L0_attn_bqkv"])
    H["l0_wout"] = pack_w(ii["L0_attn_Wout"].T)
    SM.addb("l0_bout", ii["L0_attn_bout"])
    H["l0_w1"] = pack_w(ii["L0_ffn_w1"].T)
    for j in range(3):
        SM.addb(f"l0_convw{j}", ii["L0_ffn_conv_w"][:, 0, j])
    SM.addb("l0_convb", ii["L0_ffn_conv_b"])
    H["l0_w2"] = pack_w(ii["L0_ffn_w2"].T)

    SM.addb("l1_wnorm", ii["L1_attn_w_norm_in"])
    for j in range(4):
        SM.addb(f"l1_convw{j}", ii["L1_attn_conv_w"][:, j])
    SM.addb("l1_convb", ii["L1_attn_conv_b"])
    wproj = np.zeros((1024, 22 * 128), np.float32)
    wproj[:, 0:2560] = np.concatenate([ii["L1_attn_wq"], ii["L1_attn_wk"], ii["L1_attn_wv"],
                                       ii["L1_attn_w_spike"]], axis=1)
    wproj[:, 2560:2624] = ii["L1_attn_w_au"]
    wproj[:, 2688:2696] = ii["L1_attn_w_beta"]
    H["l1_wproj"] = pack_w(wproj)
    bproj = np.zeros((22 * 128,), np.float32)
    bproj[2560:2624] = ii["L1_attn_b_au"]
    bproj[2688:2696] = ii["L1_attn_b_beta"]
    SM.addb("l1_bproj", bproj, nm=22)
    H["l1_wad"] = pack_w(ii["L1_attn_w_ad"])
    SM.addb("l1_bad", ii["L1_attn_b_ad"])
    SM.addb("l1_aspk", ii["L1_attn_alpha_spike"].reshape(-1))
    SM.addb("l1_bspk", ii["L1_attn_beta_spike"].reshape(-1))
    SM.addb("l1_whn", ii["L1_attn_w_hn"].reshape(-1))
    H["l1_wu1"] = pack_w(ii["L1_attn_w_u1"])
    SM.addb("l1_bu1", ii["L1_attn_b_u1"])
    H["l1_wu2"] = pack_w(ii["L1_attn_w_u2"])
    SM.addb("l1_bu2", ii["L1_attn_b_u2"])
    H["l1_wout"] = pack_w(ii["L1_attn_w_out"])
    SM.addb("l1_wffn", ii["L1_attn_w_ffn"])
    H["l1_ff13"] = pack_w(np.concatenate([ii["L1_attn_w_ff1"], ii["L1_attn_w_ff3"]], axis=1))
    H["l1_ff2"] = pack_w(ii["L1_attn_w_ff2"])

    H["l1f_gate"] = pack_w(ii["L1_ffn_gate_w"].T)
    for e in range(8):
        H[f"l1f_w13_{e}"] = pack_w(np.concatenate([ii["L1_ffn_w1"][e], ii["L1_ffn_w3"][e]], axis=1))
        H[f"l1f_w2_{e}"] = pack_w(ii["L1_ffn_w2"][e])
        SM.addb(f"l1f_b13_{e}", np.concatenate([ii["L1_ffn_b1"][e], ii["L1_ffn_b3"][e]]))
        SM.addb(f"l1f_b2_{e}", ii["L1_ffn_b2"][e])
    H["l1f_sw13"] = pack_w(np.concatenate([ii["L1_ffn_sw1"], ii["L1_ffn_sw3"]], axis=1))
    SM.addb("l1f_sb13", np.concatenate([ii["L1_ffn_sb1"], ii["L1_ffn_sb3"]]))
    H["l1f_sw2"] = pack_w(ii["L1_ffn_sw2"])
    SM.addb("l1f_sb2", ii["L1_ffn_sb2"])

    perm = np.zeros((3072,), np.int64)
    i = 0
    for s in range(3):
        for h in range(16):
            for dd in range(64):
                perm[i] = h * 192 + s * 64 + dd
                i += 1
    H["l2_wqkv"] = pack_w(ii["L2_attn_w_qkv"][:, perm])
    SM.addb("l2_bqkv", ii["L2_attn_b_qkv"][perm])
    H["l2_wo"] = pack_w(ii["L2_attn_w_o"])
    SM.addb("l2_bo", ii["L2_attn_b_o"])
    SM.addb("l2_ln1w", ii["L2_ffn_ln1_w"])
    SM.addb("l2_ln1b", ii["L2_ffn_ln1_b"])
    H["l2_w1"] = pack_w(ii["L2_ffn_w1"].T)
    SM.addb("l2_b1", ii["L2_ffn_b1"])
    SM.addb("l2_ln2w", ii["L2_ffn_ln2_w"])
    SM.addb("l2_ln2b", ii["L2_ffn_ln2_b"])
    H["l2_w2"] = pack_w(ii["L2_ffn_w2"].T)
    SM.addb("l2_b2", ii["L2_ffn_b2"])

    H["l3_wqkv"] = pack_w(ii["L3_attn_Wqkv"].T)
    SM.addb("l3_bqkv", ii["L3_attn_bqkv"])
    H["l3_wproj"] = pack_w(ii["L3_attn_Wproj"].T)
    SM.addb("l3_bproj", ii["L3_attn_bproj"])
    SM.add("l3_scale", np.full((128, 1), np.float32(np.asarray(ii["L3_attn_scale"]))))
    H["l3_wg"] = pack_w(ii["L3_ffn_Wg"].T)
    for e in range(8):
        H[f"l3_w1_{e}"] = pack_w(ii["L3_ffn_W1"][e])
        H[f"l3_w2_{e}"] = pack_w(ii["L3_ffn_W2"][e])
        SM.addb(f"l3_b1_{e}", ii["L3_ffn_b1"][e])
        SM.addb(f"l3_b2_{e}", ii["L3_ffn_b2"][e])
        SM.addb(f"l3_lns_{e}", ii["L3_ffn_ln_scale"][e])
        SM.addb(f"l3_lnb_{e}", ii["L3_ffn_ln_bias"][e])
    H["c_smalls"] = SM.array()
    return H, SM



ENGS = ("sync", "vector", "scalar", "tensor", "gpsimd")


ALL_SEMS = []


class Sem:
    def __init__(self, nc, name):
        self.h = nc.alloc_semaphore(name)
        self.n = 0
        ALL_SEMS.append(self.h)


class Buf:
    def __init__(self, t, off, cols, dtype):
        self.t, self.off, self.cols, self.dtype = t, off, cols, dtype

    def ap(self, r0=0, r1=128, c0=0, c1=None):
        c1 = self.cols if c1 is None else c1
        return self.t[r0:r1, self.off + c0:self.off + c1]


class Arena:
    def __init__(self, nc, name, cols, dtype):
        self.t = nc.alloc_sbuf_tensor(name, [128, cols], dtype)
        self.cols = cols
        self.off = 0
        self.peak = 0
        self.dtype = dtype

    def alloc(self, cols, tag=""):
        cols = (cols + 1) & ~1
        assert self.off + cols <= self.cols, f"arena {tag}: {self.off}+{cols}>{self.cols}"
        b = Buf(self.t, self.off, cols, self.dtype)
        self.off += cols
        self.peak = max(self.peak, self.off)
        return b

    def mark(self):
        return self.off

    def reset(self, m):
        self.off = m


class P:
    def __init__(self, nc, sb):
        ALL_SEMS.clear()
        self.nc = nc
        self.sb = sb
        self.ops = {e: [] for e in ENGS}
        self.sems = {e: Sem(nc, f"s_{e}") for e in ENGS}
        self.sem_dma = Sem(nc, "s_dma")
        self.fa = Arena(nc, "fa", 32800, F32)
        self.ba = Arena(nc, "ba", 34000, BF16)
        self.ia = Arena(nc, "ia", 512, I32)
        self.ps = [nc.alloc_psum_tensor(f"ps{i}", [128, 512], F32) for i in range(7)]
        self.psb = nc.alloc_psum_tensor("psb", [128, 1024], BF16)
        self.tiny = nc.alloc_sbuf_tensor("tinybar", [128, 8], F32)

    def small(self, name):
        c0, n = self.sb["_smoff"][name]
        return self.sb["sb_smalls"][0:128, c0:c0 + n]

    def scol(self, name, j=0):
        c0, n = self.sb["_smoff"][name]
        assert j < n, (name, j, n)
        return self.sb["sb_smalls"][0:128, c0 + j:c0 + j + 1]

    def raw(self, eng, fn):
        self.ops[eng].append(fn)

    def op(self, eng, fn, inc=None, k=1):
        if inc is not None:
            inc.n += k
            self.raw(eng, lambda e, fn=fn, h=inc.h, k=k: fn(e).then_inc(h, k))
        else:
            self.raw(eng, lambda e, fn=fn: fn(e))

    def v(self, fn, inc=False):
        self.op("vector", fn, inc=self.sems["vector"] if inc else None)

    def s(self, fn, inc=False):
        self.op("scalar", fn, inc=self.sems["scalar"] if inc else None)

    def g(self, fn, inc=False):
        self.op("gpsimd", fn, inc=self.sems["gpsimd"] if inc else None)

    def t(self, fn):
        self.op("tensor", fn)

    def vtiny(self, fn, inc=False):
        self.raw("vector", lambda e, fn=fn: fn(e))
        if inc:
            s = self.sems["vector"]; s.n += 1
            self.raw("vector", lambda e, h=s.h: e.drain().then_inc(h, 1))
        else:
            self.raw("vector", lambda e: e.drain())

    def stiny(self, fn, inc=False):
        self.raw("scalar", lambda e, fn=fn: fn(e))
        if inc:
            s = self.sems["scalar"]; s.n += 1
            self.raw("scalar", lambda e, h=s.h: e.drain().then_inc(h, 1))
        else:
            self.raw("scalar", lambda e: e.drain())

    def wait(self, eng, sem, val=None):
        v = sem.n if val is None else val
        if v > 0:
            self.raw(eng, lambda e, s=sem.h, v=v: e.wait_ge(s, v))

    def pe_done(self):
        s = self.sems["tensor"]
        s.n += 1
        self.raw("tensor", lambda e, h=s.h: e.drain(fusable=False).then_inc(h, 1))
        return s.n

    def dma(self, out_ap, in_ap, eng="sync"):
        self.sem_dma.n += 16
        self.raw(eng, lambda e, o=out_ap, i=in_ap, h=self.sem_dma.h: e.dma_start(out=o, in_=i).then_inc(h, 16))
        return self.sem_dma.n

    def barrier(self, engs=("vector", "scalar", "tensor")):
        snap = {}
        for i, e in enumerate(engs):
            s = self.sems[e]
            s.n += 1
            if e == "tensor":
                self.raw(e, lambda x, h=s.h: x.drain(fusable=False).then_inc(h, 1))
            else:
                self.raw(e, lambda x, h=s.h: x.drain().then_inc(h, 1))
            snap[e] = s.n
        for e in engs:
            for o in engs:
                if o != e:
                    self.wait(e, self.sems[o], snap[o])

    def finalize(self):
        sems = list(ALL_SEMS)
        with self.nc.Block() as block:
            for eng in ENGS:
                fns = self.ops[eng]
                if not fns:
                    continue
                def mk(fns, eng):
                    def run(e):
                        for s in sems:
                            e.sem_clear(s)
                        if eng == "vector":
                            e.memset(self.tiny[0:1, 0:2], 0.0)
                        elif eng == "scalar":
                            e.memzero(self.tiny[0:1, 2:4])
                        elif eng == "gpsimd":
                            e.memset(self.tiny[0:1, 4:6], 0.0)
                        for f in fns:
                            f(e)
                    return run
                getattr(block, eng)(mk(fns, eng))



def bcast_row(p, row_ap_fn, width, out_buf, scale=1.0, out_c0=0):
    sb = p.sb
    for c0 in range(0, width, 512):
        c1 = min(c0 + 512, width)
        w = c1 - c0
        pb = p.ps[6]
        p.wait("tensor", p.sems["vector"])
        p.t(lambda e, pb=pb, w=w, c0=c0, c1=c1: e.matmul(
            pb[:, 0:w], sb["sb_ones"][0:1, :], row_ap_fn(c0, c1), start=True, stop=True))
        tv = p.pe_done()
        p.wait("vector", p.sems["tensor"], tv)
        if scale == 1.0:
            p.v(lambda e, pb=pb, w=w, c0=c0, c1=c1: e.tensor_copy(
                out_buf.ap(0, 128, out_c0 + c0, out_c0 + c1), pb[:, 0:w]), inc=True)
        else:
            p.v(lambda e, pb=pb, w=w, c0=c0, c1=c1: e.tensor_scalar_mul(
                out_buf.ap(0, 128, out_c0 + c0, out_c0 + c1), pb[:, 0:w], float(scale)), inc=True)


def bcast_col(p, scalar_ap, out_ap, scale=1.0):
    pb = p.ps[6]
    p.wait("tensor", p.sems["vector"])
    p.t(lambda e: e.matmul(pb[:, 0:1], p.sb["sb_ones"][0:1, :], scalar_ap, start=True, stop=True))
    tv = p.pe_done()
    p.wait("vector", p.sems["tensor"], tv)
    if scale == 1.0:
        p.vtiny(lambda e: e.tensor_copy(out_ap, pb[:, 0:1]), inc=True)
    else:
        p.vtiny(lambda e: e.tensor_scalar_mul(out_ap, pb[:, 0:1], float(scale)), inc=True)


def part_max_to_row(p, m_buf, width, out_row_fn, scr=None):
    sb = p.sb
    fa = p.fa
    own = scr is None
    scr = fa.alloc(width, "pmscr") if own else scr
    for c0 in range(0, width, 128):
        w = min(128, width - c0)
        pb = p.ps[6]
        p.wait("tensor", p.sems["vector"])
        p.t(lambda e, c0=c0, w=w, pb=pb: e.transpose(pb[0:w, 0:128], m_buf.ap(0, 128, c0, c0 + w),
                                                     sb["sb_ident"][:]))
        tv = p.pe_done()
        p.wait("vector", p.sems["tensor"], tv)
        p.vtiny(lambda e, pb=pb, w=w, c0=c0: e.reduce_max(scr.ap(0, w, c0, c0 + 1), pb[0:w, 0:128],
                                                          axis=AX.X), inc=True)
        p.wait("tensor", p.sems["vector"])
        p.t(lambda e, c0=c0, w=w, pb=pb: e.transpose(pb[0:1, 0:w], scr.ap(0, w, c0, c0 + 1),
                                                     sb["sb_ident"][:]))
        tv = p.pe_done()
        p.wait("vector", p.sems["tensor"], tv)
        out_row_fn(c0, w, pb)
    if own:
        fa.reset(scr.off)


def quant_tile(p, in_ap, sB_ap, out_ap, f_ap, i_ap):
    p.v(lambda e: e.tensor_mul(f_ap, in_ap, sB_ap))
    p.v(lambda e: e.tensor_scalar(f_ap, f_ap, QHI, QLO, ALU.min, ALU.max))
    p.v(lambda e: e.tensor_scalar_add(f_ap, f_ap, 0.5))
    p.v(lambda e: e.tensor_copy(i_ap, f_ap))
    p.v(lambda e: e.tensor_copy(out_ap, i_ap))
    p.v(lambda e: e.tensor_sub(f_ap, f_ap, out_ap))
    p.v(lambda e: e.scalar_tensor_tensor(out_ap, f_ap, 1.0, out_ap, ALU.is_ge, ALU.add))
    p.v(lambda e: e.scalar_tensor_tensor(out_ap, f_ap, 0.0, out_ap, ALU.is_ge, ALU.add))
    p.v(lambda e: e.tensor_scalar_add(out_ap, out_ap, -1.0), inc=True)


def emit_linear(p, wname, nm, K, rhs_fn, n0, n1, evac, m_list=None, ternary_thr=None,
                wq_bf16_dram=None, nwide=512, k0=0):
    sb = p.sb
    fa, ba = p.fa, p.ba
    nk = (K + 127) // 128
    wbf = ba.alloc(nk * 128, "wbf")
    stage = None if wq_bf16_dram is not None else fa.alloc(nk * 128, "wstage")
    mlist = list(range(nm)) if m_list is None else m_list
    ev_marks = []
    for mi in mlist:
        if wq_bf16_dram is not None:
            p.wait("sync", p.sems["tensor"])
            dv = p.dma(wbf.ap().rearrange("p (k c) -> p k c", c=128), wq_bf16_dram[mi, k0:k0 + nk * 128].rearrange("(k p) c -> p k c", p=128))
            p.wait("tensor", p.sem_dma, dv)
            p.wait("tensor", p.sems["vector"])
            p.wait("tensor", p.sems["scalar"])
        else:
            p.wait("sync", p.sems["vector"])
            dv = p.dma(stage.ap().rearrange("p (k c) -> p k c", c=128), sb[wname][mi, k0:k0 + nk * 128].rearrange("(k p) c -> p k c", p=128))
            p.wait("vector", p.sem_dma, dv)
            p.wait("vector", p.sems["tensor"])
            if ternary_thr is None:
                p.v(lambda e: e.tensor_copy(wbf.ap(), stage.ap()), inc=True)
            else:
                p.v(lambda e: e.tensor_single_scalar(wbf.ap(), stage.ap(), ternary_thr, ALU.is_lt))
                p.v(lambda e: e.scalar_tensor_tensor(wbf.ap(), stage.ap(), ternary_thr, wbf.ap(),
                                                     ALU.is_gt, ALU.subtract), inc=True)
            p.wait("tensor", p.sems["vector"])
        for ci, c0 in enumerate(range(n0, n1, nwide)):
            c1 = min(c0 + nwide, n1)
            w = c1 - c0
            pband = p.ps[ci % 2]
            p.wait("tensor", p.sems["vector"])
            p.wait("tensor", p.sems["scalar"])
            if len(ev_marks) >= 2:
                eng, val = ev_marks[-2]
                p.wait("tensor", p.sems[eng], val)
            for k in range(nk):
                p.t(lambda e, pband=pband, k=k, c0=c0, c1=c1, w=w: e.matmul(
                    pband[:, 0:w], wbf.ap(0, 128, k * 128, (k + 1) * 128),
                    rhs_fn(k, c0, c1), start=(k == 0), stop=(k == nk - 1)))
            tv = p.pe_done()
            vb, s_b = p.sems["vector"].n, p.sems["scalar"].n
            evac(mi, c0, c1, pband, tv)
            if p.sems["vector"].n > vb:
                ev_marks.append(("vector", p.sems["vector"].n))
            elif p.sems["scalar"].n > s_b:
                ev_marks.append(("scalar", p.sems["scalar"].n))
            else:
                raise RuntimeError("evac must inc vector or scalar sem")
    if stage is not None:
        fa.reset(stage.off)
    ba.reset(wbf.off)


def emit_mean_abs(p, wname, nm, K, true_count, out_buf, out_c0):
    sb = p.sb
    fa = p.fa
    nk = (K + 127) // 128
    stage = fa.alloc(nk * 128, "mst")
    acc = fa.alloc(2, "mac")
    p.v(lambda e: e.memset(acc.ap(0, 128, 0, 1), 0.0), inc=True)
    for mi in range(nm):
        p.wait("sync", p.sems["vector"])
        dv = p.dma(stage.ap().rearrange("p (k c) -> p k c", c=128), sb[wname][mi].rearrange("(k p) c -> p k c", p=128))
        p.wait("vector", p.sem_dma, dv)
        p.vtiny(lambda e: e.tensor_reduce(acc.ap(0, 128, 1, 2), stage.ap(), axis=AX.X, op=ALU.add,
                                          apply_absolute_value=True))
        p.vtiny(lambda e: e.tensor_add(acc.ap(0, 128, 0, 1), acc.ap(0, 128, 0, 1), acc.ap(0, 128, 1, 2)),
                inc=True)
    pb = p.ps[6]
    p.wait("tensor", p.sems["vector"])
    p.t(lambda e: e.transpose(pb[0:1, 0:128], acc.ap(0, 128, 0, 1), sb["sb_ident"][:]))
    tv = p.pe_done()
    p.wait("vector", p.sems["tensor"], tv)
    p.vtiny(lambda e: e.reduce_sum(out_buf.ap(0, 1, out_c0, out_c0 + 1), pb[0:1, 0:128], axis=AX.X))
    p.vtiny(lambda e: e.tensor_scalar_mul(out_buf.ap(0, 1, out_c0, out_c0 + 1),
                                          out_buf.ap(0, 1, out_c0, out_c0 + 1), float(1.0 / true_count)),
            inc=True)
    fa.reset(stage.off)



def emit_ln(p, xacc, xn_dram, negmuB, rstdB, eps=1e-5):
    sb = p.sb
    fa = p.fa
    mk = fa.mark()
    rows = fa.alloc(2 * T, "lnrows")
    sq = fa.alloc(512, "lnsq")
    for c0 in range(0, T, 512):
        pbs = p.ps[5]
        pbq = p.ps[6]
        p.wait("tensor", p.sems["vector"])
        for r in range(NRC):
            p.wait("scalar", p.sems["tensor"])
            p.s(lambda e, r=r, c0=c0: e.activation(sq.ap(0, 128, 0, 512),
                                                   xacc.ap(0, 128, r * T + c0, r * T + c0 + 512),
                                                   AF.Square), inc=True)
            p.wait("tensor", p.sems["scalar"])
            p.t(lambda e, r=r, c0=c0, pbs=pbs: e.matmul(
                pbs[0:1, 0:512], sb["sb_ones"][:, 0:1], xacc.ap(0, 128, r * T + c0, r * T + c0 + 512),
                start=(r == 0), stop=(r == NRC - 1)))
            p.t(lambda e, r=r, pbq=pbq: e.matmul(pbq[0:1, 0:512], sb["sb_ones"][:, 0:1],
                                                 sq.ap(0, 128, 0, 512),
                                                 start=(r == 0), stop=(r == NRC - 1)))
            p.pe_done()
        tv = p.sems["tensor"].n
        p.wait("vector", p.sems["tensor"], tv)
        p.v(lambda e, c0=c0, pbs=pbs: e.tensor_scalar_mul(rows.ap(0, 1, c0, c0 + 512), pbs[0:1, 0:512],
                                                          1.0 / D))
        p.v(lambda e, c0=c0, pbq=pbq: e.tensor_scalar_mul(rows.ap(0, 1, T + c0, T + c0 + 512),
                                                          pbq[0:1, 0:512], 1.0 / D))
        p.v(lambda e, c0=c0: e.tensor_mul(sq.ap(0, 1, 0, 512), rows.ap(0, 1, c0, c0 + 512),
                                          rows.ap(0, 1, c0, c0 + 512)))
        p.v(lambda e, c0=c0: e.tensor_sub(rows.ap(0, 1, T + c0, T + c0 + 512),
                                          rows.ap(0, 1, T + c0, T + c0 + 512), sq.ap(0, 1, 0, 512)),
            inc=True)
        p.wait("scalar", p.sems["vector"])
        p.s(lambda e, c0=c0: e.activation(rows.ap(0, 1, T + c0, T + c0 + 512),
                                          rows.ap(0, 1, T + c0, T + c0 + 512), AF.Sqrt,
                                          bias=p.scol("k_eps5")[0:1, 0:1]),
            inc=True)
        p.wait("vector", p.sems["scalar"])
        p.v(lambda e, c0=c0: e.reciprocal(rows.ap(0, 1, T + c0, T + c0 + 512),
                                          rows.ap(0, 1, T + c0, T + c0 + 512)), inc=True)
    bcast_row(p, lambda c0, c1: rows.ap(0, 1, c0, c1), T, negmuB, scale=-1.0)
    bcast_row(p, lambda c0, c1: rows.ap(0, 1, T + c0, T + c1), T, rstdB)
    tmp = fa.alloc(T, "lntmp")
    xff = fa.alloc(T, "lnxf")
    xnb = p.ba.alloc(T, "lnxnb")
    for r in range(NRC):
        p.wait("vector", p.sem_dma)
        p.v(lambda e, r=r: e.tensor_add(tmp.ap(), xacc.ap(0, 128, r * T, (r + 1) * T), negmuB.ap()))
        p.v(lambda e, r=r: e.tensor_mul(xff.ap(), tmp.ap(), rstdB.ap()))
        p.v(lambda e, r=r: e.tensor_copy(xnb.ap(), xff.ap()), inc=True)
        p.wait("sync", p.sems["vector"])
        p.dma(xn_dram[r * 128:(r + 1) * 128, :], xnb.ap())
        p.dma(p.sb["_scr_xnf"][r * 128:(r + 1) * 128, :], xff.ap())
    fa.reset(mk)
    p.ba.reset(xnb.off)


def xn32(p, xacc, negmuB, rstdB, r, out_ap, tmp_ap, c0=0, c1=T):
    p.v(lambda e: e.tensor_add(tmp_ap, xacc.ap(0, 128, r * T + c0, r * T + c1), negmuB.ap(0, 128, c0, c1)))
    p.v(lambda e: e.tensor_mul(out_ap, tmp_ap, rstdB.ap(0, 128, c0, c1)), inc=True)


def load_xn_slice(p, xn_dram, xsl, c0, c1):
    w = c1 - c0
    for k in range(NRC):
        p.dma(xsl.ap(0, 128, k * w, (k + 1) * w), xn_dram[k * 128:(k + 1) * 128, c0:c1])
    return p.sem_dma.n



def emit_l0_attn(p, xacc, xn_dram):
    sb = p.sb
    fa, ba = p.fa, p.ba
    famark, bamark = fa.mark(), ba.mark()

    pswap = ba.alloc(128, "pswap")
    p.wait("sync", p.sems["vector"])
    p.wait("sync", p.sems["scalar"])
    p.wait("sync", p.sems["tensor"])
    dv = p.dma(pswap.ap(), sb["c_pswapb"][:, :])
    p.wait("vector", p.sem_dma, dv)

    ngroups = 1 if DBG_STOP in ("qkv", "stab", "heads", "g0") else 2
    for g in range(ngroups):
        gmark_f, gmark_b = fa.mark(), ba.mark()
        qg = ba.alloc(4 * T, "qg")
        kg = ba.alloc(4 * T, "kg")
        vtok = ba.alloc(16 * 512, "vtok")
        ccb = fa.alloc(T, "cc"); ssb = fa.alloc(T, "ss")
        p.wait("sync", p.sems["vector"])
        p.wait("sync", p.sems["scalar"])
        p.wait("sync", p.sems["tensor"])
        p.dma(ccb.ap(), sb["c_ropecc"][:, :])
        p.dma(ssb.ap(), sb["c_ropess"][:, :])
        xsl = ba.alloc(8 * 512, "xsl")

        def evac_qkv(mi, c0, c1, pband, tv, g=g):
            w = c1 - c0
            if mi < 16:
                ch = (mi - 4 * g) if mi < 8 else (mi - 8 - 4 * g + 4)
                tb = qg if ch < 4 else kg
                chl = ch % 4
                tgt = tb.ap(0, 128, chl * T + c0, chl * T + c1)
                p.wait("vector", p.sems["tensor"], tv)
                p.v(lambda e, pband=pband, w=w, tgt=tgt, mi=mi: e.tensor_scalar_add(
                    tgt, pband[:, 0:w], p.scol("l0_bqkv", mi)), inc=True)
                p.wait("tensor", p.sems["vector"])
                pb2 = p.ps[2]
                p.t(lambda e, tgt=tgt, w=w, pb2=pb2: e.matmul(pb2[:, 0:w], pswap.ap(), tgt,
                                                              start=True, stop=True))
                tv2 = p.pe_done()
                p.wait("vector", p.sems["tensor"], tv2)
                t1 = fa.alloc(512, "ropet")
                p.v(lambda e, pb2=pb2, w=w, c0=c0, c1=c1, t1=t1: e.tensor_mul(
                    t1.ap(0, 128, 0, w), pb2[:, 0:w], ssb.ap(0, 128, c0, c1)))
                p.v(lambda e, tgt=tgt, c0=c0, c1=c1: e.tensor_mul(tgt, tgt, ccb.ap(0, 128, c0, c1)))
                p.v(lambda e, tgt=tgt, w=w, t1=t1: e.tensor_add(tgt, tgt, t1.ap(0, 128, 0, w)), inc=True)
                fa.reset(t1.off)
            else:
                vb = ba.alloc(512, "vb")
                p.wait("vector", p.sems["tensor"], tv)
                p.v(lambda e, pband=pband, w=w, vb=vb, mi=mi: e.tensor_scalar_add(
                    vb.ap(0, 128, 0, w), pband[:, 0:w], p.scol("l0_bqkv", mi)), inc=True)
                p.wait("tensor", p.sems["vector"])
                mv = mi - 16 - 4 * g
                for s0 in range(0, w, 128):
                    kt = (c0 + s0) // 128
                    p.wait("tensor", p.sems["vector"])
                    p.t(lambda e, vb=vb, s0=s0: e.transpose(p.psb[:, 0:128], vb.ap(0, 128, s0, s0 + 128),
                                                            sb["sb_identb"][:]))
                    tv3 = p.pe_done()
                    p.wait("vector", p.sems["tensor"], tv3)
                    for hh in range(2):
                        h = 2 * mv + hh
                        p.v(lambda e, kt=kt, h=h, hh=hh: e.tensor_copy(
                            vtok.ap(0, 128, kt * 512 + h * 64, kt * 512 + h * 64 + 64),
                            p.psb[:, hh * 64:hh * 64 + 64]), inc=True)
                ba.reset(vb.off)

        mlist = ([4 * g + j for j in range(4)] + [8 + 4 * g + j for j in range(4)]
                 + [16 + 4 * g + j for j in range(4)])
        for c0 in range(0, T, 512):
            p.wait("sync", p.sems["tensor"])
            dvx = load_xn_slice(p, xn_dram, xsl, c0, c0 + 512)
            p.wait("tensor", p.sem_dma, dvx)
            emit_linear(p, "l0_wqkv", 24, D,
                        lambda k, cc0, cc1, c0=c0: xsl.ap(0, 128, k * 512 + (cc0 - c0), k * 512 + (cc1 - c0)),
                        c0, c0 + 512, evac_qkv, m_list=mlist)
        fa.reset(ccb.off)
        ba.reset(xsl.off)
        if DBG_STOP == "qkv":
            fa.reset(famark); ba.reset(bamark); return
        if DBG_DUMP and g == 0:
            p.wait("sync", p.sems["vector"])
            p.wait("sync", p.sems["tensor"])
            dbg = p.sb["dbg"]
            for cch in range(4):
                p.dma(dbg[0 + cch * 128:(cch + 1) * 128, :], qg.ap(0, 128, cch * T, (cch + 1) * T))
                p.dma(dbg[512 + cch * 128:512 + (cch + 1) * 128, :], kg.ap(0, 128, cch * T, (cch + 1) * T))
            for kt in range(4):
                p.dma(dbg[1024 + kt * 128:1024 + (kt + 1) * 128, 0:512],
                      vtok.ap(0, 128, kt * 512, (kt + 1) * 512))

        nrm = fa.alloc(8, "nrm")
        sqt = fa.alloc(T, "sqt")
        first = [True, True]
        for ch in range(8):
            which = 0 if ch < 4 else 1
            tb = qg if ch < 4 else kg
            p.wait("scalar", p.sems["vector"])
            p.s(lambda e, tb=tb, ch=ch: e.activation(sqt.ap(), tb.ap(0, 128, (ch % 4) * T, (ch % 4 + 1) * T),
                                                     AF.Square), inc=True)
            p.wait("tensor", p.sems["scalar"])
            p.wait("tensor", p.sems["vector"])
            for c0 in range(0, T, 512):
                pb = p.ps[6]
                p.wait("tensor", p.sems["vector"])
                for half in range(2):
                    p.t(lambda e, c0=c0, pb=pb, half=half: e.matmul(
                        pb[half:half + 1, 0:512] if False else pb[0:2, half * 256:half * 256 + 512][0:1, 0:512] if False else pb[0:1, 0:512] if half == 0 else pb[0:1, 0:512],
                        p.small("ones64bd")[:, half:half + 1],
                        sqt.ap(0, 128, c0, c0 + 512), start=True, stop=True) if False else e.matmul(
                        (p.ps[5] if half else p.ps[6])[0:1, 0:512],
                        p.small("ones64bd")[:, half:half + 1],
                        sqt.ap(0, 128, c0, c0 + 512), start=True, stop=True))
                tv = p.pe_done()
                p.wait("vector", p.sems["tensor"], tv)
                for half in range(2):
                    p.vtiny(lambda e, half=half: e.reduce_max(nrm.ap(0, 1, 2, 3),
                                                              (p.ps[5] if half else p.ps[6])[0:1, 0:512],
                                                              axis=AX.X))
                    w = which
                    if first[w]:
                        p.vtiny(lambda e, w=w: e.tensor_copy(nrm.ap(0, 1, w, w + 1), nrm.ap(0, 1, 2, 3)),
                                inc=True)
                        first[w] = False
                    else:
                        p.vtiny(lambda e, w=w: e.tensor_max(nrm.ap(0, 1, w, w + 1),
                                                            nrm.ap(0, 1, w, w + 1),
                                                            nrm.ap(0, 1, 2, 3)), inc=True)
        p.vtiny(lambda e: e.tensor_mul(nrm.ap(0, 1, 5, 6), nrm.ap(0, 1, 0, 1), nrm.ap(0, 1, 1, 2)),
                inc=True)
        p.wait("scalar", p.sems["vector"])
        p.stiny(lambda e: e.activation(nrm.ap(0, 1, 6, 7), nrm.ap(0, 1, 5, 6), AF.Sqrt,
                                       scale=float(1.0 / 64.0)), inc=True)
        cbias = fa.alloc(2, "cbias")
        p.wait("vector", p.sems["scalar"])
        bcast_col(p, nrm.ap(0, 1, 6, 7), cbias.ap(0, 128, 0, 1), scale=-1.0)
        fa.reset(sqt.off)
        if DBG_STOP == "stab":
            fa.reset(famark); ba.reset(bamark); return

        ytil = ba.alloc(4 * T, "ytil")
        pbuf = ba.alloc(2 * 512, "pbuf")
        osc = fa.alloc(512, "osc")
        rrow = fa.alloc(512, "rrow")
        for h in range(8):
            ch = h // 2
            r0 = (h % 2) * 64
            for nq in range(4):
                c0 = nq * 512
                pso = p.ps[3]
                psm = p.ps[5]
                for kt in range(16):
                    psS = p.ps[kt % 2]
                    if kt >= 2:
                        p.wait("tensor", p.sems["scalar"])
                    p.t(lambda e, psS=psS, ch=ch, r0=r0, kt=kt, c0=c0: e.matmul(
                        psS[:, 0:512],
                        kg.ap(r0, r0 + 64, ch * T + kt * 128, ch * T + kt * 128 + 128),
                        qg.ap(r0, r0 + 64, ch * T + c0, ch * T + c0 + 512), start=True, stop=True))
                    tv = p.pe_done()
                    p.wait("scalar", p.sems["tensor"], tv)
                    pslot = pbuf.ap(0, 128, (kt % 2) * 512, (kt % 2) * 512 + 512)
                    p.s(lambda e, psS=psS, pslot=pslot: e.activation(
                        pslot, psS[:, 0:512], AF.Exp, bias=cbias.ap(0, 128, 0, 1), scale=0.125),
                        inc=True)
                    p.wait("tensor", p.sems["scalar"])
                    p.t(lambda e, pso=pso, kt=kt, h=h, r0=r0, pslot=pslot: e.matmul(
                        pso[r0:r0 + 64, 0:512], vtok.ap(0, 128, kt * 512 + h * 64, kt * 512 + (h + 1) * 64),
                        pslot, start=(kt == 0), stop=(kt == 15)))
                    p.t(lambda e, psm=psm, kt=kt, pslot=pslot: e.matmul(
                        psm[0:1, 0:512], p.onesb.ap(0, 128, 0, 1),
                        pslot, start=(kt == 0), stop=(kt == 15)))
                tv = p.pe_done()
                p.wait("vector", p.sems["tensor"], tv)
                p.v(lambda e, psm=psm: e.tensor_copy(rrow.ap(0, 1, 0, 512), psm[0:1, 0:512]))
                p.v(lambda e: e.reciprocal(rrow.ap(0, 1, 0, 512), rrow.ap(0, 1, 0, 512)), inc=True)
                p.wait("scalar", p.sems["vector"])
                p.s(lambda e, pso=pso, r0=r0: e.activation(osc.ap(r0, r0 + 64, 0, 512),
                                                           pso[r0:r0 + 64, 0:512], AF.Copy),
                    inc=True)
                pbB = p.ps[4]
                p.wait("tensor", p.sems["vector"])
                p.t(lambda e, pbB=pbB, r0=r0: e.matmul(pbB[r0:r0 + 64, 0:512], sb["sb_ones"][0:1, 0:64],
                                                       rrow.ap(0, 1, 0, 512), start=True, stop=True))
                tv = p.pe_done()
                p.wait("vector", p.sems["tensor"], tv)
                p.wait("vector", p.sems["scalar"])
                p.v(lambda e, pbB=pbB, h=h, r0=r0, c0=c0: e.tensor_mul(
                    ytil.ap(r0, r0 + 64, (h // 2) * T + c0, (h // 2) * T + c0 + 512),
                    osc.ap(r0, r0 + 64, 0, 512), pbB[r0:r0 + 64, 0:512]), inc=True)
        if DBG_STOP == "heads":
            fa.reset(famark); ba.reset(bamark); return
        if DBG_DUMP:
            p.wait("sync", p.sems["vector"])
            base = 1536 if g == 0 else 0
            for cch in range(4):
                p.dma(p.sb["dbg"][base + cch * 128:base + (cch + 1) * 128, :],
                      ytil.ap(0, 128, cch * T, (cch + 1) * T))
        ba.reset(pbuf.off)
        def evac_wout(mi, c0, c1, pband, tv, g=g):
            p.wait("vector", p.sems["tensor"], tv)
            if g == 0:
                p.v(lambda e, mi=mi, c0=c0, c1=c1, pband=pband: e.scalar_tensor_tensor(
                    xacc.ap(0, 128, mi * T + c0, mi * T + c1), pband[:, 0:c1 - c0],
                    p.scol("l0_bout", mi), xacc.ap(0, 128, mi * T + c0, mi * T + c1),
                    ALU.add, ALU.add), inc=True)
            else:
                p.v(lambda e, mi=mi, c0=c0, c1=c1, pband=pband: e.tensor_add(
                    xacc.ap(0, 128, mi * T + c0, mi * T + c1),
                    xacc.ap(0, 128, mi * T + c0, mi * T + c1), pband[:, 0:c1 - c0]), inc=True)
        emit_linear(p, "l0_wout", 8, 512,
                    lambda k, c0, c1: ytil.ap(0, 128, k * T + c0, k * T + c1),
                    0, T, evac_wout, k0=g * 512)
        fa.reset(gmark_f)
        ba.reset(gmark_b)
    fa.reset(famark)
    ba.reset(bamark)


def emit_l0_ffn(p, xacc, scr):
    sb = p.sb
    fa, ba = p.fa, p.ba
    famark, bamark = fa.mark(), ba.mark()

    sc = fa.alloc(8, "l0sc")
    emit_mean_abs(p, "l0_w1", 32, D, 4096 * 1024, sc, 0)
    emit_mean_abs(p, "l0_w2", 8, 4096, 4096 * 1024, sc, 1)
    p.vtiny(lambda e: e.tensor_scalar_max(sc.ap(0, 1, 0, 2), sc.ap(0, 1, 0, 2), 1e-5))
    p.vtiny(lambda e: e.tensor_scalar_mul(sc.ap(0, 1, 2, 4), sc.ap(0, 1, 0, 2), 0.5), inc=True)
    thr = fa.alloc(4, "l0thr")
    bcast_col(p, sc.ap(0, 1, 2, 3), thr.ap(0, 128, 0, 1))
    bcast_col(p, sc.ap(0, 1, 2, 3), thr.ap(0, 128, 1, 2), scale=-1.0)
    bcast_col(p, sc.ap(0, 1, 3, 4), thr.ap(0, 128, 2, 3))
    bcast_col(p, sc.ap(0, 1, 3, 4), thr.ap(0, 128, 3, 4), scale=-1.0)

    for wname, nm, K, ti, qname in (("l0_w1", 32, D, 0, "w1q"), ("l0_w2", 8, 4096, 1, "w2q")):
        nk = (K + 127) // 128
        stage = fa.alloc(nk * 128, "qst")
        wq = ba.alloc(nk * 128, "qwq")
        for mi in range(nm):
            p.wait("sync", p.sems["vector"])
            dvv = p.dma(stage.ap().rearrange("p (k c) -> p k c", c=128), sb[wname][mi].rearrange("(k p) c -> p k c", p=128))
            p.wait("vector", p.sem_dma, dvv)
            p.v(lambda e, ti=ti: e.tensor_single_scalar(wq.ap(), stage.ap(),
                                                        thr.ap(0, 128, 2 * ti + 1, 2 * ti + 2),
                                                        ALU.is_lt))
            p.v(lambda e, ti=ti: e.scalar_tensor_tensor(wq.ap(), stage.ap(),
                                                        thr.ap(0, 128, 2 * ti, 2 * ti + 1),
                                                        wq.ap(), ALU.is_gt, ALU.subtract), inc=True)
            p.wait("sync", p.sems["vector"])
            p.dma(scr[qname][mi].rearrange("(k p) c -> p k c", p=128), wq.ap().rearrange("p (k c) -> p k c", c=128))
        fa.reset(stage.off)
        ba.reset(wq.off)

    w1 = fa.alloc(T, "w1")
    w2 = fa.alloc(T, "w2")
    w3 = fa.alloc(T, "w3")
    sxB = fa.alloc(T, "sxB")
    hsB = fa.alloc(T, "hsB")
    mrow = fa.alloc(T, "mrow")
    sfb = Buf(w1.t, w1.off, 512, F32)
    it = p.ia.alloc(512, "l0i")

    def make_rows(inv_col):
        p.v(lambda e: e.tensor_scalar_max(mrow.ap(0, 1, 0, T), mrow.ap(0, 1, 0, T), 1e-5), inc=True)
        p.v(lambda e: e.reciprocal(w1.ap(0, 1, 0, T), mrow.ap(0, 1, 0, T)))
        p.v(lambda e: e.tensor_scalar_mul(w1.ap(0, 1, 0, T), w1.ap(0, 1, 0, T), 127.0))
        p.v(lambda e: e.tensor_scalar(w2.ap(0, 1, 0, T), mrow.ap(0, 1, 0, T), inv_col,
                                      float(1.0 / 127.0), ALU.mult, ALU.mult), inc=True)
        bcast_row(p, lambda c0, c1: w1.ap(0, 1, c0, c1), T, sxB)
        bcast_row(p, lambda c0, c1: w2.ap(0, 1, c0, c1), T, hsB)

    def wrow(c0, w, pb):
        p.v(lambda e, pb=pb, c0=c0, w=w: e.tensor_copy(mrow.ap(0, 1, c0, c0 + w), pb[0:1, 0:w]), inc=True)

    hb = ba.alloc(T, "hb")
    xq = ba.alloc(8 * T, "xq")
    for r in range(NRC):
        p.wait("sync", p.sems["scalar"])
        p.wait("sync", p.sems["vector"])
        dvx = p.dma(w1.ap(), scr["xnf"][r * 128:(r + 1) * 128, :])
        p.wait("scalar", p.sem_dma, dvx)
        p.s(lambda e: e.activation(hsB.ap(), w1.ap(), AF.Abs), inc=True)
        p.wait("vector", p.sems["scalar"])
        if r == 0:
            p.v(lambda e: e.tensor_copy(w3.ap(), hsB.ap()), inc=True)
        else:
            p.v(lambda e: e.tensor_max(w3.ap(), w3.ap(), hsB.ap()), inc=True)
    part_max_to_row(p, w3, T, wrow, scr=w2)
    make_rows(sc.ap(0, 1, 0, 1))
    for r in range(NRC):
        p.wait("sync", p.sems["vector"])
        dvx = p.dma(w1.ap(), scr["xnf"][r * 128:(r + 1) * 128, :])
        p.wait("vector", p.sem_dma, dvx)
        for qc in range(0, T, 512):
            quant_tile(p, w1.ap(0, 128, qc, qc + 512), sxB.ap(0, 128, qc, qc + 512),
                       xq.ap(0, 128, r * T + qc, r * T + qc + 512),
                       w2.ap(0, 128, qc, qc + 512), it.ap(0, 128, 0, 512))
    if DBG_DUMP:
        p.wait("sync", p.sems["vector"])
        for r in range(NRC):
            p.dma(p.sb["dbg"][r * 128:(r + 1) * 128, :], xq.ap(0, 128, r * T, (r + 1) * T))


    for mi in range(32):
        def evac_h(mi2, c0, c1, pband, tv):
            p.wait("vector", p.sems["tensor"], tv)
            p.wait("vector", p.sem_dma)
            p.v(lambda e, c0=c0, c1=c1, pband=pband: e.tensor_mul(
                hb.ap(0, 128, c0, c1), pband[:, 0:c1 - c0], hsB.ap(0, 128, c0, c1)), inc=True)
        emit_linear(p, None, 32, D, lambda k, c0, c1: xq.ap(0, 128, k * T + c0, k * T + c1),
                    0, T, evac_h, m_list=[mi], wq_bf16_dram=scr["w1q"])
        p.v(lambda e, mi=mi: e.tensor_single_scalar(w1.ap(), hb.ap(), p.scol("l0_convw1", mi), ALU.mult))
        p.v(lambda e, mi=mi: e.scalar_tensor_tensor(w1.ap(0, 128, 1, T), hb.ap(0, 128, 0, T - 1),
                                                    p.scol("l0_convw0", mi), w1.ap(0, 128, 1, T),
                                                    ALU.mult, ALU.add))
        p.v(lambda e, mi=mi: e.scalar_tensor_tensor(w1.ap(0, 128, 0, T - 1), hb.ap(0, 128, 1, T),
                                                    p.scol("l0_convw2", mi), w1.ap(0, 128, 0, T - 1),
                                                    ALU.mult, ALU.add), inc=True)
        p.wait("scalar", p.sems["vector"])
        p.s(lambda e, mi=mi: e.activation(hb.ap(), w1.ap(), AF.Silu, bias=p.scol("l0_convb", mi)))
        p.s(lambda e: e.activation(w2.ap(), hb.ap(), AF.Abs), inc=True)
        p.wait("vector", p.sems["scalar"])
        if mi == 0:
            p.v(lambda e: e.tensor_copy(w3.ap(), w2.ap()), inc=True)
        else:
            p.v(lambda e: e.tensor_max(w3.ap(), w3.ap(), w2.ap()), inc=True)
        p.wait("sync", p.sems["scalar"])
        p.dma(scr["s"][mi * 128:(mi + 1) * 128, :], hb.ap())
    part_max_to_row(p, w3, T, wrow, scr=w2)
    make_rows(sc.ap(0, 1, 1, 2))

    ba.reset(xq.off)
    hq = ba.alloc(32 * 512, "hq")
    for nchunk in range(4):
        c0 = nchunk * 512
        c1 = c0 + 512
        for k in range(32):
            p.wait("sync", p.sems["vector"])
            dvv = p.dma(hb.ap(0, 128, 0, 512), scr["s"][k * 128:(k + 1) * 128, c0:c1])
            p.wait("vector", p.sem_dma, dvv)
            p.v(lambda e: e.tensor_copy(sfb.ap(), hb.ap(0, 128, 0, 512)), inc=True)
            quant_tile(p, sfb.ap(), sxB.ap(0, 128, c0, c1), hq.ap(0, 128, k * 512, (k + 1) * 512),
                       w2.ap(0, 128, 0, 512), it.ap(0, 128, 0, 512))

        def evac_o(mi, cc0, cc1, pband, tv, c0=c0):
            p.wait("vector", p.sems["tensor"], tv)
            p.v(lambda e, pband=pband, c0=c0: e.tensor_mul(sfb.ap(), pband[:, 0:512],
                                                           hsB.ap(0, 128, c0, c0 + 512)))
            p.v(lambda e, mi=mi, c0=c0: e.tensor_add(xacc.ap(0, 128, mi * T + c0, mi * T + c0 + 512),
                                                     xacc.ap(0, 128, mi * T + c0, mi * T + c0 + 512),
                                                     sfb.ap()), inc=True)
        emit_linear(p, None, 8, 4096, lambda k, cc0, cc1: hq.ap(0, 128, k * 512, (k + 1) * 512),
                    c0, c1, evac_o, wq_bf16_dram=scr["w2q"])
    fa.reset(famark)
    ba.reset(bamark)
    p.ia.reset(0)





def load_xn_rolled(p, xn_dram, xsl, c0, c1, shift=64):
    w = c1 - c0
    a0 = (c0 + shift) % T
    for k in range(NRC):
        if a0 + w <= T:
            p.dma(xsl.ap(0, 128, k * w, (k + 1) * w), xn_dram[k * 128:(k + 1) * 128, a0:a0 + w])
        else:
            w1 = T - a0
            p.dma(xsl.ap(0, 128, k * w, k * w + w1), xn_dram[k * 128:(k + 1) * 128, a0:T])
            p.dma(xsl.ap(0, 128, k * w + w1, (k + 1) * w), xn_dram[k * 128:(k + 1) * 128, 0:w - w1])
    return p.sem_dma.n


def emit_l2_attn(p, xacc, xn_dram):
    sb = p.sb
    fa, ba = p.fa, p.ba
    famark, bamark = fa.mark(), ba.mark()
    maskb = fa.alloc(128, "l2mask")
    p.wait("sync", p.sems["vector"])
    p.wait("sync", p.sems["scalar"])
    p.wait("sync", p.sems["tensor"])
    p.dma(maskb.ap(), sb["c_l2mask"][:, :])

    for g in range(2):
        gmark_f, gmark_b = fa.mark(), ba.mark()
        qg = ba.alloc(4 * T, "qg")
        kg = ba.alloc(4 * T, "kg")
        vtok = ba.alloc(16 * 512, "vtok")
        xsl = ba.alloc(8 * 512, "xsl")

        def evac_qkv(mi, c0, c1, pband, tv, g=g):
            w = c1 - c0
            p.wait("vector", p.sems["tensor"], tv)
            if mi < 16:
                ch = (mi - 4 * g) if mi < 8 else (mi - 8 - 4 * g + 4)
                tb = qg if ch < 4 else kg
                chl = ch % 4
                tgt = tb.ap(0, 128, chl * T + c0, chl * T + c1)
                p.v(lambda e, pband=pband, w=w, tgt=tgt, mi=mi: e.tensor_scalar_add(
                    tgt, pband[:, 0:w], p.scol("l2_bqkv", mi)), inc=True)
            else:
                vb = ba.alloc(512, "vb")
                p.v(lambda e, pband=pband, w=w, vb=vb, mi=mi: e.tensor_scalar_add(
                    vb.ap(0, 128, 0, w), pband[:, 0:w], p.scol("l2_bqkv", mi)), inc=True)
                p.wait("tensor", p.sems["vector"])
                mv = mi - 16 - 4 * g
                for s0 in range(0, w, 128):
                    kt = (c0 + s0) // 128
                    p.wait("tensor", p.sems["vector"])
                    p.t(lambda e, vb=vb, s0=s0: e.transpose(p.psb[:, 0:128], vb.ap(0, 128, s0, s0 + 128),
                                                            sb["sb_identb"][:]))
                    tv3 = p.pe_done()
                    p.wait("vector", p.sems["tensor"], tv3)
                    for hh in range(2):
                        h = 2 * mv + hh
                        p.v(lambda e, kt=kt, h=h, hh=hh: e.tensor_copy(
                            vtok.ap(0, 128, kt * 512 + h * 64, kt * 512 + h * 64 + 64),
                            p.psb[:, hh * 64:hh * 64 + 64]), inc=True)
                ba.reset(vb.off)

        mlist = ([4 * g + j for j in range(4)] + [8 + 4 * g + j for j in range(4)]
                 + [16 + 4 * g + j for j in range(4)])
        for c0 in range(0, T, 512):
            p.wait("sync", p.sems["tensor"])
            dvx = load_xn_rolled(p, xn_dram, xsl, c0, c0 + 512)
            p.wait("tensor", p.sem_dma, dvx)
            emit_linear(p, "l2_wqkv", 24, D,
                        lambda k, cc0, cc1, c0=c0: xsl.ap(0, 128, k * 512 + (cc0 - c0), k * 512 + (cc1 - c0)),
                        c0, c0 + 512, evac_qkv, m_list=mlist)
        ba.reset(xsl.off)

        nrm = fa.alloc(8, "nrm")
        sqt = fa.alloc(T, "sqt")
        first = [True, True]
        for ch in range(8):
            which = 0 if ch < 4 else 1
            tb = qg if ch < 4 else kg
            p.wait("scalar", p.sems["vector"])
            p.s(lambda e, tb=tb, ch=ch: e.activation(sqt.ap(), tb.ap(0, 128, (ch % 4) * T, (ch % 4 + 1) * T),
                                                     AF.Square), inc=True)
            p.wait("tensor", p.sems["scalar"])
            p.wait("tensor", p.sems["vector"])
            for c0 in range(0, T, 512):
                p.wait("tensor", p.sems["vector"])
                for half in range(2):
                    p.t(lambda e, c0=c0, half=half: e.matmul(
                        (p.ps[5] if half else p.ps[6])[0:1, 0:512],
                        p.small("ones64bd")[:, half:half + 1],
                        sqt.ap(0, 128, c0, c0 + 512), start=True, stop=True))
                tv = p.pe_done()
                p.wait("vector", p.sems["tensor"], tv)
                for half in range(2):
                    p.vtiny(lambda e, half=half: e.reduce_max(nrm.ap(0, 1, 2, 3),
                                                              (p.ps[5] if half else p.ps[6])[0:1, 0:512],
                                                              axis=AX.X))
                    w = which
                    if first[w]:
                        p.vtiny(lambda e, w=w: e.tensor_copy(nrm.ap(0, 1, w, w + 1), nrm.ap(0, 1, 2, 3)),
                                inc=True)
                        first[w] = False
                    else:
                        p.vtiny(lambda e, w=w: e.tensor_max(nrm.ap(0, 1, w, w + 1),
                                                            nrm.ap(0, 1, w, w + 1),
                                                            nrm.ap(0, 1, 2, 3)), inc=True)
        p.vtiny(lambda e: e.tensor_mul(nrm.ap(0, 1, 5, 6), nrm.ap(0, 1, 0, 1), nrm.ap(0, 1, 1, 2)),
                inc=True)
        p.wait("scalar", p.sems["vector"])
        p.stiny(lambda e: e.activation(nrm.ap(0, 1, 6, 7), nrm.ap(0, 1, 5, 6), AF.Sqrt,
                                       scale=float(1.0 / 64.0)), inc=True)
        cbias = fa.alloc(2, "cbias")
        p.wait("vector", p.sems["scalar"])
        bcast_col(p, nrm.ap(0, 1, 6, 7), cbias.ap(0, 128, 0, 1), scale=-1.0)
        fa.reset(sqt.off)

        ytil = ba.alloc(4 * T, "ytil")
        pbuf = ba.alloc(512, "pbuf")
        osc = fa.alloc(512, "osc")
        rrow = fa.alloc(512, "rrow")
        for h in range(8):
            ch = h // 2
            r0 = (h % 2) * 64
            for wn in range(16):
                c0 = wn * 128
                psS = p.ps[wn % 2]
                pso = p.ps[3]
                psm = p.ps[5]
                if wn >= 2:
                    p.wait("tensor", p.sems["scalar"])
                p.wait("tensor", p.sems["vector"])
                p.t(lambda e, psS=psS, ch=ch, r0=r0, c0=c0: e.matmul(
                    psS[:, 0:128],
                    kg.ap(r0, r0 + 64, ch * T + c0, ch * T + c0 + 128),
                    qg.ap(r0, r0 + 64, ch * T + c0, ch * T + c0 + 128), start=True, stop=True))
                tv = p.pe_done()
                p.wait("scalar", p.sems["tensor"], tv)
                p.s(lambda e, psS=psS: e.activation(pbuf.ap(0, 128, 0, 128), psS[:, 0:128], AF.Exp,
                                                    bias=cbias.ap(0, 128, 0, 1), scale=0.125), inc=True)
                p.wait("vector", p.sems["scalar"])
                p.v(lambda e: e.tensor_mul(pbuf.ap(0, 128, 0, 128), pbuf.ap(0, 128, 0, 128), maskb.ap()),
                    inc=True)
                p.wait("tensor", p.sems["vector"])
                p.t(lambda e, pso=pso, h=h, r0=r0, c0=c0: e.matmul(
                    pso[r0:r0 + 64, 0:128], vtok.ap(0, 128, (c0 // 128) * 512 + h * 64,
                                                    (c0 // 128) * 512 + (h + 1) * 64),
                    pbuf.ap(0, 128, 0, 128), start=True, stop=True))
                p.t(lambda e, psm=psm: e.matmul(psm[0:1, 0:128], p.onesb.ap(0, 128, 0, 1),
                                                pbuf.ap(0, 128, 0, 128), start=True, stop=True))
                tv = p.pe_done()
                p.wait("vector", p.sems["tensor"], tv)
                p.v(lambda e, psm=psm: e.tensor_copy(rrow.ap(0, 1, 0, 128), psm[0:1, 0:128]))
                p.vtiny(lambda e: e.reciprocal(rrow.ap(0, 1, 0, 128), rrow.ap(0, 1, 0, 128)), inc=True)
                p.wait("scalar", p.sems["vector"])
                p.s(lambda e, pso=pso, r0=r0: e.activation(osc.ap(r0, r0 + 64, 0, 128),
                                                           pso[r0:r0 + 64, 0:128], AF.Copy), inc=True)
                pbB = p.ps[4]
                p.wait("tensor", p.sems["vector"])
                p.t(lambda e, pbB=pbB, r0=r0: e.matmul(pbB[r0:r0 + 64, 0:128], sb["sb_ones"][0:1, 0:64],
                                                       rrow.ap(0, 1, 0, 128), start=True, stop=True))
                tv = p.pe_done()
                p.wait("vector", p.sems["tensor"], tv)
                p.wait("vector", p.sems["scalar"])
                p.v(lambda e, pbB=pbB, h=h, r0=r0, c0=c0: e.tensor_mul(
                    ytil.ap(r0, r0 + 64, (h // 2) * T + c0, (h // 2) * T + c0 + 128),
                    osc.ap(r0, r0 + 64, 0, 128), pbB[r0:r0 + 64, 0:128]), inc=True)
        ba.reset(pbuf.off)

        def evac_wo(mi, c0, c1, pband, tv, g=g):
            p.wait("vector", p.sems["tensor"], tv)
            spans = []
            a0 = (c0 + 64) % T
            w = c1 - c0
            if a0 + w <= T:
                spans.append((0, a0, w))
            else:
                spans.append((0, a0, T - a0))
                spans.append((T - a0, 0, w - (T - a0)))
            for (off, dst, ww) in spans:
                if g == 0:
                    p.v(lambda e, mi=mi, off=off, dst=dst, ww=ww, pband=pband: e.scalar_tensor_tensor(
                        xacc.ap(0, 128, mi * T + dst, mi * T + dst + ww), pband[:, off:off + ww],
                        p.scol("l2_bo", mi), xacc.ap(0, 128, mi * T + dst, mi * T + dst + ww),
                        ALU.add, ALU.add), inc=True)
                else:
                    p.v(lambda e, mi=mi, off=off, dst=dst, ww=ww, pband=pband: e.tensor_add(
                        xacc.ap(0, 128, mi * T + dst, mi * T + dst + ww),
                        xacc.ap(0, 128, mi * T + dst, mi * T + dst + ww),
                        pband[:, off:off + ww]), inc=True)
        emit_linear(p, "l2_wo", 8, 512,
                    lambda k, c0, c1: ytil.ap(0, 128, k * T + c0, k * T + c1),
                    0, T, evac_wo, k0=g * 512)
        fa.reset(gmark_f)
        ba.reset(gmark_b)
    fa.reset(famark)
    ba.reset(bamark)


def emit_l2_ffn(p, xacc, xn_dram, negmuB, rstdB, scr):
    sb = p.sb
    fa, ba = p.fa, p.ba
    famark, bamark = fa.mark(), ba.mark()

    sc = fa.alloc(8, "l2sc")
    emit_mean_abs(p, "l2_w1", 32, D, 4096 * 1024, sc, 0)
    emit_mean_abs(p, "l2_w2", 8, 4096, 4096 * 1024, sc, 1)
    p.vtiny(lambda e: e.tensor_scalar_max(sc.ap(0, 1, 0, 2), sc.ap(0, 1, 0, 2), 1e-5))
    p.vtiny(lambda e: e.tensor_scalar_mul(sc.ap(0, 1, 2, 4), sc.ap(0, 1, 0, 2), 0.5), inc=True)
    thr = fa.alloc(4, "l2thr")
    bcast_col(p, sc.ap(0, 1, 2, 3), thr.ap(0, 128, 0, 1))
    bcast_col(p, sc.ap(0, 1, 2, 3), thr.ap(0, 128, 1, 2), scale=-1.0)
    bcast_col(p, sc.ap(0, 1, 3, 4), thr.ap(0, 128, 2, 3))
    bcast_col(p, sc.ap(0, 1, 3, 4), thr.ap(0, 128, 3, 4), scale=-1.0)
    for wname, nm, K, ti, qname in (("l2_w1", 32, D, 0, "w1q"), ("l2_w2", 8, 4096, 1, "w2q")):
        nk = (K + 127) // 128
        stage = fa.alloc(nk * 128, "qst")
        wq = ba.alloc(nk * 128, "qwq")
        for mi in range(nm):
            p.wait("sync", p.sems["vector"])
            dvv = p.dma(stage.ap().rearrange("p (k c) -> p k c", c=128),
                        sb[wname][mi].rearrange("(k p) c -> p k c", p=128))
            p.wait("vector", p.sem_dma, dvv)
            p.v(lambda e, ti=ti: e.tensor_single_scalar(wq.ap(), stage.ap(),
                                                        thr.ap(0, 128, 2 * ti + 1, 2 * ti + 2),
                                                        ALU.is_lt))
            p.v(lambda e, ti=ti: e.scalar_tensor_tensor(wq.ap(), stage.ap(),
                                                        thr.ap(0, 128, 2 * ti, 2 * ti + 1),
                                                        wq.ap(), ALU.is_gt, ALU.subtract), inc=True)
            p.wait("sync", p.sems["vector"])
            p.dma(scr[qname][mi].rearrange("(k p) c -> p k c", p=128),
                  wq.ap().rearrange("p (k c) -> p k c", c=128))
        fa.reset(stage.off)
        ba.reset(wq.off)

    w1b = fa.alloc(T, "w1b")
    w2b = fa.alloc(T, "w2b")
    sxB = fa.alloc(T, "sxB")
    hsB = fa.alloc(T, "hsB")
    rows = fa.alloc(2 * T, "l2rows")
    mrow = Buf(rows.t, rows.off, T, F32)
    sfb = Buf(w1b.t, w1b.off, 512, F32)
    it = p.ia.alloc(512, "l2i")

    def make_rows(inv_col):
        p.v(lambda e: e.tensor_scalar_max(mrow.ap(0, 1, 0, T), mrow.ap(0, 1, 0, T), 1e-5), inc=True)
        p.v(lambda e: e.reciprocal(w1b.ap(0, 1, 0, T), mrow.ap(0, 1, 0, T)))
        p.v(lambda e: e.tensor_scalar_mul(w1b.ap(0, 1, 0, T), w1b.ap(0, 1, 0, T), 127.0))
        p.v(lambda e: e.tensor_scalar(w2b.ap(0, 1, 0, T), mrow.ap(0, 1, 0, T), inv_col,
                                      float(1.0 / 127.0), ALU.mult, ALU.mult), inc=True)
        bcast_row(p, lambda c0, c1: w1b.ap(0, 1, c0, c1), T, sxB)
        bcast_row(p, lambda c0, c1: w2b.ap(0, 1, c0, c1), T, hsB)

    def wrow(c0, w, pb):
        p.v(lambda e, pb=pb, c0=c0, w=w: e.tensor_copy(mrow.ap(0, 1, c0, c0 + w), pb[0:1, 0:w]), inc=True)

    def stats_rows_finish():
        tvs = p.sems["tensor"].n
        p.wait("vector", p.sems["tensor"], tvs)
        for j in range(4):
            c0 = j * 512
            p.v(lambda e, j=j, c0=c0: e.tensor_scalar_mul(rows.ap(0, 1, T + c0, T + c0 + 512),
                                                          p.ps[3 + j][0:1, 0:512], 1.0 / 4096.0))
            p.v(lambda e, j=j, c0=c0: e.tensor_mul(sfb.ap(0, 1, 0, 512), rows.ap(0, 1, c0, c0 + 512),
                                                   rows.ap(0, 1, c0, c0 + 512)))
            p.v(lambda e, j=j, c0=c0: e.tensor_sub(rows.ap(0, 1, T + c0, T + c0 + 512),
                                                   rows.ap(0, 1, T + c0, T + c0 + 512),
                                                   sfb.ap(0, 1, 0, 512)), inc=True)
            p.wait("scalar", p.sems["vector"])
            p.s(lambda e, c0=c0: e.activation(rows.ap(0, 1, T + c0, T + c0 + 512),
                                              rows.ap(0, 1, T + c0, T + c0 + 512), AF.Sqrt,
                                              bias=p.scol("k_eps5")[0:1, 0:1]), inc=True)
            p.wait("vector", p.sems["scalar"])
            p.v(lambda e, c0=c0: e.reciprocal(rows.ap(0, 1, T + c0, T + c0 + 512),
                                              rows.ap(0, 1, T + c0, T + c0 + 512)), inc=True)

    xnl = Buf(sxB.t, sxB.off, T, F32)
    for passi in range(2):
        for r in range(NRC):
            p.wait("sync", p.sems["tensor"])
            p.wait("sync", p.sems["scalar"])
            dvx = p.dma(xnl.ap(), scr["xnf"][r * 128:(r + 1) * 128, :])
            if passi == 1:
                p.wait("scalar", p.sem_dma, dvx)
                p.wait("scalar", p.sems["tensor"])
                p.s(lambda e: e.activation(w1b.ap(), xnl.ap(), AF.Square), inc=True)
                p.wait("tensor", p.sems["scalar"])
                srcb = w1b
            else:
                p.wait("tensor", p.sem_dma, dvx)
                srcb = xnl
            if r == 0:
                p.wait("tensor", p.sems["vector"])
            for j in range(4):
                p.t(lambda e, j=j, r=r, srcb=srcb: e.matmul(
                    p.ps[3 + j][0:1, 0:512], sb["sb_ones"][:, 0:1],
                    srcb.ap(0, 128, j * 512, j * 512 + 512), start=(r == 0), stop=(r == NRC - 1)))
            p.pe_done()
        tvs = p.sems["tensor"].n
        p.wait("vector", p.sems["tensor"], tvs)
        for j in range(4):
            c0 = j * 512
            if passi == 0:
                p.v(lambda e, j=j, c0=c0: e.tensor_scalar_mul(rows.ap(0, 1, c0, c0 + 512),
                                                              p.ps[3 + j][0:1, 0:512], 1.0 / D), inc=(j == 3))
            else:
                p.v(lambda e, j=j, c0=c0: e.tensor_scalar_mul(rows.ap(0, 1, T + c0, T + c0 + 512),
                                                              p.ps[3 + j][0:1, 0:512], 1.0 / D))
                p.v(lambda e, j=j, c0=c0: e.tensor_mul(sfb.ap(0, 1, 0, 512), rows.ap(0, 1, c0, c0 + 512),
                                                       rows.ap(0, 1, c0, c0 + 512)))
                p.v(lambda e, j=j, c0=c0: e.tensor_sub(rows.ap(0, 1, T + c0, T + c0 + 512),
                                                       rows.ap(0, 1, T + c0, T + c0 + 512),
                                                       sfb.ap(0, 1, 0, 512)), inc=True)
                p.wait("scalar", p.sems["vector"])
                p.s(lambda e, c0=c0: e.activation(rows.ap(0, 1, T + c0, T + c0 + 512),
                                                  rows.ap(0, 1, T + c0, T + c0 + 512), AF.Sqrt,
                                                  bias=p.scol("k_eps5")[0:1, 0:1]), inc=True)
                p.wait("vector", p.sems["scalar"])
                p.v(lambda e, c0=c0: e.reciprocal(rows.ap(0, 1, T + c0, T + c0 + 512),
                                                  rows.ap(0, 1, T + c0, T + c0 + 512)), inc=True)
    bcast_row(p, lambda c0, c1: rows.ap(0, 1, c0, c1), T, negmuB, scale=-1.0)
    bcast_row(p, lambda c0, c1: rows.ap(0, 1, T + c0, T + c1), T, rstdB)

    h1 = ba.alloc(8 * T, "h1")
    xq = h1
    for r in range(NRC):
        p.wait("sync", p.sems["vector"])
        dvx = p.dma(xnl.ap(), scr["xnf"][r * 128:(r + 1) * 128, :])
        p.wait("vector", p.sem_dma, dvx)
        p.v(lambda e: e.tensor_add(w1b.ap(), xnl.ap(), negmuB.ap()))
        p.v(lambda e: e.tensor_mul(w1b.ap(), w1b.ap(), rstdB.ap()))
        p.v(lambda e, r=r: e.tensor_scalar(h1.ap(0, 128, r * T, (r + 1) * T), w1b.ap(),
                                           p.scol("l2_ln1w", r), p.scol("l2_ln1b", r),
                                           ALU.mult, ALU.add), inc=True)
        p.wait("scalar", p.sems["vector"])
        p.s(lambda e, r=r: e.activation(hsB.ap(), h1.ap(0, 128, r * T, (r + 1) * T), AF.Abs), inc=True)
        p.wait("vector", p.sems["scalar"])
        if r == 0:
            p.v(lambda e: e.tensor_copy(w2b.ap(), hsB.ap()), inc=True)
        else:
            p.v(lambda e: e.tensor_max(w2b.ap(), w2b.ap(), hsB.ap()), inc=True)
    part_max_to_row(p, w2b, T, wrow, scr=w1b)
    make_rows(sc.ap(0, 1, 0, 1))
    for r in range(NRC):
        for qc in range(0, T, 512):
            quant_tile(p, h1.ap(0, 128, r * T + qc, r * T + qc + 512), sxB.ap(0, 128, qc, qc + 512),
                       xq.ap(0, 128, r * T + qc, r * T + qc + 512),
                       w1b.ap(0, 128, qc, qc + 512), it.ap(0, 128, 0, 512))

    hb = ba.alloc(T, "hb2")
    for mi in range(32):
        def evac_h(mi2, c0, c1, pband, tv, mi=mi):
            p.wait("vector", p.sems["tensor"], tv)
            p.v(lambda e, c0=c0, c1=c1, pband=pband: e.tensor_mul(
                w1b.ap(0, 128, c0, c1), pband[:, 0:c1 - c0], hsB.ap(0, 128, c0, c1)))
            p.v(lambda e, c0=c0, c1=c1, mi=mi: e.tensor_scalar_add(
                w1b.ap(0, 128, c0, c1), w1b.ap(0, 128, c0, c1), p.scol("l2_b1", mi)), inc=True)
        emit_linear(p, None, 32, D, lambda k, c0, c1: xq.ap(0, 128, k * T + c0, k * T + c1),
                    0, T, evac_h, m_list=[mi], wq_bf16_dram=scr["w1q"])
        p.wait("scalar", p.sems["vector"])
        p.wait("scalar", p.sem_dma)
        p.s(lambda e: e.activation(hb.ap(), w1b.ap(), AF.Gelu), inc=True)
        p.wait("sync", p.sems["scalar"])
        p.dma(scr["s"][mi * 128:(mi + 1) * 128, :], hb.ap())
        p.wait("tensor", p.sems["scalar"])
        if mi == 0:
            p.wait("tensor", p.sems["vector"])
        for j in range(4):
            p.t(lambda e, j=j, mi=mi: e.matmul(p.ps[3 + j][0:1, 0:512], p.onesb.ap(0, 128, 0, 1),
                                               hb.ap(0, 128, j * 512, j * 512 + 512),
                                               start=(mi == 0), stop=(mi == 31)))
        p.pe_done()
    tvs = p.sems["tensor"].n
    p.wait("vector", p.sems["tensor"], tvs)
    for j in range(4):
        c0 = j * 512
        p.v(lambda e, j=j, c0=c0: e.tensor_scalar_mul(rows.ap(0, 1, c0, c0 + 512),
                                                      p.ps[3 + j][0:1, 0:512], 1.0 / 4096.0), inc=(j == 3))
    for mi in range(32):
        p.wait("sync", p.sems["tensor"])
        p.wait("sync", p.sems["scalar"])
        dvx = p.dma(hb.ap(), scr["s"][mi * 128:(mi + 1) * 128, :])
        p.wait("scalar", p.sem_dma, dvx)
        p.wait("scalar", p.sems["tensor"])
        p.s(lambda e: e.activation(w1b.ap(), hb.ap(), AF.Square), inc=True)
        p.wait("tensor", p.sems["scalar"])
        if mi == 0:
            p.wait("tensor", p.sems["vector"])
        for j in range(4):
            p.t(lambda e, j=j, mi=mi: e.matmul(p.ps[3 + j][0:1, 0:512], sb["sb_ones"][:, 0:1],
                                               w1b.ap(0, 128, j * 512, j * 512 + 512),
                                               start=(mi == 0), stop=(mi == 31)))
        p.pe_done()
    stats_rows_finish()
    bcast_row(p, lambda c0, c1: rows.ap(0, 1, c0, c1), T, negmuB, scale=-1.0)
    bcast_row(p, lambda c0, c1: rows.ap(0, 1, T + c0, T + c1), T, rstdB)

    ba.reset(bamark)
    hq = ba.alloc(32 * 512, "hq2")
    gld = ba.alloc(512, "gld")

    def h2_tile(k, c0, out_ap):
        p.v(lambda e, c0=c0: e.tensor_add(sfb.ap(0, 128, 0, 512), gld.ap(0, 128, 0, 512),
                                          negmuB.ap(0, 128, c0, c0 + 512)))
        p.v(lambda e, c0=c0: e.tensor_mul(sfb.ap(0, 128, 0, 512), sfb.ap(0, 128, 0, 512),
                                          rstdB.ap(0, 128, c0, c0 + 512)))
        p.v(lambda e, k=k, out_ap=out_ap: e.tensor_scalar(out_ap, sfb.ap(0, 128, 0, 512),
                                                          p.scol("l2_ln2w", k), p.scol("l2_ln2b", k),
                                                          ALU.mult, ALU.add), inc=True)

    for nchunk in range(4):
        c0 = nchunk * 512
        c1 = c0 + 512
        for k in range(32):
            p.wait("sync", p.sems["vector"])
            p.wait("sync", p.sems["scalar"])
            dvv = p.dma(gld.ap(0, 128, 0, 512), scr["s"][k * 128:(k + 1) * 128, c0:c1])
            p.wait("vector", p.sem_dma, dvv)
            h2_tile(k, c0, w1b.ap(0, 128, 1024, 1536))
            p.wait("scalar", p.sems["vector"])
            p.s(lambda e: e.activation(w1b.ap(0, 128, 512, 1024),
                                       w1b.ap(0, 128, 1024, 1536), AF.Abs), inc=True)
            p.wait("vector", p.sems["scalar"])
            if k == 0:
                p.v(lambda e: e.tensor_copy(w2b.ap(0, 128, 0, 512), w1b.ap(0, 128, 512, 1024)), inc=True)
            else:
                p.v(lambda e: e.tensor_max(w2b.ap(0, 128, 0, 512), w2b.ap(0, 128, 0, 512),
                                           w1b.ap(0, 128, 512, 1024)), inc=True)
        def wrow_c(cc0, w, pb, c0=c0):
            p.v(lambda e, pb=pb, cc0=cc0, w=w: e.tensor_copy(mrow.ap(0, 1, c0 + cc0, c0 + cc0 + w),
                                                             pb[0:1, 0:w]), inc=True)
        part_max_to_row(p, Buf(w2b.t, w2b.off, 512, F32), 512, wrow_c,
                        scr=Buf(w1b.t, w1b.off + 512, 512, F32))
        p.v(lambda e, c0=c0, c1=c1: e.tensor_scalar_max(mrow.ap(0, 1, c0, c1), mrow.ap(0, 1, c0, c1),
                                                        1e-5), inc=True)
        p.v(lambda e, c0=c0, c1=c1: e.reciprocal(w1b.ap(0, 1, 0, 512), mrow.ap(0, 1, c0, c1)))
        p.v(lambda e: e.tensor_scalar_mul(w1b.ap(0, 1, 0, 512), w1b.ap(0, 1, 0, 512), 127.0))
        p.v(lambda e, c0=c0, c1=c1: e.tensor_scalar(w2b.ap(0, 1, 0, 512), mrow.ap(0, 1, c0, c1),
                                                    sc.ap(0, 1, 1, 2), float(1.0 / 127.0),
                                                    ALU.mult, ALU.mult), inc=True)
        bcast_row(p, lambda cc0, cc1: w1b.ap(0, 1, cc0, cc1), 512, sxB, out_c0=c0)
        bcast_row(p, lambda cc0, cc1: w2b.ap(0, 1, cc0, cc1), 512, hsB, out_c0=c0)
        for k in range(32):
            p.wait("sync", p.sems["vector"])
            dvv = p.dma(gld.ap(0, 128, 0, 512), scr["s"][k * 128:(k + 1) * 128, c0:c1])
            p.wait("vector", p.sem_dma, dvv)
            h2_tile(k, c0, w1b.ap(0, 128, 1024, 1536))
            quant_tile(p, w1b.ap(0, 128, 1024, 1536), sxB.ap(0, 128, c0, c1),
                       hq.ap(0, 128, k * 512, (k + 1) * 512), w1b.ap(0, 128, 512, 1024),
                       it.ap(0, 128, 0, 512))
        def evac_o(mi, cc0, cc1, pband, tv, c0=c0):
            p.wait("vector", p.sems["tensor"], tv)
            p.v(lambda e, pband=pband, c0=c0: e.tensor_mul(sfb.ap(0, 128, 0, 512), pband[:, 0:512],
                                                           hsB.ap(0, 128, c0, c0 + 512)))
            p.v(lambda e, mi=mi, c0=c0: e.scalar_tensor_tensor(
                xacc.ap(0, 128, mi * T + c0, mi * T + c0 + 512), sfb.ap(0, 128, 0, 512),
                p.scol("l2_b2", mi), xacc.ap(0, 128, mi * T + c0, mi * T + c0 + 512),
                ALU.add, ALU.add), inc=True)
        emit_linear(p, None, 8, 4096, lambda k, cc0, cc1: hq.ap(0, 128, k * 512, (k + 1) * 512),
                    c0, c1, evac_o, wq_bf16_dram=scr["w2q"])
    fa.reset(famark)
    ba.reset(bamark)
    p.ia.reset(0)





def emit_l3_attn(p, xacc, xn_dram, negmuB, rstdB, scr):
    sb = p.sb
    fa, ba = p.fa, p.ba
    famark, bamark = fa.mark(), ba.mark()
    ones_t = negmuB
    tmp = None
    p.wait("vector", p.sem_dma)
    p.v(lambda e: e.memset(ones_t.ap(), 1.0), inc=True)

    qz = fa.alloc(T, "qz")
    kz = fa.alloc(T, "kz")
    kvz = fa.alloc(T, "kvz")
    kcum = fa.alloc(T, "kcum")
    kvcum = Buf(rstdB.t, rstdB.off, T, F32)
    ytl = ba.alloc(8 * T, "ytl")
    vb = ba.alloc(T, "vb")
    xsl = ba.alloc(8 * 512, "xsl")

    def elu1(buf, tmp):
        p.v(lambda e: e.tensor_scalar_min(tmp.ap(), buf.ap(), 0.0))
        p.wait("scalar", p.sems["vector"])
        p.v(lambda e: e.tensor_scalar_max(buf.ap(), buf.ap(), 0.0), inc=True)
        p.wait("scalar", p.sems["vector"])
        p.s(lambda e: e.activation(tmp.ap(), tmp.ap(), AF.Exp), inc=True)
        p.wait("vector", p.sems["scalar"])
        p.v(lambda e: e.tensor_add(buf.ap(), buf.ap(), tmp.ap()), inc=True)

    for r in range(NRC):
        def evac_qkv(mi, c0, c1, pband, tv, r=r):
            w = c1 - c0
            p.wait("vector", p.sems["tensor"], tv)
            if mi < 8:
                p.v(lambda e, pband=pband, w=w, mi=mi, c0=c0, c1=c1: e.tensor_scalar(
                    qz.ap(0, 128, c0, c1), pband[:, 0:w], p.scol("l3_bqkv", mi),
                    p.scol("l3_scale"), ALU.add, ALU.mult), inc=True)
            elif mi < 16:
                p.v(lambda e, pband=pband, w=w, mi=mi, c0=c0, c1=c1: e.tensor_scalar(
                    kz.ap(0, 128, c0, c1), pband[:, 0:w], p.scol("l3_bqkv", mi),
                    p.scol("l3_scale"), ALU.add, ALU.mult), inc=True)
            else:
                p.v(lambda e, pband=pband, w=w, mi=mi, c0=c0, c1=c1: e.tensor_scalar_add(
                    vb.ap(0, 128, c0, c1), pband[:, 0:w], p.scol("l3_bqkv", mi)), inc=True)
        for c0 in range(0, T, 512):
            p.wait("sync", p.sems["tensor"])
            dvx = load_xn_slice(p, xn_dram, xsl, c0, c0 + 512)
            p.wait("tensor", p.sem_dma, dvx)
            emit_linear(p, "l3_wqkv", 24, D,
                        lambda k, cc0, cc1, c0=c0: xsl.ap(0, 128, k * 512 + (cc0 - c0), k * 512 + (cc1 - c0)),
                        c0, c0 + 512, evac_qkv, m_list=[r, 8 + r, 16 + r])
        elu1(qz, kcum)
        elu1(kz, kcum)
        p.v(lambda e: e.tensor_mul(kvz.ap(), kz.ap(), vb.ap()), inc=True)
        p.v(lambda e: e.tensor_tensor_scan(kcum.ap(), ones_t.ap(), kz.ap(), 0.0, ALU.mult, ALU.add))
        p.v(lambda e: e.tensor_tensor_scan(kvcum.ap(), ones_t.ap(), kvz.ap(), 0.0, ALU.mult, ALU.add),
            inc=True)
        tmp = kz
        p.v(lambda e: e.tensor_mul(tmp.ap(), qz.ap(), kcum.ap()), inc=True)
        for c0 in range(0, T, 512):
            p.wait("tensor", p.sems["vector"])
            p.t(lambda e, c0=c0: e.matmul(p.ps[5][0:2, 0:512], p.small("ones64bd"),
                                          tmp.ap(0, 128, c0, c0 + 512), start=True, stop=True))
            tv = p.pe_done()
            p.wait("vector", p.sems["tensor"], tv)
            p.v(lambda e, c0=c0: e.tensor_copy(kcum.ap(0, 2, c0, c0 + 512), p.ps[5][0:2, 0:512]),
                inc=True)
        p.v(lambda e: e.tensor_scalar_max(kcum.ap(0, 2, 0, T), kcum.ap(0, 2, 0, T), 1e-3))
        p.v(lambda e: e.reciprocal(kcum.ap(0, 2, 0, T), kcum.ap(0, 2, 0, T)), inc=True)
        p.v(lambda e: e.tensor_mul(tmp.ap(), qz.ap(), kvcum.ap()), inc=True)
        for c0 in range(0, T, 512):
            p.wait("tensor", p.sems["vector"])
            p.t(lambda e, c0=c0: e.matmul(p.ps[4][:, 0:512], sb["sb_bd2"][:, :],
                                          kcum.ap(0, 2, c0, c0 + 512), start=True, stop=True))
            tv = p.pe_done()
            p.wait("vector", p.sems["tensor"], tv)
            p.v(lambda e, r=r, c0=c0: e.tensor_mul(ytl.ap(0, 128, r * T + c0, r * T + c0 + 512),
                                                   tmp.ap(0, 128, c0, c0 + 512), p.ps[4][:, 0:512]),
                inc=True)
    def evac_proj(mi, c0, c1, pband, tv):
        p.wait("vector", p.sems["tensor"], tv)
        p.v(lambda e, mi=mi, c0=c0, c1=c1, pband=pband: e.scalar_tensor_tensor(
            xacc.ap(0, 128, mi * T + c0, mi * T + c1), pband[:, 0:c1 - c0],
            p.scol("l3_bproj", mi), xacc.ap(0, 128, mi * T + c0, mi * T + c1),
            ALU.add, ALU.add), inc=True)
    emit_linear(p, "l3_wproj", 8, D, lambda k, c0, c1: ytl.ap(0, 128, k * T + c0, k * T + c1),
                0, T, evac_proj)
    fa.reset(famark)
    ba.reset(bamark)


def emit_l3_ffn(p, xacc, xn_dram, negmuB, rstdB, scr):
    sb = p.sb
    fa, ba = p.fa, p.ba
    famark, bamark = fa.mark(), ba.mark()

    lgt = fa.alloc(T, "lgt")
    selx = fa.alloc(T, "selx")
    srow0 = fa.alloc(64, "srow0")
    xsl = ba.alloc(8 * 512, "xsl")

    def evac_lgt(mi, c0, c1, pband, tv):
        p.wait("vector", p.sems["tensor"], tv)
        p.v(lambda e, c0=c0, c1=c1, pband=pband: e.tensor_copy(lgt.ap(0, 128, c0, c1),
                                                               pband[:, 0:c1 - c0]), inc=True)
    for c0 in range(0, T, 512):
        p.wait("sync", p.sems["tensor"])
        dvx = load_xn_slice(p, xn_dram, xsl, c0, c0 + 512)
        p.wait("tensor", p.sem_dma, dvx)
        emit_linear(p, "l3_wg", 1, D,
                    lambda k, cc0, cc1, c0=c0: xsl.ap(0, 128, k * 512 + (cc0 - c0), k * 512 + (cc1 - c0)),
                    c0, c0 + 512, evac_lgt)

    for t0 in range(0, T, 128):
        pb = p.ps[6]
        p.wait("tensor", p.sems["vector"])
        p.t(lambda e, t0=t0, pb=pb: e.transpose(pb[:, 0:128], lgt.ap(0, 128, t0, t0 + 128),
                                                sb["sb_ident"][:]))
        tv = p.pe_done()
        p.wait("vector", p.sems["tensor"], tv)
        p.vtiny(lambda e, pb=pb: e.tensor_copy(srow0.ap(0, 128, 0, 8), pb[:, 0:8]))
        p.vtiny(lambda e: e.reduce_max(srow0.ap(0, 128, 8, 9), srow0.ap(0, 128, 0, 8), axis=AX.X))
        p.vtiny(lambda e: e.tensor_single_scalar(srow0.ap(0, 128, 16, 24), srow0.ap(0, 128, 0, 8),
                                                 srow0.ap(0, 128, 8, 9), ALU.is_ge))
        p.vtiny(lambda e: e.tensor_tensor_scan(srow0.ap(0, 128, 24, 32), sb["sb_ones"][:, 0:8],
                                               srow0.ap(0, 128, 16, 24), 0.0, ALU.mult, ALU.add))
        p.vtiny(lambda e: e.tensor_single_scalar(srow0.ap(0, 128, 24, 32), srow0.ap(0, 128, 24, 32),
                                                 1.0, ALU.is_equal))
        p.vtiny(lambda e: e.tensor_mul(srow0.ap(0, 128, 16, 24), srow0.ap(0, 128, 16, 24),
                                       srow0.ap(0, 128, 24, 32)), inc=True)
        p.wait("tensor", p.sems["vector"])
        p.t(lambda e, pb=pb: e.transpose(pb[0:8, 0:128], srow0.ap(0, 128, 16, 24), sb["sb_ident"][:]))
        tv = p.pe_done()
        p.wait("vector", p.sems["tensor"], tv)
        p.v(lambda e, t0=t0, pb=pb: e.tensor_copy(selx.ap(0, 8, t0, t0 + 128), pb[0:8, 0:128]),
            inc=True)
    p.wait("sync", p.sems["vector"])
    dvsel = p.dma(scr["sel"][:, :], selx.ap(0, 8, 0, T))
    p.wait("vector", p.sem_dma)
    p.wait("sync", p.sem_dma, dvsel)
    fa.reset(lgt.off)
    rows = fa.alloc(2 * T, "l3rows")
    selB = fa.alloc(T, "selB")
    srow = fa.alloc(T, "srow")
    bcol = fa.alloc(4, "bcol")

    xnl = Buf(selB.t, selB.off, T, F32)
    for passi in range(2):
        for r in range(NRC):
            p.wait("sync", p.sems["tensor"])
            p.wait("sync", p.sems["scalar"])
            dvx = p.dma(xnl.ap(), scr["xnf"][r * 128:(r + 1) * 128, :])
            if passi == 1:
                p.wait("scalar", p.sem_dma, dvx)
                p.wait("scalar", p.sems["tensor"])
                p.s(lambda e: e.activation(srow.ap(), xnl.ap(), AF.Square), inc=True)
                p.wait("tensor", p.sems["scalar"])
                srcb = srow
            else:
                p.wait("tensor", p.sem_dma, dvx)
                srcb = xnl
            if r == 0:
                p.wait("tensor", p.sems["vector"])
            for j in range(4):
                p.t(lambda e, j=j, r=r, srcb=srcb: e.matmul(
                    p.ps[3 + j][0:1, 0:512], sb["sb_ones"][:, 0:1],
                    srcb.ap(0, 128, j * 512, j * 512 + 512), start=(r == 0), stop=(r == NRC - 1)))
            p.pe_done()
        tvs = p.sems["tensor"].n
        p.wait("vector", p.sems["tensor"], tvs)
        for j in range(4):
            c0 = j * 512
            if passi == 0:
                p.v(lambda e, j=j, c0=c0: e.tensor_scalar_mul(rows.ap(0, 1, c0, c0 + 512),
                                                              p.ps[3 + j][0:1, 0:512], 1.0 / D),
                    inc=(j == 3))
            else:
                p.v(lambda e, j=j, c0=c0: e.tensor_scalar_mul(rows.ap(0, 1, T + c0, T + c0 + 512),
                                                              p.ps[3 + j][0:1, 0:512], 1.0 / D))
                p.v(lambda e, j=j, c0=c0: e.tensor_mul(srow.ap(0, 1, 0, 512), rows.ap(0, 1, c0, c0 + 512),
                                                       rows.ap(0, 1, c0, c0 + 512)))
                p.v(lambda e, j=j, c0=c0: e.tensor_sub(rows.ap(0, 1, T + c0, T + c0 + 512),
                                                       rows.ap(0, 1, T + c0, T + c0 + 512),
                                                       srow.ap(0, 1, 0, 512)), inc=True)
                p.wait("scalar", p.sems["vector"])
                p.s(lambda e, c0=c0: e.activation(rows.ap(0, 1, T + c0, T + c0 + 512),
                                                  rows.ap(0, 1, T + c0, T + c0 + 512), AF.Sqrt,
                                                  bias=p.scol("k_eps5")[0:1, 0:1]), inc=True)
                p.wait("vector", p.sems["scalar"])
                p.v(lambda e, c0=c0: e.reciprocal(rows.ap(0, 1, T + c0, T + c0 + 512),
                                                  rows.ap(0, 1, T + c0, T + c0 + 512)), inc=True)
    bcast_row(p, lambda c0, c1: rows.ap(0, 1, c0, c1), T, negmuB, scale=-1.0)
    bcast_row(p, lambda c0, c1: rows.ap(0, 1, T + c0, T + c1), T, rstdB)
    xht = ba.alloc(8 * T, "xht")
    for r in range(NRC):
        p.wait("sync", p.sems["vector"])
        dvx = p.dma(xnl.ap(), scr["xnf"][r * 128:(r + 1) * 128, :])
        p.wait("vector", p.sem_dma, dvx)
        p.v(lambda e: e.tensor_add(srow.ap(), xnl.ap(), negmuB.ap()))
        p.v(lambda e, r=r: e.tensor_mul(xht.ap(0, 128, r * T, (r + 1) * T), srow.ap(), rstdB.ap()),
            inc=True)

    he = ba.alloc(2 * T, "he")
    wstg = fa.alloc(1024, "wstg")
    wbfU = ba.alloc(1024, "wbfU")
    lnbb = ba.alloc(8, "lnbb")
    for e_i in range(8):
        p.wait("sync", p.sems["vector"])
        p.wait("sync", p.sems["tensor"])
        dvs = p.dma(srow.ap(0, 1, 0, T), scr["sel"][e_i:e_i + 1, :])
        p.wait("vector", p.sem_dma, dvs)
        p.wait("tensor", p.sem_dma, dvs)
        bcast_row(p, lambda c0, c1: srow.ap(0, 1, c0, c1), T, selB)
        for mi in range(2):
            p.wait("sync", p.sems["vector"])
            p.wait("sync", p.sems["tensor"])
            dvv = p.dma(wstg.ap().rearrange("p (k c) -> p k c", c=128),
                        sb[f"l3_w1_{e_i}"][mi].rearrange("(k p) c -> p k c", p=128))
            p.wait("vector", p.sem_dma, dvv)
            p.v(lambda e: e.tensor_copy(wbfU.ap(), wstg.ap()), inc=True)
            for k in range(8):
                p.vtiny(lambda e, k=k, e_i=e_i: e.tensor_copy(lnbb.ap(0, 128, k, k + 1),
                                                              p.scol(f"l3_lnb_{e_i}", k)),
                        inc=(k == 7))
            p.wait("tensor", p.sems["vector"])
            for k in range(8):
                p.t(lambda e, k=k: e.matmul(p.ps[5][:, 0:1], wbfU.ap(0, 128, k * 128, (k + 1) * 128),
                                            lnbb.ap(0, 128, k, k + 1), start=(k == 0), stop=(k == 7)))
            tv = p.pe_done()
            p.wait("vector", p.sems["tensor"], tv)
            p.vtiny(lambda e, mi=mi, e_i=e_i: e.scalar_tensor_tensor(
                bcol.ap(0, 128, mi, mi + 1), p.ps[5][:, 0:1], 1.0,
                p.scol(f"l3_b1_{e_i}", mi), ALU.mult, ALU.add), inc=True)
            for k in range(8):
                p.v(lambda e, k=k, e_i=e_i: e.tensor_scalar_mul(
                    wbfU.ap(0, 128, k * 128, (k + 1) * 128), wbfU.ap(0, 128, k * 128, (k + 1) * 128),
                    p.scol(f"l3_lns_{e_i}", k)), inc=(k == 7))
            p.wait("tensor", p.sems["vector"])
            for ci in range(4):
                c0 = ci * 512
                pband = p.ps[ci % 2]
                p.wait("tensor", p.sems["scalar"])
                for k in range(8):
                    p.t(lambda e, pband=pband, k=k, c0=c0: e.matmul(
                        pband[:, 0:512], wbfU.ap(0, 128, k * 128, (k + 1) * 128),
                        xht.ap(0, 128, k * T + c0, k * T + c0 + 512), start=(k == 0), stop=(k == 7)))
                tv = p.pe_done()
                p.wait("scalar", p.sems["tensor"], tv)
                p.wait("scalar", p.sems["vector"])
                p.s(lambda e, mi=mi, c0=c0, pband=pband: e.activation(
                    he.ap(0, 128, mi * T + c0, mi * T + c0 + 512), pband[:, 0:512], AF.Gelu,
                    bias=bcol.ap(0, 128, mi, mi + 1)), inc=True)
        def evac_o(mi, c0, c1, pband, tv, e_i=e_i):
            p.wait("vector", p.sems["tensor"], tv)
            p.v(lambda e, mi=mi, c0=c0, c1=c1, pband=pband, e_i=e_i: e.tensor_scalar_add(
                srow.ap(0, 128, 0, 512), pband[:, 0:c1 - c0], p.scol(f"l3_b2_{e_i}", mi)))
            p.v(lambda e, c0=c0, c1=c1: e.tensor_mul(srow.ap(0, 128, 0, 512), srow.ap(0, 128, 0, 512),
                                                     selB.ap(0, 128, c0, c1)))
            p.v(lambda e, mi=mi, c0=c0, c1=c1: e.tensor_add(
                xacc.ap(0, 128, mi * T + c0, mi * T + c1),
                xacc.ap(0, 128, mi * T + c0, mi * T + c1), srow.ap(0, 128, 0, 512)), inc=True)
        emit_linear(p, f"l3_w2_{e_i}", 8, 256,
                    lambda k, c0, c1: he.ap(0, 128, k * T + c0, k * T + c1),
                    0, T, evac_o)
    fa.reset(famark)
    ba.reset(bamark)


def emit_l1_attn(p, xacc, xn_dram, negmuB, rstdB, scr):
    sb = p.sb
    fa, ba = p.fa, p.ba
    famark, bamark = fa.mark(), ba.mark()
    C = 32
    BCL = 43.0

    sdT = ba.alloc(4 * T, "sdT")
    aubuf = ba.alloc(T, "aubuf")
    stg = [ba.alloc(T, "stgA"), ba.alloc(T, "stgB")]
    hb = ba.alloc(NRC * T, "hb")
    spikeT = Buf(ba.t, hb.off, 4 * T, BF16)

    qtmp = Buf(negmuB.t, negmuB.off, T, F32)
    bcB = Buf(rstdB.t, rstdB.off, T, F32)
    rows_t = fa.alloc(T, "l1rows")
    sq_t = fa.alloc(T, "l1sq")

    for r in range(NRC):
        p.wait("sync", p.sems["scalar"])
        dvx = p.dma(qtmp.ap(), scr["xnf"][r * 128:(r + 1) * 128, :])
        p.wait("scalar", p.sem_dma, dvx)
        p.wait("scalar", p.sems["tensor"])
        p.s(lambda e: e.activation(sq_t.ap(), qtmp.ap(), AF.Square), inc=True)
        p.wait("tensor", p.sems["scalar"])
        if r == 0:
            p.wait("tensor", p.sems["vector"])
        for j in range(4):
            p.t(lambda e, j=j, r=r: e.matmul(p.ps[3 + j][0:1, 0:512], sb["sb_ones"][:, 0:1],
                                             sq_t.ap(0, 128, j * 512, (j + 1) * 512),
                                             start=(r == 0), stop=(r == NRC - 1)))
        p.pe_done()
    tvs = p.sems["tensor"].n
    p.wait("vector", p.sems["tensor"], tvs)
    for j in range(4):
        p.v(lambda e, j=j: e.tensor_copy(rows_t.ap(0, 1, j * 512, (j + 1) * 512),
                                         p.ps[3 + j][0:1, 0:512]), inc=(j == 3))
    p.wait("scalar", p.sems["vector"])
    p.s(lambda e: e.activation(rows_t.ap(0, 1, 0, T), rows_t.ap(0, 1, 0, T), AF.Sqrt,
                               scale=float(1.0 / D), bias=p.scol("k_eps6")[0:1, 0:1]), inc=True)
    p.wait("vector", p.sems["scalar"])
    p.v(lambda e: e.reciprocal(rows_t.ap(0, 1, 0, T), rows_t.ap(0, 1, 0, T)), inc=True)
    bcast_row(p, lambda c0, c1: rows_t.ap(0, 1, c0, c1), T, bcB)
    for r in range(NRC):
        p.wait("sync", p.sems["vector"])
        dvx = p.dma(qtmp.ap(), scr["xnf"][r * 128:(r + 1) * 128, :])
        p.wait("vector", p.sem_dma, dvx)
        p.wait("vector", p.sems["scalar"])
        p.v(lambda e, r=r: e.scalar_tensor_tensor(sq_t.ap(), qtmp.ap(), p.scol("l1_wnorm", r),
                                                  bcB.ap(), ALU.mult, ALU.mult), inc=True)
        p.v(lambda e, r=r: e.tensor_scalar(rows_t.ap(), sq_t.ap(), p.scol("l1_convw3", r),
                                           p.scol("l1_convb", r), ALU.mult, ALU.add))
        for j in range(3):
            s_ = 3 - j
            p.v(lambda e, j=j, s_=s_, r=r: e.scalar_tensor_tensor(
                rows_t.ap(0, 128, s_, T), sq_t.ap(0, 128, 0, T - s_), p.scol(f"l1_convw{j}", r),
                rows_t.ap(0, 128, s_, T), ALU.mult, ALU.add), inc=(j == 2))
        p.wait("scalar", p.sems["vector"])
        p.s(lambda e, r=r: e.activation(hb.ap(0, 128, r * T, (r + 1) * T), rows_t.ap(),
                                        AF.Silu), inc=True)
        p.wait("vector", p.sems["scalar"])
    if DBG_L1 == "h":
        p.wait("sync", p.sems["scalar"])
        for r in range(NRC):
            p.dma(sb["dbg"][r * 128:(r + 1) * 128, :], hb.ap(0, 128, r * T, (r + 1) * T))
        p.wait("sync", p.sem_dma, p.sem_dma.n)
        fa.reset(famark); ba.reset(bamark)
        return

    stg_dv = [0, 0]

    def finish_qk(mi):
        p.wait("scalar", p.sems["vector"])
        p.s(lambda e: e.activation(sq_t.ap(), qtmp.ap(), AF.Square), inc=True)
        p.wait("tensor", p.sems["scalar"])
        for c0 in range(0, T, 512):
            p.wait("tensor", p.sems["vector"])
            p.t(lambda e, c0=c0: e.matmul(p.ps[5][0:2, 0:512], p.small("ones64bd"),
                                          sq_t.ap(0, 128, c0, c0 + 512), start=True, stop=True))
            tv2 = p.pe_done()
            p.wait("vector", p.sems["tensor"], tv2)
            p.v(lambda e, c0=c0: e.tensor_copy(rows_t.ap(0, 2, c0, c0 + 512), p.ps[5][0:2, 0:512]),
                inc=True)
        p.wait("scalar", p.sems["vector"])
        p.s(lambda e: e.activation(rows_t.ap(0, 2, 0, T), rows_t.ap(0, 2, 0, T), AF.Sqrt),
            inc=True)
        p.wait("vector", p.sems["scalar"])
        p.v(lambda e: e.tensor_scalar_add(rows_t.ap(0, 2, 0, T), rows_t.ap(0, 2, 0, T), 1e-6))
        p.v(lambda e: e.reciprocal(rows_t.ap(0, 2, 0, T), rows_t.ap(0, 2, 0, T)), inc=True)
        sidx = mi % 2
        p.wait("vector", p.sem_dma, stg_dv[sidx])
        for c0 in range(0, T, 512):
            p.wait("tensor", p.sems["vector"])
            p.t(lambda e, c0=c0: e.matmul(p.ps[4][:, 0:512], sb["sb_bd2"][:, :],
                                          rows_t.ap(0, 2, c0, c0 + 512), start=True, stop=True))
            tv2 = p.pe_done()
            p.wait("vector", p.sems["tensor"], tv2)
            p.v(lambda e, c0=c0, sidx=sidx: e.tensor_mul(stg[sidx].ap(0, 128, c0, c0 + 512),
                                                         qtmp.ap(0, 128, c0, c0 + 512),
                                                         p.ps[4][:, 0:512]), inc=True)
        p.wait("sync", p.sems["vector"])
        dst = scr["q"] if mi < 4 else scr["k"]
        g4 = mi % 4
        stg_dv[sidx] = p.dma(dst[g4 * 128:(g4 + 1) * 128, :], stg[sidx].ap())

    def evac_proj(mi, c0, c1, pband, tv):
        w = c1 - c0
        if mi < 8:
            p.wait("vector", p.sems["tensor"], tv)
            if c0 == 0:
                p.wait("vector", p.sems["scalar"])
            p.v(lambda e, c0=c0, c1=c1, pband=pband, w=w: e.tensor_copy(
                qtmp.ap(0, 128, c0, c1), pband[:, 0:w]), inc=True)
            if c1 == T:
                finish_qk(mi)
        elif mi < 16:
            sidx = mi % 2
            p.wait("vector", p.sems["tensor"], tv)
            if c0 == 0:
                p.wait("vector", p.sem_dma, stg_dv[sidx])
            p.v(lambda e, c0=c0, c1=c1, pband=pband, w=w, sidx=sidx: e.tensor_copy(
                stg[sidx].ap(0, 128, c0, c1), pband[:, 0:w]), inc=True)
            if c1 == T:
                p.wait("sync", p.sems["vector"])
                stg_dv[sidx] = p.dma(scr["v"][(mi - 8) * 128:(mi - 7) * 128, :], stg[sidx].ap())
        elif mi < 20:
            p.wait("vector", p.sems["tensor"], tv)
            p.v(lambda e, mi=mi, c0=c0, c1=c1, pband=pband, w=w: e.tensor_scalar_mul(
                sdT.ap(0, 128, (mi - 16) * T + c0, (mi - 16) * T + c1), pband[:, 0:w], -2.0),
                inc=True)
        elif mi == 20:
            p.wait("scalar", p.sems["tensor"], tv)
            p.s(lambda e, c0=c0, c1=c1, pband=pband, w=w: e.activation(
                aubuf.ap(0, 128, c0, c1), pband[:, 0:w], AF.Silu,
                bias=p.scol("l1_bproj", 20)), inc=True)
        else:
            p.wait("vector", p.sems["tensor"], tv)
            p.v(lambda e, c0=c0, c1=c1, pband=pband, w=w: e.tensor_scalar_add(
                rows_t.ap(0, 128, c0, c1), pband[:, 0:w], p.scol("l1_bproj", 21)), inc=True)
            if c1 == T:
                p.wait("sync", p.sems["vector"])
                p.dma(scr["bb"][:, :], rows_t.ap(0, 8, 0, T))

    emit_linear(p, "l1_wproj", 22, D,
                lambda k, c0, c1: hb.ap(0, 128, k * T + c0, k * T + c1), 0, T, evac_proj)
    p.wait("vector", p.sem_dma)

    wst64 = fa.alloc(128, "wst64")
    wbf64 = ba.alloc(128, "wbf64")
    for g4 in range(4):
        p.wait("sync", p.sems["vector"])
        dv = p.dma(wst64.ap(0, 64, 0, 128), sb["l1_wad"][g4, 0:64])
        p.wait("vector", p.sem_dma, dv)
        p.v(lambda e: e.tensor_copy(wbf64.ap(0, 64, 0, 128), wst64.ap(0, 64, 0, 128)), inc=True)
        p.wait("tensor", p.sems["vector"])
        p.wait("tensor", p.sems["scalar"])
        for c0 in range(0, T, 512):
            p.wait("tensor", p.sems["vector"])
            p.t(lambda e, c0=c0: e.matmul(p.ps[0][:, 0:512], wbf64.ap(0, 64, 0, 128),
                                          aubuf.ap(0, 64, c0, c0 + 512), start=True, stop=True))
            tv = p.pe_done()
            p.wait("vector", p.sems["tensor"], tv)
            p.v(lambda e, c0=c0, g4=g4: e.tensor_scalar_add(sq_t.ap(0, 128, c0, c0 + 512),
                                                            p.ps[0][:, 0:512],
                                                            p.scol("l1_bad", g4)), inc=True)
        p.wait("sync", p.sems["vector"])
        dvA = p.dma(scr["ab"][g4 * 128:(g4 + 1) * 128, :], sq_t.ap())
        p.wait("vector", p.sem_dma, dvA)

    fa.reset(rows_t.off)
    nh = fa.alloc(4 * T, "nh")
    w4 = fa.alloc(4, "w4")
    nh3 = fa.t[0:128, nh.off:nh.off + 4 * T].rearrange("p (g t) -> p g t", g=4)
    sd3 = ba.t[0:128, sdT.off:sdT.off + 4 * T].rearrange("p (g t) -> p g t", g=4)
    p.vtiny(lambda e: e.memset(w4.ap(0, 128, 0, 4), 0.0))
    for t in range(T):
        p.vtiny(lambda e, t=t: e.scalar_tensor_tensor(nh3[:, :, t], w4.ap(0, 128, 0, 4), 0.9,
                                                      sd3[:, :, t], ALU.mult, ALU.add))
        p.vtiny(lambda e, t=t: e.scalar_tensor_tensor(w4.ap(0, 128, 0, 4), nh3[:, :, t], -1.0,
                                                      nh3[:, :, t], ALU.is_lt, ALU.add))
    p.wait("vector", p.sems["tensor"])
    for g4 in range(4):
        p.v(lambda e, g4=g4: e.tensor_scalar(spikeT.ap(0, 128, g4 * T, (g4 + 1) * T),
                                             nh.ap(0, 128, g4 * T, (g4 + 1) * T),
                                             -1.0, 1.0, ALU.is_lt, ALU.mult), inc=(g4 == 3))
    if DBG_L1 == "scan":
        p.wait("sync", p.sems["vector"])
        for g4 in range(4):
            p.dma(sb["dbg"][1024 + g4 * 128:1024 + (g4 + 1) * 128, :],
                  spikeT.ap(0, 128, g4 * T, (g4 + 1) * T))
        p.wait("sync", p.sem_dma, p.sem_dma.n)
        fa.reset(famark); ba.reset(bamark)
        return

    fa.reset(nh.off)
    actrow = fa.alloc(T, "actrow")
    brow = fa.alloc(T, "brow")
    betaBA = fa.alloc(T, "betaBA")
    betaBB = fa.alloc(T, "betaBB")
    Spair = fa.alloc(128, "Spair")
    SpSb = fa.alloc(128, "SpS")
    g32 = fa.alloc(32, "g32")
    argx = fa.alloc(32, "argx")
    eg = fa.alloc(32, "eg")
    emg = fa.alloc(32, "emg")
    egbA = fa.alloc(32, "egbA")
    egbB = fa.alloc(32, "egbB")
    vb64 = fa.alloc(64, "vb64")
    cols8 = fa.alloc(8, "cols8")
    bdk = fa.alloc(64, "bdk")
    bdq = fa.alloc(64, "bdq")
    bdkap = fa.alloc(64, "bdkap")
    BLb = fa.alloc(64, "BL")
    BLt = fa.alloc(64, "BLt")
    Eb = fa.alloc(64, "E")
    Etb = fa.alloc(64, "Et")
    Yb = fa.alloc(128, "Y")
    Wb = fa.alloc(128, "W")
    kaptb = fa.alloc(128, "kapt")
    bcolb = fa.alloc(2, "bcol")
    laf = bcB
    abf = qtmp

    ba.reset(sdT.off)
    qfb = ba.alloc(T, "qfb")
    kfb = ba.alloc(T, "kfb")
    vfA = ba.alloc(T, "vfA")
    vfB = ba.alloc(T, "vfB")
    ystg = [ba.alloc(T, "ystgA"), ba.alloc(T, "ystgB")]
    onesbd_b = ba.alloc(2, "onesbdb")
    bs2b = ba.alloc(2, "bs2b")
    p.wait("vector", p.sems["tensor"])
    p.vtiny(lambda e: e.tensor_copy(onesbd_b.ap(0, 128, 0, 2), p.small("ones64bd")), inc=True)

    id64 = sb["sb_ident"][0:64, 0:64]
    for p4 in range(DBG_L1_NPAIR):
        p.wait("sync", p.sems["vector"])
        p.wait("sync", p.sems["tensor"])
        p.wait("sync", p.sems["scalar"])
        p.wait("sync", p.sem_dma, p.sem_dma.n)
        p.dma(qfb.ap(), scr["q"][p4 * 128:(p4 + 1) * 128, :])
        p.dma(kfb.ap(), scr["k"][p4 * 128:(p4 + 1) * 128, :])
        p.dma(vfA.ap(), scr["v"][(2 * p4) * 128:(2 * p4 + 1) * 128, :])
        p.dma(vfB.ap(), scr["v"][(2 * p4 + 1) * 128:(2 * p4 + 2) * 128, :])
        p.dma(abf.ap(), scr["ab"][p4 * 128:(p4 + 1) * 128, :])
        dvall = p.dma(brow.ap(0, 2, 0, T), scr["bb"][2 * p4:2 * p4 + 2, :])
        p.wait("vector", p.sem_dma, dvall)
        p.wait("tensor", p.sem_dma, dvall)
        p.vtiny(lambda e, p4=p4: e.tensor_mul(bs2b.ap(0, 128, 0, 1), p.scol("l1_bspk", p4),
                                              p.scol("l1_mA")))
        p.vtiny(lambda e, p4=p4: e.tensor_mul(bs2b.ap(0, 128, 1, 2), p.scol("l1_bspk", p4),
                                              p.scol("l1_mB")), inc=True)
        p.wait("tensor", p.sems["vector"])
        for c0 in range(0, T, 512):
            p.wait("tensor", p.sems["vector"])
            p.t(lambda e, c0=c0, p4=p4: e.matmul(p.ps[5][0:2, 0:512], onesbd_b.ap(0, 128, 0, 2),
                                                 spikeT.ap(0, 128, p4 * T + c0, p4 * T + c0 + 512),
                                                 start=True, stop=True))
            p.t(lambda e, c0=c0, p4=p4: e.matmul(p.ps[6][0:2, 0:512], bs2b.ap(0, 128, 0, 2),
                                                 spikeT.ap(0, 128, p4 * T + c0, p4 * T + c0 + 512),
                                                 start=True, stop=True))
            tv = p.pe_done()
            p.wait("vector", p.sems["tensor"], tv)
            p.v(lambda e, c0=c0: e.tensor_scalar(actrow.ap(0, 2, c0, c0 + 512), p.ps[5][0:2, 0:512],
                                                 0.0, 1.0, ALU.is_gt, ALU.mult))
            p.v(lambda e, c0=c0: e.tensor_add(brow.ap(0, 2, c0, c0 + 512),
                                              brow.ap(0, 2, c0, c0 + 512), p.ps[6][0:2, 0:512]),
                inc=True)
        p.wait("scalar", p.sems["vector"])
        p.s(lambda e: e.activation(brow.ap(0, 2, 0, T), brow.ap(0, 2, 0, T), AF.Sigmoid),
            inc=True)
        p.wait("vector", p.sems["scalar"])
        p.v(lambda e: e.tensor_mul(brow.ap(0, 2, 0, T), brow.ap(0, 2, 0, T),
                                   actrow.ap(0, 2, 0, T)), inc=True)
        for c0 in range(0, T, 512):
            p.wait("tensor", p.sems["vector"])
            p.t(lambda e, c0=c0: e.matmul(p.ps[4][:, 0:512], sb["sb_bd2A"][:, :],
                                          brow.ap(0, 2, c0, c0 + 512), start=True, stop=True))
            p.t(lambda e, c0=c0: e.matmul(p.ps[5][:, 0:512], sb["sb_bd2B"][:, :],
                                          brow.ap(0, 2, c0, c0 + 512), start=True, stop=True))
            tv = p.pe_done()
            p.wait("vector", p.sems["tensor"], tv)
            p.v(lambda e, c0=c0: e.tensor_copy(betaBA.ap(0, 128, c0, c0 + 512), p.ps[4][:, 0:512]))
            p.v(lambda e, c0=c0: e.tensor_copy(betaBB.ap(0, 128, c0, c0 + 512), p.ps[5][:, 0:512]),
                inc=True)
        p.v(lambda e, p4=p4: e.scalar_tensor_tensor(laf.ap(), spikeT.ap(0, 128, p4 * T, (p4 + 1) * T),
                                                    p.scol("l1_aspk", p4), abf.ap(),
                                                    ALU.mult, ALU.add), inc=True)
        p.wait("scalar", p.sems["vector"])
        p.s(lambda e: e.activation(laf.ap(), laf.ap(), AF.Sigmoid), inc=True)
        p.wait("vector", p.sems["scalar"])
        p.v(lambda e: e.tensor_scalar_add(laf.ap(), laf.ap(), -1.0), inc=True)
        for c0 in range(0, T, 512):
            p.wait("tensor", p.sems["vector"])
            p.t(lambda e, c0=c0: e.matmul(p.ps[4][:, 0:512], sb["sb_bd2"][:, :],
                                          actrow.ap(0, 2, c0, c0 + 512), start=True, stop=True))
            tv = p.pe_done()
            p.wait("vector", p.sems["tensor"], tv)
            p.v(lambda e, c0=c0: e.tensor_mul(laf.ap(0, 128, c0, c0 + 512),
                                              laf.ap(0, 128, c0, c0 + 512), p.ps[4][:, 0:512]),
                inc=True)
        p.v(lambda e: e.tensor_scalar_add(laf.ap(), laf.ap(), 1.0), inc=True)
        p.wait("scalar", p.sems["vector"])
        p.s(lambda e: e.activation(laf.ap(), laf.ap(), AF.Ln), inc=True)
        p.wait("vector", p.sems["scalar"])
        p.v(lambda e: e.memset(Spair.ap(), 0.0), inc=True)

        for c in range(DBG_L1_NCHUNK):
            t0 = c * C
            ST = DBG_L1_STEPS
            if "g" not in ST:
                continue
            p.vtiny(lambda e, t0=t0: e.tensor_tensor_scan(g32.ap(0, 128, 0, C),
                                                          sb["sb_ones"][:, 0:C],
                                                          laf.ap(0, 128, t0, t0 + C),
                                                          0.0, ALU.mult, ALU.add))
            p.vtiny(lambda e: e.tensor_single_scalar(argx.ap(0, 128, 0, C), g32.ap(0, 128, 0, C),
                                                     g32.ap(0, 128, 15, 16), ALU.subtract))
            p.vtiny(lambda e: e.tensor_scalar(argx.ap(0, 128, 0, C), argx.ap(0, 128, 0, C),
                                              BCL, -BCL, ALU.min, ALU.max))
            p.vtiny(lambda e: e.tensor_single_scalar(cols8.ap(0, 128, 3, 4), g32.ap(0, 128, 31, 32),
                                                     g32.ap(0, 128, 15, 16), ALU.subtract),
                    inc=True)
            p.wait("scalar", p.sems["vector"])
            p.stiny(lambda e: e.activation(eg.ap(0, 128, 0, C), argx.ap(0, 128, 0, C), AF.Exp))
            p.stiny(lambda e: e.activation(emg.ap(0, 128, 0, C), argx.ap(0, 128, 0, C), AF.Exp,
                                           scale=-1.0))
            p.stiny(lambda e: e.activation(cols8.ap(0, 128, 0, 1), g32.ap(0, 128, 15, 16), AF.Exp))
            p.stiny(lambda e: e.activation(cols8.ap(0, 128, 1, 2), cols8.ap(0, 128, 3, 4), AF.Exp),
                    inc=True)
            p.wait("vector", p.sems["scalar"])
            p.vtiny(lambda e: e.tensor_mul(cols8.ap(0, 128, 2, 3), cols8.ap(0, 128, 1, 2),
                                           cols8.ap(0, 128, 0, 1)))
            p.vtiny(lambda e, t0=t0: e.tensor_mul(egbA.ap(0, 128, 0, C), eg.ap(0, 128, 0, C),
                                                  betaBA.ap(0, 128, t0, t0 + C)))
            p.vtiny(lambda e, t0=t0: e.tensor_mul(egbB.ap(0, 128, 0, C), eg.ap(0, 128, 0, C),
                                                  betaBB.ap(0, 128, t0, t0 + C)))
            p.vtiny(lambda e, t0=t0: e.tensor_mul(vb64.ap(0, 128, 0, C), vfA.ap(0, 128, t0, t0 + C),
                                                  betaBA.ap(0, 128, t0, t0 + C)))
            p.vtiny(lambda e, t0=t0: e.tensor_mul(vb64.ap(0, 128, C, 2 * C),
                                                  vfB.ap(0, 128, t0, t0 + C),
                                                  betaBB.ap(0, 128, t0, t0 + C)))
            for (dstb, srcb, egxA, egxB) in ((bdk, kfb, egbA, egbB), (bdq, qfb, eg, eg),
                                             (bdkap, kfb, emg, emg)):
                p.vtiny(lambda e, dstb=dstb, srcb=srcb, egx=egxA, t0=t0: e.scalar_tensor_tensor(
                    dstb.ap(0, 128, 0, C), srcb.ap(0, 128, t0, t0 + C), p.scol("l1_mA"),
                    egx.ap(0, 128, 0, C), ALU.mult, ALU.mult))
                p.vtiny(lambda e, dstb=dstb, srcb=srcb, egx=egxB, t0=t0: e.scalar_tensor_tensor(
                    dstb.ap(0, 128, C, 2 * C), srcb.ap(0, 128, t0, t0 + C), p.scol("l1_mB"),
                    egx.ap(0, 128, 0, C), ALU.mult, ALU.mult))
            p.v(lambda e: e.tensor_single_scalar(SpSb.ap(), Spair.ap(), cols8.ap(0, 128, 0, 1),
                                                 ALU.mult), inc=True)
            if "L" not in ST:
                continue
            p.wait("tensor", p.sems["vector"])
            p.t(lambda e: e.matmul(p.ps[0][0:64, 0:64], bdk.ap(0, 128, 0, 64),
                                   bdkap.ap(0, 128, 0, 64), start=True, stop=True))
            p.t(lambda e: e.matmul(p.ps[3][0:64, 0:128], bdk.ap(0, 128, 0, 64), SpSb.ap(),
                                   start=True, stop=True))
            p.t(lambda e: e.transpose(p.ps[3][0:64, 128:256], vb64.ap(0, 128, 0, 2 * C),
                                      sb["sb_ident"][:, :]))
            tv = p.pe_done()
            p.wait("vector", p.sems["tensor"], tv)
            p.vtiny(lambda e: e.tensor_mul(BLb.ap(0, 64, 0, 64), p.ps[0][0:64, 0:64],
                                           sb["sb_l1stril"][0:64, 0:64]), inc=True)
            p.v(lambda e: e.tensor_copy(Wb.ap(0, 64, 0, 128), p.ps[3][0:64, 0:128]))
            p.v(lambda e: e.tensor_sub(Yb.ap(0, 64, 0, 128), p.ps[3][0:64, 128:256],
                                       Wb.ap(0, 64, 0, 128)), inc=True)
            p.wait("tensor", p.sems["vector"])
            p.t(lambda e: e.transpose(p.ps[1][0:64, 0:64], BLb.ap(0, 64, 0, 64), id64))
            tv = p.pe_done()
            p.wait("vector", p.sems["tensor"], tv)
            p.vtiny(lambda e: e.tensor_copy(BLt.ap(0, 64, 0, 64), p.ps[1][0:64, 0:64]), inc=True)
            p.wait("tensor", p.sems["vector"])
            p.t(lambda e: e.matmul(p.ps[2][0:64, 0:64], BLt.ap(0, 64, 0, 64), BLb.ap(0, 64, 0, 64),
                                   start=True, stop=True))
            tv = p.pe_done()
            p.wait("vector", p.sems["tensor"], tv)
            p.vtiny(lambda e: e.tensor_copy(Eb.ap(0, 64, 0, 64), p.ps[2][0:64, 0:64]), inc=True)
            for m in range(4 if "newton" in ST else 0):
                p.wait("tensor", p.sems["vector"])
                p.t(lambda e: e.transpose(p.ps[1][0:64, 0:64], Eb.ap(0, 64, 0, 64), id64))
                tv = p.pe_done()
                p.wait("vector", p.sems["tensor"], tv)
                p.vtiny(lambda e: e.tensor_copy(Etb.ap(0, 64, 0, 64), p.ps[1][0:64, 0:64]),
                        inc=True)
                p.wait("tensor", p.sems["vector"])
                p.t(lambda e: e.matmul(p.ps[4][0:64, 0:128], Etb.ap(0, 64, 0, 64),
                                       Yb.ap(0, 64, 0, 128), start=True, stop=True))
                if m < 3:
                    p.t(lambda e: e.matmul(p.ps[2][0:64, 0:64], Etb.ap(0, 64, 0, 64),
                                           Eb.ap(0, 64, 0, 64), start=True, stop=True))
                tv = p.pe_done()
                p.wait("vector", p.sems["tensor"], tv)
                p.v(lambda e: e.tensor_add(Yb.ap(0, 64, 0, 128), Yb.ap(0, 64, 0, 128),
                                           p.ps[4][0:64, 0:128]), inc=True)
                if m < 3:
                    p.vtiny(lambda e: e.tensor_copy(Eb.ap(0, 64, 0, 64), p.ps[2][0:64, 0:64]),
                            inc=True)
            if "out" not in ST:
                continue
            p.wait("tensor", p.sems["vector"])
            p.t(lambda e: e.matmul(p.ps[4][0:64, 0:128], BLt.ap(0, 64, 0, 64),
                                   Yb.ap(0, 64, 0, 128), start=True, stop=True))
            p.t(lambda e: e.matmul(p.ps[0][0:64, 0:64], bdkap.ap(0, 128, 0, 64),
                                   bdq.ap(0, 128, 0, 64), start=True, stop=True))
            tv = p.pe_done()
            p.wait("vector", p.sems["tensor"], tv)
            p.v(lambda e: e.tensor_sub(Wb.ap(0, 64, 0, 128), Yb.ap(0, 64, 0, 128),
                                       p.ps[4][0:64, 0:128]), inc=True)
            p.vtiny(lambda e: e.tensor_mul(Etb.ap(0, 64, 0, 64), p.ps[0][0:64, 0:64],
                                           sb["sb_l1triu"][0:64, 0:64]), inc=True)
            p.wait("tensor", p.sems["vector"])
            p.t(lambda e: e.matmul(p.ps[5][0:64, 0:128], bdq.ap(0, 128, 0, 64), SpSb.ap(),
                                   start=True, stop=True))
            p.t(lambda e: e.matmul(p.ps[4][0:64, 0:128], Etb.ap(0, 64, 0, 64),
                                   Wb.ap(0, 64, 0, 128), start=True, stop=True))
            tv = p.pe_done()
            p.wait("vector", p.sems["tensor"], tv)
            p.v(lambda e: e.tensor_copy(Yb.ap(0, 64, 0, 128), p.ps[5][0:64, 0:128]))
            p.v(lambda e: e.tensor_add(Yb.ap(0, 64, 0, 128), Yb.ap(0, 64, 0, 128),
                                       p.ps[4][0:64, 0:128]), inc=True)
            p.vtiny(lambda e: e.tensor_single_scalar(bdk.ap(0, 128, 0, 64), bdkap.ap(0, 128, 0, 64),
                                                     cols8.ap(0, 128, 1, 2), ALU.mult), inc=True)
            p.wait("tensor", p.sems["vector"])
            p.t(lambda e: e.transpose(p.ps[6][0:128, 128:192], Yb.ap(0, 64, 0, 128),
                                      sb["sb_ident"][0:64, 0:64]))
            p.t(lambda e: e.transpose(p.ps[1][0:64, 0:128], bdk.ap(0, 128, 0, 64),
                                      sb["sb_ident"][:, :]))
            tv = p.pe_done()
            p.wait("vector", p.sems["tensor"], tv)
            p.vtiny(lambda e, t0=t0: e.tensor_copy(ystg[0].ap(0, 128, t0, t0 + C),
                                                   p.ps[6][:, 128:160]))
            p.vtiny(lambda e, t0=t0: e.tensor_copy(ystg[1].ap(0, 128, t0, t0 + C),
                                                   p.ps[6][:, 160:192]))
            p.v(lambda e: e.tensor_copy(kaptb.ap(0, 64, 0, 128), p.ps[1][0:64, 0:128]), inc=True)
            if "state" not in ST:
                continue
            p.wait("tensor", p.sems["vector"])
            p.t(lambda e: e.matmul(p.ps[6][0:128, 0:128], kaptb.ap(0, 64, 0, 128),
                                   Wb.ap(0, 64, 0, 128), start=True, stop=True))
            tv = p.pe_done()
            p.wait("vector", p.sems["tensor"], tv)
            p.v(lambda e: e.scalar_tensor_tensor(Spair.ap(), Spair.ap(), cols8.ap(0, 128, 2, 3),
                                                 p.ps[6][0:128, 0:128], ALU.mult, ALU.add),
                inc=True)
        p.wait("sync", p.sems["vector"])
        p.dma(scr["y"][(2 * p4) * 128:(2 * p4 + 1) * 128, :], ystg[0].ap())
        p.dma(scr["y"][(2 * p4 + 1) * 128:(2 * p4 + 2) * 128, :], ystg[1].ap())
    p.wait("vector", p.sem_dma)
    p.wait("sync", p.sem_dma, p.sem_dma.n)
    if DBG_L1 == "delta":
        fa.reset(famark); ba.reset(bamark)
        return

    fa.reset(nh.off)
    sqt2 = fa.alloc(T, "sqt2")
    rowt2 = fa.alloc(T, "rowt2")
    ba.reset(sdT.off)
    ytl = ba.alloc(8 * T, "ytl")
    ybuf = ba.alloc(T, "ybuf")
    for hh in range(8):
        p.wait("sync", p.sems["vector"])
        p.wait("sync", p.sems["scalar"])
        p.wait("sync", p.sems["tensor"])
        dvy = p.dma(ybuf.ap(), scr["y"][hh * 128:(hh + 1) * 128, :])
        p.wait("scalar", p.sem_dma, dvy)
        p.wait("scalar", p.sems["vector"])
        p.s(lambda e: e.activation(sqt2.ap(), ybuf.ap(), AF.Square), inc=True)
        p.wait("tensor", p.sems["scalar"])
        for c0 in range(0, T, 512):
            p.wait("tensor", p.sems["vector"])
            p.t(lambda e, c0=c0: e.matmul(p.ps[5][0:1, 0:512], sb["sb_ones"][:, 0:1],
                                          sqt2.ap(0, 128, c0, c0 + 512), start=True, stop=True))
            tv = p.pe_done()
            p.wait("vector", p.sems["tensor"], tv)
            p.v(lambda e, c0=c0: e.tensor_copy(rowt2.ap(0, 1, c0, c0 + 512), p.ps[5][0:1, 0:512]),
                inc=True)
        p.wait("scalar", p.sems["vector"])
        p.s(lambda e: e.activation(rowt2.ap(0, 1, 0, T), rowt2.ap(0, 1, 0, T), AF.Sqrt,
                                   scale=float(1.0 / 128.0), bias=p.scol("k_eps6")[0:1, 0:1]),
            inc=True)
        p.wait("vector", p.sems["scalar"])
        p.v(lambda e: e.reciprocal(rowt2.ap(0, 1, 0, T), rowt2.ap(0, 1, 0, T)), inc=True)
        for c0 in range(0, T, 512):
            p.wait("tensor", p.sems["vector"])
            p.t(lambda e, c0=c0: e.matmul(p.ps[4][:, 0:512], sb["sb_ones"][0:1, :],
                                          rowt2.ap(0, 1, c0, c0 + 512), start=True, stop=True))
            tv = p.pe_done()
            p.wait("vector", p.sems["tensor"], tv)
            p.v(lambda e, c0=c0, hh=hh: e.scalar_tensor_tensor(
                ytl.ap(0, 128, hh * T + c0, hh * T + c0 + 512), ybuf.ap(0, 128, c0, c0 + 512),
                p.scol("l1_whn", hh), p.ps[4][:, 0:512], ALU.mult, ALU.mult), inc=True)
    if DBG_L1 == "y":
        p.wait("sync", p.sems["vector"])
        for hh in range(8):
            p.dma(sb["dbg"][hh * 128:(hh + 1) * 128, :], ytl.ap(0, 128, hh * T, (hh + 1) * T))
        p.wait("sync", p.sem_dma, p.sem_dma.n)
        fa.reset(famark); ba.reset(bamark)
        return

    au2 = ba.alloc(T, "au2")
    xsl = ba.alloc(8 * 512, "xsl")
    xbuf = ba.alloc(T, "xbuf")
    y2stg = [ba.alloc(T, "y2A"), ba.alloc(T, "y2B")]
    wbf64b = ba.alloc(128, "wu2bf")
    wbfL = ba.alloc(1024, "woutbf")
    wst64b = fa.alloc(128, "wu2st")
    wstL = fa.alloc(1024, "woutst")
    t1 = fa.alloc(512, "t1")
    gbuf = abf

    def evac_au2(mi, c0, c1, pband, tv):
        p.wait("scalar", p.sems["tensor"], tv)
        p.s(lambda e, c0=c0, c1=c1, pband=pband: e.activation(
            au2.ap(0, 128, c0, c1), pband[:, 0:c1 - c0], AF.Silu,
            bias=p.scol("l1_bu1", 0)), inc=True)
    for c0 in range(0, T, 512):
        p.wait("sync", p.sems["tensor"])
        dvx = load_xn_slice(p, xn_dram, xsl, c0, c0 + 512)
        p.wait("tensor", p.sem_dma, dvx)
        emit_linear(p, "l1_wu1", 1, D,
                    lambda k, cc0, cc1, c0=c0: xsl.ap(0, 128, k * 512 + (cc0 - c0), k * 512 + (cc1 - c0)),
                    c0, c0 + 512, evac_au2)

    y2_dv = [0, 0]
    for mi in range(8):
        p.wait("sync", p.sems["vector"])
        dv = p.dma(wst64b.ap(0, 64, 0, 128), sb["l1_wu2"][mi, 0:64])
        p.wait("vector", p.sem_dma, dv)
        p.v(lambda e: e.tensor_copy(wbf64b.ap(0, 64, 0, 128), wst64b.ap(0, 64, 0, 128)), inc=True)
        p.wait("tensor", p.sems["vector"])
        p.wait("tensor", p.sems["scalar"])
        for c0 in range(0, T, 512):
            p.wait("tensor", p.sems["scalar"])
            p.t(lambda e, c0=c0: e.matmul(p.ps[5][:, 0:512], wbf64b.ap(0, 64, 0, 128),
                                          au2.ap(0, 64, c0, c0 + 512), start=True, stop=True))
            tv = p.pe_done()
            p.wait("scalar", p.sems["tensor"], tv)
            p.wait("scalar", p.sems["vector"])
            p.s(lambda e, c0=c0, mi=mi: e.activation(gbuf.ap(0, 128, c0, c0 + 512),
                                                     p.ps[5][:, 0:512], AF.Sigmoid,
                                                     bias=p.scol("l1_bu2", mi)), inc=True)
        p.wait("sync", p.sems["vector"])
        dvx = p.dma(xbuf.ap(), xn_dram[mi * 128:(mi + 1) * 128, :])
        dvw = p.dma(wstL.ap().rearrange("p (k c) -> p k c", c=128),
                    sb["l1_wout"][mi].rearrange("(k p) c -> p k c", p=128))
        p.wait("vector", p.sem_dma, dvw)
        p.wait("vector", p.sems["tensor"])
        p.v(lambda e: e.tensor_copy(wbfL.ap(), wstL.ap()), inc=True)
        p.wait("tensor", p.sems["vector"])
        sidx = mi % 2
        for ci, c0 in enumerate(range(0, T, 512)):
            pband = p.ps[ci % 2]
            p.wait("tensor", p.sems["vector"])
            for k in range(8):
                p.t(lambda e, pband=pband, k=k, c0=c0: e.matmul(
                    pband[:, 0:512], wbfL.ap(0, 128, k * 128, (k + 1) * 128),
                    ytl.ap(0, 128, k * T + c0, k * T + c0 + 512), start=(k == 0), stop=(k == 7)))
            tv = p.pe_done()
            p.wait("vector", p.sems["tensor"], tv)
            p.wait("vector", p.sem_dma, dvx)
            p.wait("vector", p.sems["scalar"])
            if c0 == 0:
                p.wait("vector", p.sem_dma, y2_dv[sidx])
            p.v(lambda e, c0=c0, pband=pband: e.tensor_mul(t1.ap(0, 128, 0, 512), pband[:, 0:512],
                                                           gbuf.ap(0, 128, c0, c0 + 512)))
            p.v(lambda e, c0=c0: e.tensor_add(t1.ap(0, 128, 0, 512), t1.ap(0, 128, 0, 512),
                                              xbuf.ap(0, 128, c0, c0 + 512)))
            p.v(lambda e, mi=mi, c0=c0: e.tensor_add(
                xacc.ap(0, 128, mi * T + c0, mi * T + c0 + 512),
                xacc.ap(0, 128, mi * T + c0, mi * T + c0 + 512), t1.ap(0, 128, 0, 512)))
            p.v(lambda e, c0=c0, sidx=sidx: e.tensor_copy(y2stg[sidx].ap(0, 128, c0, c0 + 512),
                                                          t1.ap(0, 128, 0, 512)), inc=True)
        p.wait("sync", p.sems["vector"])
        y2_dv[sidx] = p.dma(scr["y2"][mi * 128:(mi + 1) * 128, :], y2stg[sidx].ap())
    p.wait("vector", p.sem_dma)
    if DBG_L1 == "gate":
        fa.reset(famark); ba.reset(bamark)
        return

    p.wait("sync", p.sem_dma, p.sem_dma.n)
    for r in range(NRC):
        p.wait("sync", p.sems["scalar"])
        p.wait("sync", p.sems["vector"])
        dvx = p.dma(xbuf.ap(), scr["y2"][r * 128:(r + 1) * 128, :])
        p.wait("scalar", p.sem_dma, dvx)
        p.wait("scalar", p.sems["tensor"])
        p.wait("scalar", p.sems["vector"])
        p.s(lambda e: e.activation(sqt2.ap(), xbuf.ap(), AF.Square), inc=True)
        p.wait("tensor", p.sems["scalar"])
        if r == 0:
            p.wait("tensor", p.sems["vector"])
        for j in range(4):
            p.t(lambda e, j=j, r=r: e.matmul(p.ps[3 + j][0:1, 0:512], sb["sb_ones"][:, 0:1],
                                             sqt2.ap(0, 128, j * 512, (j + 1) * 512),
                                             start=(r == 0), stop=(r == NRC - 1)))
        p.pe_done()
    tvs = p.sems["tensor"].n
    p.wait("vector", p.sems["tensor"], tvs)
    for j in range(4):
        p.v(lambda e, j=j: e.tensor_copy(rowt2.ap(0, 1, j * 512, (j + 1) * 512),
                                         p.ps[3 + j][0:1, 0:512]), inc=(j == 3))
    p.wait("scalar", p.sems["vector"])
    p.s(lambda e: e.activation(rowt2.ap(0, 1, 0, T), rowt2.ap(0, 1, 0, T), AF.Sqrt,
                               scale=float(1.0 / D), bias=p.scol("k_eps6")[0:1, 0:1]), inc=True)
    p.wait("vector", p.sems["scalar"])
    p.v(lambda e: e.reciprocal(rowt2.ap(0, 1, 0, T), rowt2.ap(0, 1, 0, T)), inc=True)
    bcast_row(p, lambda c0, c1: rowt2.ap(0, 1, c0, c1), T, bcB)
    ba.reset(sdT.off)
    h2 = ba.alloc(8 * T, "h2")
    xbuf2 = ba.alloc(T, "xbuf2")
    stmp = [ba.alloc(T, "stmpA"), ba.alloc(T, "stmpB")]
    for r in range(NRC):
        p.wait("sync", p.sems["vector"])
        dvx = p.dma(xbuf2.ap(), scr["y2"][r * 128:(r + 1) * 128, :])
        p.wait("vector", p.sem_dma, dvx)
        p.v(lambda e, r=r: e.scalar_tensor_tensor(h2.ap(0, 128, r * T, (r + 1) * T), xbuf2.ap(),
                                                  p.scol("l1_wffn", r), bcB.ap(),
                                                  ALU.mult, ALU.mult), inc=True)

    stmp_dv = [0, 0]

    def evac_ff13(mi, c0, c1, pband, tv):
        w = c1 - c0
        par = mi % 2
        if mi < 32:
            p.wait("scalar", p.sems["tensor"], tv)
            if c0 == 0:
                p.wait("scalar", p.sem_dma, stmp_dv[par])
                p.wait("scalar", p.sems["vector"])
            p.s(lambda e, c0=c0, c1=c1, pband=pband, par=par, w=w: e.activation(
                stmp[par].ap(0, 128, c0, c1), pband[:, 0:w], AF.Silu), inc=True)
        else:
            p.wait("vector", p.sems["tensor"], tv)
            p.wait("vector", p.sems["scalar"])
            p.v(lambda e, c0=c0, c1=c1, pband=pband, par=par, w=w: e.tensor_mul(
                stmp[par].ap(0, 128, c0, c1), stmp[par].ap(0, 128, c0, c1), pband[:, 0:w]),
                inc=True)
            if c1 == T:
                p.wait("sync", p.sems["vector"])
                stmp_dv[par] = p.dma(scr["s"][(mi - 32) * 128:(mi - 31) * 128, :], stmp[par].ap())
    m_list = []
    for j in range(32):
        m_list += [j, j + 32]
    emit_linear(p, "l1_ff13", 64, D, lambda k, c0, c1: h2.ap(0, 128, k * T + c0, k * T + c1),
                0, T, evac_ff13, m_list=m_list)
    p.wait("vector", p.sem_dma)
    p.wait("scalar", p.sem_dma)

    ba.reset(sdT.off)
    sfb = ba.alloc(32 * 512, "sfb")
    p.wait("sync", p.sem_dma, p.sem_dma.n)

    def evac_ff2(mi, c0, c1, pband, tv):
        p.wait("vector", p.sems["tensor"], tv)
        p.v(lambda e, mi=mi, c0=c0, c1=c1, pband=pband: e.tensor_add(
            xacc.ap(0, 128, mi * T + c0, mi * T + c1),
            xacc.ap(0, 128, mi * T + c0, mi * T + c1), pband[:, 0:c1 - c0]), inc=True)
    for c0 in range(0, T, 512):
        p.wait("sync", p.sems["tensor"])
        for k in range(32):
            p.dma(sfb.ap(0, 128, k * 512, (k + 1) * 512),
                  scr["s"][k * 128:(k + 1) * 128, c0:c0 + 512])
        dvs = p.sem_dma.n
        p.wait("tensor", p.sem_dma, dvs)
        emit_linear(p, "l1_ff2", 8, 4096,
                    lambda k, cc0, cc1, c0=c0: sfb.ap(0, 128, k * 512 + (cc0 - c0), k * 512 + (cc1 - c0)),
                    c0, c0 + 512, evac_ff2)
    fa.reset(famark)
    ba.reset(bamark)


def emit_l1_ffn(p, xacc, xn_dram, negmuB, rstdB, scr):
    sb = p.sb
    fa, ba = p.fa, p.ba
    famark, bamark = fa.mark(), ba.mark()

    xnb = ba.alloc(NRC * T, "xnb")
    he = ba.alloc(8 * T, "he")
    p.wait("sync", p.sems["tensor"])
    p.wait("sync", p.sems["vector"])
    p.wait("sync", p.sems["scalar"])
    for r in range(NRC):
        p.dma(xnb.ap(0, 128, r * T, (r + 1) * T), xn_dram[r * 128:(r + 1) * 128, :])
    dvx = p.sem_dma.n
    p.wait("tensor", p.sem_dma, dvx)

    lgt = fa.alloc(T, "lgt")
    selx = fa.alloc(T, "selx")
    srow0 = fa.alloc(64, "srow0")

    def evac_lgt(mi, c0, c1, pband, tv):
        p.wait("vector", p.sems["tensor"], tv)
        p.v(lambda e, c0=c0, c1=c1, pband=pband: e.tensor_copy(lgt.ap(0, 128, c0, c1),
                                                               pband[:, 0:c1 - c0]), inc=True)
    emit_linear(p, "l1f_gate", 1, D, lambda k, c0, c1: xnb.ap(0, 128, k * T + c0, k * T + c1),
                0, T, evac_lgt)

    s0 = lambda a, b: srow0.ap(0, 128, a, b)
    for t0 in range(0, T, 128):
        pb = p.ps[6]
        p.wait("tensor", p.sems["vector"])
        p.t(lambda e, t0=t0, pb=pb: e.transpose(pb[:, 0:128], lgt.ap(0, 128, t0, t0 + 128),
                                                sb["sb_ident"][:]))
        tv = p.pe_done()
        p.wait("vector", p.sems["tensor"], tv)
        p.vtiny(lambda e, pb=pb: e.tensor_copy(s0(0, 8), pb[:, 0:8]))
        p.vtiny(lambda e: e.reduce_max(s0(8, 9), s0(0, 8), axis=AX.X))
        p.vtiny(lambda e: e.tensor_single_scalar(s0(0, 8), s0(0, 8), s0(8, 9), ALU.subtract),
                inc=True)
        p.wait("scalar", p.sems["vector"])
        p.stiny(lambda e: e.activation(s0(0, 8), s0(0, 8), AF.Exp), inc=True)
        p.wait("vector", p.sems["scalar"])
        p.vtiny(lambda e: e.reduce_sum(s0(9, 10), s0(0, 8), axis=AX.X))
        p.vtiny(lambda e: e.reciprocal(s0(9, 10), s0(9, 10)))
        p.vtiny(lambda e: e.reduce_max(s0(10, 11), s0(0, 8), axis=AX.X))
        p.vtiny(lambda e: e.tensor_single_scalar(s0(16, 24), s0(0, 8), s0(10, 11), ALU.is_ge))
        p.vtiny(lambda e: e.tensor_tensor_scan(s0(24, 32), sb["sb_ones"][:, 0:8], s0(16, 24),
                                               0.0, ALU.mult, ALU.add))
        p.vtiny(lambda e: e.tensor_single_scalar(s0(24, 32), s0(24, 32), 1.0, ALU.is_equal))
        p.vtiny(lambda e: e.tensor_mul(s0(32, 40), s0(16, 24), s0(24, 32)))
        p.vtiny(lambda e: e.scalar_tensor_tensor(s0(48, 56), s0(32, 40), -1e30, s0(0, 8),
                                                 ALU.mult, ALU.add))
        p.vtiny(lambda e: e.reduce_max(s0(11, 12), s0(48, 56), axis=AX.X))
        p.vtiny(lambda e: e.tensor_single_scalar(s0(16, 24), s0(48, 56), s0(11, 12), ALU.is_ge))
        p.vtiny(lambda e: e.tensor_tensor_scan(s0(24, 32), sb["sb_ones"][:, 0:8], s0(16, 24),
                                               0.0, ALU.mult, ALU.add))
        p.vtiny(lambda e: e.tensor_single_scalar(s0(24, 32), s0(24, 32), 1.0, ALU.is_equal))
        p.vtiny(lambda e: e.tensor_mul(s0(40, 48), s0(16, 24), s0(24, 32)))
        p.vtiny(lambda e: e.tensor_add(s0(32, 40), s0(32, 40), s0(40, 48)))
        p.vtiny(lambda e: e.tensor_mul(s0(32, 40), s0(32, 40), s0(0, 8)))
        p.vtiny(lambda e: e.tensor_single_scalar(s0(32, 40), s0(32, 40), s0(9, 10), ALU.mult),
                inc=True)
        p.wait("tensor", p.sems["vector"])
        p.t(lambda e, pb=pb: e.transpose(pb[0:8, 0:128], s0(32, 40), sb["sb_ident"][:]))
        tv = p.pe_done()
        p.wait("vector", p.sems["tensor"], tv)
        p.v(lambda e, t0=t0, pb=pb: e.tensor_copy(selx.ap(0, 8, t0, t0 + 128), pb[0:8, 0:128]),
            inc=True)
    p.wait("sync", p.sems["vector"])
    dvsel = p.dma(scr["sel"][:, :], selx.ap(0, 8, 0, T))
    p.wait("vector", p.sem_dma)
    p.wait("sync", p.sem_dma, dvsel)
    fa.reset(lgt.off)
    selB = fa.alloc(T, "selB")
    srow = fa.alloc(T, "srow")
    tmp = fa.alloc(512, "l1ftmp")

    for pass_i in range(9):
        e_i = pass_i if pass_i < 8 else None
        if e_i is not None:
            p.wait("sync", p.sems["vector"])
            p.wait("sync", p.sems["tensor"])
            dvs = p.dma(srow.ap(0, 1, 0, T), scr["sel"][e_i:e_i + 1, :])
            p.wait("vector", p.sem_dma, dvs)
            p.wait("tensor", p.sem_dma, dvs)
            bcast_row(p, lambda c0, c1: srow.ap(0, 1, c0, c1), T, selB)
            w13, w2 = f"l1f_w13_{e_i}", f"l1f_w2_{e_i}"
            b13, b2 = f"l1f_b13_{e_i}", f"l1f_b2_{e_i}"
        else:
            p.wait("vector", p.sems["tensor"])
            p.v(lambda e: e.memset(selB.ap(), 1.0), inc=True)
            w13, w2 = "l1f_sw13", "l1f_sw2"
            b13, b2 = "l1f_sb13", "l1f_sb2"

        def evac13(mi, c0, c1, pband, tv, b13=b13):
            w = c1 - c0
            if mi < 8:
                p.wait("scalar", p.sems["tensor"], tv)
                p.s(lambda e, mi=mi, c0=c0, c1=c1, pband=pband, w=w: e.activation(
                    he.ap(0, 128, mi * T + c0, mi * T + c1), pband[:, 0:w], AF.Silu,
                    bias=p.scol(b13, mi)), inc=True)
            else:
                p.wait("vector", p.sems["tensor"], tv)
                p.wait("vector", p.sems["scalar"])
                p.v(lambda e, mi=mi, c0=c0, c1=c1, pband=pband, w=w: e.scalar_tensor_tensor(
                    he.ap(0, 128, (mi - 8) * T + c0, (mi - 8) * T + c1), pband[:, 0:w],
                    p.scol(b13, mi), he.ap(0, 128, (mi - 8) * T + c0, (mi - 8) * T + c1),
                    ALU.add, ALU.mult), inc=True)
        m_list = []
        for j in range(8):
            m_list += [j, j + 8]
        emit_linear(p, w13, 16, D, lambda k, c0, c1: xnb.ap(0, 128, k * T + c0, k * T + c1),
                    0, T, evac13, m_list=m_list)

        def evac_o(mi, c0, c1, pband, tv, b2=b2):
            w = c1 - c0
            p.wait("vector", p.sems["tensor"], tv)
            p.v(lambda e, mi=mi, c0=c0, c1=c1, pband=pband, w=w: e.scalar_tensor_tensor(
                tmp.ap(0, 128, 0, w), pband[:, 0:w], p.scol(b2, mi), selB.ap(0, 128, c0, c1),
                ALU.add, ALU.mult))
            p.v(lambda e, mi=mi, c0=c0, c1=c1, w=w: e.tensor_add(
                xacc.ap(0, 128, mi * T + c0, mi * T + c1),
                xacc.ap(0, 128, mi * T + c0, mi * T + c1), tmp.ap(0, 128, 0, w)), inc=True)
        emit_linear(p, w2, 8, 1024, lambda k, c0, c1: he.ap(0, 128, k * T + c0, k * T + c1),
                    0, T, evac_o)
    fa.reset(famark)
    ba.reset(bamark)



def build_program(H, SM, layers):
    nc = bass.Bass("TRN2", target_bir_lowering=False, debug=False, detect_race_conditions=False)
    sb = {"_smoff": SM.off}
    for name, arr in H.items():
        dt = BF16 if arr.dtype.name == "bfloat16" else F32
        sb[name] = nc.dram_tensor(name, list(arr.shape), dt, kind="ExternalInput").ap()
    xin = nc.dram_tensor("xT", [D, T], F32, kind="ExternalInput").ap()
    xout = nc.dram_tensor("outT", [D, T], F32, kind="ExternalOutput").ap()
    sb["dbg"] = nc.dram_tensor("dbg", [2048, T], BF16, kind="ExternalOutput").ap()
    scr = {
        "xn": nc.dram_tensor("scr_xn", [D, T], BF16, kind="Internal").ap(),
        "xnf": nc.dram_tensor("scr_xnf", [D, T], F32, kind="Internal").ap(),
        "w1q": nc.dram_tensor("scr_w1q", [32, D, 128], BF16, kind="Internal").ap(),
        "w2q": nc.dram_tensor("scr_w2q", [8, 4096, 128], BF16, kind="Internal").ap(),
        "s": nc.dram_tensor("scr_s", [4096, T], BF16, kind="Internal").ap(),
        "sel": nc.dram_tensor("scr_sel", [8, T], F32,
                              kind=("ExternalOutput" if DBG_SEL_OUT else "Internal")).ap(),
    }
    for nm_, shp_, dt_ in (("q", [512, T], BF16), ("k", [512, T], BF16),
                           ("v", [1024, T], BF16), ("ab", [512, T], F32),
                           ("bb", [8, T], F32), ("y", [1024, T], BF16),
                           ("y2", [1024, T], BF16)):
        kind = "ExternalOutput" if nm_ in DBG_SCR_OUT else "Internal"
        scr[nm_] = nc.dram_tensor("scr_" + nm_, shp_, dt_, kind=kind).ap()

    p = P(nc, sb)
    for cname, dt, shape in (("c_ident", F32, [128, 128]), ("c_identb", BF16, [128, 128]),
                             ("c_ones", F32, [128, 128])):
        h = nc.alloc_sbuf_tensor("sb" + cname[1:], shape, dt)
        sb["sb" + cname[1:]] = h
        p.dma(h[:, :], sb[cname])
    hbd2 = nc.alloc_sbuf_tensor("sb_bd2", [2, 128], F32)
    sb["sb_bd2"] = hbd2
    p.dma(hbd2[:, :], sb["c_bd2"])
    for cn in ("c_bd2A", "c_bd2B"):
        hh = nc.alloc_sbuf_tensor("sb" + cn[1:], [2, 128], F32)
        sb["sb" + cn[1:]] = hh
        p.dma(hh[:, :], sb[cn])
    for cn in ("c_l1stril", "c_l1triu"):
        hh = nc.alloc_sbuf_tensor("sb" + cn[1:], [64, 64], F32)
        sb["sb" + cn[1:]] = hh
        p.dma(hh[:, :], sb[cn])
    hsm = nc.alloc_sbuf_tensor("sb_smalls", [128, H["c_smalls"].shape[1]], F32)
    sb["sb_smalls"] = hsm
    p.dma(hsm[:, :], sb["c_smalls"])

    p.onesb = p.ba.alloc(2, "onesb")
    p.vtiny(lambda e: e.memset(p.onesb.ap(), 1.0), inc=True)
    xacc = p.fa.alloc(NRC * T, "xacc")
    negmuB = p.fa.alloc(T, "negmu")
    rstdB = p.fa.alloc(T, "rstd")
    for r in range(NRC):
        p.dma(xacc.ap(0, 128, r * T, (r + 1) * T), xin[r * 128:(r + 1) * 128, :])
    dv = p.sem_dma.n
    for e in ("vector", "scalar", "tensor"):
        p.wait(e, p.sem_dma, dv)

    sb["_scr_xnf"] = scr["xnf"]
    for L in layers:
        emit_ln(p, xacc, scr["xn"], negmuB, rstdB)
        if L == 0:
            if DBG_STOP != "ln":
                emit_l0_attn(p, xacc, scr["xn"])
                if DBG_STOP in (None, "ffn", "full"):
                    emit_l0_ffn(p, xacc, scr)
        elif L == 1:
            if DBG_STOP != "ln":
                if DBG_STOP != "l1ffn":
                    emit_l1_attn(p, xacc, scr["xn"], negmuB, rstdB, scr)
                if DBG_STOP in (None, "ffn", "full", "l1ffn"):
                    emit_l1_ffn(p, xacc, scr["xn"], negmuB, rstdB, scr)
        elif L == 2:
            if DBG_STOP != "ln":
                emit_l2_attn(p, xacc, scr["xn"])
                if DBG_STOP in (None, "ffn", "full"):
                    emit_l2_ffn(p, xacc, scr["xn"], negmuB, rstdB, scr)
        elif L == 3:
            if DBG_STOP != "ln":
                emit_l3_attn(p, xacc, scr["xn"], negmuB, rstdB, scr)
                if DBG_STOP in (None, "ffn", "full"):
                    emit_l3_ffn(p, xacc, scr["xn"], negmuB, rstdB, scr)
        else:
            raise NotImplementedError(f"layer {L}")
        p.barrier()

    p.barrier()
    for e in ("vector", "scalar", "tensor"):
        pass
    p.wait("sync", p.sems["vector"])
    p.wait("sync", p.sems["scalar"])
    p.wait("sync", p.sems["tensor"])
    for r in range(NRC):
        p.dma(xout[r * 128:(r + 1) * 128, :], xacc.ap(0, 128, r * T, (r + 1) * T))
    p.wait("sync", p.sem_dma, p.sem_dma.n)
    p.finalize()
    return nc


def run_layers(inputs, layers, x_override=None, cores=8):
    H, SM = host_prep(inputs)
    nc = build_program(H, SM, layers)
    x = np.asarray(inputs["L0_attn_x"], np.float32) if x_override is None else x_override
    in_maps = []
    for c in range(cores):
        m = dict(H)
        m["xT"] = np.ascontiguousarray(x[c].T)
        in_maps.append(m)
    res = run_bass_kernel_spmd(nc, in_maps, core_ids=list(range(cores)))
    out = np.stack([np.ascontiguousarray(r["outT"].T) for r in res.results], axis=0)
    return out


def kernel(**inputs):
    return run_layers(inputs, [0, 1, 2, 3]).astype(np.float32)


# version 02
# speedup vs baseline: 1.0512x; 1.0512x over previous
import numpy as np
import concourse.bass as bass
import concourse.mybir as mybir
from concourse.bass_utils import run_bass_kernel_spmd

DT = mybir.dt
F32, BF16, I32 = DT.float32, DT.bfloat16, DT.int32
AF = mybir.ActivationFunctionType
ALU = mybir.AluOpType
AX = mybir.AxisListType

D = 1024
T = 2048
DBG_STOP = None
DBG_DUMP = False
DBG_SEL_OUT = False
DBG_SCR_OUT = ()
DBG_L1 = None
DBG_L1_NPAIR = 4
DBG_L1_NCHUNK = 64
DBG_L1_STEPS = ("g", "L", "newton", "out", "state")
NRC = D // 128
QHI = 127.49999
QLO = -128.49999


def pack_w(w):
    w = np.asarray(w, np.float32)
    K, M = w.shape
    Kp = (K + 127) // 128 * 128
    Mp = (M + 127) // 128 * 128
    wp = np.zeros((Kp, Mp), np.float32)
    wp[:K, :M] = w
    return np.ascontiguousarray(wp.reshape(Kp, Mp // 128, 128).transpose(1, 0, 2))


class Smalls:
    def __init__(self):
        self.cols = []
        self.off = {}
        self.n = 0

    def add(self, name, arr):
        a = np.asarray(arr, np.float32)
        assert a.shape[0] == 128
        self.off[name] = (self.n, a.shape[1])
        self.cols.append(a)
        self.n += a.shape[1]

    def addb(self, name, vec, nm=None):
        v = np.asarray(vec, np.float32).reshape(-1)
        nm_ = (len(v) + 127) // 128 if nm is None else nm
        bp = np.zeros((nm_ * 128,), np.float32)
        bp[:len(v)] = v
        self.add(name, np.ascontiguousarray(bp.reshape(nm_, 128).T))

    def array(self):
        return np.ascontiguousarray(np.concatenate(self.cols, axis=1))


def rope_tables():
    inv = 1.0 / 10000.0 ** (np.arange(0, 64, 2, dtype=np.float64) / 64.0)
    t = np.arange(T, dtype=np.float64)
    fr = np.outer(inv, t)
    cos = np.cos(fr); sin = np.sin(fr)
    cc = np.zeros((128, T), np.float64)
    ss = np.zeros((128, T), np.float64)
    for g in range(2):
        r = g * 64
        cc[r:r+32] = cos; cc[r+32:r+64] = cos
        ss[r:r+32] = -sin; ss[r+32:r+64] = sin
    return cc.astype(np.float32), ss.astype(np.float32)


def host_prep(ii):
    import ml_dtypes
    H = {}
    SM = Smalls()
    H["c_ident"] = np.eye(128, dtype=np.float32)
    H["c_identb"] = np.eye(128, dtype=np.float32).astype(ml_dtypes.bfloat16)
    H["c_ones"] = np.ones((128, 128), np.float32)
    ob = np.zeros((128, 2), np.float32)
    ob[0:64, 0] = 1.0; ob[64:128, 1] = 1.0
    SM.add("ones64bd", ob)
    pswap = np.zeros((128, 128), np.float32)
    for g in range(2):
        r = g * 64
        pswap[r:r+32, r+32:r+64] = np.eye(32)
        pswap[r+32:r+64, r:r+32] = np.eye(32)
    H["c_pswapb"] = pswap.astype(ml_dtypes.bfloat16)
    cc, ss = rope_tables()
    H["c_ropecc"] = cc
    H["c_ropess"] = ss
    m2 = np.ones((128, 128), np.float32)
    m2[:64, -64:] = 0.0
    m2[-64:, :64] = 0.0
    H["c_l2mask"] = m2
    C1 = 32
    st = np.tril(np.ones((C1, C1), np.float32), -1)
    blk = np.zeros((64, 64), np.float32)
    blk[:C1, :C1] = st; blk[C1:, C1:] = st
    H["c_l1stril"] = blk
    tu = np.triu(np.ones((C1, C1), np.float32), 0)
    blk2 = np.zeros((64, 64), np.float32)
    blk2[:C1, :C1] = tu; blk2[C1:, C1:] = tu
    H["c_l1triu"] = blk2
    mA = np.zeros((128, 1), np.float32); mA[0:64] = 1.0
    mB = np.zeros((128, 1), np.float32); mB[64:128] = 1.0
    SM.add("l1_mA", mA)
    SM.add("l1_mB", mB)
    bd2 = np.zeros((2, 128), np.float32)
    bd2[0, 0:64] = 1.0; bd2[1, 64:128] = 1.0
    H["c_bd2"] = bd2
    bd2a = np.zeros((2, 128), np.float32); bd2a[0, :] = 1.0
    bd2b = np.zeros((2, 128), np.float32); bd2b[1, :] = 1.0
    H["c_bd2A"] = bd2a
    H["c_bd2B"] = bd2b
    for nm_, v_ in (("k_eps5", 1e-5), ("k_eps6", 1e-6), ("k_eps3", 1e-3), ("k_half", 0.5)):
        SM.add(nm_, np.full((128, 1), v_, np.float32))

    H["l0_wqkv"] = pack_w(ii["L0_attn_Wqkv"].T)
    SM.addb("l0_bqkv", ii["L0_attn_bqkv"])
    H["l0_wout"] = pack_w(ii["L0_attn_Wout"].T)
    SM.addb("l0_bout", ii["L0_attn_bout"])
    H["l0_w1"] = pack_w(ii["L0_ffn_w1"].T)
    for j in range(3):
        SM.addb(f"l0_convw{j}", ii["L0_ffn_conv_w"][:, 0, j])
    SM.addb("l0_convb", ii["L0_ffn_conv_b"])
    H["l0_w2"] = pack_w(ii["L0_ffn_w2"].T)

    SM.addb("l1_wnorm", ii["L1_attn_w_norm_in"])
    for j in range(4):
        SM.addb(f"l1_convw{j}", ii["L1_attn_conv_w"][:, j])
    SM.addb("l1_convb", ii["L1_attn_conv_b"])
    wproj = np.zeros((1024, 22 * 128), np.float32)
    wproj[:, 0:2560] = np.concatenate([ii["L1_attn_wq"], ii["L1_attn_wk"], ii["L1_attn_wv"],
                                       ii["L1_attn_w_spike"]], axis=1)
    wproj[:, 2560:2624] = ii["L1_attn_w_au"]
    wproj[:, 2688:2696] = ii["L1_attn_w_beta"]
    H["l1_wproj"] = pack_w(wproj)
    bproj = np.zeros((22 * 128,), np.float32)
    bproj[2560:2624] = ii["L1_attn_b_au"]
    bproj[2688:2696] = ii["L1_attn_b_beta"]
    SM.addb("l1_bproj", bproj, nm=22)
    H["l1_wad"] = pack_w(ii["L1_attn_w_ad"])
    SM.addb("l1_bad", ii["L1_attn_b_ad"])
    SM.addb("l1_aspk", ii["L1_attn_alpha_spike"].reshape(-1))
    SM.addb("l1_bspk", ii["L1_attn_beta_spike"].reshape(-1))
    SM.addb("l1_whn", ii["L1_attn_w_hn"].reshape(-1))
    H["l1_wu1"] = pack_w(ii["L1_attn_w_u1"])
    SM.addb("l1_bu1", ii["L1_attn_b_u1"])
    H["l1_wu2"] = pack_w(ii["L1_attn_w_u2"])
    SM.addb("l1_bu2", ii["L1_attn_b_u2"])
    H["l1_wout"] = pack_w(ii["L1_attn_w_out"])
    SM.addb("l1_wffn", ii["L1_attn_w_ffn"])
    H["l1_ff13"] = pack_w(np.concatenate([ii["L1_attn_w_ff1"], ii["L1_attn_w_ff3"]], axis=1))
    H["l1_ff2"] = pack_w(ii["L1_attn_w_ff2"])

    H["l1f_gate"] = pack_w(ii["L1_ffn_gate_w"].T)
    for e in range(8):
        H[f"l1f_w13_{e}"] = pack_w(np.concatenate([ii["L1_ffn_w1"][e], ii["L1_ffn_w3"][e]], axis=1))
        H[f"l1f_w2_{e}"] = pack_w(ii["L1_ffn_w2"][e])
        SM.addb(f"l1f_b13_{e}", np.concatenate([ii["L1_ffn_b1"][e], ii["L1_ffn_b3"][e]]))
        SM.addb(f"l1f_b2_{e}", ii["L1_ffn_b2"][e])
    H["l1f_sw13"] = pack_w(np.concatenate([ii["L1_ffn_sw1"], ii["L1_ffn_sw3"]], axis=1))
    SM.addb("l1f_sb13", np.concatenate([ii["L1_ffn_sb1"], ii["L1_ffn_sb3"]]))
    H["l1f_sw2"] = pack_w(ii["L1_ffn_sw2"])
    SM.addb("l1f_sb2", ii["L1_ffn_sb2"])

    perm = np.zeros((3072,), np.int64)
    i = 0
    for s in range(3):
        for h in range(16):
            for dd in range(64):
                perm[i] = h * 192 + s * 64 + dd
                i += 1
    H["l2_wqkv"] = pack_w(ii["L2_attn_w_qkv"][:, perm])
    SM.addb("l2_bqkv", ii["L2_attn_b_qkv"][perm])
    H["l2_wo"] = pack_w(ii["L2_attn_w_o"])
    SM.addb("l2_bo", ii["L2_attn_b_o"])
    SM.addb("l2_ln1w", ii["L2_ffn_ln1_w"])
    SM.addb("l2_ln1b", ii["L2_ffn_ln1_b"])
    H["l2_w1"] = pack_w(ii["L2_ffn_w1"].T)
    SM.addb("l2_b1", ii["L2_ffn_b1"])
    SM.addb("l2_ln2w", ii["L2_ffn_ln2_w"])
    SM.addb("l2_ln2b", ii["L2_ffn_ln2_b"])
    H["l2_w2"] = pack_w(ii["L2_ffn_w2"].T)
    SM.addb("l2_b2", ii["L2_ffn_b2"])

    H["l3_wqkv"] = pack_w(ii["L3_attn_Wqkv"].T)
    SM.addb("l3_bqkv", ii["L3_attn_bqkv"])
    H["l3_wproj"] = pack_w(ii["L3_attn_Wproj"].T)
    SM.addb("l3_bproj", ii["L3_attn_bproj"])
    SM.add("l3_scale", np.full((128, 1), np.float32(np.asarray(ii["L3_attn_scale"]))))
    H["l3_wg"] = pack_w(ii["L3_ffn_Wg"].T)
    for e in range(8):
        H[f"l3_w1_{e}"] = pack_w(ii["L3_ffn_W1"][e])
        H[f"l3_w2_{e}"] = pack_w(ii["L3_ffn_W2"][e])
        SM.addb(f"l3_b1_{e}", ii["L3_ffn_b1"][e])
        SM.addb(f"l3_b2_{e}", ii["L3_ffn_b2"][e])
        SM.addb(f"l3_lns_{e}", ii["L3_ffn_ln_scale"][e])
        SM.addb(f"l3_lnb_{e}", ii["L3_ffn_ln_bias"][e])
    H["c_smalls"] = SM.array()
    return H, SM



ENGS = ("sync", "vector", "scalar", "tensor", "gpsimd")


ALL_SEMS = []


class Sem:
    def __init__(self, nc, name):
        self.h = nc.alloc_semaphore(name)
        self.n = 0
        ALL_SEMS.append(self.h)


class Buf:
    def __init__(self, t, off, cols, dtype):
        self.t, self.off, self.cols, self.dtype = t, off, cols, dtype

    def ap(self, r0=0, r1=128, c0=0, c1=None):
        c1 = self.cols if c1 is None else c1
        return self.t[r0:r1, self.off + c0:self.off + c1]


class Arena:
    def __init__(self, nc, name, cols, dtype):
        self.t = nc.alloc_sbuf_tensor(name, [128, cols], dtype)
        self.cols = cols
        self.off = 0
        self.peak = 0
        self.dtype = dtype

    def alloc(self, cols, tag=""):
        cols = (cols + 1) & ~1
        assert self.off + cols <= self.cols, f"arena {tag}: {self.off}+{cols}>{self.cols}"
        b = Buf(self.t, self.off, cols, self.dtype)
        self.off += cols
        self.peak = max(self.peak, self.off)
        return b

    def mark(self):
        return self.off

    def reset(self, m):
        self.off = m


class P:
    def __init__(self, nc, sb):
        ALL_SEMS.clear()
        self.nc = nc
        self.sb = sb
        self.ops = {e: [] for e in ENGS}
        self.sems = {e: Sem(nc, f"s_{e}") for e in ENGS}
        self.sem_dma = Sem(nc, "s_dma")
        self.fa = Arena(nc, "fa", 32800, F32)
        self.ba = Arena(nc, "ba", 34000, BF16)
        self.ia = Arena(nc, "ia", 512, I32)
        self.ps = [nc.alloc_psum_tensor(f"ps{i}", [128, 512], F32) for i in range(7)]
        self.psb = nc.alloc_psum_tensor("psb", [128, 1024], BF16)
        self.tiny = nc.alloc_sbuf_tensor("tinybar", [128, 8], F32)

    def small(self, name):
        c0, n = self.sb["_smoff"][name]
        return self.sb["sb_smalls"][0:128, c0:c0 + n]

    def scol(self, name, j=0):
        c0, n = self.sb["_smoff"][name]
        assert j < n, (name, j, n)
        return self.sb["sb_smalls"][0:128, c0 + j:c0 + j + 1]

    def raw(self, eng, fn):
        self.ops[eng].append(fn)

    def op(self, eng, fn, inc=None, k=1):
        if inc is not None:
            inc.n += k
            self.raw(eng, lambda e, fn=fn, h=inc.h, k=k: fn(e).then_inc(h, k))
        else:
            self.raw(eng, lambda e, fn=fn: fn(e))

    def v(self, fn, inc=False):
        self.op("vector", fn, inc=self.sems["vector"] if inc else None)

    def s(self, fn, inc=False):
        self.op("scalar", fn, inc=self.sems["scalar"] if inc else None)

    def g(self, fn, inc=False):
        self.op("gpsimd", fn, inc=self.sems["gpsimd"] if inc else None)

    def t(self, fn):
        self.op("tensor", fn)

    def vtiny(self, fn, inc=False):
        self.raw("vector", lambda e, fn=fn: fn(e))
        if inc:
            s = self.sems["vector"]; s.n += 1
            self.raw("vector", lambda e, h=s.h: e.drain().then_inc(h, 1))
        else:
            self.raw("vector", lambda e: e.drain())

    def stiny(self, fn, inc=False):
        self.raw("scalar", lambda e, fn=fn: fn(e))
        if inc:
            s = self.sems["scalar"]; s.n += 1
            self.raw("scalar", lambda e, h=s.h: e.drain().then_inc(h, 1))
        else:
            self.raw("scalar", lambda e: e.drain())

    def wait(self, eng, sem, val=None):
        v = sem.n if val is None else val
        if v > 0:
            self.raw(eng, lambda e, s=sem.h, v=v: e.wait_ge(s, v))

    def pe_done(self):
        s = self.sems["tensor"]
        s.n += 1
        self.raw("tensor", lambda e, h=s.h: e.drain(fusable=False).then_inc(h, 1))
        return s.n

    def dma(self, out_ap, in_ap, eng="sync"):
        self.sem_dma.n += 16
        self.raw(eng, lambda e, o=out_ap, i=in_ap, h=self.sem_dma.h: e.dma_start(out=o, in_=i).then_inc(h, 16))
        return self.sem_dma.n

    def barrier(self, engs=("vector", "scalar", "tensor")):
        snap = {}
        for i, e in enumerate(engs):
            s = self.sems[e]
            s.n += 1
            if e == "tensor":
                self.raw(e, lambda x, h=s.h: x.drain(fusable=False).then_inc(h, 1))
            else:
                self.raw(e, lambda x, h=s.h: x.drain().then_inc(h, 1))
            snap[e] = s.n
        for e in engs:
            for o in engs:
                if o != e:
                    self.wait(e, self.sems[o], snap[o])

    def finalize(self):
        sems = list(ALL_SEMS)
        with self.nc.Block() as block:
            for eng in ENGS:
                fns = self.ops[eng]
                if not fns:
                    continue
                def mk(fns, eng):
                    def run(e):
                        for s in sems:
                            e.sem_clear(s)
                        if eng == "vector":
                            e.memset(self.tiny[0:1, 0:2], 0.0)
                        elif eng == "scalar":
                            e.memzero(self.tiny[0:1, 2:4])
                        elif eng == "gpsimd":
                            e.memset(self.tiny[0:1, 4:6], 0.0)
                        for f in fns:
                            f(e)
                    return run
                getattr(block, eng)(mk(fns, eng))



def bcast_row(p, row_ap_fn, width, out_buf, scale=1.0, out_c0=0):
    sb = p.sb
    for c0 in range(0, width, 512):
        c1 = min(c0 + 512, width)
        w = c1 - c0
        pb = p.ps[6]
        p.wait("tensor", p.sems["vector"])
        p.t(lambda e, pb=pb, w=w, c0=c0, c1=c1: e.matmul(
            pb[:, 0:w], sb["sb_ones"][0:1, :], row_ap_fn(c0, c1), start=True, stop=True))
        tv = p.pe_done()
        p.wait("vector", p.sems["tensor"], tv)
        if scale == 1.0:
            p.v(lambda e, pb=pb, w=w, c0=c0, c1=c1: e.tensor_copy(
                out_buf.ap(0, 128, out_c0 + c0, out_c0 + c1), pb[:, 0:w]), inc=True)
        else:
            p.v(lambda e, pb=pb, w=w, c0=c0, c1=c1: e.tensor_scalar_mul(
                out_buf.ap(0, 128, out_c0 + c0, out_c0 + c1), pb[:, 0:w], float(scale)), inc=True)


def bcast_col(p, scalar_ap, out_ap, scale=1.0):
    pb = p.ps[6]
    p.wait("tensor", p.sems["vector"])
    p.t(lambda e: e.matmul(pb[:, 0:1], p.sb["sb_ones"][0:1, :], scalar_ap, start=True, stop=True))
    tv = p.pe_done()
    p.wait("vector", p.sems["tensor"], tv)
    if scale == 1.0:
        p.vtiny(lambda e: e.tensor_copy(out_ap, pb[:, 0:1]), inc=True)
    else:
        p.vtiny(lambda e: e.tensor_scalar_mul(out_ap, pb[:, 0:1], float(scale)), inc=True)


def part_max_to_row(p, m_buf, width, out_row_fn, scr=None):
    sb = p.sb
    fa = p.fa
    own = scr is None
    scr = fa.alloc(width, "pmscr") if own else scr
    for c0 in range(0, width, 128):
        w = min(128, width - c0)
        pb = p.ps[6]
        p.wait("tensor", p.sems["vector"])
        p.t(lambda e, c0=c0, w=w, pb=pb: e.transpose(pb[0:w, 0:128], m_buf.ap(0, 128, c0, c0 + w),
                                                     sb["sb_ident"][:]))
        tv = p.pe_done()
        p.wait("vector", p.sems["tensor"], tv)
        p.vtiny(lambda e, pb=pb, w=w, c0=c0: e.reduce_max(scr.ap(0, w, c0, c0 + 1), pb[0:w, 0:128],
                                                          axis=AX.X), inc=True)
        p.wait("tensor", p.sems["vector"])
        p.t(lambda e, c0=c0, w=w, pb=pb: e.transpose(pb[0:1, 0:w], scr.ap(0, w, c0, c0 + 1),
                                                     sb["sb_ident"][:]))
        tv = p.pe_done()
        p.wait("vector", p.sems["tensor"], tv)
        out_row_fn(c0, w, pb)
    if own:
        fa.reset(scr.off)


def quant_tile(p, in_ap, sB_ap, out_ap, f_ap, i_ap):
    p.v(lambda e: e.tensor_mul(f_ap, in_ap, sB_ap))
    p.v(lambda e: e.tensor_scalar(f_ap, f_ap, QHI, QLO, ALU.min, ALU.max))
    p.v(lambda e: e.tensor_scalar_add(f_ap, f_ap, 0.5))
    p.v(lambda e: e.tensor_copy(i_ap, f_ap))
    p.v(lambda e: e.tensor_copy(out_ap, i_ap))
    p.v(lambda e: e.tensor_sub(f_ap, f_ap, out_ap))
    p.v(lambda e: e.scalar_tensor_tensor(out_ap, f_ap, 1.0, out_ap, ALU.is_ge, ALU.add))
    p.v(lambda e: e.scalar_tensor_tensor(out_ap, f_ap, 0.0, out_ap, ALU.is_ge, ALU.add))
    p.v(lambda e: e.tensor_scalar_add(out_ap, out_ap, -1.0), inc=True)


def emit_linear(p, wname, nm, K, rhs_fn, n0, n1, evac, m_list=None, ternary_thr=None,
                wq_bf16_dram=None, nwide=512, k0=0, nbanks=4):
    sb = p.sb
    fa, ba = p.fa, p.ba
    nk = (K + 127) // 128
    wbf = ba.alloc(nk * 128, "wbf")
    stage = None if wq_bf16_dram is not None else fa.alloc(nk * 128, "wstage")
    mlist = list(range(nm)) if m_list is None else m_list
    ev_marks = []
    gi = 0
    for mi in mlist:
        if wq_bf16_dram is not None:
            p.wait("sync", p.sems["tensor"])
            dv = p.dma(wbf.ap().rearrange("p (k c) -> p k c", c=128), wq_bf16_dram[mi, k0:k0 + nk * 128].rearrange("(k p) c -> p k c", p=128))
            p.wait("tensor", p.sem_dma, dv)
            p.wait("tensor", p.sems["vector"])
            p.wait("tensor", p.sems["scalar"])
        else:
            p.wait("sync", p.sems["vector"])
            dv = p.dma(stage.ap().rearrange("p (k c) -> p k c", c=128), sb[wname][mi, k0:k0 + nk * 128].rearrange("(k p) c -> p k c", p=128))
            p.wait("vector", p.sem_dma, dv)
            p.wait("vector", p.sems["tensor"])
            if ternary_thr is None:
                p.v(lambda e: e.tensor_copy(wbf.ap(), stage.ap()), inc=True)
            else:
                p.v(lambda e: e.tensor_single_scalar(wbf.ap(), stage.ap(), ternary_thr, ALU.is_lt))
                p.v(lambda e: e.scalar_tensor_tensor(wbf.ap(), stage.ap(), ternary_thr, wbf.ap(),
                                                     ALU.is_gt, ALU.subtract), inc=True)
            p.wait("tensor", p.sems["vector"])
        p.wait("tensor", p.sems["scalar"])
        chunks = [(c0, min(c0 + nwide, n1)) for c0 in range(n0, n1, nwide)]
        gsz = 2 if nbanks == 4 else 1
        for g0 in range(0, len(chunks), gsz):
            grp = chunks[g0:g0 + gsz]
            if nbanks == 4:
                banks = [p.ps[2 * (gi % 2)], p.ps[2 * (gi % 2) + 1]]
            else:
                banks = [p.ps[gi % 2]]
            if len(ev_marks) >= 2:
                for eng, val in ev_marks[-2]:
                    p.wait("tensor", p.sems[eng], val)
            for bi, (c0, c1) in enumerate(grp):
                w = c1 - c0
                for k in range(nk):
                    p.t(lambda e, pband=banks[bi], k=k, c0=c0, c1=c1, w=w: e.matmul(
                        pband[:, 0:w], wbf.ap(0, 128, k * 128, (k + 1) * 128),
                        rhs_fn(k, c0, c1), start=(k == 0), stop=(k == nk - 1)))
            tv = p.pe_done()
            vb, s_b = p.sems["vector"].n, p.sems["scalar"].n
            for bi, (c0, c1) in enumerate(grp):
                evac(mi, c0, c1, banks[bi], tv)
            mk = []
            if p.sems["vector"].n > vb:
                mk.append(("vector", p.sems["vector"].n))
            if p.sems["scalar"].n > s_b:
                mk.append(("scalar", p.sems["scalar"].n))
            if not mk:
                raise RuntimeError("evac must inc vector or scalar sem")
            ev_marks.append(mk)
            gi += 1
    if stage is not None:
        fa.reset(stage.off)
    ba.reset(wbf.off)


def emit_mean_abs(p, wname, nm, K, true_count, out_buf, out_c0):
    sb = p.sb
    fa = p.fa
    nk = (K + 127) // 128
    stage = fa.alloc(nk * 128, "mst")
    acc = fa.alloc(2, "mac")
    p.v(lambda e: e.memset(acc.ap(0, 128, 0, 1), 0.0), inc=True)
    for mi in range(nm):
        p.wait("sync", p.sems["vector"])
        dv = p.dma(stage.ap().rearrange("p (k c) -> p k c", c=128), sb[wname][mi].rearrange("(k p) c -> p k c", p=128))
        p.wait("vector", p.sem_dma, dv)
        p.vtiny(lambda e: e.tensor_reduce(acc.ap(0, 128, 1, 2), stage.ap(), axis=AX.X, op=ALU.add,
                                          apply_absolute_value=True))
        p.vtiny(lambda e: e.tensor_add(acc.ap(0, 128, 0, 1), acc.ap(0, 128, 0, 1), acc.ap(0, 128, 1, 2)),
                inc=True)
    pb = p.ps[6]
    p.wait("tensor", p.sems["vector"])
    p.t(lambda e: e.transpose(pb[0:1, 0:128], acc.ap(0, 128, 0, 1), sb["sb_ident"][:]))
    tv = p.pe_done()
    p.wait("vector", p.sems["tensor"], tv)
    p.vtiny(lambda e: e.reduce_sum(out_buf.ap(0, 1, out_c0, out_c0 + 1), pb[0:1, 0:128], axis=AX.X))
    p.vtiny(lambda e: e.tensor_scalar_mul(out_buf.ap(0, 1, out_c0, out_c0 + 1),
                                          out_buf.ap(0, 1, out_c0, out_c0 + 1), float(1.0 / true_count)),
            inc=True)
    fa.reset(stage.off)



def emit_ln(p, xacc, xn_dram, negmuB, rstdB, eps=1e-5):
    sb = p.sb
    fa = p.fa
    mk = fa.mark()
    rows = fa.alloc(2 * T, "lnrows")
    sq = fa.alloc(512, "lnsq")
    for c0 in range(0, T, 512):
        pbs = p.ps[5]
        pbq = p.ps[6]
        p.wait("tensor", p.sems["vector"])
        for r in range(NRC):
            p.wait("scalar", p.sems["tensor"])
            p.s(lambda e, r=r, c0=c0: e.activation(sq.ap(0, 128, 0, 512),
                                                   xacc.ap(0, 128, r * T + c0, r * T + c0 + 512),
                                                   AF.Square), inc=True)
            p.wait("tensor", p.sems["scalar"])
            p.t(lambda e, r=r, c0=c0, pbs=pbs: e.matmul(
                pbs[0:1, 0:512], sb["sb_ones"][:, 0:1], xacc.ap(0, 128, r * T + c0, r * T + c0 + 512),
                start=(r == 0), stop=(r == NRC - 1)))
            p.t(lambda e, r=r, pbq=pbq: e.matmul(pbq[0:1, 0:512], sb["sb_ones"][:, 0:1],
                                                 sq.ap(0, 128, 0, 512),
                                                 start=(r == 0), stop=(r == NRC - 1)))
            p.pe_done()
        tv = p.sems["tensor"].n
        p.wait("vector", p.sems["tensor"], tv)
        p.v(lambda e, c0=c0, pbs=pbs: e.tensor_scalar_mul(rows.ap(0, 1, c0, c0 + 512), pbs[0:1, 0:512],
                                                          1.0 / D))
        p.v(lambda e, c0=c0, pbq=pbq: e.tensor_scalar_mul(rows.ap(0, 1, T + c0, T + c0 + 512),
                                                          pbq[0:1, 0:512], 1.0 / D))
        p.v(lambda e, c0=c0: e.tensor_mul(sq.ap(0, 1, 0, 512), rows.ap(0, 1, c0, c0 + 512),
                                          rows.ap(0, 1, c0, c0 + 512)))
        p.v(lambda e, c0=c0: e.tensor_sub(rows.ap(0, 1, T + c0, T + c0 + 512),
                                          rows.ap(0, 1, T + c0, T + c0 + 512), sq.ap(0, 1, 0, 512)),
            inc=True)
        p.wait("scalar", p.sems["vector"])
        p.s(lambda e, c0=c0: e.activation(rows.ap(0, 1, T + c0, T + c0 + 512),
                                          rows.ap(0, 1, T + c0, T + c0 + 512), AF.Sqrt,
                                          bias=p.scol("k_eps5")[0:1, 0:1]),
            inc=True)
        p.wait("vector", p.sems["scalar"])
        p.v(lambda e, c0=c0: e.reciprocal(rows.ap(0, 1, T + c0, T + c0 + 512),
                                          rows.ap(0, 1, T + c0, T + c0 + 512)), inc=True)
    bcast_row(p, lambda c0, c1: rows.ap(0, 1, c0, c1), T, negmuB, scale=-1.0)
    bcast_row(p, lambda c0, c1: rows.ap(0, 1, T + c0, T + c1), T, rstdB)
    tmp = fa.alloc(T, "lntmp")
    xff = fa.alloc(T, "lnxf")
    xnb = p.ba.alloc(T, "lnxnb")
    for r in range(NRC):
        p.wait("vector", p.sem_dma)
        p.v(lambda e, r=r: e.tensor_add(tmp.ap(), xacc.ap(0, 128, r * T, (r + 1) * T), negmuB.ap()))
        p.v(lambda e, r=r: e.tensor_mul(xff.ap(), tmp.ap(), rstdB.ap()))
        p.v(lambda e, r=r: e.tensor_copy(xnb.ap(), xff.ap()), inc=True)
        p.wait("sync", p.sems["vector"])
        p.dma(xn_dram[r * 128:(r + 1) * 128, :], xnb.ap())
        p.dma(p.sb["_scr_xnf"][r * 128:(r + 1) * 128, :], xff.ap())
    fa.reset(mk)
    p.ba.reset(xnb.off)


def xn32(p, xacc, negmuB, rstdB, r, out_ap, tmp_ap, c0=0, c1=T):
    p.v(lambda e: e.tensor_add(tmp_ap, xacc.ap(0, 128, r * T + c0, r * T + c1), negmuB.ap(0, 128, c0, c1)))
    p.v(lambda e: e.tensor_mul(out_ap, tmp_ap, rstdB.ap(0, 128, c0, c1)), inc=True)


def load_xn_slice(p, xn_dram, xsl, c0, c1):
    w = c1 - c0
    for k in range(NRC):
        p.dma(xsl.ap(0, 128, k * w, (k + 1) * w), xn_dram[k * 128:(k + 1) * 128, c0:c1])
    return p.sem_dma.n



def emit_l0_attn(p, xacc, xn_dram):
    sb = p.sb
    fa, ba = p.fa, p.ba
    famark, bamark = fa.mark(), ba.mark()

    pswap = ba.alloc(128, "pswap")
    p.wait("sync", p.sems["vector"])
    p.wait("sync", p.sems["scalar"])
    p.wait("sync", p.sems["tensor"])
    dv = p.dma(pswap.ap(), sb["c_pswapb"][:, :])
    p.wait("vector", p.sem_dma, dv)

    ngroups = 1 if DBG_STOP in ("qkv", "stab", "heads", "g0") else 2
    for g in range(ngroups):
        gmark_f, gmark_b = fa.mark(), ba.mark()
        qg = ba.alloc(4 * T, "qg")
        kg = ba.alloc(4 * T, "kg")
        vtok = ba.alloc(16 * 512, "vtok")
        ccb = fa.alloc(T, "cc"); ssb = fa.alloc(T, "ss")
        p.wait("sync", p.sems["vector"])
        p.wait("sync", p.sems["scalar"])
        p.wait("sync", p.sems["tensor"])
        p.dma(ccb.ap(), sb["c_ropecc"][:, :])
        p.dma(ssb.ap(), sb["c_ropess"][:, :])
        xsl = ba.alloc(8 * 512, "xsl")

        def evac_qkv(mi, c0, c1, pband, tv, g=g):
            w = c1 - c0
            if mi < 16:
                ch = (mi - 4 * g) if mi < 8 else (mi - 8 - 4 * g + 4)
                tb = qg if ch < 4 else kg
                chl = ch % 4
                tgt = tb.ap(0, 128, chl * T + c0, chl * T + c1)
                p.wait("vector", p.sems["tensor"], tv)
                p.v(lambda e, pband=pband, w=w, tgt=tgt, mi=mi: e.tensor_scalar_add(
                    tgt, pband[:, 0:w], p.scol("l0_bqkv", mi)), inc=True)
                p.wait("tensor", p.sems["vector"])
                pb2 = p.ps[2]
                p.t(lambda e, tgt=tgt, w=w, pb2=pb2: e.matmul(pb2[:, 0:w], pswap.ap(), tgt,
                                                              start=True, stop=True))
                tv2 = p.pe_done()
                p.wait("vector", p.sems["tensor"], tv2)
                t1 = fa.alloc(512, "ropet")
                p.v(lambda e, pb2=pb2, w=w, c0=c0, c1=c1, t1=t1: e.tensor_mul(
                    t1.ap(0, 128, 0, w), pb2[:, 0:w], ssb.ap(0, 128, c0, c1)))
                p.v(lambda e, tgt=tgt, c0=c0, c1=c1: e.tensor_mul(tgt, tgt, ccb.ap(0, 128, c0, c1)))
                p.v(lambda e, tgt=tgt, w=w, t1=t1: e.tensor_add(tgt, tgt, t1.ap(0, 128, 0, w)), inc=True)
                fa.reset(t1.off)
            else:
                vb = ba.alloc(512, "vb")
                p.wait("vector", p.sems["tensor"], tv)
                p.v(lambda e, pband=pband, w=w, vb=vb, mi=mi: e.tensor_scalar_add(
                    vb.ap(0, 128, 0, w), pband[:, 0:w], p.scol("l0_bqkv", mi)), inc=True)
                p.wait("tensor", p.sems["vector"])
                mv = mi - 16 - 4 * g
                for s0 in range(0, w, 128):
                    kt = (c0 + s0) // 128
                    p.wait("tensor", p.sems["vector"])
                    p.t(lambda e, vb=vb, s0=s0: e.transpose(p.psb[:, 0:128], vb.ap(0, 128, s0, s0 + 128),
                                                            sb["sb_identb"][:]))
                    tv3 = p.pe_done()
                    p.wait("vector", p.sems["tensor"], tv3)
                    for hh in range(2):
                        h = 2 * mv + hh
                        p.v(lambda e, kt=kt, h=h, hh=hh: e.tensor_copy(
                            vtok.ap(0, 128, kt * 512 + h * 64, kt * 512 + h * 64 + 64),
                            p.psb[:, hh * 64:hh * 64 + 64]), inc=True)
                ba.reset(vb.off)

        mlist = ([4 * g + j for j in range(4)] + [8 + 4 * g + j for j in range(4)]
                 + [16 + 4 * g + j for j in range(4)])
        for c0 in range(0, T, 512):
            p.wait("sync", p.sems["tensor"])
            dvx = load_xn_slice(p, xn_dram, xsl, c0, c0 + 512)
            p.wait("tensor", p.sem_dma, dvx)
            emit_linear(p, "l0_wqkv", 24, D,
                        lambda k, cc0, cc1, c0=c0: xsl.ap(0, 128, k * 512 + (cc0 - c0), k * 512 + (cc1 - c0)),
                        c0, c0 + 512, evac_qkv, m_list=mlist, nbanks=2)
        fa.reset(ccb.off)
        ba.reset(xsl.off)
        if DBG_STOP == "qkv":
            fa.reset(famark); ba.reset(bamark); return
        if DBG_DUMP and g == 0:
            p.wait("sync", p.sems["vector"])
            p.wait("sync", p.sems["tensor"])
            dbg = p.sb["dbg"]
            for cch in range(4):
                p.dma(dbg[0 + cch * 128:(cch + 1) * 128, :], qg.ap(0, 128, cch * T, (cch + 1) * T))
                p.dma(dbg[512 + cch * 128:512 + (cch + 1) * 128, :], kg.ap(0, 128, cch * T, (cch + 1) * T))
            for kt in range(4):
                p.dma(dbg[1024 + kt * 128:1024 + (kt + 1) * 128, 0:512],
                      vtok.ap(0, 128, kt * 512, (kt + 1) * 512))

        nrm = fa.alloc(8, "nrm")
        sqt = fa.alloc(T, "sqt")
        first = [True, True]
        for ch in range(8):
            which = 0 if ch < 4 else 1
            tb = qg if ch < 4 else kg
            p.wait("scalar", p.sems["vector"])
            p.s(lambda e, tb=tb, ch=ch: e.activation(sqt.ap(), tb.ap(0, 128, (ch % 4) * T, (ch % 4 + 1) * T),
                                                     AF.Square), inc=True)
            p.wait("tensor", p.sems["scalar"])
            p.wait("tensor", p.sems["vector"])
            for c0 in range(0, T, 512):
                pb = p.ps[6]
                p.wait("tensor", p.sems["vector"])
                for half in range(2):
                    p.t(lambda e, c0=c0, pb=pb, half=half: e.matmul(
                        pb[half:half + 1, 0:512] if False else pb[0:2, half * 256:half * 256 + 512][0:1, 0:512] if False else pb[0:1, 0:512] if half == 0 else pb[0:1, 0:512],
                        p.small("ones64bd")[:, half:half + 1],
                        sqt.ap(0, 128, c0, c0 + 512), start=True, stop=True) if False else e.matmul(
                        (p.ps[5] if half else p.ps[6])[0:1, 0:512],
                        p.small("ones64bd")[:, half:half + 1],
                        sqt.ap(0, 128, c0, c0 + 512), start=True, stop=True))
                tv = p.pe_done()
                p.wait("vector", p.sems["tensor"], tv)
                for half in range(2):
                    p.vtiny(lambda e, half=half: e.reduce_max(nrm.ap(0, 1, 2, 3),
                                                              (p.ps[5] if half else p.ps[6])[0:1, 0:512],
                                                              axis=AX.X))
                    w = which
                    if first[w]:
                        p.vtiny(lambda e, w=w: e.tensor_copy(nrm.ap(0, 1, w, w + 1), nrm.ap(0, 1, 2, 3)),
                                inc=True)
                        first[w] = False
                    else:
                        p.vtiny(lambda e, w=w: e.tensor_max(nrm.ap(0, 1, w, w + 1),
                                                            nrm.ap(0, 1, w, w + 1),
                                                            nrm.ap(0, 1, 2, 3)), inc=True)
        p.vtiny(lambda e: e.tensor_mul(nrm.ap(0, 1, 5, 6), nrm.ap(0, 1, 0, 1), nrm.ap(0, 1, 1, 2)),
                inc=True)
        p.wait("scalar", p.sems["vector"])
        p.stiny(lambda e: e.activation(nrm.ap(0, 1, 6, 7), nrm.ap(0, 1, 5, 6), AF.Sqrt,
                                       scale=float(1.0 / 64.0)), inc=True)
        cbias = fa.alloc(2, "cbias")
        p.wait("vector", p.sems["scalar"])
        bcast_col(p, nrm.ap(0, 1, 6, 7), cbias.ap(0, 128, 0, 1), scale=-1.0)
        fa.reset(sqt.off)
        if DBG_STOP == "stab":
            fa.reset(famark); ba.reset(bamark); return

        ytil = ba.alloc(4 * T, "ytil")
        pbuf = ba.alloc(2 * 512, "pbuf")
        osc = fa.alloc(512, "osc")
        rrow = fa.alloc(512, "rrow")
        for h in range(8):
            ch = h // 2
            r0 = (h % 2) * 64
            for nq in range(4):
                c0 = nq * 512
                pso = p.ps[3]
                psm = p.ps[5]
                for kt in range(16):
                    psS = p.ps[kt % 2]
                    if kt >= 2:
                        p.wait("tensor", p.sems["scalar"])
                    p.t(lambda e, psS=psS, ch=ch, r0=r0, kt=kt, c0=c0: e.matmul(
                        psS[:, 0:512],
                        kg.ap(r0, r0 + 64, ch * T + kt * 128, ch * T + kt * 128 + 128),
                        qg.ap(r0, r0 + 64, ch * T + c0, ch * T + c0 + 512), start=True, stop=True))
                    tv = p.pe_done()
                    p.wait("scalar", p.sems["tensor"], tv)
                    pslot = pbuf.ap(0, 128, (kt % 2) * 512, (kt % 2) * 512 + 512)
                    p.s(lambda e, psS=psS, pslot=pslot: e.activation(
                        pslot, psS[:, 0:512], AF.Exp, bias=cbias.ap(0, 128, 0, 1), scale=0.125),
                        inc=True)
                    p.wait("tensor", p.sems["scalar"])
                    p.t(lambda e, pso=pso, kt=kt, h=h, r0=r0, pslot=pslot: e.matmul(
                        pso[r0:r0 + 64, 0:512], vtok.ap(0, 128, kt * 512 + h * 64, kt * 512 + (h + 1) * 64),
                        pslot, start=(kt == 0), stop=(kt == 15)))
                    p.t(lambda e, psm=psm, kt=kt, pslot=pslot: e.matmul(
                        psm[0:1, 0:512], p.onesb.ap(0, 128, 0, 1),
                        pslot, start=(kt == 0), stop=(kt == 15)))
                tv = p.pe_done()
                p.wait("vector", p.sems["tensor"], tv)
                p.v(lambda e, psm=psm: e.tensor_copy(rrow.ap(0, 1, 0, 512), psm[0:1, 0:512]))
                p.v(lambda e: e.reciprocal(rrow.ap(0, 1, 0, 512), rrow.ap(0, 1, 0, 512)), inc=True)
                p.wait("scalar", p.sems["vector"])
                p.s(lambda e, pso=pso, r0=r0: e.activation(osc.ap(r0, r0 + 64, 0, 512),
                                                           pso[r0:r0 + 64, 0:512], AF.Copy),
                    inc=True)
                pbB = p.ps[4]
                p.wait("tensor", p.sems["vector"])
                p.t(lambda e, pbB=pbB, r0=r0: e.matmul(pbB[r0:r0 + 64, 0:512], sb["sb_ones"][0:1, 0:64],
                                                       rrow.ap(0, 1, 0, 512), start=True, stop=True))
                tv = p.pe_done()
                p.wait("vector", p.sems["tensor"], tv)
                p.wait("vector", p.sems["scalar"])
                p.v(lambda e, pbB=pbB, h=h, r0=r0, c0=c0: e.tensor_mul(
                    ytil.ap(r0, r0 + 64, (h // 2) * T + c0, (h // 2) * T + c0 + 512),
                    osc.ap(r0, r0 + 64, 0, 512), pbB[r0:r0 + 64, 0:512]), inc=True)
        if DBG_STOP == "heads":
            fa.reset(famark); ba.reset(bamark); return
        if DBG_DUMP:
            p.wait("sync", p.sems["vector"])
            base = 1536 if g == 0 else 0
            for cch in range(4):
                p.dma(p.sb["dbg"][base + cch * 128:base + (cch + 1) * 128, :],
                      ytil.ap(0, 128, cch * T, (cch + 1) * T))
        ba.reset(pbuf.off)
        def evac_wout(mi, c0, c1, pband, tv, g=g):
            p.wait("vector", p.sems["tensor"], tv)
            if g == 0:
                p.v(lambda e, mi=mi, c0=c0, c1=c1, pband=pband: e.scalar_tensor_tensor(
                    xacc.ap(0, 128, mi * T + c0, mi * T + c1), pband[:, 0:c1 - c0],
                    p.scol("l0_bout", mi), xacc.ap(0, 128, mi * T + c0, mi * T + c1),
                    ALU.add, ALU.add), inc=True)
            else:
                p.v(lambda e, mi=mi, c0=c0, c1=c1, pband=pband: e.tensor_add(
                    xacc.ap(0, 128, mi * T + c0, mi * T + c1),
                    xacc.ap(0, 128, mi * T + c0, mi * T + c1), pband[:, 0:c1 - c0]), inc=True)
        emit_linear(p, "l0_wout", 8, 512,
                    lambda k, c0, c1: ytil.ap(0, 128, k * T + c0, k * T + c1),
                    0, T, evac_wout, k0=g * 512)
        fa.reset(gmark_f)
        ba.reset(gmark_b)
    fa.reset(famark)
    ba.reset(bamark)


def emit_l0_ffn(p, xacc, scr):
    sb = p.sb
    fa, ba = p.fa, p.ba
    famark, bamark = fa.mark(), ba.mark()

    sc = fa.alloc(8, "l0sc")
    emit_mean_abs(p, "l0_w1", 32, D, 4096 * 1024, sc, 0)
    emit_mean_abs(p, "l0_w2", 8, 4096, 4096 * 1024, sc, 1)
    p.vtiny(lambda e: e.tensor_scalar_max(sc.ap(0, 1, 0, 2), sc.ap(0, 1, 0, 2), 1e-5))
    p.vtiny(lambda e: e.tensor_scalar_mul(sc.ap(0, 1, 2, 4), sc.ap(0, 1, 0, 2), 0.5), inc=True)
    thr = fa.alloc(4, "l0thr")
    bcast_col(p, sc.ap(0, 1, 2, 3), thr.ap(0, 128, 0, 1))
    bcast_col(p, sc.ap(0, 1, 2, 3), thr.ap(0, 128, 1, 2), scale=-1.0)
    bcast_col(p, sc.ap(0, 1, 3, 4), thr.ap(0, 128, 2, 3))
    bcast_col(p, sc.ap(0, 1, 3, 4), thr.ap(0, 128, 3, 4), scale=-1.0)

    for wname, nm, K, ti, qname in (("l0_w1", 32, D, 0, "w1q"), ("l0_w2", 8, 4096, 1, "w2q")):
        nk = (K + 127) // 128
        stage = fa.alloc(nk * 128, "qst")
        wq = ba.alloc(nk * 128, "qwq")
        for mi in range(nm):
            p.wait("sync", p.sems["vector"])
            dvv = p.dma(stage.ap().rearrange("p (k c) -> p k c", c=128), sb[wname][mi].rearrange("(k p) c -> p k c", p=128))
            p.wait("vector", p.sem_dma, dvv)
            p.v(lambda e, ti=ti: e.tensor_single_scalar(wq.ap(), stage.ap(),
                                                        thr.ap(0, 128, 2 * ti + 1, 2 * ti + 2),
                                                        ALU.is_lt))
            p.v(lambda e, ti=ti: e.scalar_tensor_tensor(wq.ap(), stage.ap(),
                                                        thr.ap(0, 128, 2 * ti, 2 * ti + 1),
                                                        wq.ap(), ALU.is_gt, ALU.subtract), inc=True)
            p.wait("sync", p.sems["vector"])
            p.dma(scr[qname][mi].rearrange("(k p) c -> p k c", p=128), wq.ap().rearrange("p (k c) -> p k c", c=128))
        fa.reset(stage.off)
        ba.reset(wq.off)

    w1 = fa.alloc(T, "w1")
    w2 = fa.alloc(T, "w2")
    w3 = fa.alloc(T, "w3")
    sxB = fa.alloc(T, "sxB")
    hsB = fa.alloc(T, "hsB")
    mrow = fa.alloc(T, "mrow")
    sfb = Buf(w1.t, w1.off, 512, F32)
    it = p.ia.alloc(512, "l0i")

    def make_rows(inv_col):
        p.v(lambda e: e.tensor_scalar_max(mrow.ap(0, 1, 0, T), mrow.ap(0, 1, 0, T), 1e-5), inc=True)
        p.v(lambda e: e.reciprocal(w1.ap(0, 1, 0, T), mrow.ap(0, 1, 0, T)))
        p.v(lambda e: e.tensor_scalar_mul(w1.ap(0, 1, 0, T), w1.ap(0, 1, 0, T), 127.0))
        p.v(lambda e: e.tensor_scalar(w2.ap(0, 1, 0, T), mrow.ap(0, 1, 0, T), inv_col,
                                      float(1.0 / 127.0), ALU.mult, ALU.mult), inc=True)
        bcast_row(p, lambda c0, c1: w1.ap(0, 1, c0, c1), T, sxB)
        bcast_row(p, lambda c0, c1: w2.ap(0, 1, c0, c1), T, hsB)

    def wrow(c0, w, pb):
        p.v(lambda e, pb=pb, c0=c0, w=w: e.tensor_copy(mrow.ap(0, 1, c0, c0 + w), pb[0:1, 0:w]), inc=True)

    hb = ba.alloc(T, "hb")
    xq = ba.alloc(8 * T, "xq")
    for r in range(NRC):
        p.wait("sync", p.sems["scalar"])
        p.wait("sync", p.sems["vector"])
        dvx = p.dma(w1.ap(), scr["xnf"][r * 128:(r + 1) * 128, :])
        p.wait("scalar", p.sem_dma, dvx)
        p.s(lambda e: e.activation(hsB.ap(), w1.ap(), AF.Abs), inc=True)
        p.wait("vector", p.sems["scalar"])
        if r == 0:
            p.v(lambda e: e.tensor_copy(w3.ap(), hsB.ap()), inc=True)
        else:
            p.v(lambda e: e.tensor_max(w3.ap(), w3.ap(), hsB.ap()), inc=True)
    part_max_to_row(p, w3, T, wrow, scr=w2)
    make_rows(sc.ap(0, 1, 0, 1))
    for r in range(NRC):
        p.wait("sync", p.sems["vector"])
        dvx = p.dma(w1.ap(), scr["xnf"][r * 128:(r + 1) * 128, :])
        p.wait("vector", p.sem_dma, dvx)
        for qc in range(0, T, 512):
            quant_tile(p, w1.ap(0, 128, qc, qc + 512), sxB.ap(0, 128, qc, qc + 512),
                       xq.ap(0, 128, r * T + qc, r * T + qc + 512),
                       w2.ap(0, 128, qc, qc + 512), it.ap(0, 128, 0, 512))
    if DBG_DUMP:
        p.wait("sync", p.sems["vector"])
        for r in range(NRC):
            p.dma(p.sb["dbg"][r * 128:(r + 1) * 128, :], xq.ap(0, 128, r * T, (r + 1) * T))


    for mi in range(32):
        def evac_h(mi2, c0, c1, pband, tv):
            p.wait("vector", p.sems["tensor"], tv)
            p.wait("vector", p.sem_dma)
            p.v(lambda e, c0=c0, c1=c1, pband=pband: e.tensor_mul(
                hb.ap(0, 128, c0, c1), pband[:, 0:c1 - c0], hsB.ap(0, 128, c0, c1)), inc=True)
        emit_linear(p, None, 32, D, lambda k, c0, c1: xq.ap(0, 128, k * T + c0, k * T + c1),
                    0, T, evac_h, m_list=[mi], wq_bf16_dram=scr["w1q"])
        p.v(lambda e, mi=mi: e.tensor_single_scalar(w1.ap(), hb.ap(), p.scol("l0_convw1", mi), ALU.mult))
        p.v(lambda e, mi=mi: e.scalar_tensor_tensor(w1.ap(0, 128, 1, T), hb.ap(0, 128, 0, T - 1),
                                                    p.scol("l0_convw0", mi), w1.ap(0, 128, 1, T),
                                                    ALU.mult, ALU.add))
        p.v(lambda e, mi=mi: e.scalar_tensor_tensor(w1.ap(0, 128, 0, T - 1), hb.ap(0, 128, 1, T),
                                                    p.scol("l0_convw2", mi), w1.ap(0, 128, 0, T - 1),
                                                    ALU.mult, ALU.add), inc=True)
        p.wait("scalar", p.sems["vector"])
        p.s(lambda e, mi=mi: e.activation(hb.ap(), w1.ap(), AF.Silu, bias=p.scol("l0_convb", mi)))
        p.s(lambda e: e.activation(w2.ap(), hb.ap(), AF.Abs), inc=True)
        p.wait("vector", p.sems["scalar"])
        if mi == 0:
            p.v(lambda e: e.tensor_copy(w3.ap(), w2.ap()), inc=True)
        else:
            p.v(lambda e: e.tensor_max(w3.ap(), w3.ap(), w2.ap()), inc=True)
        p.wait("sync", p.sems["scalar"])
        p.dma(scr["s"][mi * 128:(mi + 1) * 128, :], hb.ap())
    part_max_to_row(p, w3, T, wrow, scr=w2)
    make_rows(sc.ap(0, 1, 1, 2))

    ba.reset(xq.off)
    hq = ba.alloc(32 * 512, "hq")
    for nchunk in range(4):
        c0 = nchunk * 512
        c1 = c0 + 512
        for k in range(32):
            p.wait("sync", p.sems["vector"])
            dvv = p.dma(hb.ap(0, 128, 0, 512), scr["s"][k * 128:(k + 1) * 128, c0:c1])
            p.wait("vector", p.sem_dma, dvv)
            p.v(lambda e: e.tensor_copy(sfb.ap(), hb.ap(0, 128, 0, 512)), inc=True)
            quant_tile(p, sfb.ap(), sxB.ap(0, 128, c0, c1), hq.ap(0, 128, k * 512, (k + 1) * 512),
                       w2.ap(0, 128, 0, 512), it.ap(0, 128, 0, 512))

        def evac_o(mi, cc0, cc1, pband, tv, c0=c0):
            p.wait("vector", p.sems["tensor"], tv)
            p.v(lambda e, pband=pband, c0=c0: e.tensor_mul(sfb.ap(), pband[:, 0:512],
                                                           hsB.ap(0, 128, c0, c0 + 512)))
            p.v(lambda e, mi=mi, c0=c0: e.tensor_add(xacc.ap(0, 128, mi * T + c0, mi * T + c0 + 512),
                                                     xacc.ap(0, 128, mi * T + c0, mi * T + c0 + 512),
                                                     sfb.ap()), inc=True)
        emit_linear(p, None, 8, 4096, lambda k, cc0, cc1: hq.ap(0, 128, k * 512, (k + 1) * 512),
                    c0, c1, evac_o, wq_bf16_dram=scr["w2q"])
    fa.reset(famark)
    ba.reset(bamark)
    p.ia.reset(0)





def load_xn_rolled(p, xn_dram, xsl, c0, c1, shift=64):
    w = c1 - c0
    a0 = (c0 + shift) % T
    for k in range(NRC):
        if a0 + w <= T:
            p.dma(xsl.ap(0, 128, k * w, (k + 1) * w), xn_dram[k * 128:(k + 1) * 128, a0:a0 + w])
        else:
            w1 = T - a0
            p.dma(xsl.ap(0, 128, k * w, k * w + w1), xn_dram[k * 128:(k + 1) * 128, a0:T])
            p.dma(xsl.ap(0, 128, k * w + w1, (k + 1) * w), xn_dram[k * 128:(k + 1) * 128, 0:w - w1])
    return p.sem_dma.n


def emit_l2_attn(p, xacc, xn_dram):
    sb = p.sb
    fa, ba = p.fa, p.ba
    famark, bamark = fa.mark(), ba.mark()
    maskb = fa.alloc(128, "l2mask")
    p.wait("sync", p.sems["vector"])
    p.wait("sync", p.sems["scalar"])
    p.wait("sync", p.sems["tensor"])
    p.dma(maskb.ap(), sb["c_l2mask"][:, :])

    for g in range(2):
        gmark_f, gmark_b = fa.mark(), ba.mark()
        qg = ba.alloc(4 * T, "qg")
        kg = ba.alloc(4 * T, "kg")
        vtok = ba.alloc(16 * 512, "vtok")
        xsl = ba.alloc(8 * 512, "xsl")

        def evac_qkv(mi, c0, c1, pband, tv, g=g):
            w = c1 - c0
            p.wait("vector", p.sems["tensor"], tv)
            if mi < 16:
                ch = (mi - 4 * g) if mi < 8 else (mi - 8 - 4 * g + 4)
                tb = qg if ch < 4 else kg
                chl = ch % 4
                tgt = tb.ap(0, 128, chl * T + c0, chl * T + c1)
                p.v(lambda e, pband=pband, w=w, tgt=tgt, mi=mi: e.tensor_scalar_add(
                    tgt, pband[:, 0:w], p.scol("l2_bqkv", mi)), inc=True)
            else:
                vb = ba.alloc(512, "vb")
                p.v(lambda e, pband=pband, w=w, vb=vb, mi=mi: e.tensor_scalar_add(
                    vb.ap(0, 128, 0, w), pband[:, 0:w], p.scol("l2_bqkv", mi)), inc=True)
                p.wait("tensor", p.sems["vector"])
                mv = mi - 16 - 4 * g
                for s0 in range(0, w, 128):
                    kt = (c0 + s0) // 128
                    p.wait("tensor", p.sems["vector"])
                    p.t(lambda e, vb=vb, s0=s0: e.transpose(p.psb[:, 0:128], vb.ap(0, 128, s0, s0 + 128),
                                                            sb["sb_identb"][:]))
                    tv3 = p.pe_done()
                    p.wait("vector", p.sems["tensor"], tv3)
                    for hh in range(2):
                        h = 2 * mv + hh
                        p.v(lambda e, kt=kt, h=h, hh=hh: e.tensor_copy(
                            vtok.ap(0, 128, kt * 512 + h * 64, kt * 512 + h * 64 + 64),
                            p.psb[:, hh * 64:hh * 64 + 64]), inc=True)
                ba.reset(vb.off)

        mlist = ([4 * g + j for j in range(4)] + [8 + 4 * g + j for j in range(4)]
                 + [16 + 4 * g + j for j in range(4)])
        for c0 in range(0, T, 512):
            p.wait("sync", p.sems["tensor"])
            dvx = load_xn_rolled(p, xn_dram, xsl, c0, c0 + 512)
            p.wait("tensor", p.sem_dma, dvx)
            emit_linear(p, "l2_wqkv", 24, D,
                        lambda k, cc0, cc1, c0=c0: xsl.ap(0, 128, k * 512 + (cc0 - c0), k * 512 + (cc1 - c0)),
                        c0, c0 + 512, evac_qkv, m_list=mlist)
        ba.reset(xsl.off)

        nrm = fa.alloc(8, "nrm")
        sqt = fa.alloc(T, "sqt")
        first = [True, True]
        for ch in range(8):
            which = 0 if ch < 4 else 1
            tb = qg if ch < 4 else kg
            p.wait("scalar", p.sems["vector"])
            p.s(lambda e, tb=tb, ch=ch: e.activation(sqt.ap(), tb.ap(0, 128, (ch % 4) * T, (ch % 4 + 1) * T),
                                                     AF.Square), inc=True)
            p.wait("tensor", p.sems["scalar"])
            p.wait("tensor", p.sems["vector"])
            for c0 in range(0, T, 512):
                p.wait("tensor", p.sems["vector"])
                for half in range(2):
                    p.t(lambda e, c0=c0, half=half: e.matmul(
                        (p.ps[5] if half else p.ps[6])[0:1, 0:512],
                        p.small("ones64bd")[:, half:half + 1],
                        sqt.ap(0, 128, c0, c0 + 512), start=True, stop=True))
                tv = p.pe_done()
                p.wait("vector", p.sems["tensor"], tv)
                for half in range(2):
                    p.vtiny(lambda e, half=half: e.reduce_max(nrm.ap(0, 1, 2, 3),
                                                              (p.ps[5] if half else p.ps[6])[0:1, 0:512],
                                                              axis=AX.X))
                    w = which
                    if first[w]:
                        p.vtiny(lambda e, w=w: e.tensor_copy(nrm.ap(0, 1, w, w + 1), nrm.ap(0, 1, 2, 3)),
                                inc=True)
                        first[w] = False
                    else:
                        p.vtiny(lambda e, w=w: e.tensor_max(nrm.ap(0, 1, w, w + 1),
                                                            nrm.ap(0, 1, w, w + 1),
                                                            nrm.ap(0, 1, 2, 3)), inc=True)
        p.vtiny(lambda e: e.tensor_mul(nrm.ap(0, 1, 5, 6), nrm.ap(0, 1, 0, 1), nrm.ap(0, 1, 1, 2)),
                inc=True)
        p.wait("scalar", p.sems["vector"])
        p.stiny(lambda e: e.activation(nrm.ap(0, 1, 6, 7), nrm.ap(0, 1, 5, 6), AF.Sqrt,
                                       scale=float(1.0 / 64.0)), inc=True)
        cbias = fa.alloc(2, "cbias")
        p.wait("vector", p.sems["scalar"])
        bcast_col(p, nrm.ap(0, 1, 6, 7), cbias.ap(0, 128, 0, 1), scale=-1.0)
        fa.reset(sqt.off)

        ytil = ba.alloc(4 * T, "ytil")
        pbuf = ba.alloc(512, "pbuf")
        osc = fa.alloc(512, "osc")
        rrow = fa.alloc(512, "rrow")
        for h in range(8):
            ch = h // 2
            r0 = (h % 2) * 64
            for wn in range(16):
                c0 = wn * 128
                psS = p.ps[wn % 2]
                pso = p.ps[3]
                psm = p.ps[5]
                if wn >= 2:
                    p.wait("tensor", p.sems["scalar"])
                p.wait("tensor", p.sems["vector"])
                p.t(lambda e, psS=psS, ch=ch, r0=r0, c0=c0: e.matmul(
                    psS[:, 0:128],
                    kg.ap(r0, r0 + 64, ch * T + c0, ch * T + c0 + 128),
                    qg.ap(r0, r0 + 64, ch * T + c0, ch * T + c0 + 128), start=True, stop=True))
                tv = p.pe_done()
                p.wait("scalar", p.sems["tensor"], tv)
                p.s(lambda e, psS=psS: e.activation(pbuf.ap(0, 128, 0, 128), psS[:, 0:128], AF.Exp,
                                                    bias=cbias.ap(0, 128, 0, 1), scale=0.125), inc=True)
                p.wait("vector", p.sems["scalar"])
                p.v(lambda e: e.tensor_mul(pbuf.ap(0, 128, 0, 128), pbuf.ap(0, 128, 0, 128), maskb.ap()),
                    inc=True)
                p.wait("tensor", p.sems["vector"])
                p.t(lambda e, pso=pso, h=h, r0=r0, c0=c0: e.matmul(
                    pso[r0:r0 + 64, 0:128], vtok.ap(0, 128, (c0 // 128) * 512 + h * 64,
                                                    (c0 // 128) * 512 + (h + 1) * 64),
                    pbuf.ap(0, 128, 0, 128), start=True, stop=True))
                p.t(lambda e, psm=psm: e.matmul(psm[0:1, 0:128], p.onesb.ap(0, 128, 0, 1),
                                                pbuf.ap(0, 128, 0, 128), start=True, stop=True))
                tv = p.pe_done()
                p.wait("vector", p.sems["tensor"], tv)
                p.v(lambda e, psm=psm: e.tensor_copy(rrow.ap(0, 1, 0, 128), psm[0:1, 0:128]))
                p.vtiny(lambda e: e.reciprocal(rrow.ap(0, 1, 0, 128), rrow.ap(0, 1, 0, 128)), inc=True)
                p.wait("scalar", p.sems["vector"])
                p.s(lambda e, pso=pso, r0=r0: e.activation(osc.ap(r0, r0 + 64, 0, 128),
                                                           pso[r0:r0 + 64, 0:128], AF.Copy), inc=True)
                pbB = p.ps[4]
                p.wait("tensor", p.sems["vector"])
                p.t(lambda e, pbB=pbB, r0=r0: e.matmul(pbB[r0:r0 + 64, 0:128], sb["sb_ones"][0:1, 0:64],
                                                       rrow.ap(0, 1, 0, 128), start=True, stop=True))
                tv = p.pe_done()
                p.wait("vector", p.sems["tensor"], tv)
                p.wait("vector", p.sems["scalar"])
                p.v(lambda e, pbB=pbB, h=h, r0=r0, c0=c0: e.tensor_mul(
                    ytil.ap(r0, r0 + 64, (h // 2) * T + c0, (h // 2) * T + c0 + 128),
                    osc.ap(r0, r0 + 64, 0, 128), pbB[r0:r0 + 64, 0:128]), inc=True)
        ba.reset(pbuf.off)

        def evac_wo(mi, c0, c1, pband, tv, g=g):
            p.wait("vector", p.sems["tensor"], tv)
            spans = []
            a0 = (c0 + 64) % T
            w = c1 - c0
            if a0 + w <= T:
                spans.append((0, a0, w))
            else:
                spans.append((0, a0, T - a0))
                spans.append((T - a0, 0, w - (T - a0)))
            for (off, dst, ww) in spans:
                if g == 0:
                    p.v(lambda e, mi=mi, off=off, dst=dst, ww=ww, pband=pband: e.scalar_tensor_tensor(
                        xacc.ap(0, 128, mi * T + dst, mi * T + dst + ww), pband[:, off:off + ww],
                        p.scol("l2_bo", mi), xacc.ap(0, 128, mi * T + dst, mi * T + dst + ww),
                        ALU.add, ALU.add), inc=True)
                else:
                    p.v(lambda e, mi=mi, off=off, dst=dst, ww=ww, pband=pband: e.tensor_add(
                        xacc.ap(0, 128, mi * T + dst, mi * T + dst + ww),
                        xacc.ap(0, 128, mi * T + dst, mi * T + dst + ww),
                        pband[:, off:off + ww]), inc=True)
        emit_linear(p, "l2_wo", 8, 512,
                    lambda k, c0, c1: ytil.ap(0, 128, k * T + c0, k * T + c1),
                    0, T, evac_wo, k0=g * 512)
        fa.reset(gmark_f)
        ba.reset(gmark_b)
    fa.reset(famark)
    ba.reset(bamark)


def emit_l2_ffn(p, xacc, xn_dram, negmuB, rstdB, scr):
    sb = p.sb
    fa, ba = p.fa, p.ba
    famark, bamark = fa.mark(), ba.mark()

    sc = fa.alloc(8, "l2sc")
    emit_mean_abs(p, "l2_w1", 32, D, 4096 * 1024, sc, 0)
    emit_mean_abs(p, "l2_w2", 8, 4096, 4096 * 1024, sc, 1)
    p.vtiny(lambda e: e.tensor_scalar_max(sc.ap(0, 1, 0, 2), sc.ap(0, 1, 0, 2), 1e-5))
    p.vtiny(lambda e: e.tensor_scalar_mul(sc.ap(0, 1, 2, 4), sc.ap(0, 1, 0, 2), 0.5), inc=True)
    thr = fa.alloc(4, "l2thr")
    bcast_col(p, sc.ap(0, 1, 2, 3), thr.ap(0, 128, 0, 1))
    bcast_col(p, sc.ap(0, 1, 2, 3), thr.ap(0, 128, 1, 2), scale=-1.0)
    bcast_col(p, sc.ap(0, 1, 3, 4), thr.ap(0, 128, 2, 3))
    bcast_col(p, sc.ap(0, 1, 3, 4), thr.ap(0, 128, 3, 4), scale=-1.0)
    for wname, nm, K, ti, qname in (("l2_w1", 32, D, 0, "w1q"), ("l2_w2", 8, 4096, 1, "w2q")):
        nk = (K + 127) // 128
        stage = fa.alloc(nk * 128, "qst")
        wq = ba.alloc(nk * 128, "qwq")
        for mi in range(nm):
            p.wait("sync", p.sems["vector"])
            dvv = p.dma(stage.ap().rearrange("p (k c) -> p k c", c=128),
                        sb[wname][mi].rearrange("(k p) c -> p k c", p=128))
            p.wait("vector", p.sem_dma, dvv)
            p.v(lambda e, ti=ti: e.tensor_single_scalar(wq.ap(), stage.ap(),
                                                        thr.ap(0, 128, 2 * ti + 1, 2 * ti + 2),
                                                        ALU.is_lt))
            p.v(lambda e, ti=ti: e.scalar_tensor_tensor(wq.ap(), stage.ap(),
                                                        thr.ap(0, 128, 2 * ti, 2 * ti + 1),
                                                        wq.ap(), ALU.is_gt, ALU.subtract), inc=True)
            p.wait("sync", p.sems["vector"])
            p.dma(scr[qname][mi].rearrange("(k p) c -> p k c", p=128),
                  wq.ap().rearrange("p (k c) -> p k c", c=128))
        fa.reset(stage.off)
        ba.reset(wq.off)

    w1b = fa.alloc(T, "w1b")
    w2b = fa.alloc(T, "w2b")
    sxB = fa.alloc(T, "sxB")
    hsB = fa.alloc(T, "hsB")
    rows = fa.alloc(2 * T, "l2rows")
    mrow = Buf(rows.t, rows.off, T, F32)
    sfb = Buf(w1b.t, w1b.off, 512, F32)
    it = p.ia.alloc(512, "l2i")

    def make_rows(inv_col):
        p.v(lambda e: e.tensor_scalar_max(mrow.ap(0, 1, 0, T), mrow.ap(0, 1, 0, T), 1e-5), inc=True)
        p.v(lambda e: e.reciprocal(w1b.ap(0, 1, 0, T), mrow.ap(0, 1, 0, T)))
        p.v(lambda e: e.tensor_scalar_mul(w1b.ap(0, 1, 0, T), w1b.ap(0, 1, 0, T), 127.0))
        p.v(lambda e: e.tensor_scalar(w2b.ap(0, 1, 0, T), mrow.ap(0, 1, 0, T), inv_col,
                                      float(1.0 / 127.0), ALU.mult, ALU.mult), inc=True)
        bcast_row(p, lambda c0, c1: w1b.ap(0, 1, c0, c1), T, sxB)
        bcast_row(p, lambda c0, c1: w2b.ap(0, 1, c0, c1), T, hsB)

    def wrow(c0, w, pb):
        p.v(lambda e, pb=pb, c0=c0, w=w: e.tensor_copy(mrow.ap(0, 1, c0, c0 + w), pb[0:1, 0:w]), inc=True)

    def stats_rows_finish():
        tvs = p.sems["tensor"].n
        p.wait("vector", p.sems["tensor"], tvs)
        for j in range(4):
            c0 = j * 512
            p.v(lambda e, j=j, c0=c0: e.tensor_scalar_mul(rows.ap(0, 1, T + c0, T + c0 + 512),
                                                          p.ps[3 + j][0:1, 0:512], 1.0 / 4096.0))
            p.v(lambda e, j=j, c0=c0: e.tensor_mul(sfb.ap(0, 1, 0, 512), rows.ap(0, 1, c0, c0 + 512),
                                                   rows.ap(0, 1, c0, c0 + 512)))
            p.v(lambda e, j=j, c0=c0: e.tensor_sub(rows.ap(0, 1, T + c0, T + c0 + 512),
                                                   rows.ap(0, 1, T + c0, T + c0 + 512),
                                                   sfb.ap(0, 1, 0, 512)), inc=True)
            p.wait("scalar", p.sems["vector"])
            p.s(lambda e, c0=c0: e.activation(rows.ap(0, 1, T + c0, T + c0 + 512),
                                              rows.ap(0, 1, T + c0, T + c0 + 512), AF.Sqrt,
                                              bias=p.scol("k_eps5")[0:1, 0:1]), inc=True)
            p.wait("vector", p.sems["scalar"])
            p.v(lambda e, c0=c0: e.reciprocal(rows.ap(0, 1, T + c0, T + c0 + 512),
                                              rows.ap(0, 1, T + c0, T + c0 + 512)), inc=True)

    xnl = Buf(sxB.t, sxB.off, T, F32)
    for passi in range(2):
        for r in range(NRC):
            p.wait("sync", p.sems["tensor"])
            p.wait("sync", p.sems["scalar"])
            dvx = p.dma(xnl.ap(), scr["xnf"][r * 128:(r + 1) * 128, :])
            if passi == 1:
                p.wait("scalar", p.sem_dma, dvx)
                p.wait("scalar", p.sems["tensor"])
                p.s(lambda e: e.activation(w1b.ap(), xnl.ap(), AF.Square), inc=True)
                p.wait("tensor", p.sems["scalar"])
                srcb = w1b
            else:
                p.wait("tensor", p.sem_dma, dvx)
                srcb = xnl
            if r == 0:
                p.wait("tensor", p.sems["vector"])
            for j in range(4):
                p.t(lambda e, j=j, r=r, srcb=srcb: e.matmul(
                    p.ps[3 + j][0:1, 0:512], sb["sb_ones"][:, 0:1],
                    srcb.ap(0, 128, j * 512, j * 512 + 512), start=(r == 0), stop=(r == NRC - 1)))
            p.pe_done()
        tvs = p.sems["tensor"].n
        p.wait("vector", p.sems["tensor"], tvs)
        for j in range(4):
            c0 = j * 512
            if passi == 0:
                p.v(lambda e, j=j, c0=c0: e.tensor_scalar_mul(rows.ap(0, 1, c0, c0 + 512),
                                                              p.ps[3 + j][0:1, 0:512], 1.0 / D), inc=(j == 3))
            else:
                p.v(lambda e, j=j, c0=c0: e.tensor_scalar_mul(rows.ap(0, 1, T + c0, T + c0 + 512),
                                                              p.ps[3 + j][0:1, 0:512], 1.0 / D))
                p.v(lambda e, j=j, c0=c0: e.tensor_mul(sfb.ap(0, 1, 0, 512), rows.ap(0, 1, c0, c0 + 512),
                                                       rows.ap(0, 1, c0, c0 + 512)))
                p.v(lambda e, j=j, c0=c0: e.tensor_sub(rows.ap(0, 1, T + c0, T + c0 + 512),
                                                       rows.ap(0, 1, T + c0, T + c0 + 512),
                                                       sfb.ap(0, 1, 0, 512)), inc=True)
                p.wait("scalar", p.sems["vector"])
                p.s(lambda e, c0=c0: e.activation(rows.ap(0, 1, T + c0, T + c0 + 512),
                                                  rows.ap(0, 1, T + c0, T + c0 + 512), AF.Sqrt,
                                                  bias=p.scol("k_eps5")[0:1, 0:1]), inc=True)
                p.wait("vector", p.sems["scalar"])
                p.v(lambda e, c0=c0: e.reciprocal(rows.ap(0, 1, T + c0, T + c0 + 512),
                                                  rows.ap(0, 1, T + c0, T + c0 + 512)), inc=True)
    bcast_row(p, lambda c0, c1: rows.ap(0, 1, c0, c1), T, negmuB, scale=-1.0)
    bcast_row(p, lambda c0, c1: rows.ap(0, 1, T + c0, T + c1), T, rstdB)

    h1 = ba.alloc(8 * T, "h1")
    xq = h1
    for r in range(NRC):
        p.wait("sync", p.sems["vector"])
        dvx = p.dma(xnl.ap(), scr["xnf"][r * 128:(r + 1) * 128, :])
        p.wait("vector", p.sem_dma, dvx)
        p.v(lambda e: e.tensor_add(w1b.ap(), xnl.ap(), negmuB.ap()))
        p.v(lambda e: e.tensor_mul(w1b.ap(), w1b.ap(), rstdB.ap()))
        p.v(lambda e, r=r: e.tensor_scalar(h1.ap(0, 128, r * T, (r + 1) * T), w1b.ap(),
                                           p.scol("l2_ln1w", r), p.scol("l2_ln1b", r),
                                           ALU.mult, ALU.add), inc=True)
        p.wait("scalar", p.sems["vector"])
        p.s(lambda e, r=r: e.activation(hsB.ap(), h1.ap(0, 128, r * T, (r + 1) * T), AF.Abs), inc=True)
        p.wait("vector", p.sems["scalar"])
        if r == 0:
            p.v(lambda e: e.tensor_copy(w2b.ap(), hsB.ap()), inc=True)
        else:
            p.v(lambda e: e.tensor_max(w2b.ap(), w2b.ap(), hsB.ap()), inc=True)
    part_max_to_row(p, w2b, T, wrow, scr=w1b)
    make_rows(sc.ap(0, 1, 0, 1))
    for r in range(NRC):
        for qc in range(0, T, 512):
            quant_tile(p, h1.ap(0, 128, r * T + qc, r * T + qc + 512), sxB.ap(0, 128, qc, qc + 512),
                       xq.ap(0, 128, r * T + qc, r * T + qc + 512),
                       w1b.ap(0, 128, qc, qc + 512), it.ap(0, 128, 0, 512))

    hb = ba.alloc(T, "hb2")
    for mi in range(32):
        def evac_h(mi2, c0, c1, pband, tv, mi=mi):
            p.wait("vector", p.sems["tensor"], tv)
            p.v(lambda e, c0=c0, c1=c1, pband=pband: e.tensor_mul(
                w1b.ap(0, 128, c0, c1), pband[:, 0:c1 - c0], hsB.ap(0, 128, c0, c1)))
            p.v(lambda e, c0=c0, c1=c1, mi=mi: e.tensor_scalar_add(
                w1b.ap(0, 128, c0, c1), w1b.ap(0, 128, c0, c1), p.scol("l2_b1", mi)), inc=True)
        emit_linear(p, None, 32, D, lambda k, c0, c1: xq.ap(0, 128, k * T + c0, k * T + c1),
                    0, T, evac_h, m_list=[mi], wq_bf16_dram=scr["w1q"], nbanks=2)
        p.wait("scalar", p.sems["vector"])
        p.wait("scalar", p.sem_dma)
        p.s(lambda e: e.activation(hb.ap(), w1b.ap(), AF.Gelu), inc=True)
        p.wait("sync", p.sems["scalar"])
        p.dma(scr["s"][mi * 128:(mi + 1) * 128, :], hb.ap())
        p.wait("tensor", p.sems["scalar"])
        if mi == 0:
            p.wait("tensor", p.sems["vector"])
        for j in range(4):
            p.t(lambda e, j=j, mi=mi: e.matmul(p.ps[3 + j][0:1, 0:512], p.onesb.ap(0, 128, 0, 1),
                                               hb.ap(0, 128, j * 512, j * 512 + 512),
                                               start=(mi == 0), stop=(mi == 31)))
        p.pe_done()
    tvs = p.sems["tensor"].n
    p.wait("vector", p.sems["tensor"], tvs)
    for j in range(4):
        c0 = j * 512
        p.v(lambda e, j=j, c0=c0: e.tensor_scalar_mul(rows.ap(0, 1, c0, c0 + 512),
                                                      p.ps[3 + j][0:1, 0:512], 1.0 / 4096.0), inc=(j == 3))
    for mi in range(32):
        p.wait("sync", p.sems["tensor"])
        p.wait("sync", p.sems["scalar"])
        dvx = p.dma(hb.ap(), scr["s"][mi * 128:(mi + 1) * 128, :])
        p.wait("scalar", p.sem_dma, dvx)
        p.wait("scalar", p.sems["tensor"])
        p.s(lambda e: e.activation(w1b.ap(), hb.ap(), AF.Square), inc=True)
        p.wait("tensor", p.sems["scalar"])
        if mi == 0:
            p.wait("tensor", p.sems["vector"])
        for j in range(4):
            p.t(lambda e, j=j, mi=mi: e.matmul(p.ps[3 + j][0:1, 0:512], sb["sb_ones"][:, 0:1],
                                               w1b.ap(0, 128, j * 512, j * 512 + 512),
                                               start=(mi == 0), stop=(mi == 31)))
        p.pe_done()
    stats_rows_finish()
    bcast_row(p, lambda c0, c1: rows.ap(0, 1, c0, c1), T, negmuB, scale=-1.0)
    bcast_row(p, lambda c0, c1: rows.ap(0, 1, T + c0, T + c1), T, rstdB)

    ba.reset(bamark)
    hq = ba.alloc(32 * 512, "hq2")
    gld = ba.alloc(512, "gld")

    def h2_tile(k, c0, out_ap):
        p.v(lambda e, c0=c0: e.tensor_add(sfb.ap(0, 128, 0, 512), gld.ap(0, 128, 0, 512),
                                          negmuB.ap(0, 128, c0, c0 + 512)))
        p.v(lambda e, c0=c0: e.tensor_mul(sfb.ap(0, 128, 0, 512), sfb.ap(0, 128, 0, 512),
                                          rstdB.ap(0, 128, c0, c0 + 512)))
        p.v(lambda e, k=k, out_ap=out_ap: e.tensor_scalar(out_ap, sfb.ap(0, 128, 0, 512),
                                                          p.scol("l2_ln2w", k), p.scol("l2_ln2b", k),
                                                          ALU.mult, ALU.add), inc=True)

    for nchunk in range(4):
        c0 = nchunk * 512
        c1 = c0 + 512
        for k in range(32):
            p.wait("sync", p.sems["vector"])
            p.wait("sync", p.sems["scalar"])
            dvv = p.dma(gld.ap(0, 128, 0, 512), scr["s"][k * 128:(k + 1) * 128, c0:c1])
            p.wait("vector", p.sem_dma, dvv)
            h2_tile(k, c0, w1b.ap(0, 128, 1024, 1536))
            p.wait("scalar", p.sems["vector"])
            p.s(lambda e: e.activation(w1b.ap(0, 128, 512, 1024),
                                       w1b.ap(0, 128, 1024, 1536), AF.Abs), inc=True)
            p.wait("vector", p.sems["scalar"])
            if k == 0:
                p.v(lambda e: e.tensor_copy(w2b.ap(0, 128, 0, 512), w1b.ap(0, 128, 512, 1024)), inc=True)
            else:
                p.v(lambda e: e.tensor_max(w2b.ap(0, 128, 0, 512), w2b.ap(0, 128, 0, 512),
                                           w1b.ap(0, 128, 512, 1024)), inc=True)
        def wrow_c(cc0, w, pb, c0=c0):
            p.v(lambda e, pb=pb, cc0=cc0, w=w: e.tensor_copy(mrow.ap(0, 1, c0 + cc0, c0 + cc0 + w),
                                                             pb[0:1, 0:w]), inc=True)
        part_max_to_row(p, Buf(w2b.t, w2b.off, 512, F32), 512, wrow_c,
                        scr=Buf(w1b.t, w1b.off + 512, 512, F32))
        p.v(lambda e, c0=c0, c1=c1: e.tensor_scalar_max(mrow.ap(0, 1, c0, c1), mrow.ap(0, 1, c0, c1),
                                                        1e-5), inc=True)
        p.v(lambda e, c0=c0, c1=c1: e.reciprocal(w1b.ap(0, 1, 0, 512), mrow.ap(0, 1, c0, c1)))
        p.v(lambda e: e.tensor_scalar_mul(w1b.ap(0, 1, 0, 512), w1b.ap(0, 1, 0, 512), 127.0))
        p.v(lambda e, c0=c0, c1=c1: e.tensor_scalar(w2b.ap(0, 1, 0, 512), mrow.ap(0, 1, c0, c1),
                                                    sc.ap(0, 1, 1, 2), float(1.0 / 127.0),
                                                    ALU.mult, ALU.mult), inc=True)
        bcast_row(p, lambda cc0, cc1: w1b.ap(0, 1, cc0, cc1), 512, sxB, out_c0=c0)
        bcast_row(p, lambda cc0, cc1: w2b.ap(0, 1, cc0, cc1), 512, hsB, out_c0=c0)
        for k in range(32):
            p.wait("sync", p.sems["vector"])
            dvv = p.dma(gld.ap(0, 128, 0, 512), scr["s"][k * 128:(k + 1) * 128, c0:c1])
            p.wait("vector", p.sem_dma, dvv)
            h2_tile(k, c0, w1b.ap(0, 128, 1024, 1536))
            quant_tile(p, w1b.ap(0, 128, 1024, 1536), sxB.ap(0, 128, c0, c1),
                       hq.ap(0, 128, k * 512, (k + 1) * 512), w1b.ap(0, 128, 512, 1024),
                       it.ap(0, 128, 0, 512))
        def evac_o(mi, cc0, cc1, pband, tv, c0=c0):
            p.wait("vector", p.sems["tensor"], tv)
            p.v(lambda e, pband=pband, c0=c0: e.tensor_mul(sfb.ap(0, 128, 0, 512), pband[:, 0:512],
                                                           hsB.ap(0, 128, c0, c0 + 512)))
            p.v(lambda e, mi=mi, c0=c0: e.scalar_tensor_tensor(
                xacc.ap(0, 128, mi * T + c0, mi * T + c0 + 512), sfb.ap(0, 128, 0, 512),
                p.scol("l2_b2", mi), xacc.ap(0, 128, mi * T + c0, mi * T + c0 + 512),
                ALU.add, ALU.add), inc=True)
        emit_linear(p, None, 8, 4096, lambda k, cc0, cc1: hq.ap(0, 128, k * 512, (k + 1) * 512),
                    c0, c1, evac_o, wq_bf16_dram=scr["w2q"])
    fa.reset(famark)
    ba.reset(bamark)
    p.ia.reset(0)





def emit_l3_attn(p, xacc, xn_dram, negmuB, rstdB, scr):
    sb = p.sb
    fa, ba = p.fa, p.ba
    famark, bamark = fa.mark(), ba.mark()
    ones_t = negmuB
    tmp = None
    p.wait("vector", p.sem_dma)
    p.v(lambda e: e.memset(ones_t.ap(), 1.0), inc=True)

    qz = fa.alloc(T, "qz")
    kz = fa.alloc(T, "kz")
    kvz = fa.alloc(T, "kvz")
    kcum = fa.alloc(T, "kcum")
    kvcum = Buf(rstdB.t, rstdB.off, T, F32)
    ytl = ba.alloc(8 * T, "ytl")
    vb = ba.alloc(T, "vb")
    xsl = ba.alloc(8 * 512, "xsl")

    def elu1(buf, tmp):
        p.v(lambda e: e.tensor_scalar_min(tmp.ap(), buf.ap(), 0.0))
        p.wait("scalar", p.sems["vector"])
        p.v(lambda e: e.tensor_scalar_max(buf.ap(), buf.ap(), 0.0), inc=True)
        p.wait("scalar", p.sems["vector"])
        p.s(lambda e: e.activation(tmp.ap(), tmp.ap(), AF.Exp), inc=True)
        p.wait("vector", p.sems["scalar"])
        p.v(lambda e: e.tensor_add(buf.ap(), buf.ap(), tmp.ap()), inc=True)

    for r in range(NRC):
        def evac_qkv(mi, c0, c1, pband, tv, r=r):
            w = c1 - c0
            p.wait("vector", p.sems["tensor"], tv)
            if mi < 8:
                p.v(lambda e, pband=pband, w=w, mi=mi, c0=c0, c1=c1: e.tensor_scalar(
                    qz.ap(0, 128, c0, c1), pband[:, 0:w], p.scol("l3_bqkv", mi),
                    p.scol("l3_scale"), ALU.add, ALU.mult), inc=True)
            elif mi < 16:
                p.v(lambda e, pband=pband, w=w, mi=mi, c0=c0, c1=c1: e.tensor_scalar(
                    kz.ap(0, 128, c0, c1), pband[:, 0:w], p.scol("l3_bqkv", mi),
                    p.scol("l3_scale"), ALU.add, ALU.mult), inc=True)
            else:
                p.v(lambda e, pband=pband, w=w, mi=mi, c0=c0, c1=c1: e.tensor_scalar_add(
                    vb.ap(0, 128, c0, c1), pband[:, 0:w], p.scol("l3_bqkv", mi)), inc=True)
        for c0 in range(0, T, 512):
            p.wait("sync", p.sems["tensor"])
            dvx = load_xn_slice(p, xn_dram, xsl, c0, c0 + 512)
            p.wait("tensor", p.sem_dma, dvx)
            emit_linear(p, "l3_wqkv", 24, D,
                        lambda k, cc0, cc1, c0=c0: xsl.ap(0, 128, k * 512 + (cc0 - c0), k * 512 + (cc1 - c0)),
                        c0, c0 + 512, evac_qkv, m_list=[r, 8 + r, 16 + r])
        elu1(qz, kcum)
        elu1(kz, kcum)
        p.v(lambda e: e.tensor_mul(kvz.ap(), kz.ap(), vb.ap()), inc=True)
        p.v(lambda e: e.tensor_tensor_scan(kcum.ap(), ones_t.ap(), kz.ap(), 0.0, ALU.mult, ALU.add))
        p.v(lambda e: e.tensor_tensor_scan(kvcum.ap(), ones_t.ap(), kvz.ap(), 0.0, ALU.mult, ALU.add),
            inc=True)
        tmp = kz
        p.v(lambda e: e.tensor_mul(tmp.ap(), qz.ap(), kcum.ap()), inc=True)
        for c0 in range(0, T, 512):
            p.wait("tensor", p.sems["vector"])
            p.t(lambda e, c0=c0: e.matmul(p.ps[5][0:2, 0:512], p.small("ones64bd"),
                                          tmp.ap(0, 128, c0, c0 + 512), start=True, stop=True))
            tv = p.pe_done()
            p.wait("vector", p.sems["tensor"], tv)
            p.v(lambda e, c0=c0: e.tensor_copy(kcum.ap(0, 2, c0, c0 + 512), p.ps[5][0:2, 0:512]),
                inc=True)
        p.v(lambda e: e.tensor_scalar_max(kcum.ap(0, 2, 0, T), kcum.ap(0, 2, 0, T), 1e-3))
        p.v(lambda e: e.reciprocal(kcum.ap(0, 2, 0, T), kcum.ap(0, 2, 0, T)), inc=True)
        p.v(lambda e: e.tensor_mul(tmp.ap(), qz.ap(), kvcum.ap()), inc=True)
        for c0 in range(0, T, 512):
            p.wait("tensor", p.sems["vector"])
            p.t(lambda e, c0=c0: e.matmul(p.ps[4][:, 0:512], sb["sb_bd2"][:, :],
                                          kcum.ap(0, 2, c0, c0 + 512), start=True, stop=True))
            tv = p.pe_done()
            p.wait("vector", p.sems["tensor"], tv)
            p.v(lambda e, r=r, c0=c0: e.tensor_mul(ytl.ap(0, 128, r * T + c0, r * T + c0 + 512),
                                                   tmp.ap(0, 128, c0, c0 + 512), p.ps[4][:, 0:512]),
                inc=True)
    def evac_proj(mi, c0, c1, pband, tv):
        p.wait("vector", p.sems["tensor"], tv)
        p.v(lambda e, mi=mi, c0=c0, c1=c1, pband=pband: e.scalar_tensor_tensor(
            xacc.ap(0, 128, mi * T + c0, mi * T + c1), pband[:, 0:c1 - c0],
            p.scol("l3_bproj", mi), xacc.ap(0, 128, mi * T + c0, mi * T + c1),
            ALU.add, ALU.add), inc=True)
    emit_linear(p, "l3_wproj", 8, D, lambda k, c0, c1: ytl.ap(0, 128, k * T + c0, k * T + c1),
                0, T, evac_proj)
    fa.reset(famark)
    ba.reset(bamark)


def emit_l3_ffn(p, xacc, xn_dram, negmuB, rstdB, scr):
    sb = p.sb
    fa, ba = p.fa, p.ba
    famark, bamark = fa.mark(), ba.mark()

    lgt = fa.alloc(T, "lgt")
    selx = fa.alloc(T, "selx")
    srow0 = fa.alloc(64, "srow0")
    xsl = ba.alloc(8 * 512, "xsl")

    def evac_lgt(mi, c0, c1, pband, tv):
        p.wait("vector", p.sems["tensor"], tv)
        p.v(lambda e, c0=c0, c1=c1, pband=pband: e.tensor_copy(lgt.ap(0, 128, c0, c1),
                                                               pband[:, 0:c1 - c0]), inc=True)
    for c0 in range(0, T, 512):
        p.wait("sync", p.sems["tensor"])
        dvx = load_xn_slice(p, xn_dram, xsl, c0, c0 + 512)
        p.wait("tensor", p.sem_dma, dvx)
        emit_linear(p, "l3_wg", 1, D,
                    lambda k, cc0, cc1, c0=c0: xsl.ap(0, 128, k * 512 + (cc0 - c0), k * 512 + (cc1 - c0)),
                    c0, c0 + 512, evac_lgt)

    for t0 in range(0, T, 128):
        pb = p.ps[6]
        p.wait("tensor", p.sems["vector"])
        p.t(lambda e, t0=t0, pb=pb: e.transpose(pb[:, 0:128], lgt.ap(0, 128, t0, t0 + 128),
                                                sb["sb_ident"][:]))
        tv = p.pe_done()
        p.wait("vector", p.sems["tensor"], tv)
        p.vtiny(lambda e, pb=pb: e.tensor_copy(srow0.ap(0, 128, 0, 8), pb[:, 0:8]))
        p.vtiny(lambda e: e.reduce_max(srow0.ap(0, 128, 8, 9), srow0.ap(0, 128, 0, 8), axis=AX.X))
        p.vtiny(lambda e: e.tensor_single_scalar(srow0.ap(0, 128, 16, 24), srow0.ap(0, 128, 0, 8),
                                                 srow0.ap(0, 128, 8, 9), ALU.is_ge))
        p.vtiny(lambda e: e.tensor_tensor_scan(srow0.ap(0, 128, 24, 32), sb["sb_ones"][:, 0:8],
                                               srow0.ap(0, 128, 16, 24), 0.0, ALU.mult, ALU.add))
        p.vtiny(lambda e: e.tensor_single_scalar(srow0.ap(0, 128, 24, 32), srow0.ap(0, 128, 24, 32),
                                                 1.0, ALU.is_equal))
        p.vtiny(lambda e: e.tensor_mul(srow0.ap(0, 128, 16, 24), srow0.ap(0, 128, 16, 24),
                                       srow0.ap(0, 128, 24, 32)), inc=True)
        p.wait("tensor", p.sems["vector"])
        p.t(lambda e, pb=pb: e.transpose(pb[0:8, 0:128], srow0.ap(0, 128, 16, 24), sb["sb_ident"][:]))
        tv = p.pe_done()
        p.wait("vector", p.sems["tensor"], tv)
        p.v(lambda e, t0=t0, pb=pb: e.tensor_copy(selx.ap(0, 8, t0, t0 + 128), pb[0:8, 0:128]),
            inc=True)
    p.wait("sync", p.sems["vector"])
    dvsel = p.dma(scr["sel"][:, :], selx.ap(0, 8, 0, T))
    p.wait("vector", p.sem_dma)
    p.wait("sync", p.sem_dma, dvsel)
    fa.reset(lgt.off)
    rows = fa.alloc(2 * T, "l3rows")
    selB = fa.alloc(T, "selB")
    srow = fa.alloc(T, "srow")
    bcol = fa.alloc(4, "bcol")

    xnl = Buf(selB.t, selB.off, T, F32)
    for passi in range(2):
        for r in range(NRC):
            p.wait("sync", p.sems["tensor"])
            p.wait("sync", p.sems["scalar"])
            dvx = p.dma(xnl.ap(), scr["xnf"][r * 128:(r + 1) * 128, :])
            if passi == 1:
                p.wait("scalar", p.sem_dma, dvx)
                p.wait("scalar", p.sems["tensor"])
                p.s(lambda e: e.activation(srow.ap(), xnl.ap(), AF.Square), inc=True)
                p.wait("tensor", p.sems["scalar"])
                srcb = srow
            else:
                p.wait("tensor", p.sem_dma, dvx)
                srcb = xnl
            if r == 0:
                p.wait("tensor", p.sems["vector"])
            for j in range(4):
                p.t(lambda e, j=j, r=r, srcb=srcb: e.matmul(
                    p.ps[3 + j][0:1, 0:512], sb["sb_ones"][:, 0:1],
                    srcb.ap(0, 128, j * 512, j * 512 + 512), start=(r == 0), stop=(r == NRC - 1)))
            p.pe_done()
        tvs = p.sems["tensor"].n
        p.wait("vector", p.sems["tensor"], tvs)
        for j in range(4):
            c0 = j * 512
            if passi == 0:
                p.v(lambda e, j=j, c0=c0: e.tensor_scalar_mul(rows.ap(0, 1, c0, c0 + 512),
                                                              p.ps[3 + j][0:1, 0:512], 1.0 / D),
                    inc=(j == 3))
            else:
                p.v(lambda e, j=j, c0=c0: e.tensor_scalar_mul(rows.ap(0, 1, T + c0, T + c0 + 512),
                                                              p.ps[3 + j][0:1, 0:512], 1.0 / D))
                p.v(lambda e, j=j, c0=c0: e.tensor_mul(srow.ap(0, 1, 0, 512), rows.ap(0, 1, c0, c0 + 512),
                                                       rows.ap(0, 1, c0, c0 + 512)))
                p.v(lambda e, j=j, c0=c0: e.tensor_sub(rows.ap(0, 1, T + c0, T + c0 + 512),
                                                       rows.ap(0, 1, T + c0, T + c0 + 512),
                                                       srow.ap(0, 1, 0, 512)), inc=True)
                p.wait("scalar", p.sems["vector"])
                p.s(lambda e, c0=c0: e.activation(rows.ap(0, 1, T + c0, T + c0 + 512),
                                                  rows.ap(0, 1, T + c0, T + c0 + 512), AF.Sqrt,
                                                  bias=p.scol("k_eps5")[0:1, 0:1]), inc=True)
                p.wait("vector", p.sems["scalar"])
                p.v(lambda e, c0=c0: e.reciprocal(rows.ap(0, 1, T + c0, T + c0 + 512),
                                                  rows.ap(0, 1, T + c0, T + c0 + 512)), inc=True)
    bcast_row(p, lambda c0, c1: rows.ap(0, 1, c0, c1), T, negmuB, scale=-1.0)
    bcast_row(p, lambda c0, c1: rows.ap(0, 1, T + c0, T + c1), T, rstdB)
    xht = ba.alloc(8 * T, "xht")
    for r in range(NRC):
        p.wait("sync", p.sems["vector"])
        dvx = p.dma(xnl.ap(), scr["xnf"][r * 128:(r + 1) * 128, :])
        p.wait("vector", p.sem_dma, dvx)
        p.v(lambda e: e.tensor_add(srow.ap(), xnl.ap(), negmuB.ap()))
        p.v(lambda e, r=r: e.tensor_mul(xht.ap(0, 128, r * T, (r + 1) * T), srow.ap(), rstdB.ap()),
            inc=True)

    he = ba.alloc(2 * T, "he")
    wstg = fa.alloc(1024, "wstg")
    wbfU = ba.alloc(1024, "wbfU")
    lnbb = ba.alloc(8, "lnbb")
    for e_i in range(8):
        p.wait("sync", p.sems["vector"])
        p.wait("sync", p.sems["tensor"])
        dvs = p.dma(srow.ap(0, 1, 0, T), scr["sel"][e_i:e_i + 1, :])
        p.wait("vector", p.sem_dma, dvs)
        p.wait("tensor", p.sem_dma, dvs)
        bcast_row(p, lambda c0, c1: srow.ap(0, 1, c0, c1), T, selB)
        for mi in range(2):
            p.wait("sync", p.sems["vector"])
            p.wait("sync", p.sems["tensor"])
            dvv = p.dma(wstg.ap().rearrange("p (k c) -> p k c", c=128),
                        sb[f"l3_w1_{e_i}"][mi].rearrange("(k p) c -> p k c", p=128))
            p.wait("vector", p.sem_dma, dvv)
            p.v(lambda e: e.tensor_copy(wbfU.ap(), wstg.ap()), inc=True)
            for k in range(8):
                p.vtiny(lambda e, k=k, e_i=e_i: e.tensor_copy(lnbb.ap(0, 128, k, k + 1),
                                                              p.scol(f"l3_lnb_{e_i}", k)),
                        inc=(k == 7))
            p.wait("tensor", p.sems["vector"])
            for k in range(8):
                p.t(lambda e, k=k: e.matmul(p.ps[5][:, 0:1], wbfU.ap(0, 128, k * 128, (k + 1) * 128),
                                            lnbb.ap(0, 128, k, k + 1), start=(k == 0), stop=(k == 7)))
            tv = p.pe_done()
            p.wait("vector", p.sems["tensor"], tv)
            p.vtiny(lambda e, mi=mi, e_i=e_i: e.scalar_tensor_tensor(
                bcol.ap(0, 128, mi, mi + 1), p.ps[5][:, 0:1], 1.0,
                p.scol(f"l3_b1_{e_i}", mi), ALU.mult, ALU.add), inc=True)
            for k in range(8):
                p.v(lambda e, k=k, e_i=e_i: e.tensor_scalar_mul(
                    wbfU.ap(0, 128, k * 128, (k + 1) * 128), wbfU.ap(0, 128, k * 128, (k + 1) * 128),
                    p.scol(f"l3_lns_{e_i}", k)), inc=(k == 7))
            p.wait("tensor", p.sems["vector"])
            for ci in range(4):
                c0 = ci * 512
                pband = p.ps[ci % 2]
                p.wait("tensor", p.sems["scalar"])
                for k in range(8):
                    p.t(lambda e, pband=pband, k=k, c0=c0: e.matmul(
                        pband[:, 0:512], wbfU.ap(0, 128, k * 128, (k + 1) * 128),
                        xht.ap(0, 128, k * T + c0, k * T + c0 + 512), start=(k == 0), stop=(k == 7)))
                tv = p.pe_done()
                p.wait("scalar", p.sems["tensor"], tv)
                p.wait("scalar", p.sems["vector"])
                p.s(lambda e, mi=mi, c0=c0, pband=pband: e.activation(
                    he.ap(0, 128, mi * T + c0, mi * T + c0 + 512), pband[:, 0:512], AF.Gelu,
                    bias=bcol.ap(0, 128, mi, mi + 1)), inc=True)
        def evac_o(mi, c0, c1, pband, tv, e_i=e_i):
            p.wait("vector", p.sems["tensor"], tv)
            p.v(lambda e, mi=mi, c0=c0, c1=c1, pband=pband, e_i=e_i: e.tensor_scalar_add(
                srow.ap(0, 128, 0, 512), pband[:, 0:c1 - c0], p.scol(f"l3_b2_{e_i}", mi)))
            p.v(lambda e, c0=c0, c1=c1: e.tensor_mul(srow.ap(0, 128, 0, 512), srow.ap(0, 128, 0, 512),
                                                     selB.ap(0, 128, c0, c1)))
            p.v(lambda e, mi=mi, c0=c0, c1=c1: e.tensor_add(
                xacc.ap(0, 128, mi * T + c0, mi * T + c1),
                xacc.ap(0, 128, mi * T + c0, mi * T + c1), srow.ap(0, 128, 0, 512)), inc=True)
        emit_linear(p, f"l3_w2_{e_i}", 8, 256,
                    lambda k, c0, c1: he.ap(0, 128, k * T + c0, k * T + c1),
                    0, T, evac_o)
    fa.reset(famark)
    ba.reset(bamark)


def emit_l1_attn(p, xacc, xn_dram, negmuB, rstdB, scr):
    sb = p.sb
    fa, ba = p.fa, p.ba
    famark, bamark = fa.mark(), ba.mark()
    C = 32
    BCL = 43.0

    sdT = ba.alloc(4 * T, "sdT")
    aubuf = ba.alloc(T, "aubuf")
    stg = [ba.alloc(T, "stgA"), ba.alloc(T, "stgB")]
    hb = ba.alloc(NRC * T, "hb")
    spikeT = Buf(ba.t, hb.off, 4 * T, BF16)

    qtmp = Buf(negmuB.t, negmuB.off, T, F32)
    bcB = Buf(rstdB.t, rstdB.off, T, F32)
    rows_t = fa.alloc(T, "l1rows")
    sq_t = fa.alloc(T, "l1sq")

    for r in range(NRC):
        p.wait("sync", p.sems["scalar"])
        dvx = p.dma(qtmp.ap(), scr["xnf"][r * 128:(r + 1) * 128, :])
        p.wait("scalar", p.sem_dma, dvx)
        p.wait("scalar", p.sems["tensor"])
        p.s(lambda e: e.activation(sq_t.ap(), qtmp.ap(), AF.Square), inc=True)
        p.wait("tensor", p.sems["scalar"])
        if r == 0:
            p.wait("tensor", p.sems["vector"])
        for j in range(4):
            p.t(lambda e, j=j, r=r: e.matmul(p.ps[3 + j][0:1, 0:512], sb["sb_ones"][:, 0:1],
                                             sq_t.ap(0, 128, j * 512, (j + 1) * 512),
                                             start=(r == 0), stop=(r == NRC - 1)))
        p.pe_done()
    tvs = p.sems["tensor"].n
    p.wait("vector", p.sems["tensor"], tvs)
    for j in range(4):
        p.v(lambda e, j=j: e.tensor_copy(rows_t.ap(0, 1, j * 512, (j + 1) * 512),
                                         p.ps[3 + j][0:1, 0:512]), inc=(j == 3))
    p.wait("scalar", p.sems["vector"])
    p.s(lambda e: e.activation(rows_t.ap(0, 1, 0, T), rows_t.ap(0, 1, 0, T), AF.Sqrt,
                               scale=float(1.0 / D), bias=p.scol("k_eps6")[0:1, 0:1]), inc=True)
    p.wait("vector", p.sems["scalar"])
    p.v(lambda e: e.reciprocal(rows_t.ap(0, 1, 0, T), rows_t.ap(0, 1, 0, T)), inc=True)
    bcast_row(p, lambda c0, c1: rows_t.ap(0, 1, c0, c1), T, bcB)
    for r in range(NRC):
        p.wait("sync", p.sems["vector"])
        dvx = p.dma(qtmp.ap(), scr["xnf"][r * 128:(r + 1) * 128, :])
        p.wait("vector", p.sem_dma, dvx)
        p.wait("vector", p.sems["scalar"])
        p.v(lambda e, r=r: e.scalar_tensor_tensor(sq_t.ap(), qtmp.ap(), p.scol("l1_wnorm", r),
                                                  bcB.ap(), ALU.mult, ALU.mult), inc=True)
        p.v(lambda e, r=r: e.tensor_scalar(rows_t.ap(), sq_t.ap(), p.scol("l1_convw3", r),
                                           p.scol("l1_convb", r), ALU.mult, ALU.add))
        for j in range(3):
            s_ = 3 - j
            p.v(lambda e, j=j, s_=s_, r=r: e.scalar_tensor_tensor(
                rows_t.ap(0, 128, s_, T), sq_t.ap(0, 128, 0, T - s_), p.scol(f"l1_convw{j}", r),
                rows_t.ap(0, 128, s_, T), ALU.mult, ALU.add), inc=(j == 2))
        p.wait("scalar", p.sems["vector"])
        p.s(lambda e, r=r: e.activation(hb.ap(0, 128, r * T, (r + 1) * T), rows_t.ap(),
                                        AF.Silu), inc=True)
        p.wait("vector", p.sems["scalar"])
    if DBG_L1 == "h":
        p.wait("sync", p.sems["scalar"])
        for r in range(NRC):
            p.dma(sb["dbg"][r * 128:(r + 1) * 128, :], hb.ap(0, 128, r * T, (r + 1) * T))
        p.wait("sync", p.sem_dma, p.sem_dma.n)
        fa.reset(famark); ba.reset(bamark)
        return

    stg_dv = [0, 0]

    def finish_qk(mi):
        p.wait("scalar", p.sems["vector"])
        p.s(lambda e: e.activation(sq_t.ap(), qtmp.ap(), AF.Square), inc=True)
        p.wait("tensor", p.sems["scalar"])
        for c0 in range(0, T, 512):
            p.wait("tensor", p.sems["vector"])
            p.t(lambda e, c0=c0: e.matmul(p.ps[5][0:2, 0:512], p.small("ones64bd"),
                                          sq_t.ap(0, 128, c0, c0 + 512), start=True, stop=True))
            tv2 = p.pe_done()
            p.wait("vector", p.sems["tensor"], tv2)
            p.v(lambda e, c0=c0: e.tensor_copy(rows_t.ap(0, 2, c0, c0 + 512), p.ps[5][0:2, 0:512]),
                inc=True)
        p.wait("scalar", p.sems["vector"])
        p.s(lambda e: e.activation(rows_t.ap(0, 2, 0, T), rows_t.ap(0, 2, 0, T), AF.Sqrt),
            inc=True)
        p.wait("vector", p.sems["scalar"])
        p.v(lambda e: e.tensor_scalar_add(rows_t.ap(0, 2, 0, T), rows_t.ap(0, 2, 0, T), 1e-6))
        p.v(lambda e: e.reciprocal(rows_t.ap(0, 2, 0, T), rows_t.ap(0, 2, 0, T)), inc=True)
        sidx = mi % 2
        p.wait("vector", p.sem_dma, stg_dv[sidx])
        for c0 in range(0, T, 512):
            p.wait("tensor", p.sems["vector"])
            p.t(lambda e, c0=c0: e.matmul(p.ps[4][:, 0:512], sb["sb_bd2"][:, :],
                                          rows_t.ap(0, 2, c0, c0 + 512), start=True, stop=True))
            tv2 = p.pe_done()
            p.wait("vector", p.sems["tensor"], tv2)
            p.v(lambda e, c0=c0, sidx=sidx: e.tensor_mul(stg[sidx].ap(0, 128, c0, c0 + 512),
                                                         qtmp.ap(0, 128, c0, c0 + 512),
                                                         p.ps[4][:, 0:512]), inc=True)
        p.wait("sync", p.sems["vector"])
        dst = scr["q"] if mi < 4 else scr["k"]
        g4 = mi % 4
        stg_dv[sidx] = p.dma(dst[g4 * 128:(g4 + 1) * 128, :], stg[sidx].ap())

    def evac_proj(mi, c0, c1, pband, tv):
        w = c1 - c0
        if mi < 8:
            p.wait("vector", p.sems["tensor"], tv)
            if c0 == 0:
                p.wait("vector", p.sems["scalar"])
            p.v(lambda e, c0=c0, c1=c1, pband=pband, w=w: e.tensor_copy(
                qtmp.ap(0, 128, c0, c1), pband[:, 0:w]), inc=True)
            if c1 == T:
                finish_qk(mi)
        elif mi < 16:
            sidx = mi % 2
            p.wait("vector", p.sems["tensor"], tv)
            if c0 == 0:
                p.wait("vector", p.sem_dma, stg_dv[sidx])
            p.v(lambda e, c0=c0, c1=c1, pband=pband, w=w, sidx=sidx: e.tensor_copy(
                stg[sidx].ap(0, 128, c0, c1), pband[:, 0:w]), inc=True)
            if c1 == T:
                p.wait("sync", p.sems["vector"])
                stg_dv[sidx] = p.dma(scr["v"][(mi - 8) * 128:(mi - 7) * 128, :], stg[sidx].ap())
        elif mi < 20:
            p.wait("vector", p.sems["tensor"], tv)
            p.v(lambda e, mi=mi, c0=c0, c1=c1, pband=pband, w=w: e.tensor_scalar_mul(
                sdT.ap(0, 128, (mi - 16) * T + c0, (mi - 16) * T + c1), pband[:, 0:w], -2.0),
                inc=True)
        elif mi == 20:
            p.wait("scalar", p.sems["tensor"], tv)
            p.s(lambda e, c0=c0, c1=c1, pband=pband, w=w: e.activation(
                aubuf.ap(0, 128, c0, c1), pband[:, 0:w], AF.Silu,
                bias=p.scol("l1_bproj", 20)), inc=True)
        else:
            p.wait("vector", p.sems["tensor"], tv)
            p.v(lambda e, c0=c0, c1=c1, pband=pband, w=w: e.tensor_scalar_add(
                rows_t.ap(0, 128, c0, c1), pband[:, 0:w], p.scol("l1_bproj", 21)), inc=True)
            if c1 == T:
                p.wait("sync", p.sems["vector"])
                p.dma(scr["bb"][:, :], rows_t.ap(0, 8, 0, T))

    emit_linear(p, "l1_wproj", 22, D,
                lambda k, c0, c1: hb.ap(0, 128, k * T + c0, k * T + c1), 0, T, evac_proj)
    p.wait("vector", p.sem_dma)

    wst64 = fa.alloc(128, "wst64")
    wbf64 = ba.alloc(128, "wbf64")
    for g4 in range(4):
        p.wait("sync", p.sems["vector"])
        dv = p.dma(wst64.ap(0, 64, 0, 128), sb["l1_wad"][g4, 0:64])
        p.wait("vector", p.sem_dma, dv)
        p.v(lambda e: e.tensor_copy(wbf64.ap(0, 64, 0, 128), wst64.ap(0, 64, 0, 128)), inc=True)
        p.wait("tensor", p.sems["vector"])
        p.wait("tensor", p.sems["scalar"])
        for c0 in range(0, T, 512):
            p.wait("tensor", p.sems["vector"])
            p.t(lambda e, c0=c0: e.matmul(p.ps[0][:, 0:512], wbf64.ap(0, 64, 0, 128),
                                          aubuf.ap(0, 64, c0, c0 + 512), start=True, stop=True))
            tv = p.pe_done()
            p.wait("vector", p.sems["tensor"], tv)
            p.v(lambda e, c0=c0, g4=g4: e.tensor_scalar_add(sq_t.ap(0, 128, c0, c0 + 512),
                                                            p.ps[0][:, 0:512],
                                                            p.scol("l1_bad", g4)), inc=True)
        p.wait("sync", p.sems["vector"])
        dvA = p.dma(scr["ab"][g4 * 128:(g4 + 1) * 128, :], sq_t.ap())
        p.wait("vector", p.sem_dma, dvA)

    fa.reset(rows_t.off)
    nh = fa.alloc(4 * T, "nh")
    w4 = fa.alloc(4, "w4")
    nh3 = fa.t[0:128, nh.off:nh.off + 4 * T].rearrange("p (g t) -> p g t", g=4)
    sd3 = ba.t[0:128, sdT.off:sdT.off + 4 * T].rearrange("p (g t) -> p g t", g=4)
    p.vtiny(lambda e: e.memset(w4.ap(0, 128, 0, 4), 0.0))
    for t in range(T):
        p.vtiny(lambda e, t=t: e.scalar_tensor_tensor(nh3[:, :, t], w4.ap(0, 128, 0, 4), 0.9,
                                                      sd3[:, :, t], ALU.mult, ALU.add))
        p.vtiny(lambda e, t=t: e.scalar_tensor_tensor(w4.ap(0, 128, 0, 4), nh3[:, :, t], -1.0,
                                                      nh3[:, :, t], ALU.is_lt, ALU.add))
    p.wait("vector", p.sems["tensor"])
    for g4 in range(4):
        p.v(lambda e, g4=g4: e.tensor_scalar(spikeT.ap(0, 128, g4 * T, (g4 + 1) * T),
                                             nh.ap(0, 128, g4 * T, (g4 + 1) * T),
                                             -1.0, 1.0, ALU.is_lt, ALU.mult), inc=(g4 == 3))
    if DBG_L1 == "scan":
        p.wait("sync", p.sems["vector"])
        for g4 in range(4):
            p.dma(sb["dbg"][1024 + g4 * 128:1024 + (g4 + 1) * 128, :],
                  spikeT.ap(0, 128, g4 * T, (g4 + 1) * T))
        p.wait("sync", p.sem_dma, p.sem_dma.n)
        fa.reset(famark); ba.reset(bamark)
        return

    fa.reset(nh.off)
    actrow = fa.alloc(T, "actrow")
    brow = fa.alloc(T, "brow")
    betaBA = fa.alloc(T, "betaBA")
    betaBB = fa.alloc(T, "betaBB")
    Spair = fa.alloc(128, "Spair")
    SpSb = fa.alloc(128, "SpS")
    g32 = fa.alloc(32, "g32")
    argx = fa.alloc(32, "argx")
    eg = fa.alloc(32, "eg")
    emg = fa.alloc(32, "emg")
    egbA = fa.alloc(32, "egbA")
    egbB = fa.alloc(32, "egbB")
    vb64 = fa.alloc(64, "vb64")
    cols8 = fa.alloc(8, "cols8")
    bdk = fa.alloc(64, "bdk")
    bdq = fa.alloc(64, "bdq")
    bdkap = fa.alloc(64, "bdkap")
    BLb = fa.alloc(64, "BL")
    BLt = fa.alloc(64, "BLt")
    Eb = fa.alloc(64, "E")
    Etb = fa.alloc(64, "Et")
    Yb = fa.alloc(128, "Y")
    Wb = fa.alloc(128, "W")
    kaptb = fa.alloc(128, "kapt")
    bcolb = fa.alloc(2, "bcol")
    laf = bcB
    abf = qtmp

    ba.reset(sdT.off)
    qfb = ba.alloc(T, "qfb")
    kfb = ba.alloc(T, "kfb")
    vfA = ba.alloc(T, "vfA")
    vfB = ba.alloc(T, "vfB")
    ystg = [ba.alloc(T, "ystgA"), ba.alloc(T, "ystgB")]
    onesbd_b = ba.alloc(2, "onesbdb")
    bs2b = ba.alloc(2, "bs2b")
    p.wait("vector", p.sems["tensor"])
    p.vtiny(lambda e: e.tensor_copy(onesbd_b.ap(0, 128, 0, 2), p.small("ones64bd")), inc=True)

    id64 = sb["sb_ident"][0:64, 0:64]
    for p4 in range(DBG_L1_NPAIR):
        p.wait("sync", p.sems["vector"])
        p.wait("sync", p.sems["tensor"])
        p.wait("sync", p.sems["scalar"])
        p.wait("sync", p.sem_dma, p.sem_dma.n)
        p.dma(qfb.ap(), scr["q"][p4 * 128:(p4 + 1) * 128, :])
        p.dma(kfb.ap(), scr["k"][p4 * 128:(p4 + 1) * 128, :])
        p.dma(vfA.ap(), scr["v"][(2 * p4) * 128:(2 * p4 + 1) * 128, :])
        p.dma(vfB.ap(), scr["v"][(2 * p4 + 1) * 128:(2 * p4 + 2) * 128, :])
        p.dma(abf.ap(), scr["ab"][p4 * 128:(p4 + 1) * 128, :])
        dvall = p.dma(brow.ap(0, 2, 0, T), scr["bb"][2 * p4:2 * p4 + 2, :])
        p.wait("vector", p.sem_dma, dvall)
        p.wait("tensor", p.sem_dma, dvall)
        p.vtiny(lambda e, p4=p4: e.tensor_mul(bs2b.ap(0, 128, 0, 1), p.scol("l1_bspk", p4),
                                              p.scol("l1_mA")))
        p.vtiny(lambda e, p4=p4: e.tensor_mul(bs2b.ap(0, 128, 1, 2), p.scol("l1_bspk", p4),
                                              p.scol("l1_mB")), inc=True)
        p.wait("tensor", p.sems["vector"])
        for c0 in range(0, T, 512):
            p.wait("tensor", p.sems["vector"])
            p.t(lambda e, c0=c0, p4=p4: e.matmul(p.ps[5][0:2, 0:512], onesbd_b.ap(0, 128, 0, 2),
                                                 spikeT.ap(0, 128, p4 * T + c0, p4 * T + c0 + 512),
                                                 start=True, stop=True))
            p.t(lambda e, c0=c0, p4=p4: e.matmul(p.ps[6][0:2, 0:512], bs2b.ap(0, 128, 0, 2),
                                                 spikeT.ap(0, 128, p4 * T + c0, p4 * T + c0 + 512),
                                                 start=True, stop=True))
            tv = p.pe_done()
            p.wait("vector", p.sems["tensor"], tv)
            p.v(lambda e, c0=c0: e.tensor_scalar(actrow.ap(0, 2, c0, c0 + 512), p.ps[5][0:2, 0:512],
                                                 0.0, 1.0, ALU.is_gt, ALU.mult))
            p.v(lambda e, c0=c0: e.tensor_add(brow.ap(0, 2, c0, c0 + 512),
                                              brow.ap(0, 2, c0, c0 + 512), p.ps[6][0:2, 0:512]),
                inc=True)
        p.wait("scalar", p.sems["vector"])
        p.s(lambda e: e.activation(brow.ap(0, 2, 0, T), brow.ap(0, 2, 0, T), AF.Sigmoid),
            inc=True)
        p.wait("vector", p.sems["scalar"])
        p.v(lambda e: e.tensor_mul(brow.ap(0, 2, 0, T), brow.ap(0, 2, 0, T),
                                   actrow.ap(0, 2, 0, T)), inc=True)
        for c0 in range(0, T, 512):
            p.wait("tensor", p.sems["vector"])
            p.t(lambda e, c0=c0: e.matmul(p.ps[4][:, 0:512], sb["sb_bd2A"][:, :],
                                          brow.ap(0, 2, c0, c0 + 512), start=True, stop=True))
            p.t(lambda e, c0=c0: e.matmul(p.ps[5][:, 0:512], sb["sb_bd2B"][:, :],
                                          brow.ap(0, 2, c0, c0 + 512), start=True, stop=True))
            tv = p.pe_done()
            p.wait("vector", p.sems["tensor"], tv)
            p.v(lambda e, c0=c0: e.tensor_copy(betaBA.ap(0, 128, c0, c0 + 512), p.ps[4][:, 0:512]))
            p.v(lambda e, c0=c0: e.tensor_copy(betaBB.ap(0, 128, c0, c0 + 512), p.ps[5][:, 0:512]),
                inc=True)
        p.v(lambda e, p4=p4: e.scalar_tensor_tensor(laf.ap(), spikeT.ap(0, 128, p4 * T, (p4 + 1) * T),
                                                    p.scol("l1_aspk", p4), abf.ap(),
                                                    ALU.mult, ALU.add), inc=True)
        p.wait("scalar", p.sems["vector"])
        p.s(lambda e: e.activation(laf.ap(), laf.ap(), AF.Sigmoid), inc=True)
        p.wait("vector", p.sems["scalar"])
        p.v(lambda e: e.tensor_scalar_add(laf.ap(), laf.ap(), -1.0), inc=True)
        for c0 in range(0, T, 512):
            p.wait("tensor", p.sems["vector"])
            p.t(lambda e, c0=c0: e.matmul(p.ps[4][:, 0:512], sb["sb_bd2"][:, :],
                                          actrow.ap(0, 2, c0, c0 + 512), start=True, stop=True))
            tv = p.pe_done()
            p.wait("vector", p.sems["tensor"], tv)
            p.v(lambda e, c0=c0: e.tensor_mul(laf.ap(0, 128, c0, c0 + 512),
                                              laf.ap(0, 128, c0, c0 + 512), p.ps[4][:, 0:512]),
                inc=True)
        p.v(lambda e: e.tensor_scalar_add(laf.ap(), laf.ap(), 1.0), inc=True)
        p.wait("scalar", p.sems["vector"])
        p.s(lambda e: e.activation(laf.ap(), laf.ap(), AF.Ln), inc=True)
        p.wait("vector", p.sems["scalar"])
        p.v(lambda e: e.memset(Spair.ap(), 0.0), inc=True)

        for c in range(DBG_L1_NCHUNK):
            t0 = c * C
            ST = DBG_L1_STEPS
            if "g" not in ST:
                continue
            p.vtiny(lambda e, t0=t0: e.tensor_tensor_scan(g32.ap(0, 128, 0, C),
                                                          sb["sb_ones"][:, 0:C],
                                                          laf.ap(0, 128, t0, t0 + C),
                                                          0.0, ALU.mult, ALU.add))
            p.vtiny(lambda e: e.tensor_single_scalar(argx.ap(0, 128, 0, C), g32.ap(0, 128, 0, C),
                                                     g32.ap(0, 128, 15, 16), ALU.subtract))
            p.vtiny(lambda e: e.tensor_scalar(argx.ap(0, 128, 0, C), argx.ap(0, 128, 0, C),
                                              BCL, -BCL, ALU.min, ALU.max))
            p.vtiny(lambda e: e.tensor_single_scalar(cols8.ap(0, 128, 3, 4), g32.ap(0, 128, 31, 32),
                                                     g32.ap(0, 128, 15, 16), ALU.subtract),
                    inc=True)
            p.wait("scalar", p.sems["vector"])
            p.stiny(lambda e: e.activation(eg.ap(0, 128, 0, C), argx.ap(0, 128, 0, C), AF.Exp))
            p.stiny(lambda e: e.activation(emg.ap(0, 128, 0, C), argx.ap(0, 128, 0, C), AF.Exp,
                                           scale=-1.0))
            p.stiny(lambda e: e.activation(cols8.ap(0, 128, 0, 1), g32.ap(0, 128, 15, 16), AF.Exp))
            p.stiny(lambda e: e.activation(cols8.ap(0, 128, 1, 2), cols8.ap(0, 128, 3, 4), AF.Exp),
                    inc=True)
            p.wait("vector", p.sems["scalar"])
            p.vtiny(lambda e: e.tensor_mul(cols8.ap(0, 128, 2, 3), cols8.ap(0, 128, 1, 2),
                                           cols8.ap(0, 128, 0, 1)))
            p.vtiny(lambda e, t0=t0: e.tensor_mul(egbA.ap(0, 128, 0, C), eg.ap(0, 128, 0, C),
                                                  betaBA.ap(0, 128, t0, t0 + C)))
            p.vtiny(lambda e, t0=t0: e.tensor_mul(egbB.ap(0, 128, 0, C), eg.ap(0, 128, 0, C),
                                                  betaBB.ap(0, 128, t0, t0 + C)))
            p.vtiny(lambda e, t0=t0: e.tensor_mul(vb64.ap(0, 128, 0, C), vfA.ap(0, 128, t0, t0 + C),
                                                  betaBA.ap(0, 128, t0, t0 + C)))
            p.vtiny(lambda e, t0=t0: e.tensor_mul(vb64.ap(0, 128, C, 2 * C),
                                                  vfB.ap(0, 128, t0, t0 + C),
                                                  betaBB.ap(0, 128, t0, t0 + C)))
            for (dstb, srcb, egxA, egxB) in ((bdk, kfb, egbA, egbB), (bdq, qfb, eg, eg),
                                             (bdkap, kfb, emg, emg)):
                p.vtiny(lambda e, dstb=dstb, srcb=srcb, egx=egxA, t0=t0: e.scalar_tensor_tensor(
                    dstb.ap(0, 128, 0, C), srcb.ap(0, 128, t0, t0 + C), p.scol("l1_mA"),
                    egx.ap(0, 128, 0, C), ALU.mult, ALU.mult))
                p.vtiny(lambda e, dstb=dstb, srcb=srcb, egx=egxB, t0=t0: e.scalar_tensor_tensor(
                    dstb.ap(0, 128, C, 2 * C), srcb.ap(0, 128, t0, t0 + C), p.scol("l1_mB"),
                    egx.ap(0, 128, 0, C), ALU.mult, ALU.mult))
            p.v(lambda e: e.tensor_single_scalar(SpSb.ap(), Spair.ap(), cols8.ap(0, 128, 0, 1),
                                                 ALU.mult), inc=True)
            if "L" not in ST:
                continue
            p.wait("tensor", p.sems["vector"])
            p.t(lambda e: e.matmul(p.ps[0][0:64, 0:64], bdk.ap(0, 128, 0, 64),
                                   bdkap.ap(0, 128, 0, 64), start=True, stop=True))
            p.t(lambda e: e.matmul(p.ps[3][0:64, 0:128], bdk.ap(0, 128, 0, 64), SpSb.ap(),
                                   start=True, stop=True))
            p.t(lambda e: e.transpose(p.ps[3][0:64, 128:256], vb64.ap(0, 128, 0, 2 * C),
                                      sb["sb_ident"][:, :]))
            tv = p.pe_done()
            p.wait("vector", p.sems["tensor"], tv)
            p.vtiny(lambda e: e.tensor_mul(BLb.ap(0, 64, 0, 64), p.ps[0][0:64, 0:64],
                                           sb["sb_l1stril"][0:64, 0:64]), inc=True)
            p.v(lambda e: e.tensor_copy(Wb.ap(0, 64, 0, 128), p.ps[3][0:64, 0:128]))
            p.v(lambda e: e.tensor_sub(Yb.ap(0, 64, 0, 128), p.ps[3][0:64, 128:256],
                                       Wb.ap(0, 64, 0, 128)), inc=True)
            p.wait("tensor", p.sems["vector"])
            p.t(lambda e: e.transpose(p.ps[1][0:64, 0:64], BLb.ap(0, 64, 0, 64), id64))
            tv = p.pe_done()
            p.wait("vector", p.sems["tensor"], tv)
            p.vtiny(lambda e: e.tensor_copy(BLt.ap(0, 64, 0, 64), p.ps[1][0:64, 0:64]), inc=True)
            p.wait("tensor", p.sems["vector"])
            p.t(lambda e: e.matmul(p.ps[2][0:64, 0:64], BLt.ap(0, 64, 0, 64), BLb.ap(0, 64, 0, 64),
                                   start=True, stop=True))
            tv = p.pe_done()
            p.wait("vector", p.sems["tensor"], tv)
            p.vtiny(lambda e: e.tensor_copy(Eb.ap(0, 64, 0, 64), p.ps[2][0:64, 0:64]), inc=True)
            for m in range(4 if "newton" in ST else 0):
                p.wait("tensor", p.sems["vector"])
                p.t(lambda e: e.transpose(p.ps[1][0:64, 0:64], Eb.ap(0, 64, 0, 64), id64))
                tv = p.pe_done()
                p.wait("vector", p.sems["tensor"], tv)
                p.vtiny(lambda e: e.tensor_copy(Etb.ap(0, 64, 0, 64), p.ps[1][0:64, 0:64]),
                        inc=True)
                p.wait("tensor", p.sems["vector"])
                p.t(lambda e: e.matmul(p.ps[4][0:64, 0:128], Etb.ap(0, 64, 0, 64),
                                       Yb.ap(0, 64, 0, 128), start=True, stop=True))
                if m < 3:
                    p.t(lambda e: e.matmul(p.ps[2][0:64, 0:64], Etb.ap(0, 64, 0, 64),
                                           Eb.ap(0, 64, 0, 64), start=True, stop=True))
                tv = p.pe_done()
                p.wait("vector", p.sems["tensor"], tv)
                p.v(lambda e: e.tensor_add(Yb.ap(0, 64, 0, 128), Yb.ap(0, 64, 0, 128),
                                           p.ps[4][0:64, 0:128]), inc=True)
                if m < 3:
                    p.vtiny(lambda e: e.tensor_copy(Eb.ap(0, 64, 0, 64), p.ps[2][0:64, 0:64]),
                            inc=True)
            if "out" not in ST:
                continue
            p.wait("tensor", p.sems["vector"])
            p.t(lambda e: e.matmul(p.ps[4][0:64, 0:128], BLt.ap(0, 64, 0, 64),
                                   Yb.ap(0, 64, 0, 128), start=True, stop=True))
            p.t(lambda e: e.matmul(p.ps[0][0:64, 0:64], bdkap.ap(0, 128, 0, 64),
                                   bdq.ap(0, 128, 0, 64), start=True, stop=True))
            tv = p.pe_done()
            p.wait("vector", p.sems["tensor"], tv)
            p.v(lambda e: e.tensor_sub(Wb.ap(0, 64, 0, 128), Yb.ap(0, 64, 0, 128),
                                       p.ps[4][0:64, 0:128]), inc=True)
            p.vtiny(lambda e: e.tensor_mul(Etb.ap(0, 64, 0, 64), p.ps[0][0:64, 0:64],
                                           sb["sb_l1triu"][0:64, 0:64]), inc=True)
            p.wait("tensor", p.sems["vector"])
            p.t(lambda e: e.matmul(p.ps[5][0:64, 0:128], bdq.ap(0, 128, 0, 64), SpSb.ap(),
                                   start=True, stop=True))
            p.t(lambda e: e.matmul(p.ps[4][0:64, 0:128], Etb.ap(0, 64, 0, 64),
                                   Wb.ap(0, 64, 0, 128), start=True, stop=True))
            tv = p.pe_done()
            p.wait("vector", p.sems["tensor"], tv)
            p.v(lambda e: e.tensor_copy(Yb.ap(0, 64, 0, 128), p.ps[5][0:64, 0:128]))
            p.v(lambda e: e.tensor_add(Yb.ap(0, 64, 0, 128), Yb.ap(0, 64, 0, 128),
                                       p.ps[4][0:64, 0:128]), inc=True)
            p.vtiny(lambda e: e.tensor_single_scalar(bdk.ap(0, 128, 0, 64), bdkap.ap(0, 128, 0, 64),
                                                     cols8.ap(0, 128, 1, 2), ALU.mult), inc=True)
            p.wait("tensor", p.sems["vector"])
            p.t(lambda e: e.transpose(p.ps[6][0:128, 128:192], Yb.ap(0, 64, 0, 128),
                                      sb["sb_ident"][0:64, 0:64]))
            p.t(lambda e: e.transpose(p.ps[1][0:64, 0:128], bdk.ap(0, 128, 0, 64),
                                      sb["sb_ident"][:, :]))
            tv = p.pe_done()
            p.wait("vector", p.sems["tensor"], tv)
            p.vtiny(lambda e, t0=t0: e.tensor_copy(ystg[0].ap(0, 128, t0, t0 + C),
                                                   p.ps[6][:, 128:160]))
            p.vtiny(lambda e, t0=t0: e.tensor_copy(ystg[1].ap(0, 128, t0, t0 + C),
                                                   p.ps[6][:, 160:192]))
            p.v(lambda e: e.tensor_copy(kaptb.ap(0, 64, 0, 128), p.ps[1][0:64, 0:128]), inc=True)
            if "state" not in ST:
                continue
            p.wait("tensor", p.sems["vector"])
            p.t(lambda e: e.matmul(p.ps[6][0:128, 0:128], kaptb.ap(0, 64, 0, 128),
                                   Wb.ap(0, 64, 0, 128), start=True, stop=True))
            tv = p.pe_done()
            p.wait("vector", p.sems["tensor"], tv)
            p.v(lambda e: e.scalar_tensor_tensor(Spair.ap(), Spair.ap(), cols8.ap(0, 128, 2, 3),
                                                 p.ps[6][0:128, 0:128], ALU.mult, ALU.add),
                inc=True)
        p.wait("sync", p.sems["vector"])
        p.dma(scr["y"][(2 * p4) * 128:(2 * p4 + 1) * 128, :], ystg[0].ap())
        p.dma(scr["y"][(2 * p4 + 1) * 128:(2 * p4 + 2) * 128, :], ystg[1].ap())
    p.wait("vector", p.sem_dma)
    p.wait("sync", p.sem_dma, p.sem_dma.n)
    if DBG_L1 == "delta":
        fa.reset(famark); ba.reset(bamark)
        return

    fa.reset(nh.off)
    sqt2 = fa.alloc(T, "sqt2")
    rowt2 = fa.alloc(T, "rowt2")
    ba.reset(sdT.off)
    ytl = ba.alloc(8 * T, "ytl")
    ybuf = ba.alloc(T, "ybuf")
    for hh in range(8):
        p.wait("sync", p.sems["vector"])
        p.wait("sync", p.sems["scalar"])
        p.wait("sync", p.sems["tensor"])
        dvy = p.dma(ybuf.ap(), scr["y"][hh * 128:(hh + 1) * 128, :])
        p.wait("scalar", p.sem_dma, dvy)
        p.wait("scalar", p.sems["vector"])
        p.s(lambda e: e.activation(sqt2.ap(), ybuf.ap(), AF.Square), inc=True)
        p.wait("tensor", p.sems["scalar"])
        for c0 in range(0, T, 512):
            p.wait("tensor", p.sems["vector"])
            p.t(lambda e, c0=c0: e.matmul(p.ps[5][0:1, 0:512], sb["sb_ones"][:, 0:1],
                                          sqt2.ap(0, 128, c0, c0 + 512), start=True, stop=True))
            tv = p.pe_done()
            p.wait("vector", p.sems["tensor"], tv)
            p.v(lambda e, c0=c0: e.tensor_copy(rowt2.ap(0, 1, c0, c0 + 512), p.ps[5][0:1, 0:512]),
                inc=True)
        p.wait("scalar", p.sems["vector"])
        p.s(lambda e: e.activation(rowt2.ap(0, 1, 0, T), rowt2.ap(0, 1, 0, T), AF.Sqrt,
                                   scale=float(1.0 / 128.0), bias=p.scol("k_eps6")[0:1, 0:1]),
            inc=True)
        p.wait("vector", p.sems["scalar"])
        p.v(lambda e: e.reciprocal(rowt2.ap(0, 1, 0, T), rowt2.ap(0, 1, 0, T)), inc=True)
        for c0 in range(0, T, 512):
            p.wait("tensor", p.sems["vector"])
            p.t(lambda e, c0=c0: e.matmul(p.ps[4][:, 0:512], sb["sb_ones"][0:1, :],
                                          rowt2.ap(0, 1, c0, c0 + 512), start=True, stop=True))
            tv = p.pe_done()
            p.wait("vector", p.sems["tensor"], tv)
            p.v(lambda e, c0=c0, hh=hh: e.scalar_tensor_tensor(
                ytl.ap(0, 128, hh * T + c0, hh * T + c0 + 512), ybuf.ap(0, 128, c0, c0 + 512),
                p.scol("l1_whn", hh), p.ps[4][:, 0:512], ALU.mult, ALU.mult), inc=True)
    if DBG_L1 == "y":
        p.wait("sync", p.sems["vector"])
        for hh in range(8):
            p.dma(sb["dbg"][hh * 128:(hh + 1) * 128, :], ytl.ap(0, 128, hh * T, (hh + 1) * T))
        p.wait("sync", p.sem_dma, p.sem_dma.n)
        fa.reset(famark); ba.reset(bamark)
        return

    au2 = ba.alloc(T, "au2")
    xsl = ba.alloc(8 * 512, "xsl")
    xbuf = ba.alloc(T, "xbuf")
    y2stg = [ba.alloc(T, "y2A"), ba.alloc(T, "y2B")]
    wbf64b = ba.alloc(128, "wu2bf")
    wbfL = ba.alloc(1024, "woutbf")
    wst64b = fa.alloc(128, "wu2st")
    wstL = fa.alloc(1024, "woutst")
    t1 = fa.alloc(512, "t1")
    gbuf = abf

    def evac_au2(mi, c0, c1, pband, tv):
        p.wait("scalar", p.sems["tensor"], tv)
        p.s(lambda e, c0=c0, c1=c1, pband=pband: e.activation(
            au2.ap(0, 128, c0, c1), pband[:, 0:c1 - c0], AF.Silu,
            bias=p.scol("l1_bu1", 0)), inc=True)
    for c0 in range(0, T, 512):
        p.wait("sync", p.sems["tensor"])
        dvx = load_xn_slice(p, xn_dram, xsl, c0, c0 + 512)
        p.wait("tensor", p.sem_dma, dvx)
        emit_linear(p, "l1_wu1", 1, D,
                    lambda k, cc0, cc1, c0=c0: xsl.ap(0, 128, k * 512 + (cc0 - c0), k * 512 + (cc1 - c0)),
                    c0, c0 + 512, evac_au2)

    y2_dv = [0, 0]
    for mi in range(8):
        p.wait("sync", p.sems["vector"])
        dv = p.dma(wst64b.ap(0, 64, 0, 128), sb["l1_wu2"][mi, 0:64])
        p.wait("vector", p.sem_dma, dv)
        p.v(lambda e: e.tensor_copy(wbf64b.ap(0, 64, 0, 128), wst64b.ap(0, 64, 0, 128)), inc=True)
        p.wait("tensor", p.sems["vector"])
        p.wait("tensor", p.sems["scalar"])
        for c0 in range(0, T, 512):
            p.wait("tensor", p.sems["scalar"])
            p.t(lambda e, c0=c0: e.matmul(p.ps[5][:, 0:512], wbf64b.ap(0, 64, 0, 128),
                                          au2.ap(0, 64, c0, c0 + 512), start=True, stop=True))
            tv = p.pe_done()
            p.wait("scalar", p.sems["tensor"], tv)
            p.wait("scalar", p.sems["vector"])
            p.s(lambda e, c0=c0, mi=mi: e.activation(gbuf.ap(0, 128, c0, c0 + 512),
                                                     p.ps[5][:, 0:512], AF.Sigmoid,
                                                     bias=p.scol("l1_bu2", mi)), inc=True)
        p.wait("sync", p.sems["vector"])
        dvx = p.dma(xbuf.ap(), xn_dram[mi * 128:(mi + 1) * 128, :])
        dvw = p.dma(wstL.ap().rearrange("p (k c) -> p k c", c=128),
                    sb["l1_wout"][mi].rearrange("(k p) c -> p k c", p=128))
        p.wait("vector", p.sem_dma, dvw)
        p.wait("vector", p.sems["tensor"])
        p.v(lambda e: e.tensor_copy(wbfL.ap(), wstL.ap()), inc=True)
        p.wait("tensor", p.sems["vector"])
        sidx = mi % 2
        for ci, c0 in enumerate(range(0, T, 512)):
            pband = p.ps[ci % 2]
            p.wait("tensor", p.sems["vector"])
            for k in range(8):
                p.t(lambda e, pband=pband, k=k, c0=c0: e.matmul(
                    pband[:, 0:512], wbfL.ap(0, 128, k * 128, (k + 1) * 128),
                    ytl.ap(0, 128, k * T + c0, k * T + c0 + 512), start=(k == 0), stop=(k == 7)))
            tv = p.pe_done()
            p.wait("vector", p.sems["tensor"], tv)
            p.wait("vector", p.sem_dma, dvx)
            p.wait("vector", p.sems["scalar"])
            if c0 == 0:
                p.wait("vector", p.sem_dma, y2_dv[sidx])
            p.v(lambda e, c0=c0, pband=pband: e.tensor_mul(t1.ap(0, 128, 0, 512), pband[:, 0:512],
                                                           gbuf.ap(0, 128, c0, c0 + 512)))
            p.v(lambda e, c0=c0: e.tensor_add(t1.ap(0, 128, 0, 512), t1.ap(0, 128, 0, 512),
                                              xbuf.ap(0, 128, c0, c0 + 512)))
            p.v(lambda e, mi=mi, c0=c0: e.tensor_add(
                xacc.ap(0, 128, mi * T + c0, mi * T + c0 + 512),
                xacc.ap(0, 128, mi * T + c0, mi * T + c0 + 512), t1.ap(0, 128, 0, 512)))
            p.v(lambda e, c0=c0, sidx=sidx: e.tensor_copy(y2stg[sidx].ap(0, 128, c0, c0 + 512),
                                                          t1.ap(0, 128, 0, 512)), inc=True)
        p.wait("sync", p.sems["vector"])
        y2_dv[sidx] = p.dma(scr["y2"][mi * 128:(mi + 1) * 128, :], y2stg[sidx].ap())
    p.wait("vector", p.sem_dma)
    if DBG_L1 == "gate":
        fa.reset(famark); ba.reset(bamark)
        return

    p.wait("sync", p.sem_dma, p.sem_dma.n)
    for r in range(NRC):
        p.wait("sync", p.sems["scalar"])
        p.wait("sync", p.sems["vector"])
        dvx = p.dma(xbuf.ap(), scr["y2"][r * 128:(r + 1) * 128, :])
        p.wait("scalar", p.sem_dma, dvx)
        p.wait("scalar", p.sems["tensor"])
        p.wait("scalar", p.sems["vector"])
        p.s(lambda e: e.activation(sqt2.ap(), xbuf.ap(), AF.Square), inc=True)
        p.wait("tensor", p.sems["scalar"])
        if r == 0:
            p.wait("tensor", p.sems["vector"])
        for j in range(4):
            p.t(lambda e, j=j, r=r: e.matmul(p.ps[3 + j][0:1, 0:512], sb["sb_ones"][:, 0:1],
                                             sqt2.ap(0, 128, j * 512, (j + 1) * 512),
                                             start=(r == 0), stop=(r == NRC - 1)))
        p.pe_done()
    tvs = p.sems["tensor"].n
    p.wait("vector", p.sems["tensor"], tvs)
    for j in range(4):
        p.v(lambda e, j=j: e.tensor_copy(rowt2.ap(0, 1, j * 512, (j + 1) * 512),
                                         p.ps[3 + j][0:1, 0:512]), inc=(j == 3))
    p.wait("scalar", p.sems["vector"])
    p.s(lambda e: e.activation(rowt2.ap(0, 1, 0, T), rowt2.ap(0, 1, 0, T), AF.Sqrt,
                               scale=float(1.0 / D), bias=p.scol("k_eps6")[0:1, 0:1]), inc=True)
    p.wait("vector", p.sems["scalar"])
    p.v(lambda e: e.reciprocal(rowt2.ap(0, 1, 0, T), rowt2.ap(0, 1, 0, T)), inc=True)
    bcast_row(p, lambda c0, c1: rowt2.ap(0, 1, c0, c1), T, bcB)
    ba.reset(sdT.off)
    h2 = ba.alloc(8 * T, "h2")
    xbuf2 = ba.alloc(T, "xbuf2")
    stmp = [ba.alloc(T, "stmpA"), ba.alloc(T, "stmpB")]
    for r in range(NRC):
        p.wait("sync", p.sems["vector"])
        dvx = p.dma(xbuf2.ap(), scr["y2"][r * 128:(r + 1) * 128, :])
        p.wait("vector", p.sem_dma, dvx)
        p.v(lambda e, r=r: e.scalar_tensor_tensor(h2.ap(0, 128, r * T, (r + 1) * T), xbuf2.ap(),
                                                  p.scol("l1_wffn", r), bcB.ap(),
                                                  ALU.mult, ALU.mult), inc=True)

    stmp_dv = [0, 0]

    def evac_ff13(mi, c0, c1, pband, tv):
        w = c1 - c0
        par = mi % 2
        if mi < 32:
            p.wait("scalar", p.sems["tensor"], tv)
            if c0 == 0:
                p.wait("scalar", p.sem_dma, stmp_dv[par])
                p.wait("scalar", p.sems["vector"])
            p.s(lambda e, c0=c0, c1=c1, pband=pband, par=par, w=w: e.activation(
                stmp[par].ap(0, 128, c0, c1), pband[:, 0:w], AF.Silu), inc=True)
        else:
            p.wait("vector", p.sems["tensor"], tv)
            p.wait("vector", p.sems["scalar"])
            p.v(lambda e, c0=c0, c1=c1, pband=pband, par=par, w=w: e.tensor_mul(
                stmp[par].ap(0, 128, c0, c1), stmp[par].ap(0, 128, c0, c1), pband[:, 0:w]),
                inc=True)
            if c1 == T:
                p.wait("sync", p.sems["vector"])
                stmp_dv[par] = p.dma(scr["s"][(mi - 32) * 128:(mi - 31) * 128, :], stmp[par].ap())
    m_list = []
    for j in range(32):
        m_list += [j, j + 32]
    emit_linear(p, "l1_ff13", 64, D, lambda k, c0, c1: h2.ap(0, 128, k * T + c0, k * T + c1),
                0, T, evac_ff13, m_list=m_list)
    p.wait("vector", p.sem_dma)
    p.wait("scalar", p.sem_dma)

    ba.reset(sdT.off)
    sfb = ba.alloc(32 * 512, "sfb")
    p.wait("sync", p.sem_dma, p.sem_dma.n)

    def evac_ff2(mi, c0, c1, pband, tv):
        p.wait("vector", p.sems["tensor"], tv)
        p.v(lambda e, mi=mi, c0=c0, c1=c1, pband=pband: e.tensor_add(
            xacc.ap(0, 128, mi * T + c0, mi * T + c1),
            xacc.ap(0, 128, mi * T + c0, mi * T + c1), pband[:, 0:c1 - c0]), inc=True)
    for c0 in range(0, T, 512):
        p.wait("sync", p.sems["tensor"])
        for k in range(32):
            p.dma(sfb.ap(0, 128, k * 512, (k + 1) * 512),
                  scr["s"][k * 128:(k + 1) * 128, c0:c0 + 512])
        dvs = p.sem_dma.n
        p.wait("tensor", p.sem_dma, dvs)
        emit_linear(p, "l1_ff2", 8, 4096,
                    lambda k, cc0, cc1, c0=c0: sfb.ap(0, 128, k * 512 + (cc0 - c0), k * 512 + (cc1 - c0)),
                    c0, c0 + 512, evac_ff2)
    fa.reset(famark)
    ba.reset(bamark)


def emit_l1_ffn(p, xacc, xn_dram, negmuB, rstdB, scr):
    sb = p.sb
    fa, ba = p.fa, p.ba
    famark, bamark = fa.mark(), ba.mark()

    xnb = ba.alloc(NRC * T, "xnb")
    he = ba.alloc(8 * T, "he")
    p.wait("sync", p.sems["tensor"])
    p.wait("sync", p.sems["vector"])
    p.wait("sync", p.sems["scalar"])
    for r in range(NRC):
        p.dma(xnb.ap(0, 128, r * T, (r + 1) * T), xn_dram[r * 128:(r + 1) * 128, :])
    dvx = p.sem_dma.n
    p.wait("tensor", p.sem_dma, dvx)

    lgt = fa.alloc(T, "lgt")
    selx = fa.alloc(T, "selx")
    srow0 = fa.alloc(64, "srow0")

    def evac_lgt(mi, c0, c1, pband, tv):
        p.wait("vector", p.sems["tensor"], tv)
        p.v(lambda e, c0=c0, c1=c1, pband=pband: e.tensor_copy(lgt.ap(0, 128, c0, c1),
                                                               pband[:, 0:c1 - c0]), inc=True)
    emit_linear(p, "l1f_gate", 1, D, lambda k, c0, c1: xnb.ap(0, 128, k * T + c0, k * T + c1),
                0, T, evac_lgt)

    s0 = lambda a, b: srow0.ap(0, 128, a, b)
    for t0 in range(0, T, 128):
        pb = p.ps[6]
        p.wait("tensor", p.sems["vector"])
        p.t(lambda e, t0=t0, pb=pb: e.transpose(pb[:, 0:128], lgt.ap(0, 128, t0, t0 + 128),
                                                sb["sb_ident"][:]))
        tv = p.pe_done()
        p.wait("vector", p.sems["tensor"], tv)
        p.vtiny(lambda e, pb=pb: e.tensor_copy(s0(0, 8), pb[:, 0:8]))
        p.vtiny(lambda e: e.reduce_max(s0(8, 9), s0(0, 8), axis=AX.X))
        p.vtiny(lambda e: e.tensor_single_scalar(s0(0, 8), s0(0, 8), s0(8, 9), ALU.subtract),
                inc=True)
        p.wait("scalar", p.sems["vector"])
        p.stiny(lambda e: e.activation(s0(0, 8), s0(0, 8), AF.Exp), inc=True)
        p.wait("vector", p.sems["scalar"])
        p.vtiny(lambda e: e.reduce_sum(s0(9, 10), s0(0, 8), axis=AX.X))
        p.vtiny(lambda e: e.reciprocal(s0(9, 10), s0(9, 10)))
        p.vtiny(lambda e: e.reduce_max(s0(10, 11), s0(0, 8), axis=AX.X))
        p.vtiny(lambda e: e.tensor_single_scalar(s0(16, 24), s0(0, 8), s0(10, 11), ALU.is_ge))
        p.vtiny(lambda e: e.tensor_tensor_scan(s0(24, 32), sb["sb_ones"][:, 0:8], s0(16, 24),
                                               0.0, ALU.mult, ALU.add))
        p.vtiny(lambda e: e.tensor_single_scalar(s0(24, 32), s0(24, 32), 1.0, ALU.is_equal))
        p.vtiny(lambda e: e.tensor_mul(s0(32, 40), s0(16, 24), s0(24, 32)))
        p.vtiny(lambda e: e.scalar_tensor_tensor(s0(48, 56), s0(32, 40), -1e30, s0(0, 8),
                                                 ALU.mult, ALU.add))
        p.vtiny(lambda e: e.reduce_max(s0(11, 12), s0(48, 56), axis=AX.X))
        p.vtiny(lambda e: e.tensor_single_scalar(s0(16, 24), s0(48, 56), s0(11, 12), ALU.is_ge))
        p.vtiny(lambda e: e.tensor_tensor_scan(s0(24, 32), sb["sb_ones"][:, 0:8], s0(16, 24),
                                               0.0, ALU.mult, ALU.add))
        p.vtiny(lambda e: e.tensor_single_scalar(s0(24, 32), s0(24, 32), 1.0, ALU.is_equal))
        p.vtiny(lambda e: e.tensor_mul(s0(40, 48), s0(16, 24), s0(24, 32)))
        p.vtiny(lambda e: e.tensor_add(s0(32, 40), s0(32, 40), s0(40, 48)))
        p.vtiny(lambda e: e.tensor_mul(s0(32, 40), s0(32, 40), s0(0, 8)))
        p.vtiny(lambda e: e.tensor_single_scalar(s0(32, 40), s0(32, 40), s0(9, 10), ALU.mult),
                inc=True)
        p.wait("tensor", p.sems["vector"])
        p.t(lambda e, pb=pb: e.transpose(pb[0:8, 0:128], s0(32, 40), sb["sb_ident"][:]))
        tv = p.pe_done()
        p.wait("vector", p.sems["tensor"], tv)
        p.v(lambda e, t0=t0, pb=pb: e.tensor_copy(selx.ap(0, 8, t0, t0 + 128), pb[0:8, 0:128]),
            inc=True)
    p.wait("sync", p.sems["vector"])
    dvsel = p.dma(scr["sel"][:, :], selx.ap(0, 8, 0, T))
    p.wait("vector", p.sem_dma)
    p.wait("sync", p.sem_dma, dvsel)
    fa.reset(lgt.off)
    selB = fa.alloc(T, "selB")
    srow = fa.alloc(T, "srow")
    tmp = fa.alloc(512, "l1ftmp")

    for pass_i in range(9):
        e_i = pass_i if pass_i < 8 else None
        if e_i is not None:
            p.wait("sync", p.sems["vector"])
            p.wait("sync", p.sems["tensor"])
            dvs = p.dma(srow.ap(0, 1, 0, T), scr["sel"][e_i:e_i + 1, :])
            p.wait("vector", p.sem_dma, dvs)
            p.wait("tensor", p.sem_dma, dvs)
            bcast_row(p, lambda c0, c1: srow.ap(0, 1, c0, c1), T, selB)
            w13, w2 = f"l1f_w13_{e_i}", f"l1f_w2_{e_i}"
            b13, b2 = f"l1f_b13_{e_i}", f"l1f_b2_{e_i}"
        else:
            p.wait("vector", p.sems["tensor"])
            p.v(lambda e: e.memset(selB.ap(), 1.0), inc=True)
            w13, w2 = "l1f_sw13", "l1f_sw2"
            b13, b2 = "l1f_sb13", "l1f_sb2"

        def evac13(mi, c0, c1, pband, tv, b13=b13):
            w = c1 - c0
            if mi < 8:
                p.wait("scalar", p.sems["tensor"], tv)
                p.s(lambda e, mi=mi, c0=c0, c1=c1, pband=pband, w=w: e.activation(
                    he.ap(0, 128, mi * T + c0, mi * T + c1), pband[:, 0:w], AF.Silu,
                    bias=p.scol(b13, mi)), inc=True)
            else:
                p.wait("vector", p.sems["tensor"], tv)
                p.wait("vector", p.sems["scalar"])
                p.v(lambda e, mi=mi, c0=c0, c1=c1, pband=pband, w=w: e.scalar_tensor_tensor(
                    he.ap(0, 128, (mi - 8) * T + c0, (mi - 8) * T + c1), pband[:, 0:w],
                    p.scol(b13, mi), he.ap(0, 128, (mi - 8) * T + c0, (mi - 8) * T + c1),
                    ALU.add, ALU.mult), inc=True)
        m_list = []
        for j in range(8):
            m_list += [j, j + 8]
        emit_linear(p, w13, 16, D, lambda k, c0, c1: xnb.ap(0, 128, k * T + c0, k * T + c1),
                    0, T, evac13, m_list=m_list)

        def evac_o(mi, c0, c1, pband, tv, b2=b2):
            w = c1 - c0
            p.wait("vector", p.sems["tensor"], tv)
            p.v(lambda e, mi=mi, c0=c0, c1=c1, pband=pband, w=w: e.scalar_tensor_tensor(
                tmp.ap(0, 128, 0, w), pband[:, 0:w], p.scol(b2, mi), selB.ap(0, 128, c0, c1),
                ALU.add, ALU.mult))
            p.v(lambda e, mi=mi, c0=c0, c1=c1, w=w: e.tensor_add(
                xacc.ap(0, 128, mi * T + c0, mi * T + c1),
                xacc.ap(0, 128, mi * T + c0, mi * T + c1), tmp.ap(0, 128, 0, w)), inc=True)
        emit_linear(p, w2, 8, 1024, lambda k, c0, c1: he.ap(0, 128, k * T + c0, k * T + c1),
                    0, T, evac_o)
    fa.reset(famark)
    ba.reset(bamark)



def build_program(H, SM, layers):
    nc = bass.Bass("TRN2", target_bir_lowering=False, debug=False, detect_race_conditions=False)
    sb = {"_smoff": SM.off}
    for name, arr in H.items():
        dt = BF16 if arr.dtype.name == "bfloat16" else F32
        sb[name] = nc.dram_tensor(name, list(arr.shape), dt, kind="ExternalInput").ap()
    xin = nc.dram_tensor("xT", [D, T], F32, kind="ExternalInput").ap()
    xout = nc.dram_tensor("outT", [D, T], F32, kind="ExternalOutput").ap()
    sb["dbg"] = nc.dram_tensor("dbg", [2048, T], BF16, kind="ExternalOutput").ap()
    scr = {
        "xn": nc.dram_tensor("scr_xn", [D, T], BF16, kind="Internal").ap(),
        "xnf": nc.dram_tensor("scr_xnf", [D, T], F32, kind="Internal").ap(),
        "w1q": nc.dram_tensor("scr_w1q", [32, D, 128], BF16, kind="Internal").ap(),
        "w2q": nc.dram_tensor("scr_w2q", [8, 4096, 128], BF16, kind="Internal").ap(),
        "s": nc.dram_tensor("scr_s", [4096, T], BF16, kind="Internal").ap(),
        "sel": nc.dram_tensor("scr_sel", [8, T], F32,
                              kind=("ExternalOutput" if DBG_SEL_OUT else "Internal")).ap(),
    }
    for nm_, shp_, dt_ in (("q", [512, T], BF16), ("k", [512, T], BF16),
                           ("v", [1024, T], BF16), ("ab", [512, T], F32),
                           ("bb", [8, T], F32), ("y", [1024, T], BF16),
                           ("y2", [1024, T], BF16)):
        kind = "ExternalOutput" if nm_ in DBG_SCR_OUT else "Internal"
        scr[nm_] = nc.dram_tensor("scr_" + nm_, shp_, dt_, kind=kind).ap()

    p = P(nc, sb)
    for cname, dt, shape in (("c_ident", F32, [128, 128]), ("c_identb", BF16, [128, 128]),
                             ("c_ones", F32, [128, 128])):
        h = nc.alloc_sbuf_tensor("sb" + cname[1:], shape, dt)
        sb["sb" + cname[1:]] = h
        p.dma(h[:, :], sb[cname])
    hbd2 = nc.alloc_sbuf_tensor("sb_bd2", [2, 128], F32)
    sb["sb_bd2"] = hbd2
    p.dma(hbd2[:, :], sb["c_bd2"])
    for cn in ("c_bd2A", "c_bd2B"):
        hh = nc.alloc_sbuf_tensor("sb" + cn[1:], [2, 128], F32)
        sb["sb" + cn[1:]] = hh
        p.dma(hh[:, :], sb[cn])
    for cn in ("c_l1stril", "c_l1triu"):
        hh = nc.alloc_sbuf_tensor("sb" + cn[1:], [64, 64], F32)
        sb["sb" + cn[1:]] = hh
        p.dma(hh[:, :], sb[cn])
    hsm = nc.alloc_sbuf_tensor("sb_smalls", [128, H["c_smalls"].shape[1]], F32)
    sb["sb_smalls"] = hsm
    p.dma(hsm[:, :], sb["c_smalls"])

    p.onesb = p.ba.alloc(2, "onesb")
    p.vtiny(lambda e: e.memset(p.onesb.ap(), 1.0), inc=True)
    xacc = p.fa.alloc(NRC * T, "xacc")
    negmuB = p.fa.alloc(T, "negmu")
    rstdB = p.fa.alloc(T, "rstd")
    for r in range(NRC):
        p.dma(xacc.ap(0, 128, r * T, (r + 1) * T), xin[r * 128:(r + 1) * 128, :])
    dv = p.sem_dma.n
    for e in ("vector", "scalar", "tensor"):
        p.wait(e, p.sem_dma, dv)

    sb["_scr_xnf"] = scr["xnf"]
    for L in layers:
        emit_ln(p, xacc, scr["xn"], negmuB, rstdB)
        if L == 0:
            if DBG_STOP != "ln":
                emit_l0_attn(p, xacc, scr["xn"])
                if DBG_STOP in (None, "ffn", "full"):
                    emit_l0_ffn(p, xacc, scr)
        elif L == 1:
            if DBG_STOP != "ln":
                if DBG_STOP != "l1ffn":
                    emit_l1_attn(p, xacc, scr["xn"], negmuB, rstdB, scr)
                if DBG_STOP in (None, "ffn", "full", "l1ffn"):
                    emit_l1_ffn(p, xacc, scr["xn"], negmuB, rstdB, scr)
        elif L == 2:
            if DBG_STOP != "ln":
                emit_l2_attn(p, xacc, scr["xn"])
                if DBG_STOP in (None, "ffn", "full"):
                    emit_l2_ffn(p, xacc, scr["xn"], negmuB, rstdB, scr)
        elif L == 3:
            if DBG_STOP != "ln":
                emit_l3_attn(p, xacc, scr["xn"], negmuB, rstdB, scr)
                if DBG_STOP in (None, "ffn", "full"):
                    emit_l3_ffn(p, xacc, scr["xn"], negmuB, rstdB, scr)
        else:
            raise NotImplementedError(f"layer {L}")
        p.barrier()

    p.barrier()
    for e in ("vector", "scalar", "tensor"):
        pass
    p.wait("sync", p.sems["vector"])
    p.wait("sync", p.sems["scalar"])
    p.wait("sync", p.sems["tensor"])
    for r in range(NRC):
        p.dma(xout[r * 128:(r + 1) * 128, :], xacc.ap(0, 128, r * T, (r + 1) * T))
    p.wait("sync", p.sem_dma, p.sem_dma.n)
    p.finalize()
    return nc


def run_layers(inputs, layers, x_override=None, cores=8):
    H, SM = host_prep(inputs)
    nc = build_program(H, SM, layers)
    x = np.asarray(inputs["L0_attn_x"], np.float32) if x_override is None else x_override
    in_maps = []
    for c in range(cores):
        m = dict(H)
        m["xT"] = np.ascontiguousarray(x[c].T)
        in_maps.append(m)
    res = run_bass_kernel_spmd(nc, in_maps, core_ids=list(range(cores)))
    out = np.stack([np.ascontiguousarray(r["outT"].T) for r in res.results], axis=0)
    return out


def kernel(**inputs):
    return run_layers(inputs, [0, 1, 2, 3]).astype(np.float32)


# version 03 (final)
# speedup vs baseline: 1.0690x; 1.0170x over previous
import numpy as np
import concourse.bass as bass
import concourse.mybir as mybir
from concourse.bass_utils import run_bass_kernel_spmd

DT = mybir.dt
F32, BF16, I32 = DT.float32, DT.bfloat16, DT.int32
AF = mybir.ActivationFunctionType
ALU = mybir.AluOpType
AX = mybir.AxisListType

D = 1024
T = 2048
DBG_STOP = None
DBG_DUMP = False
DBG_SEL_OUT = False
DBG_SCR_OUT = ()
DBG_L1 = None
DBG_L1_NPAIR = 4
DBG_L1_NCHUNK = 64
DBG_L1_STEPS = ("g", "L", "newton", "out", "state")
NRC = D // 128
QHI = 127.49999
QLO = -128.49999


def pack_w(w):
    w = np.asarray(w, np.float32)
    K, M = w.shape
    Kp = (K + 127) // 128 * 128
    Mp = (M + 127) // 128 * 128
    wp = np.zeros((Kp, Mp), np.float32)
    wp[:K, :M] = w
    return np.ascontiguousarray(wp.reshape(Kp, Mp // 128, 128).transpose(1, 0, 2))


class Smalls:
    def __init__(self):
        self.cols = []
        self.off = {}
        self.n = 0

    def add(self, name, arr):
        a = np.asarray(arr, np.float32)
        assert a.shape[0] == 128
        self.off[name] = (self.n, a.shape[1])
        self.cols.append(a)
        self.n += a.shape[1]

    def addb(self, name, vec, nm=None):
        v = np.asarray(vec, np.float32).reshape(-1)
        nm_ = (len(v) + 127) // 128 if nm is None else nm
        bp = np.zeros((nm_ * 128,), np.float32)
        bp[:len(v)] = v
        self.add(name, np.ascontiguousarray(bp.reshape(nm_, 128).T))

    def array(self):
        return np.ascontiguousarray(np.concatenate(self.cols, axis=1))


def rope_tables():
    inv = 1.0 / 10000.0 ** (np.arange(0, 64, 2, dtype=np.float64) / 64.0)
    t = np.arange(T, dtype=np.float64)
    fr = np.outer(inv, t)
    cos = np.cos(fr); sin = np.sin(fr)
    cc = np.zeros((128, T), np.float64)
    ss = np.zeros((128, T), np.float64)
    for g in range(2):
        r = g * 64
        cc[r:r+32] = cos; cc[r+32:r+64] = cos
        ss[r:r+32] = -sin; ss[r+32:r+64] = sin
    return cc.astype(np.float32), ss.astype(np.float32)


def host_prep(ii):
    import ml_dtypes
    H = {}
    SM = Smalls()
    H["c_ident"] = np.eye(128, dtype=np.float32)
    H["c_identb"] = np.eye(128, dtype=np.float32).astype(ml_dtypes.bfloat16)
    H["c_ones"] = np.ones((128, 128), np.float32)
    ob = np.zeros((128, 2), np.float32)
    ob[0:64, 0] = 1.0; ob[64:128, 1] = 1.0
    SM.add("ones64bd", ob)
    pswap = np.zeros((128, 128), np.float32)
    for g in range(2):
        r = g * 64
        pswap[r:r+32, r+32:r+64] = np.eye(32)
        pswap[r+32:r+64, r:r+32] = np.eye(32)
    H["c_pswapb"] = pswap.astype(ml_dtypes.bfloat16)
    cc, ss = rope_tables()
    H["c_ropecc"] = cc
    H["c_ropess"] = ss
    m2 = np.ones((128, 128), np.float32)
    m2[:64, -64:] = 0.0
    m2[-64:, :64] = 0.0
    H["c_l2mask"] = m2
    C1 = 32
    st = np.tril(np.ones((C1, C1), np.float32), -1)
    blk = np.zeros((64, 64), np.float32)
    blk[:C1, :C1] = st; blk[C1:, C1:] = st
    H["c_l1stril"] = blk
    tu = np.triu(np.ones((C1, C1), np.float32), 0)
    blk2 = np.zeros((64, 64), np.float32)
    blk2[:C1, :C1] = tu; blk2[C1:, C1:] = tu
    H["c_l1triu"] = blk2
    H["c_l1strius"] = np.ascontiguousarray(blk.T)
    mA = np.zeros((128, 1), np.float32); mA[0:64] = 1.0
    mB = np.zeros((128, 1), np.float32); mB[64:128] = 1.0
    SM.add("l1_mA", mA)
    SM.add("l1_mB", mB)
    bd2 = np.zeros((2, 128), np.float32)
    bd2[0, 0:64] = 1.0; bd2[1, 64:128] = 1.0
    H["c_bd2"] = bd2
    bd2a = np.zeros((2, 128), np.float32); bd2a[0, :] = 1.0
    bd2b = np.zeros((2, 128), np.float32); bd2b[1, :] = 1.0
    H["c_bd2A"] = bd2a
    H["c_bd2B"] = bd2b
    for nm_, v_ in (("k_eps5", 1e-5), ("k_eps6", 1e-6), ("k_eps3", 1e-3), ("k_half", 0.5)):
        SM.add(nm_, np.full((128, 1), v_, np.float32))

    H["l0_wqkv"] = pack_w(ii["L0_attn_Wqkv"].T)
    SM.addb("l0_bqkv", ii["L0_attn_bqkv"])
    H["l0_wout"] = pack_w(ii["L0_attn_Wout"].T)
    SM.addb("l0_bout", ii["L0_attn_bout"])
    H["l0_w1"] = pack_w(ii["L0_ffn_w1"].T)
    for j in range(3):
        SM.addb(f"l0_convw{j}", ii["L0_ffn_conv_w"][:, 0, j])
    SM.addb("l0_convb", ii["L0_ffn_conv_b"])
    H["l0_w2"] = pack_w(ii["L0_ffn_w2"].T)

    SM.addb("l1_wnorm", ii["L1_attn_w_norm_in"])
    for j in range(4):
        SM.addb(f"l1_convw{j}", ii["L1_attn_conv_w"][:, j])
    SM.addb("l1_convb", ii["L1_attn_conv_b"])
    wproj = np.zeros((1024, 22 * 128), np.float32)
    wproj[:, 0:2560] = np.concatenate([ii["L1_attn_wq"], ii["L1_attn_wk"], ii["L1_attn_wv"],
                                       ii["L1_attn_w_spike"]], axis=1)
    wproj[:, 2560:2624] = ii["L1_attn_w_au"]
    wproj[:, 2688:2696] = ii["L1_attn_w_beta"]
    H["l1_wproj"] = pack_w(wproj)
    bproj = np.zeros((22 * 128,), np.float32)
    bproj[2560:2624] = ii["L1_attn_b_au"]
    bproj[2688:2696] = ii["L1_attn_b_beta"]
    SM.addb("l1_bproj", bproj, nm=22)
    H["l1_wad"] = pack_w(ii["L1_attn_w_ad"])
    SM.addb("l1_bad", ii["L1_attn_b_ad"])
    SM.addb("l1_aspk", ii["L1_attn_alpha_spike"].reshape(-1))
    SM.addb("l1_bspk", ii["L1_attn_beta_spike"].reshape(-1))
    SM.addb("l1_whn", ii["L1_attn_w_hn"].reshape(-1))
    H["l1_wu1"] = pack_w(ii["L1_attn_w_u1"])
    SM.addb("l1_bu1", ii["L1_attn_b_u1"])
    H["l1_wu2"] = pack_w(ii["L1_attn_w_u2"])
    SM.addb("l1_bu2", ii["L1_attn_b_u2"])
    H["l1_wout"] = pack_w(ii["L1_attn_w_out"])
    SM.addb("l1_wffn", ii["L1_attn_w_ffn"])
    H["l1_ff13"] = pack_w(np.concatenate([ii["L1_attn_w_ff1"], ii["L1_attn_w_ff3"]], axis=1))
    H["l1_ff2"] = pack_w(ii["L1_attn_w_ff2"])

    H["l1f_gate"] = pack_w(ii["L1_ffn_gate_w"].T)
    for e in range(8):
        H[f"l1f_w13_{e}"] = pack_w(np.concatenate([ii["L1_ffn_w1"][e], ii["L1_ffn_w3"][e]], axis=1))
        H[f"l1f_w2_{e}"] = pack_w(ii["L1_ffn_w2"][e])
        SM.addb(f"l1f_b13_{e}", np.concatenate([ii["L1_ffn_b1"][e], ii["L1_ffn_b3"][e]]))
        SM.addb(f"l1f_b2_{e}", ii["L1_ffn_b2"][e])
    H["l1f_sw13"] = pack_w(np.concatenate([ii["L1_ffn_sw1"], ii["L1_ffn_sw3"]], axis=1))
    SM.addb("l1f_sb13", np.concatenate([ii["L1_ffn_sb1"], ii["L1_ffn_sb3"]]))
    H["l1f_sw2"] = pack_w(ii["L1_ffn_sw2"])
    SM.addb("l1f_sb2", ii["L1_ffn_sb2"])

    perm = np.zeros((3072,), np.int64)
    i = 0
    for s in range(3):
        for h in range(16):
            for dd in range(64):
                perm[i] = h * 192 + s * 64 + dd
                i += 1
    H["l2_wqkv"] = pack_w(ii["L2_attn_w_qkv"][:, perm])
    SM.addb("l2_bqkv", ii["L2_attn_b_qkv"][perm])
    H["l2_wo"] = pack_w(ii["L2_attn_w_o"])
    SM.addb("l2_bo", ii["L2_attn_b_o"])
    SM.addb("l2_ln1w", ii["L2_ffn_ln1_w"])
    SM.addb("l2_ln1b", ii["L2_ffn_ln1_b"])
    H["l2_w1"] = pack_w(ii["L2_ffn_w1"].T)
    SM.addb("l2_b1", ii["L2_ffn_b1"])
    SM.addb("l2_ln2w", ii["L2_ffn_ln2_w"])
    SM.addb("l2_ln2b", ii["L2_ffn_ln2_b"])
    H["l2_w2"] = pack_w(ii["L2_ffn_w2"].T)
    SM.addb("l2_b2", ii["L2_ffn_b2"])

    H["l3_wqkv"] = pack_w(ii["L3_attn_Wqkv"].T)
    SM.addb("l3_bqkv", ii["L3_attn_bqkv"])
    H["l3_wproj"] = pack_w(ii["L3_attn_Wproj"].T)
    SM.addb("l3_bproj", ii["L3_attn_bproj"])
    SM.add("l3_scale", np.full((128, 1), np.float32(np.asarray(ii["L3_attn_scale"]))))
    H["l3_wg"] = pack_w(ii["L3_ffn_Wg"].T)
    for e in range(8):
        H[f"l3_w1_{e}"] = pack_w(ii["L3_ffn_W1"][e])
        H[f"l3_w2_{e}"] = pack_w(ii["L3_ffn_W2"][e])
        SM.addb(f"l3_b1_{e}", ii["L3_ffn_b1"][e])
        SM.addb(f"l3_b2_{e}", ii["L3_ffn_b2"][e])
        SM.addb(f"l3_lns_{e}", ii["L3_ffn_ln_scale"][e])
        SM.addb(f"l3_lnb_{e}", ii["L3_ffn_ln_bias"][e])
    H["c_smalls"] = SM.array()
    return H, SM



ENGS = ("sync", "vector", "scalar", "tensor", "gpsimd")


ALL_SEMS = []


class Sem:
    def __init__(self, nc, name):
        self.h = nc.alloc_semaphore(name)
        self.n = 0
        ALL_SEMS.append(self.h)


class Buf:
    def __init__(self, t, off, cols, dtype):
        self.t, self.off, self.cols, self.dtype = t, off, cols, dtype

    def ap(self, r0=0, r1=128, c0=0, c1=None):
        c1 = self.cols if c1 is None else c1
        return self.t[r0:r1, self.off + c0:self.off + c1]


class Arena:
    def __init__(self, nc, name, cols, dtype):
        self.t = nc.alloc_sbuf_tensor(name, [128, cols], dtype)
        self.cols = cols
        self.off = 0
        self.peak = 0
        self.dtype = dtype

    def alloc(self, cols, tag=""):
        cols = (cols + 1) & ~1
        assert self.off + cols <= self.cols, f"arena {tag}: {self.off}+{cols}>{self.cols}"
        b = Buf(self.t, self.off, cols, self.dtype)
        self.off += cols
        self.peak = max(self.peak, self.off)
        return b

    def mark(self):
        return self.off

    def reset(self, m):
        self.off = m


class P:
    def __init__(self, nc, sb):
        ALL_SEMS.clear()
        self.nc = nc
        self.sb = sb
        self.ops = {e: [] for e in ENGS}
        self.sems = {e: Sem(nc, f"s_{e}") for e in ENGS}
        self.sem_dma = Sem(nc, "s_dma")
        self.fa = Arena(nc, "fa", 32800, F32)
        self.ba = Arena(nc, "ba", 34000, BF16)
        self.ia = Arena(nc, "ia", 512, I32)
        self.ps = [nc.alloc_psum_tensor(f"ps{i}", [128, 512], F32) for i in range(7)]
        self.psb = nc.alloc_psum_tensor("psb", [128, 1024], BF16)
        self.tiny = nc.alloc_sbuf_tensor("tinybar", [128, 8], F32)

    def small(self, name):
        c0, n = self.sb["_smoff"][name]
        return self.sb["sb_smalls"][0:128, c0:c0 + n]

    def scol(self, name, j=0):
        c0, n = self.sb["_smoff"][name]
        assert j < n, (name, j, n)
        return self.sb["sb_smalls"][0:128, c0 + j:c0 + j + 1]

    def raw(self, eng, fn):
        self.ops[eng].append(fn)

    def op(self, eng, fn, inc=None, k=1):
        if inc is not None:
            inc.n += k
            self.raw(eng, lambda e, fn=fn, h=inc.h, k=k: fn(e).then_inc(h, k))
        else:
            self.raw(eng, lambda e, fn=fn: fn(e))

    def v(self, fn, inc=False):
        self.op("vector", fn, inc=self.sems["vector"] if inc else None)

    def s(self, fn, inc=False):
        self.op("scalar", fn, inc=self.sems["scalar"] if inc else None)

    def g(self, fn, inc=False):
        self.op("gpsimd", fn, inc=self.sems["gpsimd"] if inc else None)

    def t(self, fn):
        self.op("tensor", fn)

    def vtiny(self, fn, inc=False):
        self.raw("vector", lambda e, fn=fn: fn(e))
        if inc:
            s = self.sems["vector"]; s.n += 1
            self.raw("vector", lambda e, h=s.h: e.drain().then_inc(h, 1))
        else:
            self.raw("vector", lambda e: e.drain())

    def stiny(self, fn, inc=False):
        self.raw("scalar", lambda e, fn=fn: fn(e))
        if inc:
            s = self.sems["scalar"]; s.n += 1
            self.raw("scalar", lambda e, h=s.h: e.drain().then_inc(h, 1))
        else:
            self.raw("scalar", lambda e: e.drain())

    def wait(self, eng, sem, val=None):
        v = sem.n if val is None else val
        if v > 0:
            self.raw(eng, lambda e, s=sem.h, v=v: e.wait_ge(s, v))

    def pe_done(self):
        s = self.sems["tensor"]
        s.n += 1
        self.raw("tensor", lambda e, h=s.h: e.drain(fusable=False).then_inc(h, 1))
        return s.n

    def dma(self, out_ap, in_ap, eng="sync"):
        self.sem_dma.n += 16
        self.raw(eng, lambda e, o=out_ap, i=in_ap, h=self.sem_dma.h: e.dma_start(out=o, in_=i).then_inc(h, 16))
        return self.sem_dma.n

    def barrier(self, engs=("vector", "scalar", "tensor")):
        snap = {}
        for i, e in enumerate(engs):
            s = self.sems[e]
            s.n += 1
            if e == "tensor":
                self.raw(e, lambda x, h=s.h: x.drain(fusable=False).then_inc(h, 1))
            else:
                self.raw(e, lambda x, h=s.h: x.drain().then_inc(h, 1))
            snap[e] = s.n
        for e in engs:
            for o in engs:
                if o != e:
                    self.wait(e, self.sems[o], snap[o])

    def finalize(self):
        sems = list(ALL_SEMS)
        with self.nc.Block() as block:
            for eng in ENGS:
                fns = self.ops[eng]
                if not fns:
                    continue
                def mk(fns, eng):
                    def run(e):
                        for s in sems:
                            e.sem_clear(s)
                        if eng == "vector":
                            e.memset(self.tiny[0:1, 0:2], 0.0)
                        elif eng == "scalar":
                            e.memzero(self.tiny[0:1, 2:4])
                        elif eng == "gpsimd":
                            e.memset(self.tiny[0:1, 4:6], 0.0)
                        for f in fns:
                            f(e)
                    return run
                getattr(block, eng)(mk(fns, eng))



def bcast_row(p, row_ap_fn, width, out_buf, scale=1.0, out_c0=0):
    sb = p.sb
    for c0 in range(0, width, 512):
        c1 = min(c0 + 512, width)
        w = c1 - c0
        pb = p.ps[6]
        p.wait("tensor", p.sems["vector"])
        p.t(lambda e, pb=pb, w=w, c0=c0, c1=c1: e.matmul(
            pb[:, 0:w], sb["sb_ones"][0:1, :], row_ap_fn(c0, c1), start=True, stop=True))
        tv = p.pe_done()
        p.wait("vector", p.sems["tensor"], tv)
        if scale == 1.0:
            p.v(lambda e, pb=pb, w=w, c0=c0, c1=c1: e.tensor_copy(
                out_buf.ap(0, 128, out_c0 + c0, out_c0 + c1), pb[:, 0:w]), inc=True)
        else:
            p.v(lambda e, pb=pb, w=w, c0=c0, c1=c1: e.tensor_scalar_mul(
                out_buf.ap(0, 128, out_c0 + c0, out_c0 + c1), pb[:, 0:w], float(scale)), inc=True)


def bcast_col(p, scalar_ap, out_ap, scale=1.0):
    pb = p.ps[6]
    p.wait("tensor", p.sems["vector"])
    p.t(lambda e: e.matmul(pb[:, 0:1], p.sb["sb_ones"][0:1, :], scalar_ap, start=True, stop=True))
    tv = p.pe_done()
    p.wait("vector", p.sems["tensor"], tv)
    if scale == 1.0:
        p.vtiny(lambda e: e.tensor_copy(out_ap, pb[:, 0:1]), inc=True)
    else:
        p.vtiny(lambda e: e.tensor_scalar_mul(out_ap, pb[:, 0:1], float(scale)), inc=True)


def part_max_to_row(p, m_buf, width, out_row_fn, scr=None):
    sb = p.sb
    fa = p.fa
    own = scr is None
    scr = fa.alloc(width, "pmscr") if own else scr
    for c0 in range(0, width, 128):
        w = min(128, width - c0)
        pb = p.ps[6]
        p.wait("tensor", p.sems["vector"])
        p.t(lambda e, c0=c0, w=w, pb=pb: e.transpose(pb[0:w, 0:128], m_buf.ap(0, 128, c0, c0 + w),
                                                     sb["sb_ident"][:]))
        tv = p.pe_done()
        p.wait("vector", p.sems["tensor"], tv)
        p.vtiny(lambda e, pb=pb, w=w, c0=c0: e.reduce_max(scr.ap(0, w, c0, c0 + 1), pb[0:w, 0:128],
                                                          axis=AX.X), inc=True)
        p.wait("tensor", p.sems["vector"])
        p.t(lambda e, c0=c0, w=w, pb=pb: e.transpose(pb[0:1, 0:w], scr.ap(0, w, c0, c0 + 1),
                                                     sb["sb_ident"][:]))
        tv = p.pe_done()
        p.wait("vector", p.sems["tensor"], tv)
        out_row_fn(c0, w, pb)
    if own:
        fa.reset(scr.off)


def quant_tile(p, in_ap, sB_ap, out_ap, f_ap, i_ap):
    p.v(lambda e: e.tensor_mul(f_ap, in_ap, sB_ap))
    p.v(lambda e: e.tensor_scalar(f_ap, f_ap, QHI, QLO, ALU.min, ALU.max))
    p.v(lambda e: e.tensor_scalar_add(f_ap, f_ap, 0.5))
    p.v(lambda e: e.tensor_copy(i_ap, f_ap))
    p.v(lambda e: e.tensor_copy(out_ap, i_ap))
    p.v(lambda e: e.tensor_sub(f_ap, f_ap, out_ap))
    p.v(lambda e: e.scalar_tensor_tensor(out_ap, f_ap, 1.0, out_ap, ALU.is_ge, ALU.add))
    p.v(lambda e: e.scalar_tensor_tensor(out_ap, f_ap, 0.0, out_ap, ALU.is_ge, ALU.add))
    p.v(lambda e: e.tensor_scalar_add(out_ap, out_ap, -1.0), inc=True)


def emit_linear(p, wname, nm, K, rhs_fn, n0, n1, evac, m_list=None, ternary_thr=None,
                wq_bf16_dram=None, nwide=512, k0=0, nbanks=4):
    sb = p.sb
    fa, ba = p.fa, p.ba
    nk = (K + 127) // 128
    wbf = ba.alloc(nk * 128, "wbf")
    stage = None if wq_bf16_dram is not None else fa.alloc(nk * 128, "wstage")
    mlist = list(range(nm)) if m_list is None else m_list
    ev_marks = []
    gi = 0
    for mi in mlist:
        if wq_bf16_dram is not None:
            p.wait("sync", p.sems["tensor"])
            dv = p.dma(wbf.ap().rearrange("p (k c) -> p k c", c=128), wq_bf16_dram[mi, k0:k0 + nk * 128].rearrange("(k p) c -> p k c", p=128))
            p.wait("tensor", p.sem_dma, dv)
            p.wait("tensor", p.sems["vector"])
            p.wait("tensor", p.sems["scalar"])
        else:
            p.wait("sync", p.sems["vector"])
            dv = p.dma(stage.ap().rearrange("p (k c) -> p k c", c=128), sb[wname][mi, k0:k0 + nk * 128].rearrange("(k p) c -> p k c", p=128))
            p.wait("vector", p.sem_dma, dv)
            p.wait("vector", p.sems["tensor"])
            if ternary_thr is None:
                p.v(lambda e: e.tensor_copy(wbf.ap(), stage.ap()), inc=True)
            else:
                p.v(lambda e: e.tensor_single_scalar(wbf.ap(), stage.ap(), ternary_thr, ALU.is_lt))
                p.v(lambda e: e.scalar_tensor_tensor(wbf.ap(), stage.ap(), ternary_thr, wbf.ap(),
                                                     ALU.is_gt, ALU.subtract), inc=True)
            p.wait("tensor", p.sems["vector"])
        p.wait("tensor", p.sems["scalar"])
        chunks = [(c0, min(c0 + nwide, n1)) for c0 in range(n0, n1, nwide)]
        gsz = 2 if nbanks == 4 else 1
        for g0 in range(0, len(chunks), gsz):
            grp = chunks[g0:g0 + gsz]
            if nbanks == 4:
                banks = [p.ps[2 * (gi % 2)], p.ps[2 * (gi % 2) + 1]]
            else:
                banks = [p.ps[gi % 2]]
            if len(ev_marks) >= 2:
                for eng, val in ev_marks[-2]:
                    p.wait("tensor", p.sems[eng], val)
            for bi, (c0, c1) in enumerate(grp):
                w = c1 - c0
                for k in range(nk):
                    p.t(lambda e, pband=banks[bi], k=k, c0=c0, c1=c1, w=w: e.matmul(
                        pband[:, 0:w], wbf.ap(0, 128, k * 128, (k + 1) * 128),
                        rhs_fn(k, c0, c1), start=(k == 0), stop=(k == nk - 1)))
            tv = p.pe_done()
            vb, s_b = p.sems["vector"].n, p.sems["scalar"].n
            for bi, (c0, c1) in enumerate(grp):
                evac(mi, c0, c1, banks[bi], tv)
            mk = []
            if p.sems["vector"].n > vb:
                mk.append(("vector", p.sems["vector"].n))
            if p.sems["scalar"].n > s_b:
                mk.append(("scalar", p.sems["scalar"].n))
            if not mk:
                raise RuntimeError("evac must inc vector or scalar sem")
            ev_marks.append(mk)
            gi += 1
    if stage is not None:
        fa.reset(stage.off)
    ba.reset(wbf.off)


def emit_mean_abs(p, wname, nm, K, true_count, out_buf, out_c0):
    sb = p.sb
    fa = p.fa
    nk = (K + 127) // 128
    stage = fa.alloc(nk * 128, "mst")
    acc = fa.alloc(2, "mac")
    p.v(lambda e: e.memset(acc.ap(0, 128, 0, 1), 0.0), inc=True)
    for mi in range(nm):
        p.wait("sync", p.sems["vector"])
        dv = p.dma(stage.ap().rearrange("p (k c) -> p k c", c=128), sb[wname][mi].rearrange("(k p) c -> p k c", p=128))
        p.wait("vector", p.sem_dma, dv)
        p.vtiny(lambda e: e.tensor_reduce(acc.ap(0, 128, 1, 2), stage.ap(), axis=AX.X, op=ALU.add,
                                          apply_absolute_value=True))
        p.vtiny(lambda e: e.tensor_add(acc.ap(0, 128, 0, 1), acc.ap(0, 128, 0, 1), acc.ap(0, 128, 1, 2)),
                inc=True)
    pb = p.ps[6]
    p.wait("tensor", p.sems["vector"])
    p.t(lambda e: e.transpose(pb[0:1, 0:128], acc.ap(0, 128, 0, 1), sb["sb_ident"][:]))
    tv = p.pe_done()
    p.wait("vector", p.sems["tensor"], tv)
    p.vtiny(lambda e: e.reduce_sum(out_buf.ap(0, 1, out_c0, out_c0 + 1), pb[0:1, 0:128], axis=AX.X))
    p.vtiny(lambda e: e.tensor_scalar_mul(out_buf.ap(0, 1, out_c0, out_c0 + 1),
                                          out_buf.ap(0, 1, out_c0, out_c0 + 1), float(1.0 / true_count)),
            inc=True)
    fa.reset(stage.off)



def emit_ln(p, xacc, xn_dram, negmuB, rstdB, eps=1e-5):
    sb = p.sb
    fa = p.fa
    mk = fa.mark()
    rows = fa.alloc(2 * T, "lnrows")
    sq = fa.alloc(512, "lnsq")
    for c0 in range(0, T, 512):
        pbs = p.ps[5]
        pbq = p.ps[6]
        p.wait("tensor", p.sems["vector"])
        for r in range(NRC):
            p.wait("scalar", p.sems["tensor"])
            p.s(lambda e, r=r, c0=c0: e.activation(sq.ap(0, 128, 0, 512),
                                                   xacc.ap(0, 128, r * T + c0, r * T + c0 + 512),
                                                   AF.Square), inc=True)
            p.wait("tensor", p.sems["scalar"])
            p.t(lambda e, r=r, c0=c0, pbs=pbs: e.matmul(
                pbs[0:1, 0:512], sb["sb_ones"][:, 0:1], xacc.ap(0, 128, r * T + c0, r * T + c0 + 512),
                start=(r == 0), stop=(r == NRC - 1)))
            p.t(lambda e, r=r, pbq=pbq: e.matmul(pbq[0:1, 0:512], sb["sb_ones"][:, 0:1],
                                                 sq.ap(0, 128, 0, 512),
                                                 start=(r == 0), stop=(r == NRC - 1)))
            p.pe_done()
        tv = p.sems["tensor"].n
        p.wait("vector", p.sems["tensor"], tv)
        p.v(lambda e, c0=c0, pbs=pbs: e.tensor_scalar_mul(rows.ap(0, 1, c0, c0 + 512), pbs[0:1, 0:512],
                                                          1.0 / D))
        p.v(lambda e, c0=c0, pbq=pbq: e.tensor_scalar_mul(rows.ap(0, 1, T + c0, T + c0 + 512),
                                                          pbq[0:1, 0:512], 1.0 / D))
        p.v(lambda e, c0=c0: e.tensor_mul(sq.ap(0, 1, 0, 512), rows.ap(0, 1, c0, c0 + 512),
                                          rows.ap(0, 1, c0, c0 + 512)))
        p.v(lambda e, c0=c0: e.tensor_sub(rows.ap(0, 1, T + c0, T + c0 + 512),
                                          rows.ap(0, 1, T + c0, T + c0 + 512), sq.ap(0, 1, 0, 512)),
            inc=True)
        p.wait("scalar", p.sems["vector"])
        p.s(lambda e, c0=c0: e.activation(rows.ap(0, 1, T + c0, T + c0 + 512),
                                          rows.ap(0, 1, T + c0, T + c0 + 512), AF.Sqrt,
                                          bias=p.scol("k_eps5")[0:1, 0:1]),
            inc=True)
        p.wait("vector", p.sems["scalar"])
        p.v(lambda e, c0=c0: e.reciprocal(rows.ap(0, 1, T + c0, T + c0 + 512),
                                          rows.ap(0, 1, T + c0, T + c0 + 512)), inc=True)
    bcast_row(p, lambda c0, c1: rows.ap(0, 1, c0, c1), T, negmuB, scale=-1.0)
    bcast_row(p, lambda c0, c1: rows.ap(0, 1, T + c0, T + c1), T, rstdB)
    tmp = fa.alloc(T, "lntmp")
    xff = fa.alloc(T, "lnxf")
    xnb = p.ba.alloc(T, "lnxnb")
    for r in range(NRC):
        p.wait("vector", p.sem_dma)
        p.v(lambda e, r=r: e.tensor_add(tmp.ap(), xacc.ap(0, 128, r * T, (r + 1) * T), negmuB.ap()))
        p.v(lambda e, r=r: e.tensor_mul(xff.ap(), tmp.ap(), rstdB.ap()))
        p.v(lambda e, r=r: e.tensor_copy(xnb.ap(), xff.ap()), inc=True)
        p.wait("sync", p.sems["vector"])
        p.dma(xn_dram[r * 128:(r + 1) * 128, :], xnb.ap())
        p.dma(p.sb["_scr_xnf"][r * 128:(r + 1) * 128, :], xff.ap())
    fa.reset(mk)
    p.ba.reset(xnb.off)


def xn32(p, xacc, negmuB, rstdB, r, out_ap, tmp_ap, c0=0, c1=T):
    p.v(lambda e: e.tensor_add(tmp_ap, xacc.ap(0, 128, r * T + c0, r * T + c1), negmuB.ap(0, 128, c0, c1)))
    p.v(lambda e: e.tensor_mul(out_ap, tmp_ap, rstdB.ap(0, 128, c0, c1)), inc=True)


def load_xn_slice(p, xn_dram, xsl, c0, c1):
    w = c1 - c0
    for k in range(NRC):
        p.dma(xsl.ap(0, 128, k * w, (k + 1) * w), xn_dram[k * 128:(k + 1) * 128, c0:c1])
    return p.sem_dma.n



def emit_l0_attn(p, xacc, xn_dram):
    sb = p.sb
    fa, ba = p.fa, p.ba
    famark, bamark = fa.mark(), ba.mark()

    pswap = ba.alloc(128, "pswap")
    p.wait("sync", p.sems["vector"])
    p.wait("sync", p.sems["scalar"])
    p.wait("sync", p.sems["tensor"])
    dv = p.dma(pswap.ap(), sb["c_pswapb"][:, :])
    p.wait("vector", p.sem_dma, dv)

    ngroups = 1 if DBG_STOP in ("qkv", "stab", "heads", "g0") else 2
    for g in range(ngroups):
        gmark_f, gmark_b = fa.mark(), ba.mark()
        qg = ba.alloc(4 * T, "qg")
        kg = ba.alloc(4 * T, "kg")
        vtok = ba.alloc(16 * 512, "vtok")
        ccb = fa.alloc(T, "cc"); ssb = fa.alloc(T, "ss")
        p.wait("sync", p.sems["vector"])
        p.wait("sync", p.sems["scalar"])
        p.wait("sync", p.sems["tensor"])
        p.dma(ccb.ap(), sb["c_ropecc"][:, :])
        p.dma(ssb.ap(), sb["c_ropess"][:, :])
        xsl = ba.alloc(8 * 512, "xsl")

        def evac_qkv(mi, c0, c1, pband, tv, g=g):
            w = c1 - c0
            if mi < 16:
                ch = (mi - 4 * g) if mi < 8 else (mi - 8 - 4 * g + 4)
                tb = qg if ch < 4 else kg
                chl = ch % 4
                tgt = tb.ap(0, 128, chl * T + c0, chl * T + c1)
                p.wait("vector", p.sems["tensor"], tv)
                p.v(lambda e, pband=pband, w=w, tgt=tgt, mi=mi: e.tensor_scalar_add(
                    tgt, pband[:, 0:w], p.scol("l0_bqkv", mi)), inc=True)
                p.wait("tensor", p.sems["vector"])
                pb2 = p.ps[2]
                p.t(lambda e, tgt=tgt, w=w, pb2=pb2: e.matmul(pb2[:, 0:w], pswap.ap(), tgt,
                                                              start=True, stop=True))
                tv2 = p.pe_done()
                p.wait("vector", p.sems["tensor"], tv2)
                t1 = fa.alloc(512, "ropet")
                p.v(lambda e, pb2=pb2, w=w, c0=c0, c1=c1, t1=t1: e.tensor_mul(
                    t1.ap(0, 128, 0, w), pb2[:, 0:w], ssb.ap(0, 128, c0, c1)))
                p.v(lambda e, tgt=tgt, c0=c0, c1=c1: e.tensor_mul(tgt, tgt, ccb.ap(0, 128, c0, c1)))
                p.v(lambda e, tgt=tgt, w=w, t1=t1: e.tensor_add(tgt, tgt, t1.ap(0, 128, 0, w)), inc=True)
                fa.reset(t1.off)
            else:
                vb = ba.alloc(512, "vb")
                p.wait("vector", p.sems["tensor"], tv)
                p.v(lambda e, pband=pband, w=w, vb=vb, mi=mi: e.tensor_scalar_add(
                    vb.ap(0, 128, 0, w), pband[:, 0:w], p.scol("l0_bqkv", mi)), inc=True)
                p.wait("tensor", p.sems["vector"])
                mv = mi - 16 - 4 * g
                for s0 in range(0, w, 128):
                    kt = (c0 + s0) // 128
                    p.wait("tensor", p.sems["vector"])
                    p.t(lambda e, vb=vb, s0=s0: e.transpose(p.psb[:, 0:128], vb.ap(0, 128, s0, s0 + 128),
                                                            sb["sb_identb"][:]))
                    tv3 = p.pe_done()
                    p.wait("vector", p.sems["tensor"], tv3)
                    for hh in range(2):
                        h = 2 * mv + hh
                        p.v(lambda e, kt=kt, h=h, hh=hh: e.tensor_copy(
                            vtok.ap(0, 128, kt * 512 + h * 64, kt * 512 + h * 64 + 64),
                            p.psb[:, hh * 64:hh * 64 + 64]), inc=True)
                ba.reset(vb.off)

        mlist = ([4 * g + j for j in range(4)] + [8 + 4 * g + j for j in range(4)]
                 + [16 + 4 * g + j for j in range(4)])
        for c0 in range(0, T, 512):
            p.wait("sync", p.sems["tensor"])
            dvx = load_xn_slice(p, xn_dram, xsl, c0, c0 + 512)
            p.wait("tensor", p.sem_dma, dvx)
            emit_linear(p, "l0_wqkv", 24, D,
                        lambda k, cc0, cc1, c0=c0: xsl.ap(0, 128, k * 512 + (cc0 - c0), k * 512 + (cc1 - c0)),
                        c0, c0 + 512, evac_qkv, m_list=mlist, nbanks=2)
        fa.reset(ccb.off)
        ba.reset(xsl.off)
        if DBG_STOP == "qkv":
            fa.reset(famark); ba.reset(bamark); return
        if DBG_DUMP and g == 0:
            p.wait("sync", p.sems["vector"])
            p.wait("sync", p.sems["tensor"])
            dbg = p.sb["dbg"]
            for cch in range(4):
                p.dma(dbg[0 + cch * 128:(cch + 1) * 128, :], qg.ap(0, 128, cch * T, (cch + 1) * T))
                p.dma(dbg[512 + cch * 128:512 + (cch + 1) * 128, :], kg.ap(0, 128, cch * T, (cch + 1) * T))
            for kt in range(4):
                p.dma(dbg[1024 + kt * 128:1024 + (kt + 1) * 128, 0:512],
                      vtok.ap(0, 128, kt * 512, (kt + 1) * 512))

        nrm = fa.alloc(8, "nrm")
        sqt = fa.alloc(T, "sqt")
        first = [True, True]
        for ch in range(8):
            which = 0 if ch < 4 else 1
            tb = qg if ch < 4 else kg
            p.wait("scalar", p.sems["vector"])
            p.s(lambda e, tb=tb, ch=ch: e.activation(sqt.ap(), tb.ap(0, 128, (ch % 4) * T, (ch % 4 + 1) * T),
                                                     AF.Square), inc=True)
            p.wait("tensor", p.sems["scalar"])
            p.wait("tensor", p.sems["vector"])
            for c0 in range(0, T, 512):
                pb = p.ps[6]
                p.wait("tensor", p.sems["vector"])
                for half in range(2):
                    p.t(lambda e, c0=c0, pb=pb, half=half: e.matmul(
                        pb[half:half + 1, 0:512] if False else pb[0:2, half * 256:half * 256 + 512][0:1, 0:512] if False else pb[0:1, 0:512] if half == 0 else pb[0:1, 0:512],
                        p.small("ones64bd")[:, half:half + 1],
                        sqt.ap(0, 128, c0, c0 + 512), start=True, stop=True) if False else e.matmul(
                        (p.ps[5] if half else p.ps[6])[0:1, 0:512],
                        p.small("ones64bd")[:, half:half + 1],
                        sqt.ap(0, 128, c0, c0 + 512), start=True, stop=True))
                tv = p.pe_done()
                p.wait("vector", p.sems["tensor"], tv)
                for half in range(2):
                    p.vtiny(lambda e, half=half: e.reduce_max(nrm.ap(0, 1, 2, 3),
                                                              (p.ps[5] if half else p.ps[6])[0:1, 0:512],
                                                              axis=AX.X))
                    w = which
                    if first[w]:
                        p.vtiny(lambda e, w=w: e.tensor_copy(nrm.ap(0, 1, w, w + 1), nrm.ap(0, 1, 2, 3)),
                                inc=True)
                        first[w] = False
                    else:
                        p.vtiny(lambda e, w=w: e.tensor_max(nrm.ap(0, 1, w, w + 1),
                                                            nrm.ap(0, 1, w, w + 1),
                                                            nrm.ap(0, 1, 2, 3)), inc=True)
        p.vtiny(lambda e: e.tensor_mul(nrm.ap(0, 1, 5, 6), nrm.ap(0, 1, 0, 1), nrm.ap(0, 1, 1, 2)),
                inc=True)
        p.wait("scalar", p.sems["vector"])
        p.stiny(lambda e: e.activation(nrm.ap(0, 1, 6, 7), nrm.ap(0, 1, 5, 6), AF.Sqrt,
                                       scale=float(1.0 / 64.0)), inc=True)
        cbias = fa.alloc(2, "cbias")
        p.wait("vector", p.sems["scalar"])
        bcast_col(p, nrm.ap(0, 1, 6, 7), cbias.ap(0, 128, 0, 1), scale=-1.0)
        fa.reset(sqt.off)
        if DBG_STOP == "stab":
            fa.reset(famark); ba.reset(bamark); return

        ytil = ba.alloc(4 * T, "ytil")
        pbuf = ba.alloc(2 * 512, "pbuf")
        osc = fa.alloc(512, "osc")
        rrow = fa.alloc(512, "rrow")
        for h in range(8):
            ch = h // 2
            r0 = (h % 2) * 64
            for nq in range(4):
                c0 = nq * 512
                pso = p.ps[3]
                psm = p.ps[5]
                for kt in range(16):
                    psS = p.ps[kt % 2]
                    if kt >= 2:
                        p.wait("tensor", p.sems["scalar"])
                    p.t(lambda e, psS=psS, ch=ch, r0=r0, kt=kt, c0=c0: e.matmul(
                        psS[:, 0:512],
                        kg.ap(r0, r0 + 64, ch * T + kt * 128, ch * T + kt * 128 + 128),
                        qg.ap(r0, r0 + 64, ch * T + c0, ch * T + c0 + 512), start=True, stop=True))
                    tv = p.pe_done()
                    p.wait("scalar", p.sems["tensor"], tv)
                    pslot = pbuf.ap(0, 128, (kt % 2) * 512, (kt % 2) * 512 + 512)
                    p.s(lambda e, psS=psS, pslot=pslot: e.activation(
                        pslot, psS[:, 0:512], AF.Exp, bias=cbias.ap(0, 128, 0, 1), scale=0.125),
                        inc=True)
                    p.wait("tensor", p.sems["scalar"])
                    p.t(lambda e, pso=pso, kt=kt, h=h, r0=r0, pslot=pslot: e.matmul(
                        pso[r0:r0 + 64, 0:512], vtok.ap(0, 128, kt * 512 + h * 64, kt * 512 + (h + 1) * 64),
                        pslot, start=(kt == 0), stop=(kt == 15)))
                    p.t(lambda e, psm=psm, kt=kt, pslot=pslot: e.matmul(
                        psm[0:1, 0:512], p.onesb.ap(0, 128, 0, 1),
                        pslot, start=(kt == 0), stop=(kt == 15)))
                tv = p.pe_done()
                p.wait("vector", p.sems["tensor"], tv)
                p.v(lambda e, psm=psm: e.tensor_copy(rrow.ap(0, 1, 0, 512), psm[0:1, 0:512]))
                p.v(lambda e: e.reciprocal(rrow.ap(0, 1, 0, 512), rrow.ap(0, 1, 0, 512)), inc=True)
                p.wait("scalar", p.sems["vector"])
                p.s(lambda e, pso=pso, r0=r0: e.activation(osc.ap(r0, r0 + 64, 0, 512),
                                                           pso[r0:r0 + 64, 0:512], AF.Copy),
                    inc=True)
                pbB = p.ps[4]
                p.wait("tensor", p.sems["vector"])
                p.t(lambda e, pbB=pbB, r0=r0: e.matmul(pbB[r0:r0 + 64, 0:512], sb["sb_ones"][0:1, 0:64],
                                                       rrow.ap(0, 1, 0, 512), start=True, stop=True))
                tv = p.pe_done()
                p.wait("vector", p.sems["tensor"], tv)
                p.wait("vector", p.sems["scalar"])
                p.v(lambda e, pbB=pbB, h=h, r0=r0, c0=c0: e.tensor_mul(
                    ytil.ap(r0, r0 + 64, (h // 2) * T + c0, (h // 2) * T + c0 + 512),
                    osc.ap(r0, r0 + 64, 0, 512), pbB[r0:r0 + 64, 0:512]), inc=True)
        if DBG_STOP == "heads":
            fa.reset(famark); ba.reset(bamark); return
        if DBG_DUMP:
            p.wait("sync", p.sems["vector"])
            base = 1536 if g == 0 else 0
            for cch in range(4):
                p.dma(p.sb["dbg"][base + cch * 128:base + (cch + 1) * 128, :],
                      ytil.ap(0, 128, cch * T, (cch + 1) * T))
        ba.reset(pbuf.off)
        def evac_wout(mi, c0, c1, pband, tv, g=g):
            p.wait("vector", p.sems["tensor"], tv)
            if g == 0:
                p.v(lambda e, mi=mi, c0=c0, c1=c1, pband=pband: e.scalar_tensor_tensor(
                    xacc.ap(0, 128, mi * T + c0, mi * T + c1), pband[:, 0:c1 - c0],
                    p.scol("l0_bout", mi), xacc.ap(0, 128, mi * T + c0, mi * T + c1),
                    ALU.add, ALU.add), inc=True)
            else:
                p.v(lambda e, mi=mi, c0=c0, c1=c1, pband=pband: e.tensor_add(
                    xacc.ap(0, 128, mi * T + c0, mi * T + c1),
                    xacc.ap(0, 128, mi * T + c0, mi * T + c1), pband[:, 0:c1 - c0]), inc=True)
        emit_linear(p, "l0_wout", 8, 512,
                    lambda k, c0, c1: ytil.ap(0, 128, k * T + c0, k * T + c1),
                    0, T, evac_wout, k0=g * 512)
        fa.reset(gmark_f)
        ba.reset(gmark_b)
    fa.reset(famark)
    ba.reset(bamark)


def emit_l0_ffn(p, xacc, scr):
    sb = p.sb
    fa, ba = p.fa, p.ba
    famark, bamark = fa.mark(), ba.mark()

    sc = fa.alloc(8, "l0sc")
    emit_mean_abs(p, "l0_w1", 32, D, 4096 * 1024, sc, 0)
    emit_mean_abs(p, "l0_w2", 8, 4096, 4096 * 1024, sc, 1)
    p.vtiny(lambda e: e.tensor_scalar_max(sc.ap(0, 1, 0, 2), sc.ap(0, 1, 0, 2), 1e-5))
    p.vtiny(lambda e: e.tensor_scalar_mul(sc.ap(0, 1, 2, 4), sc.ap(0, 1, 0, 2), 0.5), inc=True)
    thr = fa.alloc(4, "l0thr")
    bcast_col(p, sc.ap(0, 1, 2, 3), thr.ap(0, 128, 0, 1))
    bcast_col(p, sc.ap(0, 1, 2, 3), thr.ap(0, 128, 1, 2), scale=-1.0)
    bcast_col(p, sc.ap(0, 1, 3, 4), thr.ap(0, 128, 2, 3))
    bcast_col(p, sc.ap(0, 1, 3, 4), thr.ap(0, 128, 3, 4), scale=-1.0)

    for wname, nm, K, ti, qname in (("l0_w1", 32, D, 0, "w1q"), ("l0_w2", 8, 4096, 1, "w2q")):
        nk = (K + 127) // 128
        stage = fa.alloc(nk * 128, "qst")
        wq = ba.alloc(nk * 128, "qwq")
        for mi in range(nm):
            p.wait("sync", p.sems["vector"])
            dvv = p.dma(stage.ap().rearrange("p (k c) -> p k c", c=128), sb[wname][mi].rearrange("(k p) c -> p k c", p=128))
            p.wait("vector", p.sem_dma, dvv)
            p.v(lambda e, ti=ti: e.tensor_single_scalar(wq.ap(), stage.ap(),
                                                        thr.ap(0, 128, 2 * ti + 1, 2 * ti + 2),
                                                        ALU.is_lt))
            p.v(lambda e, ti=ti: e.scalar_tensor_tensor(wq.ap(), stage.ap(),
                                                        thr.ap(0, 128, 2 * ti, 2 * ti + 1),
                                                        wq.ap(), ALU.is_gt, ALU.subtract), inc=True)
            p.wait("sync", p.sems["vector"])
            p.dma(scr[qname][mi].rearrange("(k p) c -> p k c", p=128), wq.ap().rearrange("p (k c) -> p k c", c=128))
        fa.reset(stage.off)
        ba.reset(wq.off)

    w1 = fa.alloc(T, "w1")
    w2 = fa.alloc(T, "w2")
    w3 = fa.alloc(T, "w3")
    sxB = fa.alloc(T, "sxB")
    hsB = fa.alloc(T, "hsB")
    mrow = fa.alloc(T, "mrow")
    sfb = Buf(w1.t, w1.off, 512, F32)
    it = p.ia.alloc(512, "l0i")

    def make_rows(inv_col):
        p.v(lambda e: e.tensor_scalar_max(mrow.ap(0, 1, 0, T), mrow.ap(0, 1, 0, T), 1e-5), inc=True)
        p.v(lambda e: e.reciprocal(w1.ap(0, 1, 0, T), mrow.ap(0, 1, 0, T)))
        p.v(lambda e: e.tensor_scalar_mul(w1.ap(0, 1, 0, T), w1.ap(0, 1, 0, T), 127.0))
        p.v(lambda e: e.tensor_scalar(w2.ap(0, 1, 0, T), mrow.ap(0, 1, 0, T), inv_col,
                                      float(1.0 / 127.0), ALU.mult, ALU.mult), inc=True)
        bcast_row(p, lambda c0, c1: w1.ap(0, 1, c0, c1), T, sxB)
        bcast_row(p, lambda c0, c1: w2.ap(0, 1, c0, c1), T, hsB)

    def wrow(c0, w, pb):
        p.v(lambda e, pb=pb, c0=c0, w=w: e.tensor_copy(mrow.ap(0, 1, c0, c0 + w), pb[0:1, 0:w]), inc=True)

    hb = ba.alloc(T, "hb")
    xq = ba.alloc(8 * T, "xq")
    for r in range(NRC):
        p.wait("sync", p.sems["scalar"])
        p.wait("sync", p.sems["vector"])
        dvx = p.dma(w1.ap(), scr["xnf"][r * 128:(r + 1) * 128, :])
        p.wait("scalar", p.sem_dma, dvx)
        p.s(lambda e: e.activation(hsB.ap(), w1.ap(), AF.Abs), inc=True)
        p.wait("vector", p.sems["scalar"])
        if r == 0:
            p.v(lambda e: e.tensor_copy(w3.ap(), hsB.ap()), inc=True)
        else:
            p.v(lambda e: e.tensor_max(w3.ap(), w3.ap(), hsB.ap()), inc=True)
    part_max_to_row(p, w3, T, wrow, scr=w2)
    make_rows(sc.ap(0, 1, 0, 1))
    for r in range(NRC):
        p.wait("sync", p.sems["vector"])
        dvx = p.dma(w1.ap(), scr["xnf"][r * 128:(r + 1) * 128, :])
        p.wait("vector", p.sem_dma, dvx)
        for qc in range(0, T, 512):
            quant_tile(p, w1.ap(0, 128, qc, qc + 512), sxB.ap(0, 128, qc, qc + 512),
                       xq.ap(0, 128, r * T + qc, r * T + qc + 512),
                       w2.ap(0, 128, qc, qc + 512), it.ap(0, 128, 0, 512))
    if DBG_DUMP:
        p.wait("sync", p.sems["vector"])
        for r in range(NRC):
            p.dma(p.sb["dbg"][r * 128:(r + 1) * 128, :], xq.ap(0, 128, r * T, (r + 1) * T))


    for mi in range(32):
        def evac_h(mi2, c0, c1, pband, tv):
            p.wait("vector", p.sems["tensor"], tv)
            p.wait("vector", p.sem_dma)
            p.v(lambda e, c0=c0, c1=c1, pband=pband: e.tensor_mul(
                hb.ap(0, 128, c0, c1), pband[:, 0:c1 - c0], hsB.ap(0, 128, c0, c1)), inc=True)
        emit_linear(p, None, 32, D, lambda k, c0, c1: xq.ap(0, 128, k * T + c0, k * T + c1),
                    0, T, evac_h, m_list=[mi], wq_bf16_dram=scr["w1q"])
        p.v(lambda e, mi=mi: e.tensor_single_scalar(w1.ap(), hb.ap(), p.scol("l0_convw1", mi), ALU.mult))
        p.v(lambda e, mi=mi: e.scalar_tensor_tensor(w1.ap(0, 128, 1, T), hb.ap(0, 128, 0, T - 1),
                                                    p.scol("l0_convw0", mi), w1.ap(0, 128, 1, T),
                                                    ALU.mult, ALU.add))
        p.v(lambda e, mi=mi: e.scalar_tensor_tensor(w1.ap(0, 128, 0, T - 1), hb.ap(0, 128, 1, T),
                                                    p.scol("l0_convw2", mi), w1.ap(0, 128, 0, T - 1),
                                                    ALU.mult, ALU.add), inc=True)
        p.wait("scalar", p.sems["vector"])
        p.s(lambda e, mi=mi: e.activation(hb.ap(), w1.ap(), AF.Silu, bias=p.scol("l0_convb", mi)))
        p.s(lambda e: e.activation(w2.ap(), hb.ap(), AF.Abs), inc=True)
        p.wait("vector", p.sems["scalar"])
        if mi == 0:
            p.v(lambda e: e.tensor_copy(w3.ap(), w2.ap()), inc=True)
        else:
            p.v(lambda e: e.tensor_max(w3.ap(), w3.ap(), w2.ap()), inc=True)
        p.wait("sync", p.sems["scalar"])
        p.dma(scr["s"][mi * 128:(mi + 1) * 128, :], hb.ap())
    part_max_to_row(p, w3, T, wrow, scr=w2)
    make_rows(sc.ap(0, 1, 1, 2))

    ba.reset(xq.off)
    hq = ba.alloc(32 * 512, "hq")
    for nchunk in range(4):
        c0 = nchunk * 512
        c1 = c0 + 512
        for k in range(32):
            p.wait("sync", p.sems["vector"])
            dvv = p.dma(hb.ap(0, 128, 0, 512), scr["s"][k * 128:(k + 1) * 128, c0:c1])
            p.wait("vector", p.sem_dma, dvv)
            p.v(lambda e: e.tensor_copy(sfb.ap(), hb.ap(0, 128, 0, 512)), inc=True)
            quant_tile(p, sfb.ap(), sxB.ap(0, 128, c0, c1), hq.ap(0, 128, k * 512, (k + 1) * 512),
                       w2.ap(0, 128, 0, 512), it.ap(0, 128, 0, 512))

        def evac_o(mi, cc0, cc1, pband, tv, c0=c0):
            p.wait("vector", p.sems["tensor"], tv)
            p.v(lambda e, pband=pband, c0=c0: e.tensor_mul(sfb.ap(), pband[:, 0:512],
                                                           hsB.ap(0, 128, c0, c0 + 512)))
            p.v(lambda e, mi=mi, c0=c0: e.tensor_add(xacc.ap(0, 128, mi * T + c0, mi * T + c0 + 512),
                                                     xacc.ap(0, 128, mi * T + c0, mi * T + c0 + 512),
                                                     sfb.ap()), inc=True)
        emit_linear(p, None, 8, 4096, lambda k, cc0, cc1: hq.ap(0, 128, k * 512, (k + 1) * 512),
                    c0, c1, evac_o, wq_bf16_dram=scr["w2q"])
    fa.reset(famark)
    ba.reset(bamark)
    p.ia.reset(0)





def load_xn_rolled(p, xn_dram, xsl, c0, c1, shift=64):
    w = c1 - c0
    a0 = (c0 + shift) % T
    for k in range(NRC):
        if a0 + w <= T:
            p.dma(xsl.ap(0, 128, k * w, (k + 1) * w), xn_dram[k * 128:(k + 1) * 128, a0:a0 + w])
        else:
            w1 = T - a0
            p.dma(xsl.ap(0, 128, k * w, k * w + w1), xn_dram[k * 128:(k + 1) * 128, a0:T])
            p.dma(xsl.ap(0, 128, k * w + w1, (k + 1) * w), xn_dram[k * 128:(k + 1) * 128, 0:w - w1])
    return p.sem_dma.n


def emit_l2_attn(p, xacc, xn_dram):
    sb = p.sb
    fa, ba = p.fa, p.ba
    famark, bamark = fa.mark(), ba.mark()
    maskb = fa.alloc(128, "l2mask")
    p.wait("sync", p.sems["vector"])
    p.wait("sync", p.sems["scalar"])
    p.wait("sync", p.sems["tensor"])
    p.dma(maskb.ap(), sb["c_l2mask"][:, :])

    for g in range(2):
        gmark_f, gmark_b = fa.mark(), ba.mark()
        qg = ba.alloc(4 * T, "qg")
        kg = ba.alloc(4 * T, "kg")
        vtok = ba.alloc(16 * 512, "vtok")
        xsl = ba.alloc(8 * 512, "xsl")

        def evac_qkv(mi, c0, c1, pband, tv, g=g):
            w = c1 - c0
            p.wait("vector", p.sems["tensor"], tv)
            if mi < 16:
                ch = (mi - 4 * g) if mi < 8 else (mi - 8 - 4 * g + 4)
                tb = qg if ch < 4 else kg
                chl = ch % 4
                tgt = tb.ap(0, 128, chl * T + c0, chl * T + c1)
                p.v(lambda e, pband=pband, w=w, tgt=tgt, mi=mi: e.tensor_scalar_add(
                    tgt, pband[:, 0:w], p.scol("l2_bqkv", mi)), inc=True)
            else:
                vb = ba.alloc(512, "vb")
                p.v(lambda e, pband=pband, w=w, vb=vb, mi=mi: e.tensor_scalar_add(
                    vb.ap(0, 128, 0, w), pband[:, 0:w], p.scol("l2_bqkv", mi)), inc=True)
                p.wait("tensor", p.sems["vector"])
                mv = mi - 16 - 4 * g
                for s0 in range(0, w, 128):
                    kt = (c0 + s0) // 128
                    p.wait("tensor", p.sems["vector"])
                    p.t(lambda e, vb=vb, s0=s0: e.transpose(p.psb[:, 0:128], vb.ap(0, 128, s0, s0 + 128),
                                                            sb["sb_identb"][:]))
                    tv3 = p.pe_done()
                    p.wait("vector", p.sems["tensor"], tv3)
                    for hh in range(2):
                        h = 2 * mv + hh
                        p.v(lambda e, kt=kt, h=h, hh=hh: e.tensor_copy(
                            vtok.ap(0, 128, kt * 512 + h * 64, kt * 512 + h * 64 + 64),
                            p.psb[:, hh * 64:hh * 64 + 64]), inc=True)
                ba.reset(vb.off)

        mlist = ([4 * g + j for j in range(4)] + [8 + 4 * g + j for j in range(4)]
                 + [16 + 4 * g + j for j in range(4)])
        for c0 in range(0, T, 512):
            p.wait("sync", p.sems["tensor"])
            dvx = load_xn_rolled(p, xn_dram, xsl, c0, c0 + 512)
            p.wait("tensor", p.sem_dma, dvx)
            emit_linear(p, "l2_wqkv", 24, D,
                        lambda k, cc0, cc1, c0=c0: xsl.ap(0, 128, k * 512 + (cc0 - c0), k * 512 + (cc1 - c0)),
                        c0, c0 + 512, evac_qkv, m_list=mlist)
        ba.reset(xsl.off)

        nrm = fa.alloc(8, "nrm")
        sqt = fa.alloc(T, "sqt")
        first = [True, True]
        for ch in range(8):
            which = 0 if ch < 4 else 1
            tb = qg if ch < 4 else kg
            p.wait("scalar", p.sems["vector"])
            p.s(lambda e, tb=tb, ch=ch: e.activation(sqt.ap(), tb.ap(0, 128, (ch % 4) * T, (ch % 4 + 1) * T),
                                                     AF.Square), inc=True)
            p.wait("tensor", p.sems["scalar"])
            p.wait("tensor", p.sems["vector"])
            for c0 in range(0, T, 512):
                p.wait("tensor", p.sems["vector"])
                for half in range(2):
                    p.t(lambda e, c0=c0, half=half: e.matmul(
                        (p.ps[5] if half else p.ps[6])[0:1, 0:512],
                        p.small("ones64bd")[:, half:half + 1],
                        sqt.ap(0, 128, c0, c0 + 512), start=True, stop=True))
                tv = p.pe_done()
                p.wait("vector", p.sems["tensor"], tv)
                for half in range(2):
                    p.vtiny(lambda e, half=half: e.reduce_max(nrm.ap(0, 1, 2, 3),
                                                              (p.ps[5] if half else p.ps[6])[0:1, 0:512],
                                                              axis=AX.X))
                    w = which
                    if first[w]:
                        p.vtiny(lambda e, w=w: e.tensor_copy(nrm.ap(0, 1, w, w + 1), nrm.ap(0, 1, 2, 3)),
                                inc=True)
                        first[w] = False
                    else:
                        p.vtiny(lambda e, w=w: e.tensor_max(nrm.ap(0, 1, w, w + 1),
                                                            nrm.ap(0, 1, w, w + 1),
                                                            nrm.ap(0, 1, 2, 3)), inc=True)
        p.vtiny(lambda e: e.tensor_mul(nrm.ap(0, 1, 5, 6), nrm.ap(0, 1, 0, 1), nrm.ap(0, 1, 1, 2)),
                inc=True)
        p.wait("scalar", p.sems["vector"])
        p.stiny(lambda e: e.activation(nrm.ap(0, 1, 6, 7), nrm.ap(0, 1, 5, 6), AF.Sqrt,
                                       scale=float(1.0 / 64.0)), inc=True)
        cbias = fa.alloc(2, "cbias")
        p.wait("vector", p.sems["scalar"])
        bcast_col(p, nrm.ap(0, 1, 6, 7), cbias.ap(0, 128, 0, 1), scale=-1.0)
        fa.reset(sqt.off)

        ytil = ba.alloc(4 * T, "ytil")
        pbuf = ba.alloc(512, "pbuf")
        osc = fa.alloc(512, "osc")
        rrow = fa.alloc(512, "rrow")
        for h in range(8):
            ch = h // 2
            r0 = (h % 2) * 64
            for wn in range(16):
                c0 = wn * 128
                psS = p.ps[wn % 2]
                pso = p.ps[3]
                psm = p.ps[5]
                if wn >= 2:
                    p.wait("tensor", p.sems["scalar"])
                p.wait("tensor", p.sems["vector"])
                p.t(lambda e, psS=psS, ch=ch, r0=r0, c0=c0: e.matmul(
                    psS[:, 0:128],
                    kg.ap(r0, r0 + 64, ch * T + c0, ch * T + c0 + 128),
                    qg.ap(r0, r0 + 64, ch * T + c0, ch * T + c0 + 128), start=True, stop=True))
                tv = p.pe_done()
                p.wait("scalar", p.sems["tensor"], tv)
                p.s(lambda e, psS=psS: e.activation(pbuf.ap(0, 128, 0, 128), psS[:, 0:128], AF.Exp,
                                                    bias=cbias.ap(0, 128, 0, 1), scale=0.125), inc=True)
                p.wait("vector", p.sems["scalar"])
                p.v(lambda e: e.tensor_mul(pbuf.ap(0, 128, 0, 128), pbuf.ap(0, 128, 0, 128), maskb.ap()),
                    inc=True)
                p.wait("tensor", p.sems["vector"])
                p.t(lambda e, pso=pso, h=h, r0=r0, c0=c0: e.matmul(
                    pso[r0:r0 + 64, 0:128], vtok.ap(0, 128, (c0 // 128) * 512 + h * 64,
                                                    (c0 // 128) * 512 + (h + 1) * 64),
                    pbuf.ap(0, 128, 0, 128), start=True, stop=True))
                p.t(lambda e, psm=psm: e.matmul(psm[0:1, 0:128], p.onesb.ap(0, 128, 0, 1),
                                                pbuf.ap(0, 128, 0, 128), start=True, stop=True))
                tv = p.pe_done()
                p.wait("vector", p.sems["tensor"], tv)
                p.v(lambda e, psm=psm: e.tensor_copy(rrow.ap(0, 1, 0, 128), psm[0:1, 0:128]))
                p.vtiny(lambda e: e.reciprocal(rrow.ap(0, 1, 0, 128), rrow.ap(0, 1, 0, 128)), inc=True)
                p.wait("scalar", p.sems["vector"])
                p.s(lambda e, pso=pso, r0=r0: e.activation(osc.ap(r0, r0 + 64, 0, 128),
                                                           pso[r0:r0 + 64, 0:128], AF.Copy), inc=True)
                pbB = p.ps[4]
                p.wait("tensor", p.sems["vector"])
                p.t(lambda e, pbB=pbB, r0=r0: e.matmul(pbB[r0:r0 + 64, 0:128], sb["sb_ones"][0:1, 0:64],
                                                       rrow.ap(0, 1, 0, 128), start=True, stop=True))
                tv = p.pe_done()
                p.wait("vector", p.sems["tensor"], tv)
                p.wait("vector", p.sems["scalar"])
                p.v(lambda e, pbB=pbB, h=h, r0=r0, c0=c0: e.tensor_mul(
                    ytil.ap(r0, r0 + 64, (h // 2) * T + c0, (h // 2) * T + c0 + 128),
                    osc.ap(r0, r0 + 64, 0, 128), pbB[r0:r0 + 64, 0:128]), inc=True)
        ba.reset(pbuf.off)

        def evac_wo(mi, c0, c1, pband, tv, g=g):
            p.wait("vector", p.sems["tensor"], tv)
            spans = []
            a0 = (c0 + 64) % T
            w = c1 - c0
            if a0 + w <= T:
                spans.append((0, a0, w))
            else:
                spans.append((0, a0, T - a0))
                spans.append((T - a0, 0, w - (T - a0)))
            for (off, dst, ww) in spans:
                if g == 0:
                    p.v(lambda e, mi=mi, off=off, dst=dst, ww=ww, pband=pband: e.scalar_tensor_tensor(
                        xacc.ap(0, 128, mi * T + dst, mi * T + dst + ww), pband[:, off:off + ww],
                        p.scol("l2_bo", mi), xacc.ap(0, 128, mi * T + dst, mi * T + dst + ww),
                        ALU.add, ALU.add), inc=True)
                else:
                    p.v(lambda e, mi=mi, off=off, dst=dst, ww=ww, pband=pband: e.tensor_add(
                        xacc.ap(0, 128, mi * T + dst, mi * T + dst + ww),
                        xacc.ap(0, 128, mi * T + dst, mi * T + dst + ww),
                        pband[:, off:off + ww]), inc=True)
        emit_linear(p, "l2_wo", 8, 512,
                    lambda k, c0, c1: ytil.ap(0, 128, k * T + c0, k * T + c1),
                    0, T, evac_wo, k0=g * 512)
        fa.reset(gmark_f)
        ba.reset(gmark_b)
    fa.reset(famark)
    ba.reset(bamark)


def emit_l2_ffn(p, xacc, xn_dram, negmuB, rstdB, scr):
    sb = p.sb
    fa, ba = p.fa, p.ba
    famark, bamark = fa.mark(), ba.mark()

    sc = fa.alloc(8, "l2sc")
    emit_mean_abs(p, "l2_w1", 32, D, 4096 * 1024, sc, 0)
    emit_mean_abs(p, "l2_w2", 8, 4096, 4096 * 1024, sc, 1)
    p.vtiny(lambda e: e.tensor_scalar_max(sc.ap(0, 1, 0, 2), sc.ap(0, 1, 0, 2), 1e-5))
    p.vtiny(lambda e: e.tensor_scalar_mul(sc.ap(0, 1, 2, 4), sc.ap(0, 1, 0, 2), 0.5), inc=True)
    thr = fa.alloc(4, "l2thr")
    bcast_col(p, sc.ap(0, 1, 2, 3), thr.ap(0, 128, 0, 1))
    bcast_col(p, sc.ap(0, 1, 2, 3), thr.ap(0, 128, 1, 2), scale=-1.0)
    bcast_col(p, sc.ap(0, 1, 3, 4), thr.ap(0, 128, 2, 3))
    bcast_col(p, sc.ap(0, 1, 3, 4), thr.ap(0, 128, 3, 4), scale=-1.0)
    for wname, nm, K, ti, qname in (("l2_w1", 32, D, 0, "w1q"), ("l2_w2", 8, 4096, 1, "w2q")):
        nk = (K + 127) // 128
        stage = fa.alloc(nk * 128, "qst")
        wq = ba.alloc(nk * 128, "qwq")
        for mi in range(nm):
            p.wait("sync", p.sems["vector"])
            dvv = p.dma(stage.ap().rearrange("p (k c) -> p k c", c=128),
                        sb[wname][mi].rearrange("(k p) c -> p k c", p=128))
            p.wait("vector", p.sem_dma, dvv)
            p.v(lambda e, ti=ti: e.tensor_single_scalar(wq.ap(), stage.ap(),
                                                        thr.ap(0, 128, 2 * ti + 1, 2 * ti + 2),
                                                        ALU.is_lt))
            p.v(lambda e, ti=ti: e.scalar_tensor_tensor(wq.ap(), stage.ap(),
                                                        thr.ap(0, 128, 2 * ti, 2 * ti + 1),
                                                        wq.ap(), ALU.is_gt, ALU.subtract), inc=True)
            p.wait("sync", p.sems["vector"])
            p.dma(scr[qname][mi].rearrange("(k p) c -> p k c", p=128),
                  wq.ap().rearrange("p (k c) -> p k c", c=128))
        fa.reset(stage.off)
        ba.reset(wq.off)

    w1b = fa.alloc(T, "w1b")
    w2b = fa.alloc(T, "w2b")
    sxB = fa.alloc(T, "sxB")
    hsB = fa.alloc(T, "hsB")
    rows = fa.alloc(2 * T, "l2rows")
    mrow = Buf(rows.t, rows.off, T, F32)
    sfb = Buf(w1b.t, w1b.off, 512, F32)
    it = p.ia.alloc(512, "l2i")

    def make_rows(inv_col):
        p.v(lambda e: e.tensor_scalar_max(mrow.ap(0, 1, 0, T), mrow.ap(0, 1, 0, T), 1e-5), inc=True)
        p.v(lambda e: e.reciprocal(w1b.ap(0, 1, 0, T), mrow.ap(0, 1, 0, T)))
        p.v(lambda e: e.tensor_scalar_mul(w1b.ap(0, 1, 0, T), w1b.ap(0, 1, 0, T), 127.0))
        p.v(lambda e: e.tensor_scalar(w2b.ap(0, 1, 0, T), mrow.ap(0, 1, 0, T), inv_col,
                                      float(1.0 / 127.0), ALU.mult, ALU.mult), inc=True)
        bcast_row(p, lambda c0, c1: w1b.ap(0, 1, c0, c1), T, sxB)
        bcast_row(p, lambda c0, c1: w2b.ap(0, 1, c0, c1), T, hsB)

    def wrow(c0, w, pb):
        p.v(lambda e, pb=pb, c0=c0, w=w: e.tensor_copy(mrow.ap(0, 1, c0, c0 + w), pb[0:1, 0:w]), inc=True)

    def stats_rows_finish():
        tvs = p.sems["tensor"].n
        p.wait("vector", p.sems["tensor"], tvs)
        for j in range(4):
            c0 = j * 512
            p.v(lambda e, j=j, c0=c0: e.tensor_scalar_mul(rows.ap(0, 1, T + c0, T + c0 + 512),
                                                          p.ps[3 + j][0:1, 0:512], 1.0 / 4096.0))
            p.v(lambda e, j=j, c0=c0: e.tensor_mul(sfb.ap(0, 1, 0, 512), rows.ap(0, 1, c0, c0 + 512),
                                                   rows.ap(0, 1, c0, c0 + 512)))
            p.v(lambda e, j=j, c0=c0: e.tensor_sub(rows.ap(0, 1, T + c0, T + c0 + 512),
                                                   rows.ap(0, 1, T + c0, T + c0 + 512),
                                                   sfb.ap(0, 1, 0, 512)), inc=True)
            p.wait("scalar", p.sems["vector"])
            p.s(lambda e, c0=c0: e.activation(rows.ap(0, 1, T + c0, T + c0 + 512),
                                              rows.ap(0, 1, T + c0, T + c0 + 512), AF.Sqrt,
                                              bias=p.scol("k_eps5")[0:1, 0:1]), inc=True)
            p.wait("vector", p.sems["scalar"])
            p.v(lambda e, c0=c0: e.reciprocal(rows.ap(0, 1, T + c0, T + c0 + 512),
                                              rows.ap(0, 1, T + c0, T + c0 + 512)), inc=True)

    xnl = Buf(sxB.t, sxB.off, T, F32)
    for passi in range(2):
        for r in range(NRC):
            p.wait("sync", p.sems["tensor"])
            p.wait("sync", p.sems["scalar"])
            dvx = p.dma(xnl.ap(), scr["xnf"][r * 128:(r + 1) * 128, :])
            if passi == 1:
                p.wait("scalar", p.sem_dma, dvx)
                p.wait("scalar", p.sems["tensor"])
                p.s(lambda e: e.activation(w1b.ap(), xnl.ap(), AF.Square), inc=True)
                p.wait("tensor", p.sems["scalar"])
                srcb = w1b
            else:
                p.wait("tensor", p.sem_dma, dvx)
                srcb = xnl
            if r == 0:
                p.wait("tensor", p.sems["vector"])
            for j in range(4):
                p.t(lambda e, j=j, r=r, srcb=srcb: e.matmul(
                    p.ps[3 + j][0:1, 0:512], sb["sb_ones"][:, 0:1],
                    srcb.ap(0, 128, j * 512, j * 512 + 512), start=(r == 0), stop=(r == NRC - 1)))
            p.pe_done()
        tvs = p.sems["tensor"].n
        p.wait("vector", p.sems["tensor"], tvs)
        for j in range(4):
            c0 = j * 512
            if passi == 0:
                p.v(lambda e, j=j, c0=c0: e.tensor_scalar_mul(rows.ap(0, 1, c0, c0 + 512),
                                                              p.ps[3 + j][0:1, 0:512], 1.0 / D), inc=(j == 3))
            else:
                p.v(lambda e, j=j, c0=c0: e.tensor_scalar_mul(rows.ap(0, 1, T + c0, T + c0 + 512),
                                                              p.ps[3 + j][0:1, 0:512], 1.0 / D))
                p.v(lambda e, j=j, c0=c0: e.tensor_mul(sfb.ap(0, 1, 0, 512), rows.ap(0, 1, c0, c0 + 512),
                                                       rows.ap(0, 1, c0, c0 + 512)))
                p.v(lambda e, j=j, c0=c0: e.tensor_sub(rows.ap(0, 1, T + c0, T + c0 + 512),
                                                       rows.ap(0, 1, T + c0, T + c0 + 512),
                                                       sfb.ap(0, 1, 0, 512)), inc=True)
                p.wait("scalar", p.sems["vector"])
                p.s(lambda e, c0=c0: e.activation(rows.ap(0, 1, T + c0, T + c0 + 512),
                                                  rows.ap(0, 1, T + c0, T + c0 + 512), AF.Sqrt,
                                                  bias=p.scol("k_eps5")[0:1, 0:1]), inc=True)
                p.wait("vector", p.sems["scalar"])
                p.v(lambda e, c0=c0: e.reciprocal(rows.ap(0, 1, T + c0, T + c0 + 512),
                                                  rows.ap(0, 1, T + c0, T + c0 + 512)), inc=True)
    bcast_row(p, lambda c0, c1: rows.ap(0, 1, c0, c1), T, negmuB, scale=-1.0)
    bcast_row(p, lambda c0, c1: rows.ap(0, 1, T + c0, T + c1), T, rstdB)

    h1 = ba.alloc(8 * T, "h1")
    xq = h1
    for r in range(NRC):
        p.wait("sync", p.sems["vector"])
        dvx = p.dma(xnl.ap(), scr["xnf"][r * 128:(r + 1) * 128, :])
        p.wait("vector", p.sem_dma, dvx)
        p.v(lambda e: e.tensor_add(w1b.ap(), xnl.ap(), negmuB.ap()))
        p.v(lambda e: e.tensor_mul(w1b.ap(), w1b.ap(), rstdB.ap()))
        p.v(lambda e, r=r: e.tensor_scalar(h1.ap(0, 128, r * T, (r + 1) * T), w1b.ap(),
                                           p.scol("l2_ln1w", r), p.scol("l2_ln1b", r),
                                           ALU.mult, ALU.add), inc=True)
        p.wait("scalar", p.sems["vector"])
        p.s(lambda e, r=r: e.activation(hsB.ap(), h1.ap(0, 128, r * T, (r + 1) * T), AF.Abs), inc=True)
        p.wait("vector", p.sems["scalar"])
        if r == 0:
            p.v(lambda e: e.tensor_copy(w2b.ap(), hsB.ap()), inc=True)
        else:
            p.v(lambda e: e.tensor_max(w2b.ap(), w2b.ap(), hsB.ap()), inc=True)
    part_max_to_row(p, w2b, T, wrow, scr=w1b)
    make_rows(sc.ap(0, 1, 0, 1))
    for r in range(NRC):
        for qc in range(0, T, 512):
            quant_tile(p, h1.ap(0, 128, r * T + qc, r * T + qc + 512), sxB.ap(0, 128, qc, qc + 512),
                       xq.ap(0, 128, r * T + qc, r * T + qc + 512),
                       w1b.ap(0, 128, qc, qc + 512), it.ap(0, 128, 0, 512))

    hb = ba.alloc(T, "hb2")
    for mi in range(32):
        def evac_h(mi2, c0, c1, pband, tv, mi=mi):
            p.wait("vector", p.sems["tensor"], tv)
            p.v(lambda e, c0=c0, c1=c1, pband=pband: e.tensor_mul(
                w1b.ap(0, 128, c0, c1), pband[:, 0:c1 - c0], hsB.ap(0, 128, c0, c1)))
            p.v(lambda e, c0=c0, c1=c1, mi=mi: e.tensor_scalar_add(
                w1b.ap(0, 128, c0, c1), w1b.ap(0, 128, c0, c1), p.scol("l2_b1", mi)), inc=True)
        emit_linear(p, None, 32, D, lambda k, c0, c1: xq.ap(0, 128, k * T + c0, k * T + c1),
                    0, T, evac_h, m_list=[mi], wq_bf16_dram=scr["w1q"], nbanks=2)
        p.wait("scalar", p.sems["vector"])
        p.wait("scalar", p.sem_dma)
        p.s(lambda e: e.activation(hb.ap(), w1b.ap(), AF.Gelu), inc=True)
        p.wait("sync", p.sems["scalar"])
        p.dma(scr["s"][mi * 128:(mi + 1) * 128, :], hb.ap())
        p.wait("tensor", p.sems["scalar"])
        if mi == 0:
            p.wait("tensor", p.sems["vector"])
        for j in range(4):
            p.t(lambda e, j=j, mi=mi: e.matmul(p.ps[3 + j][0:1, 0:512], p.onesb.ap(0, 128, 0, 1),
                                               hb.ap(0, 128, j * 512, j * 512 + 512),
                                               start=(mi == 0), stop=(mi == 31)))
        p.pe_done()
    tvs = p.sems["tensor"].n
    p.wait("vector", p.sems["tensor"], tvs)
    for j in range(4):
        c0 = j * 512
        p.v(lambda e, j=j, c0=c0: e.tensor_scalar_mul(rows.ap(0, 1, c0, c0 + 512),
                                                      p.ps[3 + j][0:1, 0:512], 1.0 / 4096.0), inc=(j == 3))
    for mi in range(32):
        p.wait("sync", p.sems["tensor"])
        p.wait("sync", p.sems["scalar"])
        dvx = p.dma(hb.ap(), scr["s"][mi * 128:(mi + 1) * 128, :])
        p.wait("scalar", p.sem_dma, dvx)
        p.wait("scalar", p.sems["tensor"])
        p.s(lambda e: e.activation(w1b.ap(), hb.ap(), AF.Square), inc=True)
        p.wait("tensor", p.sems["scalar"])
        if mi == 0:
            p.wait("tensor", p.sems["vector"])
        for j in range(4):
            p.t(lambda e, j=j, mi=mi: e.matmul(p.ps[3 + j][0:1, 0:512], sb["sb_ones"][:, 0:1],
                                               w1b.ap(0, 128, j * 512, j * 512 + 512),
                                               start=(mi == 0), stop=(mi == 31)))
        p.pe_done()
    stats_rows_finish()
    bcast_row(p, lambda c0, c1: rows.ap(0, 1, c0, c1), T, negmuB, scale=-1.0)
    bcast_row(p, lambda c0, c1: rows.ap(0, 1, T + c0, T + c1), T, rstdB)

    ba.reset(bamark)
    hq = ba.alloc(32 * 512, "hq2")
    gld = ba.alloc(512, "gld")

    def h2_tile(k, c0, out_ap):
        p.v(lambda e, c0=c0: e.tensor_add(sfb.ap(0, 128, 0, 512), gld.ap(0, 128, 0, 512),
                                          negmuB.ap(0, 128, c0, c0 + 512)))
        p.v(lambda e, c0=c0: e.tensor_mul(sfb.ap(0, 128, 0, 512), sfb.ap(0, 128, 0, 512),
                                          rstdB.ap(0, 128, c0, c0 + 512)))
        p.v(lambda e, k=k, out_ap=out_ap: e.tensor_scalar(out_ap, sfb.ap(0, 128, 0, 512),
                                                          p.scol("l2_ln2w", k), p.scol("l2_ln2b", k),
                                                          ALU.mult, ALU.add), inc=True)

    for nchunk in range(4):
        c0 = nchunk * 512
        c1 = c0 + 512
        for k in range(32):
            p.wait("sync", p.sems["vector"])
            p.wait("sync", p.sems["scalar"])
            dvv = p.dma(gld.ap(0, 128, 0, 512), scr["s"][k * 128:(k + 1) * 128, c0:c1])
            p.wait("vector", p.sem_dma, dvv)
            h2_tile(k, c0, w1b.ap(0, 128, 1024, 1536))
            p.wait("scalar", p.sems["vector"])
            p.s(lambda e: e.activation(w1b.ap(0, 128, 512, 1024),
                                       w1b.ap(0, 128, 1024, 1536), AF.Abs), inc=True)
            p.wait("vector", p.sems["scalar"])
            if k == 0:
                p.v(lambda e: e.tensor_copy(w2b.ap(0, 128, 0, 512), w1b.ap(0, 128, 512, 1024)), inc=True)
            else:
                p.v(lambda e: e.tensor_max(w2b.ap(0, 128, 0, 512), w2b.ap(0, 128, 0, 512),
                                           w1b.ap(0, 128, 512, 1024)), inc=True)
        def wrow_c(cc0, w, pb, c0=c0):
            p.v(lambda e, pb=pb, cc0=cc0, w=w: e.tensor_copy(mrow.ap(0, 1, c0 + cc0, c0 + cc0 + w),
                                                             pb[0:1, 0:w]), inc=True)
        part_max_to_row(p, Buf(w2b.t, w2b.off, 512, F32), 512, wrow_c,
                        scr=Buf(w1b.t, w1b.off + 512, 512, F32))
        p.v(lambda e, c0=c0, c1=c1: e.tensor_scalar_max(mrow.ap(0, 1, c0, c1), mrow.ap(0, 1, c0, c1),
                                                        1e-5), inc=True)
        p.v(lambda e, c0=c0, c1=c1: e.reciprocal(w1b.ap(0, 1, 0, 512), mrow.ap(0, 1, c0, c1)))
        p.v(lambda e: e.tensor_scalar_mul(w1b.ap(0, 1, 0, 512), w1b.ap(0, 1, 0, 512), 127.0))
        p.v(lambda e, c0=c0, c1=c1: e.tensor_scalar(w2b.ap(0, 1, 0, 512), mrow.ap(0, 1, c0, c1),
                                                    sc.ap(0, 1, 1, 2), float(1.0 / 127.0),
                                                    ALU.mult, ALU.mult), inc=True)
        bcast_row(p, lambda cc0, cc1: w1b.ap(0, 1, cc0, cc1), 512, sxB, out_c0=c0)
        bcast_row(p, lambda cc0, cc1: w2b.ap(0, 1, cc0, cc1), 512, hsB, out_c0=c0)
        for k in range(32):
            p.wait("sync", p.sems["vector"])
            dvv = p.dma(gld.ap(0, 128, 0, 512), scr["s"][k * 128:(k + 1) * 128, c0:c1])
            p.wait("vector", p.sem_dma, dvv)
            h2_tile(k, c0, w1b.ap(0, 128, 1024, 1536))
            quant_tile(p, w1b.ap(0, 128, 1024, 1536), sxB.ap(0, 128, c0, c1),
                       hq.ap(0, 128, k * 512, (k + 1) * 512), w1b.ap(0, 128, 512, 1024),
                       it.ap(0, 128, 0, 512))
        def evac_o(mi, cc0, cc1, pband, tv, c0=c0):
            p.wait("vector", p.sems["tensor"], tv)
            p.v(lambda e, pband=pband, c0=c0: e.tensor_mul(sfb.ap(0, 128, 0, 512), pband[:, 0:512],
                                                           hsB.ap(0, 128, c0, c0 + 512)))
            p.v(lambda e, mi=mi, c0=c0: e.scalar_tensor_tensor(
                xacc.ap(0, 128, mi * T + c0, mi * T + c0 + 512), sfb.ap(0, 128, 0, 512),
                p.scol("l2_b2", mi), xacc.ap(0, 128, mi * T + c0, mi * T + c0 + 512),
                ALU.add, ALU.add), inc=True)
        emit_linear(p, None, 8, 4096, lambda k, cc0, cc1: hq.ap(0, 128, k * 512, (k + 1) * 512),
                    c0, c1, evac_o, wq_bf16_dram=scr["w2q"])
    fa.reset(famark)
    ba.reset(bamark)
    p.ia.reset(0)





def emit_l3_attn(p, xacc, xn_dram, negmuB, rstdB, scr):
    sb = p.sb
    fa, ba = p.fa, p.ba
    famark, bamark = fa.mark(), ba.mark()
    ones_t = negmuB
    tmp = None
    p.wait("vector", p.sem_dma)
    p.v(lambda e: e.memset(ones_t.ap(), 1.0), inc=True)

    qz = fa.alloc(T, "qz")
    kz = fa.alloc(T, "kz")
    kvz = fa.alloc(T, "kvz")
    kcum = fa.alloc(T, "kcum")
    kvcum = Buf(rstdB.t, rstdB.off, T, F32)
    ytl = ba.alloc(8 * T, "ytl")
    vb = ba.alloc(T, "vb")
    xsl = ba.alloc(8 * 512, "xsl")

    def elu1(buf, tmp):
        p.v(lambda e: e.tensor_scalar_min(tmp.ap(), buf.ap(), 0.0))
        p.wait("scalar", p.sems["vector"])
        p.v(lambda e: e.tensor_scalar_max(buf.ap(), buf.ap(), 0.0), inc=True)
        p.wait("scalar", p.sems["vector"])
        p.s(lambda e: e.activation(tmp.ap(), tmp.ap(), AF.Exp), inc=True)
        p.wait("vector", p.sems["scalar"])
        p.v(lambda e: e.tensor_add(buf.ap(), buf.ap(), tmp.ap()), inc=True)

    for r in range(NRC):
        def evac_qkv(mi, c0, c1, pband, tv, r=r):
            w = c1 - c0
            p.wait("vector", p.sems["tensor"], tv)
            if mi < 8:
                p.v(lambda e, pband=pband, w=w, mi=mi, c0=c0, c1=c1: e.tensor_scalar(
                    qz.ap(0, 128, c0, c1), pband[:, 0:w], p.scol("l3_bqkv", mi),
                    p.scol("l3_scale"), ALU.add, ALU.mult), inc=True)
            elif mi < 16:
                p.v(lambda e, pband=pband, w=w, mi=mi, c0=c0, c1=c1: e.tensor_scalar(
                    kz.ap(0, 128, c0, c1), pband[:, 0:w], p.scol("l3_bqkv", mi),
                    p.scol("l3_scale"), ALU.add, ALU.mult), inc=True)
            else:
                p.v(lambda e, pband=pband, w=w, mi=mi, c0=c0, c1=c1: e.tensor_scalar_add(
                    vb.ap(0, 128, c0, c1), pband[:, 0:w], p.scol("l3_bqkv", mi)), inc=True)
        for c0 in range(0, T, 512):
            p.wait("sync", p.sems["tensor"])
            dvx = load_xn_slice(p, xn_dram, xsl, c0, c0 + 512)
            p.wait("tensor", p.sem_dma, dvx)
            emit_linear(p, "l3_wqkv", 24, D,
                        lambda k, cc0, cc1, c0=c0: xsl.ap(0, 128, k * 512 + (cc0 - c0), k * 512 + (cc1 - c0)),
                        c0, c0 + 512, evac_qkv, m_list=[r, 8 + r, 16 + r])
        elu1(qz, kcum)
        elu1(kz, kcum)
        p.v(lambda e: e.tensor_mul(kvz.ap(), kz.ap(), vb.ap()), inc=True)
        p.v(lambda e: e.tensor_tensor_scan(kcum.ap(), ones_t.ap(), kz.ap(), 0.0, ALU.mult, ALU.add))
        p.v(lambda e: e.tensor_tensor_scan(kvcum.ap(), ones_t.ap(), kvz.ap(), 0.0, ALU.mult, ALU.add),
            inc=True)
        tmp = kz
        p.v(lambda e: e.tensor_mul(tmp.ap(), qz.ap(), kcum.ap()), inc=True)
        for c0 in range(0, T, 512):
            p.wait("tensor", p.sems["vector"])
            p.t(lambda e, c0=c0: e.matmul(p.ps[5][0:2, 0:512], p.small("ones64bd"),
                                          tmp.ap(0, 128, c0, c0 + 512), start=True, stop=True))
            tv = p.pe_done()
            p.wait("vector", p.sems["tensor"], tv)
            p.v(lambda e, c0=c0: e.tensor_copy(kcum.ap(0, 2, c0, c0 + 512), p.ps[5][0:2, 0:512]),
                inc=True)
        p.v(lambda e: e.tensor_scalar_max(kcum.ap(0, 2, 0, T), kcum.ap(0, 2, 0, T), 1e-3))
        p.v(lambda e: e.reciprocal(kcum.ap(0, 2, 0, T), kcum.ap(0, 2, 0, T)), inc=True)
        p.v(lambda e: e.tensor_mul(tmp.ap(), qz.ap(), kvcum.ap()), inc=True)
        for c0 in range(0, T, 512):
            p.wait("tensor", p.sems["vector"])
            p.t(lambda e, c0=c0: e.matmul(p.ps[4][:, 0:512], sb["sb_bd2"][:, :],
                                          kcum.ap(0, 2, c0, c0 + 512), start=True, stop=True))
            tv = p.pe_done()
            p.wait("vector", p.sems["tensor"], tv)
            p.v(lambda e, r=r, c0=c0: e.tensor_mul(ytl.ap(0, 128, r * T + c0, r * T + c0 + 512),
                                                   tmp.ap(0, 128, c0, c0 + 512), p.ps[4][:, 0:512]),
                inc=True)
    def evac_proj(mi, c0, c1, pband, tv):
        p.wait("vector", p.sems["tensor"], tv)
        p.v(lambda e, mi=mi, c0=c0, c1=c1, pband=pband: e.scalar_tensor_tensor(
            xacc.ap(0, 128, mi * T + c0, mi * T + c1), pband[:, 0:c1 - c0],
            p.scol("l3_bproj", mi), xacc.ap(0, 128, mi * T + c0, mi * T + c1),
            ALU.add, ALU.add), inc=True)
    emit_linear(p, "l3_wproj", 8, D, lambda k, c0, c1: ytl.ap(0, 128, k * T + c0, k * T + c1),
                0, T, evac_proj)
    fa.reset(famark)
    ba.reset(bamark)


def emit_l3_ffn(p, xacc, xn_dram, negmuB, rstdB, scr):
    sb = p.sb
    fa, ba = p.fa, p.ba
    famark, bamark = fa.mark(), ba.mark()

    lgt = fa.alloc(T, "lgt")
    selx = fa.alloc(T, "selx")
    srow0 = fa.alloc(64, "srow0")
    xsl = ba.alloc(8 * 512, "xsl")

    def evac_lgt(mi, c0, c1, pband, tv):
        p.wait("vector", p.sems["tensor"], tv)
        p.v(lambda e, c0=c0, c1=c1, pband=pband: e.tensor_copy(lgt.ap(0, 128, c0, c1),
                                                               pband[:, 0:c1 - c0]), inc=True)
    for c0 in range(0, T, 512):
        p.wait("sync", p.sems["tensor"])
        dvx = load_xn_slice(p, xn_dram, xsl, c0, c0 + 512)
        p.wait("tensor", p.sem_dma, dvx)
        emit_linear(p, "l3_wg", 1, D,
                    lambda k, cc0, cc1, c0=c0: xsl.ap(0, 128, k * 512 + (cc0 - c0), k * 512 + (cc1 - c0)),
                    c0, c0 + 512, evac_lgt)

    for t0 in range(0, T, 128):
        pb = p.ps[6]
        p.wait("tensor", p.sems["vector"])
        p.t(lambda e, t0=t0, pb=pb: e.transpose(pb[:, 0:128], lgt.ap(0, 128, t0, t0 + 128),
                                                sb["sb_ident"][:]))
        tv = p.pe_done()
        p.wait("vector", p.sems["tensor"], tv)
        p.vtiny(lambda e, pb=pb: e.tensor_copy(srow0.ap(0, 128, 0, 8), pb[:, 0:8]))
        p.vtiny(lambda e: e.reduce_max(srow0.ap(0, 128, 8, 9), srow0.ap(0, 128, 0, 8), axis=AX.X))
        p.vtiny(lambda e: e.tensor_single_scalar(srow0.ap(0, 128, 16, 24), srow0.ap(0, 128, 0, 8),
                                                 srow0.ap(0, 128, 8, 9), ALU.is_ge))
        p.vtiny(lambda e: e.tensor_tensor_scan(srow0.ap(0, 128, 24, 32), sb["sb_ones"][:, 0:8],
                                               srow0.ap(0, 128, 16, 24), 0.0, ALU.mult, ALU.add))
        p.vtiny(lambda e: e.tensor_single_scalar(srow0.ap(0, 128, 24, 32), srow0.ap(0, 128, 24, 32),
                                                 1.0, ALU.is_equal))
        p.vtiny(lambda e: e.tensor_mul(srow0.ap(0, 128, 16, 24), srow0.ap(0, 128, 16, 24),
                                       srow0.ap(0, 128, 24, 32)), inc=True)
        p.wait("tensor", p.sems["vector"])
        p.t(lambda e, pb=pb: e.transpose(pb[0:8, 0:128], srow0.ap(0, 128, 16, 24), sb["sb_ident"][:]))
        tv = p.pe_done()
        p.wait("vector", p.sems["tensor"], tv)
        p.v(lambda e, t0=t0, pb=pb: e.tensor_copy(selx.ap(0, 8, t0, t0 + 128), pb[0:8, 0:128]),
            inc=True)
    p.wait("sync", p.sems["vector"])
    dvsel = p.dma(scr["sel"][:, :], selx.ap(0, 8, 0, T))
    p.wait("vector", p.sem_dma)
    p.wait("sync", p.sem_dma, dvsel)
    fa.reset(lgt.off)
    rows = fa.alloc(2 * T, "l3rows")
    selB = fa.alloc(T, "selB")
    srow = fa.alloc(T, "srow")
    bcol = fa.alloc(4, "bcol")

    xnl = Buf(selB.t, selB.off, T, F32)
    for passi in range(2):
        for r in range(NRC):
            p.wait("sync", p.sems["tensor"])
            p.wait("sync", p.sems["scalar"])
            dvx = p.dma(xnl.ap(), scr["xnf"][r * 128:(r + 1) * 128, :])
            if passi == 1:
                p.wait("scalar", p.sem_dma, dvx)
                p.wait("scalar", p.sems["tensor"])
                p.s(lambda e: e.activation(srow.ap(), xnl.ap(), AF.Square), inc=True)
                p.wait("tensor", p.sems["scalar"])
                srcb = srow
            else:
                p.wait("tensor", p.sem_dma, dvx)
                srcb = xnl
            if r == 0:
                p.wait("tensor", p.sems["vector"])
            for j in range(4):
                p.t(lambda e, j=j, r=r, srcb=srcb: e.matmul(
                    p.ps[3 + j][0:1, 0:512], sb["sb_ones"][:, 0:1],
                    srcb.ap(0, 128, j * 512, j * 512 + 512), start=(r == 0), stop=(r == NRC - 1)))
            p.pe_done()
        tvs = p.sems["tensor"].n
        p.wait("vector", p.sems["tensor"], tvs)
        for j in range(4):
            c0 = j * 512
            if passi == 0:
                p.v(lambda e, j=j, c0=c0: e.tensor_scalar_mul(rows.ap(0, 1, c0, c0 + 512),
                                                              p.ps[3 + j][0:1, 0:512], 1.0 / D),
                    inc=(j == 3))
            else:
                p.v(lambda e, j=j, c0=c0: e.tensor_scalar_mul(rows.ap(0, 1, T + c0, T + c0 + 512),
                                                              p.ps[3 + j][0:1, 0:512], 1.0 / D))
                p.v(lambda e, j=j, c0=c0: e.tensor_mul(srow.ap(0, 1, 0, 512), rows.ap(0, 1, c0, c0 + 512),
                                                       rows.ap(0, 1, c0, c0 + 512)))
                p.v(lambda e, j=j, c0=c0: e.tensor_sub(rows.ap(0, 1, T + c0, T + c0 + 512),
                                                       rows.ap(0, 1, T + c0, T + c0 + 512),
                                                       srow.ap(0, 1, 0, 512)), inc=True)
                p.wait("scalar", p.sems["vector"])
                p.s(lambda e, c0=c0: e.activation(rows.ap(0, 1, T + c0, T + c0 + 512),
                                                  rows.ap(0, 1, T + c0, T + c0 + 512), AF.Sqrt,
                                                  bias=p.scol("k_eps5")[0:1, 0:1]), inc=True)
                p.wait("vector", p.sems["scalar"])
                p.v(lambda e, c0=c0: e.reciprocal(rows.ap(0, 1, T + c0, T + c0 + 512),
                                                  rows.ap(0, 1, T + c0, T + c0 + 512)), inc=True)
    bcast_row(p, lambda c0, c1: rows.ap(0, 1, c0, c1), T, negmuB, scale=-1.0)
    bcast_row(p, lambda c0, c1: rows.ap(0, 1, T + c0, T + c1), T, rstdB)
    xht = ba.alloc(8 * T, "xht")
    for r in range(NRC):
        p.wait("sync", p.sems["vector"])
        dvx = p.dma(xnl.ap(), scr["xnf"][r * 128:(r + 1) * 128, :])
        p.wait("vector", p.sem_dma, dvx)
        p.v(lambda e: e.tensor_add(srow.ap(), xnl.ap(), negmuB.ap()))
        p.v(lambda e, r=r: e.tensor_mul(xht.ap(0, 128, r * T, (r + 1) * T), srow.ap(), rstdB.ap()),
            inc=True)

    he = ba.alloc(2 * T, "he")
    wstg = fa.alloc(1024, "wstg")
    wbfU = ba.alloc(1024, "wbfU")
    lnbb = ba.alloc(8, "lnbb")
    for e_i in range(8):
        p.wait("sync", p.sems["vector"])
        p.wait("sync", p.sems["tensor"])
        dvs = p.dma(srow.ap(0, 1, 0, T), scr["sel"][e_i:e_i + 1, :])
        p.wait("vector", p.sem_dma, dvs)
        p.wait("tensor", p.sem_dma, dvs)
        bcast_row(p, lambda c0, c1: srow.ap(0, 1, c0, c1), T, selB)
        for mi in range(2):
            p.wait("sync", p.sems["vector"])
            p.wait("sync", p.sems["tensor"])
            dvv = p.dma(wstg.ap().rearrange("p (k c) -> p k c", c=128),
                        sb[f"l3_w1_{e_i}"][mi].rearrange("(k p) c -> p k c", p=128))
            p.wait("vector", p.sem_dma, dvv)
            p.v(lambda e: e.tensor_copy(wbfU.ap(), wstg.ap()), inc=True)
            for k in range(8):
                p.vtiny(lambda e, k=k, e_i=e_i: e.tensor_copy(lnbb.ap(0, 128, k, k + 1),
                                                              p.scol(f"l3_lnb_{e_i}", k)),
                        inc=(k == 7))
            p.wait("tensor", p.sems["vector"])
            for k in range(8):
                p.t(lambda e, k=k: e.matmul(p.ps[5][:, 0:1], wbfU.ap(0, 128, k * 128, (k + 1) * 128),
                                            lnbb.ap(0, 128, k, k + 1), start=(k == 0), stop=(k == 7)))
            tv = p.pe_done()
            p.wait("vector", p.sems["tensor"], tv)
            p.vtiny(lambda e, mi=mi, e_i=e_i: e.scalar_tensor_tensor(
                bcol.ap(0, 128, mi, mi + 1), p.ps[5][:, 0:1], 1.0,
                p.scol(f"l3_b1_{e_i}", mi), ALU.mult, ALU.add), inc=True)
            for k in range(8):
                p.v(lambda e, k=k, e_i=e_i: e.tensor_scalar_mul(
                    wbfU.ap(0, 128, k * 128, (k + 1) * 128), wbfU.ap(0, 128, k * 128, (k + 1) * 128),
                    p.scol(f"l3_lns_{e_i}", k)), inc=(k == 7))
            p.wait("tensor", p.sems["vector"])
            for ci in range(4):
                c0 = ci * 512
                pband = p.ps[ci % 2]
                p.wait("tensor", p.sems["scalar"])
                for k in range(8):
                    p.t(lambda e, pband=pband, k=k, c0=c0: e.matmul(
                        pband[:, 0:512], wbfU.ap(0, 128, k * 128, (k + 1) * 128),
                        xht.ap(0, 128, k * T + c0, k * T + c0 + 512), start=(k == 0), stop=(k == 7)))
                tv = p.pe_done()
                p.wait("scalar", p.sems["tensor"], tv)
                p.wait("scalar", p.sems["vector"])
                p.s(lambda e, mi=mi, c0=c0, pband=pband: e.activation(
                    he.ap(0, 128, mi * T + c0, mi * T + c0 + 512), pband[:, 0:512], AF.Gelu,
                    bias=bcol.ap(0, 128, mi, mi + 1)), inc=True)
        def evac_o(mi, c0, c1, pband, tv, e_i=e_i):
            p.wait("vector", p.sems["tensor"], tv)
            p.v(lambda e, mi=mi, c0=c0, c1=c1, pband=pband, e_i=e_i: e.tensor_scalar_add(
                srow.ap(0, 128, 0, 512), pband[:, 0:c1 - c0], p.scol(f"l3_b2_{e_i}", mi)))
            p.v(lambda e, c0=c0, c1=c1: e.tensor_mul(srow.ap(0, 128, 0, 512), srow.ap(0, 128, 0, 512),
                                                     selB.ap(0, 128, c0, c1)))
            p.v(lambda e, mi=mi, c0=c0, c1=c1: e.tensor_add(
                xacc.ap(0, 128, mi * T + c0, mi * T + c1),
                xacc.ap(0, 128, mi * T + c0, mi * T + c1), srow.ap(0, 128, 0, 512)), inc=True)
        emit_linear(p, f"l3_w2_{e_i}", 8, 256,
                    lambda k, c0, c1: he.ap(0, 128, k * T + c0, k * T + c1),
                    0, T, evac_o)
    fa.reset(famark)
    ba.reset(bamark)


def emit_l1_attn(p, xacc, xn_dram, negmuB, rstdB, scr):
    sb = p.sb
    fa, ba = p.fa, p.ba
    famark, bamark = fa.mark(), ba.mark()
    C = 32
    BCL = 43.0

    sdT = ba.alloc(4 * T, "sdT")
    aubuf = ba.alloc(T, "aubuf")
    stg = [ba.alloc(T, "stgA"), ba.alloc(T, "stgB")]
    hb = ba.alloc(NRC * T, "hb")
    spikeT = Buf(ba.t, hb.off, 4 * T, BF16)

    qtmp = Buf(negmuB.t, negmuB.off, T, F32)
    bcB = Buf(rstdB.t, rstdB.off, T, F32)
    rows_t = fa.alloc(T, "l1rows")
    sq_t = fa.alloc(T, "l1sq")

    for r in range(NRC):
        p.wait("sync", p.sems["scalar"])
        dvx = p.dma(qtmp.ap(), scr["xnf"][r * 128:(r + 1) * 128, :])
        p.wait("scalar", p.sem_dma, dvx)
        p.wait("scalar", p.sems["tensor"])
        p.s(lambda e: e.activation(sq_t.ap(), qtmp.ap(), AF.Square), inc=True)
        p.wait("tensor", p.sems["scalar"])
        if r == 0:
            p.wait("tensor", p.sems["vector"])
        for j in range(4):
            p.t(lambda e, j=j, r=r: e.matmul(p.ps[3 + j][0:1, 0:512], sb["sb_ones"][:, 0:1],
                                             sq_t.ap(0, 128, j * 512, (j + 1) * 512),
                                             start=(r == 0), stop=(r == NRC - 1)))
        p.pe_done()
    tvs = p.sems["tensor"].n
    p.wait("vector", p.sems["tensor"], tvs)
    for j in range(4):
        p.v(lambda e, j=j: e.tensor_copy(rows_t.ap(0, 1, j * 512, (j + 1) * 512),
                                         p.ps[3 + j][0:1, 0:512]), inc=(j == 3))
    p.wait("scalar", p.sems["vector"])
    p.s(lambda e: e.activation(rows_t.ap(0, 1, 0, T), rows_t.ap(0, 1, 0, T), AF.Sqrt,
                               scale=float(1.0 / D), bias=p.scol("k_eps6")[0:1, 0:1]), inc=True)
    p.wait("vector", p.sems["scalar"])
    p.v(lambda e: e.reciprocal(rows_t.ap(0, 1, 0, T), rows_t.ap(0, 1, 0, T)), inc=True)
    bcast_row(p, lambda c0, c1: rows_t.ap(0, 1, c0, c1), T, bcB)
    for r in range(NRC):
        p.wait("sync", p.sems["vector"])
        dvx = p.dma(qtmp.ap(), scr["xnf"][r * 128:(r + 1) * 128, :])
        p.wait("vector", p.sem_dma, dvx)
        p.wait("vector", p.sems["scalar"])
        p.v(lambda e, r=r: e.scalar_tensor_tensor(sq_t.ap(), qtmp.ap(), p.scol("l1_wnorm", r),
                                                  bcB.ap(), ALU.mult, ALU.mult), inc=True)
        p.v(lambda e, r=r: e.tensor_scalar(rows_t.ap(), sq_t.ap(), p.scol("l1_convw3", r),
                                           p.scol("l1_convb", r), ALU.mult, ALU.add))
        for j in range(3):
            s_ = 3 - j
            p.v(lambda e, j=j, s_=s_, r=r: e.scalar_tensor_tensor(
                rows_t.ap(0, 128, s_, T), sq_t.ap(0, 128, 0, T - s_), p.scol(f"l1_convw{j}", r),
                rows_t.ap(0, 128, s_, T), ALU.mult, ALU.add), inc=(j == 2))
        p.wait("scalar", p.sems["vector"])
        p.s(lambda e, r=r: e.activation(hb.ap(0, 128, r * T, (r + 1) * T), rows_t.ap(),
                                        AF.Silu), inc=True)
        p.wait("vector", p.sems["scalar"])
    if DBG_L1 == "h":
        p.wait("sync", p.sems["scalar"])
        for r in range(NRC):
            p.dma(sb["dbg"][r * 128:(r + 1) * 128, :], hb.ap(0, 128, r * T, (r + 1) * T))
        p.wait("sync", p.sem_dma, p.sem_dma.n)
        fa.reset(famark); ba.reset(bamark)
        return

    stg_dv = [0, 0]

    def finish_qk(mi):
        p.wait("scalar", p.sems["vector"])
        p.s(lambda e: e.activation(sq_t.ap(), qtmp.ap(), AF.Square), inc=True)
        p.wait("tensor", p.sems["scalar"])
        for c0 in range(0, T, 512):
            p.wait("tensor", p.sems["vector"])
            p.t(lambda e, c0=c0: e.matmul(p.ps[5][0:2, 0:512], p.small("ones64bd"),
                                          sq_t.ap(0, 128, c0, c0 + 512), start=True, stop=True))
            tv2 = p.pe_done()
            p.wait("vector", p.sems["tensor"], tv2)
            p.v(lambda e, c0=c0: e.tensor_copy(rows_t.ap(0, 2, c0, c0 + 512), p.ps[5][0:2, 0:512]),
                inc=True)
        p.wait("scalar", p.sems["vector"])
        p.s(lambda e: e.activation(rows_t.ap(0, 2, 0, T), rows_t.ap(0, 2, 0, T), AF.Sqrt),
            inc=True)
        p.wait("vector", p.sems["scalar"])
        p.v(lambda e: e.tensor_scalar_add(rows_t.ap(0, 2, 0, T), rows_t.ap(0, 2, 0, T), 1e-6))
        p.v(lambda e: e.reciprocal(rows_t.ap(0, 2, 0, T), rows_t.ap(0, 2, 0, T)), inc=True)
        sidx = mi % 2
        p.wait("vector", p.sem_dma, stg_dv[sidx])
        for c0 in range(0, T, 512):
            p.wait("tensor", p.sems["vector"])
            p.t(lambda e, c0=c0: e.matmul(p.ps[4][:, 0:512], sb["sb_bd2"][:, :],
                                          rows_t.ap(0, 2, c0, c0 + 512), start=True, stop=True))
            tv2 = p.pe_done()
            p.wait("vector", p.sems["tensor"], tv2)
            p.v(lambda e, c0=c0, sidx=sidx: e.tensor_mul(stg[sidx].ap(0, 128, c0, c0 + 512),
                                                         qtmp.ap(0, 128, c0, c0 + 512),
                                                         p.ps[4][:, 0:512]), inc=True)
        p.wait("sync", p.sems["vector"])
        dst = scr["q"] if mi < 4 else scr["k"]
        g4 = mi % 4
        stg_dv[sidx] = p.dma(dst[g4 * 128:(g4 + 1) * 128, :], stg[sidx].ap())

    def evac_proj(mi, c0, c1, pband, tv):
        w = c1 - c0
        if mi < 8:
            p.wait("vector", p.sems["tensor"], tv)
            if c0 == 0:
                p.wait("vector", p.sems["scalar"])
            p.v(lambda e, c0=c0, c1=c1, pband=pband, w=w: e.tensor_copy(
                qtmp.ap(0, 128, c0, c1), pband[:, 0:w]), inc=True)
            if c1 == T:
                finish_qk(mi)
        elif mi < 16:
            sidx = mi % 2
            p.wait("vector", p.sems["tensor"], tv)
            if c0 == 0:
                p.wait("vector", p.sem_dma, stg_dv[sidx])
            p.v(lambda e, c0=c0, c1=c1, pband=pband, w=w, sidx=sidx: e.tensor_copy(
                stg[sidx].ap(0, 128, c0, c1), pband[:, 0:w]), inc=True)
            if c1 == T:
                p.wait("sync", p.sems["vector"])
                stg_dv[sidx] = p.dma(scr["v"][(mi - 8) * 128:(mi - 7) * 128, :], stg[sidx].ap())
        elif mi < 20:
            p.wait("vector", p.sems["tensor"], tv)
            p.v(lambda e, mi=mi, c0=c0, c1=c1, pband=pband, w=w: e.tensor_scalar_mul(
                sdT.ap(0, 128, (mi - 16) * T + c0, (mi - 16) * T + c1), pband[:, 0:w], -2.0),
                inc=True)
        elif mi == 20:
            p.wait("scalar", p.sems["tensor"], tv)
            p.s(lambda e, c0=c0, c1=c1, pband=pband, w=w: e.activation(
                aubuf.ap(0, 128, c0, c1), pband[:, 0:w], AF.Silu,
                bias=p.scol("l1_bproj", 20)), inc=True)
        else:
            p.wait("vector", p.sems["tensor"], tv)
            p.v(lambda e, c0=c0, c1=c1, pband=pband, w=w: e.tensor_scalar_add(
                rows_t.ap(0, 128, c0, c1), pband[:, 0:w], p.scol("l1_bproj", 21)), inc=True)
            if c1 == T:
                p.wait("sync", p.sems["vector"])
                p.dma(scr["bb"][:, :], rows_t.ap(0, 8, 0, T))

    emit_linear(p, "l1_wproj", 22, D,
                lambda k, c0, c1: hb.ap(0, 128, k * T + c0, k * T + c1), 0, T, evac_proj)
    p.wait("vector", p.sem_dma)

    wst64 = fa.alloc(128, "wst64")
    wbf64 = ba.alloc(128, "wbf64")
    for g4 in range(4):
        p.wait("sync", p.sems["vector"])
        dv = p.dma(wst64.ap(0, 64, 0, 128), sb["l1_wad"][g4, 0:64])
        p.wait("vector", p.sem_dma, dv)
        p.v(lambda e: e.tensor_copy(wbf64.ap(0, 64, 0, 128), wst64.ap(0, 64, 0, 128)), inc=True)
        p.wait("tensor", p.sems["vector"])
        p.wait("tensor", p.sems["scalar"])
        for c0 in range(0, T, 512):
            p.wait("tensor", p.sems["vector"])
            p.t(lambda e, c0=c0: e.matmul(p.ps[0][:, 0:512], wbf64.ap(0, 64, 0, 128),
                                          aubuf.ap(0, 64, c0, c0 + 512), start=True, stop=True))
            tv = p.pe_done()
            p.wait("vector", p.sems["tensor"], tv)
            p.v(lambda e, c0=c0, g4=g4: e.tensor_scalar_add(sq_t.ap(0, 128, c0, c0 + 512),
                                                            p.ps[0][:, 0:512],
                                                            p.scol("l1_bad", g4)), inc=True)
        p.wait("sync", p.sems["vector"])
        dvA = p.dma(scr["ab"][g4 * 128:(g4 + 1) * 128, :], sq_t.ap())
        p.wait("vector", p.sem_dma, dvA)

    fa.reset(rows_t.off)
    nh = fa.alloc(4 * T, "nh")
    w4 = fa.alloc(4, "w4")
    nh3 = fa.t[0:128, nh.off:nh.off + 4 * T].rearrange("p (g t) -> p g t", g=4)
    sd3 = ba.t[0:128, sdT.off:sdT.off + 4 * T].rearrange("p (g t) -> p g t", g=4)
    p.vtiny(lambda e: e.memset(w4.ap(0, 128, 0, 4), 0.0))
    for t in range(T):
        p.vtiny(lambda e, t=t: e.scalar_tensor_tensor(nh3[:, :, t], w4.ap(0, 128, 0, 4), 0.9,
                                                      sd3[:, :, t], ALU.mult, ALU.add))
        p.vtiny(lambda e, t=t: e.scalar_tensor_tensor(w4.ap(0, 128, 0, 4), nh3[:, :, t], -1.0,
                                                      nh3[:, :, t], ALU.is_lt, ALU.add))
    p.wait("vector", p.sems["tensor"])
    for g4 in range(4):
        p.v(lambda e, g4=g4: e.tensor_scalar(spikeT.ap(0, 128, g4 * T, (g4 + 1) * T),
                                             nh.ap(0, 128, g4 * T, (g4 + 1) * T),
                                             -1.0, 1.0, ALU.is_lt, ALU.mult), inc=(g4 == 3))
    if DBG_L1 == "scan":
        p.wait("sync", p.sems["vector"])
        for g4 in range(4):
            p.dma(sb["dbg"][1024 + g4 * 128:1024 + (g4 + 1) * 128, :],
                  spikeT.ap(0, 128, g4 * T, (g4 + 1) * T))
        p.wait("sync", p.sem_dma, p.sem_dma.n)
        fa.reset(famark); ba.reset(bamark)
        return

    fa.reset(nh.off)
    actrow = fa.alloc(T, "actrow")
    brow = fa.alloc(T, "brow")
    betaBA = fa.alloc(T, "betaBA")
    betaBB = fa.alloc(T, "betaBB")
    Spair = fa.alloc(128, "Spair")
    SpSb = fa.alloc(128, "SpS")
    g32 = fa.alloc(32, "g32")
    argx = fa.alloc(32, "argx")
    eg = fa.alloc(32, "eg")
    emg = fa.alloc(32, "emg")
    egbA = fa.alloc(32, "egbA")
    egbB = fa.alloc(32, "egbB")
    vb64 = fa.alloc(64, "vb64")
    cols8 = fa.alloc(8, "cols8")
    bdk = fa.alloc(64, "bdk")
    bdq = fa.alloc(64, "bdq")
    bdkap = fa.alloc(64, "bdkap")
    BLb = fa.alloc(64, "BL")
    BLt = fa.alloc(64, "BLt")
    Eb = fa.alloc(64, "E")
    Etb = fa.alloc(64, "Et")
    Ptm = fa.alloc(64, "Ptm")
    kap2 = fa.alloc(64, "kap2")
    Yb = fa.alloc(128, "Y")
    Wb = fa.alloc(128, "W")
    kaptb = fa.alloc(128, "kapt")
    bcolb = fa.alloc(2, "bcol")
    laf = bcB
    abf = qtmp

    ba.reset(sdT.off)
    qfb = ba.alloc(T, "qfb")
    kfb = ba.alloc(T, "kfb")
    vfA = ba.alloc(T, "vfA")
    vfB = ba.alloc(T, "vfB")
    ystg = [ba.alloc(T, "ystgA"), ba.alloc(T, "ystgB")]
    onesbd_b = ba.alloc(2, "onesbdb")
    bs2b = ba.alloc(2, "bs2b")
    p.wait("vector", p.sems["tensor"])
    p.vtiny(lambda e: e.tensor_copy(onesbd_b.ap(0, 128, 0, 2), p.small("ones64bd")), inc=True)

    id64 = sb["sb_ident"][0:64, 0:64]
    for p4 in range(DBG_L1_NPAIR):
        p.wait("sync", p.sems["vector"])
        p.wait("sync", p.sems["tensor"])
        p.wait("sync", p.sems["scalar"])
        p.wait("sync", p.sem_dma, p.sem_dma.n)
        p.dma(qfb.ap(), scr["q"][p4 * 128:(p4 + 1) * 128, :])
        p.dma(kfb.ap(), scr["k"][p4 * 128:(p4 + 1) * 128, :])
        p.dma(vfA.ap(), scr["v"][(2 * p4) * 128:(2 * p4 + 1) * 128, :])
        p.dma(vfB.ap(), scr["v"][(2 * p4 + 1) * 128:(2 * p4 + 2) * 128, :])
        p.dma(abf.ap(), scr["ab"][p4 * 128:(p4 + 1) * 128, :])
        dvall = p.dma(brow.ap(0, 2, 0, T), scr["bb"][2 * p4:2 * p4 + 2, :])
        p.wait("vector", p.sem_dma, dvall)
        p.wait("tensor", p.sem_dma, dvall)
        p.vtiny(lambda e, p4=p4: e.tensor_mul(bs2b.ap(0, 128, 0, 1), p.scol("l1_bspk", p4),
                                              p.scol("l1_mA")))
        p.vtiny(lambda e, p4=p4: e.tensor_mul(bs2b.ap(0, 128, 1, 2), p.scol("l1_bspk", p4),
                                              p.scol("l1_mB")), inc=True)
        p.wait("tensor", p.sems["vector"])
        for c0 in range(0, T, 512):
            p.wait("tensor", p.sems["vector"])
            p.t(lambda e, c0=c0, p4=p4: e.matmul(p.ps[5][0:2, 0:512], onesbd_b.ap(0, 128, 0, 2),
                                                 spikeT.ap(0, 128, p4 * T + c0, p4 * T + c0 + 512),
                                                 start=True, stop=True))
            p.t(lambda e, c0=c0, p4=p4: e.matmul(p.ps[6][0:2, 0:512], bs2b.ap(0, 128, 0, 2),
                                                 spikeT.ap(0, 128, p4 * T + c0, p4 * T + c0 + 512),
                                                 start=True, stop=True))
            tv = p.pe_done()
            p.wait("vector", p.sems["tensor"], tv)
            p.v(lambda e, c0=c0: e.tensor_scalar(actrow.ap(0, 2, c0, c0 + 512), p.ps[5][0:2, 0:512],
                                                 0.0, 1.0, ALU.is_gt, ALU.mult))
            p.v(lambda e, c0=c0: e.tensor_add(brow.ap(0, 2, c0, c0 + 512),
                                              brow.ap(0, 2, c0, c0 + 512), p.ps[6][0:2, 0:512]),
                inc=True)
        p.wait("scalar", p.sems["vector"])
        p.s(lambda e: e.activation(brow.ap(0, 2, 0, T), brow.ap(0, 2, 0, T), AF.Sigmoid),
            inc=True)
        p.wait("vector", p.sems["scalar"])
        p.v(lambda e: e.tensor_mul(brow.ap(0, 2, 0, T), brow.ap(0, 2, 0, T),
                                   actrow.ap(0, 2, 0, T)), inc=True)
        for c0 in range(0, T, 512):
            p.wait("tensor", p.sems["vector"])
            p.t(lambda e, c0=c0: e.matmul(p.ps[4][:, 0:512], sb["sb_bd2A"][:, :],
                                          brow.ap(0, 2, c0, c0 + 512), start=True, stop=True))
            p.t(lambda e, c0=c0: e.matmul(p.ps[5][:, 0:512], sb["sb_bd2B"][:, :],
                                          brow.ap(0, 2, c0, c0 + 512), start=True, stop=True))
            tv = p.pe_done()
            p.wait("vector", p.sems["tensor"], tv)
            p.v(lambda e, c0=c0: e.tensor_copy(betaBA.ap(0, 128, c0, c0 + 512), p.ps[4][:, 0:512]))
            p.v(lambda e, c0=c0: e.tensor_copy(betaBB.ap(0, 128, c0, c0 + 512), p.ps[5][:, 0:512]),
                inc=True)
        p.v(lambda e, p4=p4: e.scalar_tensor_tensor(laf.ap(), spikeT.ap(0, 128, p4 * T, (p4 + 1) * T),
                                                    p.scol("l1_aspk", p4), abf.ap(),
                                                    ALU.mult, ALU.add), inc=True)
        p.wait("scalar", p.sems["vector"])
        p.s(lambda e: e.activation(laf.ap(), laf.ap(), AF.Sigmoid), inc=True)
        p.wait("vector", p.sems["scalar"])
        p.v(lambda e: e.tensor_scalar_add(laf.ap(), laf.ap(), -1.0), inc=True)
        for c0 in range(0, T, 512):
            p.wait("tensor", p.sems["vector"])
            p.t(lambda e, c0=c0: e.matmul(p.ps[4][:, 0:512], sb["sb_bd2"][:, :],
                                          actrow.ap(0, 2, c0, c0 + 512), start=True, stop=True))
            tv = p.pe_done()
            p.wait("vector", p.sems["tensor"], tv)
            p.v(lambda e, c0=c0: e.tensor_mul(laf.ap(0, 128, c0, c0 + 512),
                                              laf.ap(0, 128, c0, c0 + 512), p.ps[4][:, 0:512]),
                inc=True)
        p.v(lambda e: e.tensor_scalar_add(laf.ap(), laf.ap(), 1.0), inc=True)
        p.wait("scalar", p.sems["vector"])
        p.s(lambda e: e.activation(laf.ap(), laf.ap(), AF.Ln), inc=True)
        p.wait("vector", p.sems["scalar"])
        p.v(lambda e: e.memset(Spair.ap(), 0.0), inc=True)

        for c in range(DBG_L1_NCHUNK):
            t0 = c * C
            ST = DBG_L1_STEPS
            if "g" not in ST:
                continue
            p.vtiny(lambda e, t0=t0: e.tensor_tensor_scan(g32.ap(0, 128, 0, C),
                                                          sb["sb_ones"][:, 0:C],
                                                          laf.ap(0, 128, t0, t0 + C),
                                                          0.0, ALU.mult, ALU.add))
            p.vtiny(lambda e: e.tensor_single_scalar(argx.ap(0, 128, 0, C), g32.ap(0, 128, 0, C),
                                                     g32.ap(0, 128, 15, 16), ALU.subtract))
            p.vtiny(lambda e: e.tensor_scalar(argx.ap(0, 128, 0, C), argx.ap(0, 128, 0, C),
                                              BCL, -BCL, ALU.min, ALU.max))
            p.vtiny(lambda e: e.tensor_single_scalar(cols8.ap(0, 128, 3, 4), g32.ap(0, 128, 31, 32),
                                                     g32.ap(0, 128, 15, 16), ALU.subtract),
                    inc=True)
            p.wait("scalar", p.sems["vector"])
            p.stiny(lambda e: e.activation(eg.ap(0, 128, 0, C), argx.ap(0, 128, 0, C), AF.Exp))
            p.stiny(lambda e: e.activation(emg.ap(0, 128, 0, C), argx.ap(0, 128, 0, C), AF.Exp,
                                           scale=-1.0))
            p.stiny(lambda e: e.activation(cols8.ap(0, 128, 0, 1), g32.ap(0, 128, 15, 16), AF.Exp))
            p.stiny(lambda e: e.activation(cols8.ap(0, 128, 1, 2), cols8.ap(0, 128, 3, 4), AF.Exp),
                    inc=True)
            p.wait("vector", p.sems["scalar"])
            p.vtiny(lambda e: e.tensor_mul(cols8.ap(0, 128, 2, 3), cols8.ap(0, 128, 1, 2),
                                           cols8.ap(0, 128, 0, 1)))
            p.vtiny(lambda e, t0=t0: e.tensor_mul(egbA.ap(0, 128, 0, C), eg.ap(0, 128, 0, C),
                                                  betaBA.ap(0, 128, t0, t0 + C)))
            p.vtiny(lambda e, t0=t0: e.tensor_mul(egbB.ap(0, 128, 0, C), eg.ap(0, 128, 0, C),
                                                  betaBB.ap(0, 128, t0, t0 + C)))
            p.vtiny(lambda e, t0=t0: e.tensor_mul(vb64.ap(0, 128, 0, C), vfA.ap(0, 128, t0, t0 + C),
                                                  betaBA.ap(0, 128, t0, t0 + C)))
            p.vtiny(lambda e, t0=t0: e.tensor_mul(vb64.ap(0, 128, C, 2 * C),
                                                  vfB.ap(0, 128, t0, t0 + C),
                                                  betaBB.ap(0, 128, t0, t0 + C)))
            for (dstb, srcb, egxA, egxB) in ((bdk, kfb, egbA, egbB), (bdq, qfb, eg, eg),
                                             (bdkap, kfb, emg, emg)):
                p.vtiny(lambda e, dstb=dstb, srcb=srcb, egx=egxA, t0=t0: e.scalar_tensor_tensor(
                    dstb.ap(0, 128, 0, C), srcb.ap(0, 128, t0, t0 + C), p.scol("l1_mA"),
                    egx.ap(0, 128, 0, C), ALU.mult, ALU.mult))
                p.vtiny(lambda e, dstb=dstb, srcb=srcb, egx=egxB, t0=t0: e.scalar_tensor_tensor(
                    dstb.ap(0, 128, C, 2 * C), srcb.ap(0, 128, t0, t0 + C), p.scol("l1_mB"),
                    egx.ap(0, 128, 0, C), ALU.mult, ALU.mult))
            p.v(lambda e: e.tensor_single_scalar(SpSb.ap(), Spair.ap(), cols8.ap(0, 128, 0, 1),
                                                 ALU.mult))
            p.vtiny(lambda e: e.tensor_single_scalar(kap2.ap(0, 128, 0, 64),
                                                     bdkap.ap(0, 128, 0, 64),
                                                     cols8.ap(0, 128, 1, 2), ALU.mult), inc=True)
            if "L" not in ST:
                continue
            p.wait("tensor", p.sems["vector"])
            p.t(lambda e: e.matmul(p.ps[0][0:64, 0:64], bdk.ap(0, 128, 0, 64),
                                   bdkap.ap(0, 128, 0, 64), start=True, stop=True))
            p.t(lambda e: e.matmul(p.ps[0][0:64, 64:128], bdkap.ap(0, 128, 0, 64),
                                   bdk.ap(0, 128, 0, 64), start=True, stop=True))
            p.t(lambda e: e.matmul(p.ps[0][0:64, 128:192], bdkap.ap(0, 128, 0, 64),
                                   bdq.ap(0, 128, 0, 64), start=True, stop=True))
            p.t(lambda e: e.matmul(p.ps[3][0:64, 0:128], bdk.ap(0, 128, 0, 64), SpSb.ap(),
                                   start=True, stop=True))
            p.t(lambda e: e.transpose(p.ps[3][0:64, 128:256], vb64.ap(0, 128, 0, 2 * C),
                                      sb["sb_ident"][:, :]))
            p.t(lambda e: e.transpose(p.ps[1][0:64, 0:128], kap2.ap(0, 128, 0, 64),
                                      sb["sb_ident"][:, :]))
            tv = p.pe_done()
            p.wait("vector", p.sems["tensor"], tv)
            p.vtiny(lambda e: e.tensor_mul(BLb.ap(0, 64, 0, 64), p.ps[0][0:64, 0:64],
                                           sb["sb_l1stril"][0:64, 0:64]))
            p.vtiny(lambda e: e.tensor_mul(BLt.ap(0, 64, 0, 64), p.ps[0][0:64, 64:128],
                                           sb["sb_l1strius"][0:64, 0:64]))
            p.vtiny(lambda e: e.tensor_mul(Ptm.ap(0, 64, 0, 64), p.ps[0][0:64, 128:192],
                                           sb["sb_l1triu"][0:64, 0:64]))
            p.v(lambda e: e.tensor_copy(kaptb.ap(0, 64, 0, 128), p.ps[1][0:64, 0:128]))
            p.v(lambda e: e.tensor_copy(Wb.ap(0, 64, 0, 128), p.ps[3][0:64, 0:128]))
            p.v(lambda e: e.tensor_sub(Yb.ap(0, 64, 0, 128), p.ps[3][0:64, 128:256],
                                       Wb.ap(0, 64, 0, 128)), inc=True)
            p.wait("tensor", p.sems["vector"])
            p.t(lambda e: e.matmul(p.ps[2][0:64, 0:64], BLt.ap(0, 64, 0, 64), BLb.ap(0, 64, 0, 64),
                                   start=True, stop=True))
            p.t(lambda e: e.matmul(p.ps[2][0:64, 64:128], BLb.ap(0, 64, 0, 64),
                                   BLt.ap(0, 64, 0, 64), start=True, stop=True))
            tv = p.pe_done()
            p.wait("vector", p.sems["tensor"], tv)
            p.vtiny(lambda e: e.tensor_copy(Eb.ap(0, 64, 0, 64), p.ps[2][0:64, 0:64]))
            p.vtiny(lambda e: e.tensor_copy(Etb.ap(0, 64, 0, 64), p.ps[2][0:64, 64:128]),
                    inc=True)
            for m in range(4 if "newton" in ST else 0):
                p.wait("tensor", p.sems["vector"])
                p.t(lambda e: e.matmul(p.ps[4][0:64, 0:128], Etb.ap(0, 64, 0, 64),
                                       Yb.ap(0, 64, 0, 128), start=True, stop=True))
                if m < 3:
                    p.t(lambda e: e.matmul(p.ps[2][0:64, 0:64], Etb.ap(0, 64, 0, 64),
                                           Eb.ap(0, 64, 0, 64), start=True, stop=True))
                    p.t(lambda e: e.matmul(p.ps[2][0:64, 64:128], Eb.ap(0, 64, 0, 64),
                                           Etb.ap(0, 64, 0, 64), start=True, stop=True))
                tv = p.pe_done()
                p.wait("vector", p.sems["tensor"], tv)
                p.v(lambda e: e.tensor_add(Yb.ap(0, 64, 0, 128), Yb.ap(0, 64, 0, 128),
                                           p.ps[4][0:64, 0:128]), inc=(m == 3))
                if m < 3:
                    p.vtiny(lambda e: e.tensor_copy(Eb.ap(0, 64, 0, 64), p.ps[2][0:64, 0:64]))
                    p.vtiny(lambda e: e.tensor_copy(Etb.ap(0, 64, 0, 64), p.ps[2][0:64, 64:128]),
                            inc=True)
            if "out" not in ST:
                continue
            p.wait("tensor", p.sems["vector"])
            p.t(lambda e: e.matmul(p.ps[4][0:64, 0:128], BLt.ap(0, 64, 0, 64),
                                   Yb.ap(0, 64, 0, 128), start=True, stop=True))
            tv = p.pe_done()
            p.wait("vector", p.sems["tensor"], tv)
            p.v(lambda e: e.tensor_sub(Wb.ap(0, 64, 0, 128), Yb.ap(0, 64, 0, 128),
                                       p.ps[4][0:64, 0:128]), inc=True)
            p.wait("tensor", p.sems["vector"])
            p.t(lambda e: e.matmul(p.ps[5][0:64, 0:128], bdq.ap(0, 128, 0, 64), SpSb.ap(),
                                   start=True, stop=True))
            p.t(lambda e: e.matmul(p.ps[4][0:64, 0:128], Ptm.ap(0, 64, 0, 64),
                                   Wb.ap(0, 64, 0, 128), start=True, stop=True))
            p.t(lambda e: e.matmul(p.ps[6][0:128, 0:128], kaptb.ap(0, 64, 0, 128),
                                   Wb.ap(0, 64, 0, 128), start=True, stop=True))
            tv = p.pe_done()
            p.wait("vector", p.sems["tensor"], tv)
            p.v(lambda e: e.tensor_copy(Yb.ap(0, 64, 0, 128), p.ps[5][0:64, 0:128]))
            p.v(lambda e: e.tensor_add(Yb.ap(0, 64, 0, 128), Yb.ap(0, 64, 0, 128),
                                       p.ps[4][0:64, 0:128]))
            p.v(lambda e: e.scalar_tensor_tensor(Spair.ap(), Spair.ap(), cols8.ap(0, 128, 2, 3),
                                                 p.ps[6][0:128, 0:128], ALU.mult, ALU.add),
                inc=True)
            p.wait("tensor", p.sems["vector"])
            p.t(lambda e: e.transpose(p.ps[6][0:128, 128:192], Yb.ap(0, 64, 0, 128),
                                      sb["sb_ident"][0:64, 0:64]))
            tv = p.pe_done()
            p.wait("vector", p.sems["tensor"], tv)
            p.vtiny(lambda e, t0=t0: e.tensor_copy(ystg[0].ap(0, 128, t0, t0 + C),
                                                   p.ps[6][:, 128:160]))
            p.vtiny(lambda e, t0=t0: e.tensor_copy(ystg[1].ap(0, 128, t0, t0 + C),
                                                   p.ps[6][:, 160:192]), inc=True)
        p.wait("sync", p.sems["vector"])
        p.dma(scr["y"][(2 * p4) * 128:(2 * p4 + 1) * 128, :], ystg[0].ap())
        p.dma(scr["y"][(2 * p4 + 1) * 128:(2 * p4 + 2) * 128, :], ystg[1].ap())
    p.wait("vector", p.sem_dma)
    p.wait("sync", p.sem_dma, p.sem_dma.n)
    if DBG_L1 == "delta":
        fa.reset(famark); ba.reset(bamark)
        return

    fa.reset(nh.off)
    sqt2 = fa.alloc(T, "sqt2")
    rowt2 = fa.alloc(T, "rowt2")
    ba.reset(sdT.off)
    ytl = ba.alloc(8 * T, "ytl")
    ybuf = ba.alloc(T, "ybuf")
    for hh in range(8):
        p.wait("sync", p.sems["vector"])
        p.wait("sync", p.sems["scalar"])
        p.wait("sync", p.sems["tensor"])
        dvy = p.dma(ybuf.ap(), scr["y"][hh * 128:(hh + 1) * 128, :])
        p.wait("scalar", p.sem_dma, dvy)
        p.wait("scalar", p.sems["vector"])
        p.s(lambda e: e.activation(sqt2.ap(), ybuf.ap(), AF.Square), inc=True)
        p.wait("tensor", p.sems["scalar"])
        for c0 in range(0, T, 512):
            p.wait("tensor", p.sems["vector"])
            p.t(lambda e, c0=c0: e.matmul(p.ps[5][0:1, 0:512], sb["sb_ones"][:, 0:1],
                                          sqt2.ap(0, 128, c0, c0 + 512), start=True, stop=True))
            tv = p.pe_done()
            p.wait("vector", p.sems["tensor"], tv)
            p.v(lambda e, c0=c0: e.tensor_copy(rowt2.ap(0, 1, c0, c0 + 512), p.ps[5][0:1, 0:512]),
                inc=True)
        p.wait("scalar", p.sems["vector"])
        p.s(lambda e: e.activation(rowt2.ap(0, 1, 0, T), rowt2.ap(0, 1, 0, T), AF.Sqrt,
                                   scale=float(1.0 / 128.0), bias=p.scol("k_eps6")[0:1, 0:1]),
            inc=True)
        p.wait("vector", p.sems["scalar"])
        p.v(lambda e: e.reciprocal(rowt2.ap(0, 1, 0, T), rowt2.ap(0, 1, 0, T)), inc=True)
        for c0 in range(0, T, 512):
            p.wait("tensor", p.sems["vector"])
            p.t(lambda e, c0=c0: e.matmul(p.ps[4][:, 0:512], sb["sb_ones"][0:1, :],
                                          rowt2.ap(0, 1, c0, c0 + 512), start=True, stop=True))
            tv = p.pe_done()
            p.wait("vector", p.sems["tensor"], tv)
            p.v(lambda e, c0=c0, hh=hh: e.scalar_tensor_tensor(
                ytl.ap(0, 128, hh * T + c0, hh * T + c0 + 512), ybuf.ap(0, 128, c0, c0 + 512),
                p.scol("l1_whn", hh), p.ps[4][:, 0:512], ALU.mult, ALU.mult), inc=True)
    if DBG_L1 == "y":
        p.wait("sync", p.sems["vector"])
        for hh in range(8):
            p.dma(sb["dbg"][hh * 128:(hh + 1) * 128, :], ytl.ap(0, 128, hh * T, (hh + 1) * T))
        p.wait("sync", p.sem_dma, p.sem_dma.n)
        fa.reset(famark); ba.reset(bamark)
        return

    au2 = ba.alloc(T, "au2")
    xsl = ba.alloc(8 * 512, "xsl")
    xbuf = ba.alloc(T, "xbuf")
    y2stg = [ba.alloc(T, "y2A"), ba.alloc(T, "y2B")]
    wbf64b = ba.alloc(128, "wu2bf")
    wbfL = ba.alloc(1024, "woutbf")
    wst64b = fa.alloc(128, "wu2st")
    wstL = fa.alloc(1024, "woutst")
    t1 = fa.alloc(512, "t1")
    gbuf = abf

    def evac_au2(mi, c0, c1, pband, tv):
        p.wait("scalar", p.sems["tensor"], tv)
        p.s(lambda e, c0=c0, c1=c1, pband=pband: e.activation(
            au2.ap(0, 128, c0, c1), pband[:, 0:c1 - c0], AF.Silu,
            bias=p.scol("l1_bu1", 0)), inc=True)
    for c0 in range(0, T, 512):
        p.wait("sync", p.sems["tensor"])
        dvx = load_xn_slice(p, xn_dram, xsl, c0, c0 + 512)
        p.wait("tensor", p.sem_dma, dvx)
        emit_linear(p, "l1_wu1", 1, D,
                    lambda k, cc0, cc1, c0=c0: xsl.ap(0, 128, k * 512 + (cc0 - c0), k * 512 + (cc1 - c0)),
                    c0, c0 + 512, evac_au2)

    y2_dv = [0, 0]
    for mi in range(8):
        p.wait("sync", p.sems["vector"])
        dv = p.dma(wst64b.ap(0, 64, 0, 128), sb["l1_wu2"][mi, 0:64])
        p.wait("vector", p.sem_dma, dv)
        p.v(lambda e: e.tensor_copy(wbf64b.ap(0, 64, 0, 128), wst64b.ap(0, 64, 0, 128)), inc=True)
        p.wait("tensor", p.sems["vector"])
        p.wait("tensor", p.sems["scalar"])
        for c0 in range(0, T, 512):
            p.wait("tensor", p.sems["scalar"])
            p.t(lambda e, c0=c0: e.matmul(p.ps[5][:, 0:512], wbf64b.ap(0, 64, 0, 128),
                                          au2.ap(0, 64, c0, c0 + 512), start=True, stop=True))
            tv = p.pe_done()
            p.wait("scalar", p.sems["tensor"], tv)
            p.wait("scalar", p.sems["vector"])
            p.s(lambda e, c0=c0, mi=mi: e.activation(gbuf.ap(0, 128, c0, c0 + 512),
                                                     p.ps[5][:, 0:512], AF.Sigmoid,
                                                     bias=p.scol("l1_bu2", mi)), inc=True)
        p.wait("sync", p.sems["vector"])
        dvx = p.dma(xbuf.ap(), xn_dram[mi * 128:(mi + 1) * 128, :])
        dvw = p.dma(wstL.ap().rearrange("p (k c) -> p k c", c=128),
                    sb["l1_wout"][mi].rearrange("(k p) c -> p k c", p=128))
        p.wait("vector", p.sem_dma, dvw)
        p.wait("vector", p.sems["tensor"])
        p.v(lambda e: e.tensor_copy(wbfL.ap(), wstL.ap()), inc=True)
        p.wait("tensor", p.sems["vector"])
        sidx = mi % 2
        for ci, c0 in enumerate(range(0, T, 512)):
            pband = p.ps[ci % 2]
            p.wait("tensor", p.sems["vector"])
            for k in range(8):
                p.t(lambda e, pband=pband, k=k, c0=c0: e.matmul(
                    pband[:, 0:512], wbfL.ap(0, 128, k * 128, (k + 1) * 128),
                    ytl.ap(0, 128, k * T + c0, k * T + c0 + 512), start=(k == 0), stop=(k == 7)))
            tv = p.pe_done()
            p.wait("vector", p.sems["tensor"], tv)
            p.wait("vector", p.sem_dma, dvx)
            p.wait("vector", p.sems["scalar"])
            if c0 == 0:
                p.wait("vector", p.sem_dma, y2_dv[sidx])
            p.v(lambda e, c0=c0, pband=pband: e.tensor_mul(t1.ap(0, 128, 0, 512), pband[:, 0:512],
                                                           gbuf.ap(0, 128, c0, c0 + 512)))
            p.v(lambda e, c0=c0: e.tensor_add(t1.ap(0, 128, 0, 512), t1.ap(0, 128, 0, 512),
                                              xbuf.ap(0, 128, c0, c0 + 512)))
            p.v(lambda e, mi=mi, c0=c0: e.tensor_add(
                xacc.ap(0, 128, mi * T + c0, mi * T + c0 + 512),
                xacc.ap(0, 128, mi * T + c0, mi * T + c0 + 512), t1.ap(0, 128, 0, 512)))
            p.v(lambda e, c0=c0, sidx=sidx: e.tensor_copy(y2stg[sidx].ap(0, 128, c0, c0 + 512),
                                                          t1.ap(0, 128, 0, 512)), inc=True)
        p.wait("sync", p.sems["vector"])
        y2_dv[sidx] = p.dma(scr["y2"][mi * 128:(mi + 1) * 128, :], y2stg[sidx].ap())
    p.wait("vector", p.sem_dma)
    if DBG_L1 == "gate":
        fa.reset(famark); ba.reset(bamark)
        return

    p.wait("sync", p.sem_dma, p.sem_dma.n)
    for r in range(NRC):
        p.wait("sync", p.sems["scalar"])
        p.wait("sync", p.sems["vector"])
        dvx = p.dma(xbuf.ap(), scr["y2"][r * 128:(r + 1) * 128, :])
        p.wait("scalar", p.sem_dma, dvx)
        p.wait("scalar", p.sems["tensor"])
        p.wait("scalar", p.sems["vector"])
        p.s(lambda e: e.activation(sqt2.ap(), xbuf.ap(), AF.Square), inc=True)
        p.wait("tensor", p.sems["scalar"])
        if r == 0:
            p.wait("tensor", p.sems["vector"])
        for j in range(4):
            p.t(lambda e, j=j, r=r: e.matmul(p.ps[3 + j][0:1, 0:512], sb["sb_ones"][:, 0:1],
                                             sqt2.ap(0, 128, j * 512, (j + 1) * 512),
                                             start=(r == 0), stop=(r == NRC - 1)))
        p.pe_done()
    tvs = p.sems["tensor"].n
    p.wait("vector", p.sems["tensor"], tvs)
    for j in range(4):
        p.v(lambda e, j=j: e.tensor_copy(rowt2.ap(0, 1, j * 512, (j + 1) * 512),
                                         p.ps[3 + j][0:1, 0:512]), inc=(j == 3))
    p.wait("scalar", p.sems["vector"])
    p.s(lambda e: e.activation(rowt2.ap(0, 1, 0, T), rowt2.ap(0, 1, 0, T), AF.Sqrt,
                               scale=float(1.0 / D), bias=p.scol("k_eps6")[0:1, 0:1]), inc=True)
    p.wait("vector", p.sems["scalar"])
    p.v(lambda e: e.reciprocal(rowt2.ap(0, 1, 0, T), rowt2.ap(0, 1, 0, T)), inc=True)
    bcast_row(p, lambda c0, c1: rowt2.ap(0, 1, c0, c1), T, bcB)
    ba.reset(sdT.off)
    h2 = ba.alloc(8 * T, "h2")
    xbuf2 = ba.alloc(T, "xbuf2")
    stmp = [ba.alloc(T, "stmpA"), ba.alloc(T, "stmpB")]
    for r in range(NRC):
        p.wait("sync", p.sems["vector"])
        dvx = p.dma(xbuf2.ap(), scr["y2"][r * 128:(r + 1) * 128, :])
        p.wait("vector", p.sem_dma, dvx)
        p.v(lambda e, r=r: e.scalar_tensor_tensor(h2.ap(0, 128, r * T, (r + 1) * T), xbuf2.ap(),
                                                  p.scol("l1_wffn", r), bcB.ap(),
                                                  ALU.mult, ALU.mult), inc=True)

    stmp_dv = [0, 0]

    def evac_ff13(mi, c0, c1, pband, tv):
        w = c1 - c0
        par = mi % 2
        if mi < 32:
            p.wait("scalar", p.sems["tensor"], tv)
            if c0 == 0:
                p.wait("scalar", p.sem_dma, stmp_dv[par])
                p.wait("scalar", p.sems["vector"])
            p.s(lambda e, c0=c0, c1=c1, pband=pband, par=par, w=w: e.activation(
                stmp[par].ap(0, 128, c0, c1), pband[:, 0:w], AF.Silu), inc=True)
        else:
            p.wait("vector", p.sems["tensor"], tv)
            p.wait("vector", p.sems["scalar"])
            p.v(lambda e, c0=c0, c1=c1, pband=pband, par=par, w=w: e.tensor_mul(
                stmp[par].ap(0, 128, c0, c1), stmp[par].ap(0, 128, c0, c1), pband[:, 0:w]),
                inc=True)
            if c1 == T:
                p.wait("sync", p.sems["vector"])
                stmp_dv[par] = p.dma(scr["s"][(mi - 32) * 128:(mi - 31) * 128, :], stmp[par].ap())
    m_list = []
    for j in range(32):
        m_list += [j, j + 32]
    emit_linear(p, "l1_ff13", 64, D, lambda k, c0, c1: h2.ap(0, 128, k * T + c0, k * T + c1),
                0, T, evac_ff13, m_list=m_list)
    p.wait("vector", p.sem_dma)
    p.wait("scalar", p.sem_dma)

    ba.reset(sdT.off)
    sfb = ba.alloc(32 * 512, "sfb")
    p.wait("sync", p.sem_dma, p.sem_dma.n)

    def evac_ff2(mi, c0, c1, pband, tv):
        p.wait("vector", p.sems["tensor"], tv)
        p.v(lambda e, mi=mi, c0=c0, c1=c1, pband=pband: e.tensor_add(
            xacc.ap(0, 128, mi * T + c0, mi * T + c1),
            xacc.ap(0, 128, mi * T + c0, mi * T + c1), pband[:, 0:c1 - c0]), inc=True)
    for c0 in range(0, T, 512):
        p.wait("sync", p.sems["tensor"])
        for k in range(32):
            p.dma(sfb.ap(0, 128, k * 512, (k + 1) * 512),
                  scr["s"][k * 128:(k + 1) * 128, c0:c0 + 512])
        dvs = p.sem_dma.n
        p.wait("tensor", p.sem_dma, dvs)
        emit_linear(p, "l1_ff2", 8, 4096,
                    lambda k, cc0, cc1, c0=c0: sfb.ap(0, 128, k * 512 + (cc0 - c0), k * 512 + (cc1 - c0)),
                    c0, c0 + 512, evac_ff2)
    fa.reset(famark)
    ba.reset(bamark)


def emit_l1_ffn(p, xacc, xn_dram, negmuB, rstdB, scr):
    sb = p.sb
    fa, ba = p.fa, p.ba
    famark, bamark = fa.mark(), ba.mark()

    xnb = ba.alloc(NRC * T, "xnb")
    he = ba.alloc(8 * T, "he")
    p.wait("sync", p.sems["tensor"])
    p.wait("sync", p.sems["vector"])
    p.wait("sync", p.sems["scalar"])
    for r in range(NRC):
        p.dma(xnb.ap(0, 128, r * T, (r + 1) * T), xn_dram[r * 128:(r + 1) * 128, :])
    dvx = p.sem_dma.n
    p.wait("tensor", p.sem_dma, dvx)

    lgt = fa.alloc(T, "lgt")
    selx = fa.alloc(T, "selx")
    srow0 = fa.alloc(64, "srow0")

    def evac_lgt(mi, c0, c1, pband, tv):
        p.wait("vector", p.sems["tensor"], tv)
        p.v(lambda e, c0=c0, c1=c1, pband=pband: e.tensor_copy(lgt.ap(0, 128, c0, c1),
                                                               pband[:, 0:c1 - c0]), inc=True)
    emit_linear(p, "l1f_gate", 1, D, lambda k, c0, c1: xnb.ap(0, 128, k * T + c0, k * T + c1),
                0, T, evac_lgt)

    s0 = lambda a, b: srow0.ap(0, 128, a, b)
    for t0 in range(0, T, 128):
        pb = p.ps[6]
        p.wait("tensor", p.sems["vector"])
        p.t(lambda e, t0=t0, pb=pb: e.transpose(pb[:, 0:128], lgt.ap(0, 128, t0, t0 + 128),
                                                sb["sb_ident"][:]))
        tv = p.pe_done()
        p.wait("vector", p.sems["tensor"], tv)
        p.vtiny(lambda e, pb=pb: e.tensor_copy(s0(0, 8), pb[:, 0:8]))
        p.vtiny(lambda e: e.reduce_max(s0(8, 9), s0(0, 8), axis=AX.X))
        p.vtiny(lambda e: e.tensor_single_scalar(s0(0, 8), s0(0, 8), s0(8, 9), ALU.subtract),
                inc=True)
        p.wait("scalar", p.sems["vector"])
        p.stiny(lambda e: e.activation(s0(0, 8), s0(0, 8), AF.Exp), inc=True)
        p.wait("vector", p.sems["scalar"])
        p.vtiny(lambda e: e.reduce_sum(s0(9, 10), s0(0, 8), axis=AX.X))
        p.vtiny(lambda e: e.reciprocal(s0(9, 10), s0(9, 10)))
        p.vtiny(lambda e: e.reduce_max(s0(10, 11), s0(0, 8), axis=AX.X))
        p.vtiny(lambda e: e.tensor_single_scalar(s0(16, 24), s0(0, 8), s0(10, 11), ALU.is_ge))
        p.vtiny(lambda e: e.tensor_tensor_scan(s0(24, 32), sb["sb_ones"][:, 0:8], s0(16, 24),
                                               0.0, ALU.mult, ALU.add))
        p.vtiny(lambda e: e.tensor_single_scalar(s0(24, 32), s0(24, 32), 1.0, ALU.is_equal))
        p.vtiny(lambda e: e.tensor_mul(s0(32, 40), s0(16, 24), s0(24, 32)))
        p.vtiny(lambda e: e.scalar_tensor_tensor(s0(48, 56), s0(32, 40), -1e30, s0(0, 8),
                                                 ALU.mult, ALU.add))
        p.vtiny(lambda e: e.reduce_max(s0(11, 12), s0(48, 56), axis=AX.X))
        p.vtiny(lambda e: e.tensor_single_scalar(s0(16, 24), s0(48, 56), s0(11, 12), ALU.is_ge))
        p.vtiny(lambda e: e.tensor_tensor_scan(s0(24, 32), sb["sb_ones"][:, 0:8], s0(16, 24),
                                               0.0, ALU.mult, ALU.add))
        p.vtiny(lambda e: e.tensor_single_scalar(s0(24, 32), s0(24, 32), 1.0, ALU.is_equal))
        p.vtiny(lambda e: e.tensor_mul(s0(40, 48), s0(16, 24), s0(24, 32)))
        p.vtiny(lambda e: e.tensor_add(s0(32, 40), s0(32, 40), s0(40, 48)))
        p.vtiny(lambda e: e.tensor_mul(s0(32, 40), s0(32, 40), s0(0, 8)))
        p.vtiny(lambda e: e.tensor_single_scalar(s0(32, 40), s0(32, 40), s0(9, 10), ALU.mult),
                inc=True)
        p.wait("tensor", p.sems["vector"])
        p.t(lambda e, pb=pb: e.transpose(pb[0:8, 0:128], s0(32, 40), sb["sb_ident"][:]))
        tv = p.pe_done()
        p.wait("vector", p.sems["tensor"], tv)
        p.v(lambda e, t0=t0, pb=pb: e.tensor_copy(selx.ap(0, 8, t0, t0 + 128), pb[0:8, 0:128]),
            inc=True)
    p.wait("sync", p.sems["vector"])
    dvsel = p.dma(scr["sel"][:, :], selx.ap(0, 8, 0, T))
    p.wait("vector", p.sem_dma)
    p.wait("sync", p.sem_dma, dvsel)
    fa.reset(lgt.off)
    selB = fa.alloc(T, "selB")
    srow = fa.alloc(T, "srow")
    tmp = fa.alloc(512, "l1ftmp")

    for pass_i in range(9):
        e_i = pass_i if pass_i < 8 else None
        if e_i is not None:
            p.wait("sync", p.sems["vector"])
            p.wait("sync", p.sems["tensor"])
            dvs = p.dma(srow.ap(0, 1, 0, T), scr["sel"][e_i:e_i + 1, :])
            p.wait("vector", p.sem_dma, dvs)
            p.wait("tensor", p.sem_dma, dvs)
            bcast_row(p, lambda c0, c1: srow.ap(0, 1, c0, c1), T, selB)
            w13, w2 = f"l1f_w13_{e_i}", f"l1f_w2_{e_i}"
            b13, b2 = f"l1f_b13_{e_i}", f"l1f_b2_{e_i}"
        else:
            p.wait("vector", p.sems["tensor"])
            p.v(lambda e: e.memset(selB.ap(), 1.0), inc=True)
            w13, w2 = "l1f_sw13", "l1f_sw2"
            b13, b2 = "l1f_sb13", "l1f_sb2"

        def evac13(mi, c0, c1, pband, tv, b13=b13):
            w = c1 - c0
            if mi < 8:
                p.wait("scalar", p.sems["tensor"], tv)
                p.s(lambda e, mi=mi, c0=c0, c1=c1, pband=pband, w=w: e.activation(
                    he.ap(0, 128, mi * T + c0, mi * T + c1), pband[:, 0:w], AF.Silu,
                    bias=p.scol(b13, mi)), inc=True)
            else:
                p.wait("vector", p.sems["tensor"], tv)
                p.wait("vector", p.sems["scalar"])
                p.v(lambda e, mi=mi, c0=c0, c1=c1, pband=pband, w=w: e.scalar_tensor_tensor(
                    he.ap(0, 128, (mi - 8) * T + c0, (mi - 8) * T + c1), pband[:, 0:w],
                    p.scol(b13, mi), he.ap(0, 128, (mi - 8) * T + c0, (mi - 8) * T + c1),
                    ALU.add, ALU.mult), inc=True)
        m_list = []
        for j in range(8):
            m_list += [j, j + 8]
        emit_linear(p, w13, 16, D, lambda k, c0, c1: xnb.ap(0, 128, k * T + c0, k * T + c1),
                    0, T, evac13, m_list=m_list)

        def evac_o(mi, c0, c1, pband, tv, b2=b2):
            w = c1 - c0
            p.wait("vector", p.sems["tensor"], tv)
            p.v(lambda e, mi=mi, c0=c0, c1=c1, pband=pband, w=w: e.scalar_tensor_tensor(
                tmp.ap(0, 128, 0, w), pband[:, 0:w], p.scol(b2, mi), selB.ap(0, 128, c0, c1),
                ALU.add, ALU.mult))
            p.v(lambda e, mi=mi, c0=c0, c1=c1, w=w: e.tensor_add(
                xacc.ap(0, 128, mi * T + c0, mi * T + c1),
                xacc.ap(0, 128, mi * T + c0, mi * T + c1), tmp.ap(0, 128, 0, w)), inc=True)
        emit_linear(p, w2, 8, 1024, lambda k, c0, c1: he.ap(0, 128, k * T + c0, k * T + c1),
                    0, T, evac_o)
    fa.reset(famark)
    ba.reset(bamark)



def build_program(H, SM, layers):
    nc = bass.Bass("TRN2", target_bir_lowering=False, debug=False, detect_race_conditions=False)
    sb = {"_smoff": SM.off}
    for name, arr in H.items():
        dt = BF16 if arr.dtype.name == "bfloat16" else F32
        sb[name] = nc.dram_tensor(name, list(arr.shape), dt, kind="ExternalInput").ap()
    xin = nc.dram_tensor("xT", [D, T], F32, kind="ExternalInput").ap()
    xout = nc.dram_tensor("outT", [D, T], F32, kind="ExternalOutput").ap()
    sb["dbg"] = nc.dram_tensor("dbg", [2048, T], BF16, kind="ExternalOutput").ap()
    scr = {
        "xn": nc.dram_tensor("scr_xn", [D, T], BF16, kind="Internal").ap(),
        "xnf": nc.dram_tensor("scr_xnf", [D, T], F32, kind="Internal").ap(),
        "w1q": nc.dram_tensor("scr_w1q", [32, D, 128], BF16, kind="Internal").ap(),
        "w2q": nc.dram_tensor("scr_w2q", [8, 4096, 128], BF16, kind="Internal").ap(),
        "s": nc.dram_tensor("scr_s", [4096, T], BF16, kind="Internal").ap(),
        "sel": nc.dram_tensor("scr_sel", [8, T], F32,
                              kind=("ExternalOutput" if DBG_SEL_OUT else "Internal")).ap(),
    }
    for nm_, shp_, dt_ in (("q", [512, T], BF16), ("k", [512, T], BF16),
                           ("v", [1024, T], BF16), ("ab", [512, T], F32),
                           ("bb", [8, T], F32), ("y", [1024, T], BF16),
                           ("y2", [1024, T], BF16)):
        kind = "ExternalOutput" if nm_ in DBG_SCR_OUT else "Internal"
        scr[nm_] = nc.dram_tensor("scr_" + nm_, shp_, dt_, kind=kind).ap()

    p = P(nc, sb)
    for cname, dt, shape in (("c_ident", F32, [128, 128]), ("c_identb", BF16, [128, 128]),
                             ("c_ones", F32, [128, 128])):
        h = nc.alloc_sbuf_tensor("sb" + cname[1:], shape, dt)
        sb["sb" + cname[1:]] = h
        p.dma(h[:, :], sb[cname])
    hbd2 = nc.alloc_sbuf_tensor("sb_bd2", [2, 128], F32)
    sb["sb_bd2"] = hbd2
    p.dma(hbd2[:, :], sb["c_bd2"])
    for cn in ("c_bd2A", "c_bd2B"):
        hh = nc.alloc_sbuf_tensor("sb" + cn[1:], [2, 128], F32)
        sb["sb" + cn[1:]] = hh
        p.dma(hh[:, :], sb[cn])
    for cn in ("c_l1stril", "c_l1triu", "c_l1strius"):
        hh = nc.alloc_sbuf_tensor("sb" + cn[1:], [64, 64], F32)
        sb["sb" + cn[1:]] = hh
        p.dma(hh[:, :], sb[cn])
    hsm = nc.alloc_sbuf_tensor("sb_smalls", [128, H["c_smalls"].shape[1]], F32)
    sb["sb_smalls"] = hsm
    p.dma(hsm[:, :], sb["c_smalls"])

    p.onesb = p.ba.alloc(2, "onesb")
    p.vtiny(lambda e: e.memset(p.onesb.ap(), 1.0), inc=True)
    xacc = p.fa.alloc(NRC * T, "xacc")
    negmuB = p.fa.alloc(T, "negmu")
    rstdB = p.fa.alloc(T, "rstd")
    for r in range(NRC):
        p.dma(xacc.ap(0, 128, r * T, (r + 1) * T), xin[r * 128:(r + 1) * 128, :])
    dv = p.sem_dma.n
    for e in ("vector", "scalar", "tensor"):
        p.wait(e, p.sem_dma, dv)

    sb["_scr_xnf"] = scr["xnf"]
    for L in layers:
        emit_ln(p, xacc, scr["xn"], negmuB, rstdB)
        if L == 0:
            if DBG_STOP != "ln":
                emit_l0_attn(p, xacc, scr["xn"])
                if DBG_STOP in (None, "ffn", "full"):
                    emit_l0_ffn(p, xacc, scr)
        elif L == 1:
            if DBG_STOP != "ln":
                if DBG_STOP != "l1ffn":
                    emit_l1_attn(p, xacc, scr["xn"], negmuB, rstdB, scr)
                if DBG_STOP in (None, "ffn", "full", "l1ffn"):
                    emit_l1_ffn(p, xacc, scr["xn"], negmuB, rstdB, scr)
        elif L == 2:
            if DBG_STOP != "ln":
                emit_l2_attn(p, xacc, scr["xn"])
                if DBG_STOP in (None, "ffn", "full"):
                    emit_l2_ffn(p, xacc, scr["xn"], negmuB, rstdB, scr)
        elif L == 3:
            if DBG_STOP != "ln":
                emit_l3_attn(p, xacc, scr["xn"], negmuB, rstdB, scr)
                if DBG_STOP in (None, "ffn", "full"):
                    emit_l3_ffn(p, xacc, scr["xn"], negmuB, rstdB, scr)
        else:
            raise NotImplementedError(f"layer {L}")
        p.barrier()

    p.barrier()
    for e in ("vector", "scalar", "tensor"):
        pass
    p.wait("sync", p.sems["vector"])
    p.wait("sync", p.sems["scalar"])
    p.wait("sync", p.sems["tensor"])
    for r in range(NRC):
        p.dma(xout[r * 128:(r + 1) * 128, :], xacc.ap(0, 128, r * T, (r + 1) * T))
    p.wait("sync", p.sem_dma, p.sem_dma.n)
    p.finalize()
    return nc


def run_layers(inputs, layers, x_override=None, cores=8):
    H, SM = host_prep(inputs)
    nc = build_program(H, SM, layers)
    x = np.asarray(inputs["L0_attn_x"], np.float32) if x_override is None else x_override
    in_maps = []
    for c in range(cores):
        m = dict(H)
        m["xT"] = np.ascontiguousarray(x[c].T)
        in_maps.append(m)
    res = run_bass_kernel_spmd(nc, in_maps, core_ids=list(range(cores)))
    out = np.stack([np.ascontiguousarray(r["outT"].T) for r in res.results], axis=0)
    return out


def kernel(**inputs):
    return run_layers(inputs, [0, 1, 2, 3]).astype(np.float32)
